# version 05
# baseline (speedup 1.0000x reference)
import contextlib
import numpy as np
import ml_dtypes
import concourse.bass as bass
import concourse.mybir as mybir
from concourse.bass_utils import run_bass_kernel_spmd

F32 = mybir.dt.float32
BF16 = mybir.dt.bfloat16
AF = mybir.ActivationFunctionType
ALU = mybir.AluOpType

L = 2048
DM = 1024
Q = 128
NCH = L // Q
NH = 32
P = 64
NST = 64
DIN = 2048
CCH = DIN + 2 * NST
DTC = NH
DPROJ = 2 * DIN + 2 * NST + NH
NKT = DM // 128
EPS_LN = 1e-5
EPS_RMS = 1e-5


class Sem:
    def __init__(self, stack, nc, name):
        self.h = stack.enter_context(nc.semaphore(name))
        self.n = 0

    def inc(self, instr, k=1):
        instr.then_inc(self.h, k)
        self.n += k
        return instr


def wait(eng, *sems):
    for s in sems:
        if s.n:
            eng.wait_ge(s.h, s.n)


def build(debug=False, stages=5, pha_parts=9):
    nc = bass.Bass("TRN2", target_bir_lowering=False)
    dram = {}

    def din(name, shape, dt=F32):
        dram[name] = nc.dram_tensor(name, shape, dt, kind="ExternalInput").ap()
        return dram[name]

    def dout(name, shape, dt=F32):
        dram[name] = nc.dram_tensor(name, shape, dt, kind="ExternalOutput").ap()
        return dram[name]

    def dint(name, shape, dt=BF16):
        dram[name] = nc.dram_tensor(name, shape, dt, kind="Internal").ap()
        return dram[name]

    x_in = din("x", [L, DM])
    xrev_in = din("xrev", [L, DM])
    win = {d: din(f"win_{d}", [DM, DPROJ]) for d in "fb"}
    wout = {d: din(f"wout_{d}", [DIN, DM]) for d in "fb"}
    convw = {d: din(f"convw_{d}", [128, 4 * 17]) for d in "fb"}
    convb = {d: din(f"convb_{d}", [128, 17]) for d in "fb"}
    dtb4 = {d: din(f"dtb4_{d}", [128, 1]) for d in "fb"}
    alog4 = {d: din(f"alog4_{d}", [128, 1]) for d in "fb"}
    drep = {d: din(f"drep_{d}", [128, DIN]) for d in "fb"}
    nwrep = {d: din(f"nwrep_{d}", [128, DIN]) for d in "fb"}
    wo_in = din("wo", [DIN, DM])
    bo_in = din("bo", [128, DM])
    wt_in = din("wt", [DM, DM])
    ffb_in = din("ffb", [1, DM])
    vr_in = din("vr", [DM, 32])
    ct_in = din("ct", [32, DM])
    if32_in = din("if32", [128, 128])
    ibf_in = din("ibf", [128, 128], BF16)

    y_out = dout("y", [L, DM])

    xnh = {"f": dint("xnh", [L, DM]), "b": dint("xnrevh", [L, DM])}
    szh = dint("szh", [L, DIN])
    odh = {"f": dint("odhf", [L, DM]), "b": dint("odhb", [L, DM])}

    dbg = {}
    if debug:
        for nm, shp, dt in [
            ("t_xn", [128, DM], BF16), ("t_sz", [128, DIN], BF16),
            ("t_xsbc", [128, CCH], BF16), ("t_dtsp", [128, 512], F32),
            ("t_a4", [128, 512], F32), ("t_ets", [128, 64], F32),
            ("t_mt", [128, 2048], BF16), ("t_xh", [128, 1024], BF16),
            ("t_q", [128, 1024], BF16), ("t_qh", [128, DIN], BF16),
            ("t_wc", [128, 32], F32), ("t_bw", [128, 1024], BF16),
            ("t_od", [128, DM], BF16), ("t_H", [64, 2048], F32),
            ("t_x2", [128, DM], F32),
        ]:
            dbg[nm] = dout(nm, shp, dt)

    st = contextlib.ExitStack()
    with st:
        sb = lambda name, shape, dt: st.enter_context(nc.sbuf_tensor(name, shape, dt))
        xnfm = sb("xnfm", [128, NKT * L], BF16)
        winsb = sb("winsb", [128, NKT * CCH], BF16)
        windt = sb("windt", [128, NKT * DTC], BF16)
        woutsb = sb("woutsb", [128, 16 * DM], BF16)
        wtsb = sb("wtsb", [128, NKT * DM], BF16)
        wfm = sb("wfm", [128, 4 * 17], F32)
        cbfm = sb("cbfm", [128, 17], F32)
        nwsb = sb("nwsb", [128, DIN], BF16)
        drsb = sb("drsb", [128, DIN], BF16)
        dtbsb = sb("dtbsb", [128, 1], F32)
        alogsb = sb("alogsb", [128, 1], F32)
        acol = sb("acol", [128, 1], F32)
        if32 = sb("if32_s", [128, 128], F32)
        ibf = sb("ibf_s", [128, 128], BF16)
        ones32 = sb("ones32", [1, 128], F32)
        onesbf = sb("onesbf", [1, 128], BF16)
        ssqall = sb("ssqall", [128, 2 * NCH], F32)
        rall = sb("rall", [128, 2 * NCH], F32)
        dtsp4 = sb("dtsp4", [128, 512], F32)
        s4 = sb("s4", [128, 512], F32)
        dts = sb("dts", [128, NCH * 96], F32)
        eslr = sb("eslr", [64, NCH * 32], F32)
        clt1 = sb("clt1", [128, 512], F32)
        clt2 = sb("clt2", [128, 512], F32)
        a4 = clt1
        es4 = s4
        xbc = sb("xbc", [128, CCH], BF16)
        xbcw = sb("xbcw", [128, 17 * 131], BF16)
        cpa = sb("cpa", [128, CCH], BF16)
        cpc = sb("cpc", [128, CCH], BF16)
        szb = [sb(f"szb{i}", [128, DIN], BF16) for i in range(2)]
        arep = sb("arep", [128, 1024], F32)
        Ebuf = sb("Ebuf", [128, 2048], BF16)
        xhat = sb("xhat", [128, 1024], BF16)
        bwb = sb("bwb", [128, 1024], BF16)
        wcols = sb("wcols", [128, 32], F32)
        qb = sb("qb", [128, 1024], BF16)
        qhb = sb("qhb", [128, 2048], BF16)
        qfmb = sb("qfmb", [128, 512], BF16)
        Hm = sb("Hm", [64, 2048], F32)
        Hbf = sb("Hbf", [64, 2048], BF16)
        odb = sb("odb", [128, DM], BF16)
        sbf = sb("sbf", [64, 128], BF16)
        scf = sb("scf", [64, 128], BF16)
        sg = sb("sg", [128, 128], BF16)
        tbuf = sb("tbuf", [128, 1024], BF16)
        ssqt = sb("ssqt", [128, 2], F32)
        stat = sb("stat", [128, 8], F32)
        ceps = sb("ceps", [128, 2], F32)
        nc.const_aps.register(F32, EPS_LN, ceps[:, 0:1])
        nc.const_aps.register(F32, 1.0, ceps[:, 1:2])

        sx = [arep[:], Ebuf[:].bitcast(F32)]
        xn0 = qhb[:, DM:2 * DM]
        odb = tbuf
        xsbc = cpa
        dxsb = qb

        ps = lambda name, shape: st.enter_context(nc.psum_tensor(name, shape, F32))
        pwin = ps("pwin", [128, 1024])
        psmall = ps("psmall", [128, 512])
        pYA = ps("pYA", [128, 1024])
        pYB = ps("pYB", [128, 1024])
        pQT = ps("pQT", [128, 512])

        SS = lambda name: Sem(st, nc, name)
        sLD = SS("sLD"); sX = SS("sX"); sXN = SS("sXN"); sXNH = SS("sXNH")
        sTR = SS("sTR"); sWG = SS("sWG"); sWE = SS("sWE"); sSZ = SS("sSZ")
        sDT = SS("sDT"); sCV = SS("sCV"); sCD = SS("sCD"); sCS = SS("sCS")
        sBC = SS("sBC"); sBE = SS("sBE"); sGW = SS("sGW"); sGE = SS("sGE")
        sAR = SS("sAR"); sAE = SS("sAE"); sSC = SS("sSC"); sWC = SS("sWC")
        sMT = SS("sMT"); sXH = SS("sXH"); sBW = SS("sBW"); sY = SS("sY")
        sEV = SS("sEV"); sSQ = SS("sSQ"); sST = SS("sST"); sHU = SS("sHU")
        sHB = SS("sHB"); sQT = SS("sQT"); sQE = SS("sQE"); sWO = SS("sWO")
        sOD = SS("sOD"); sSG = SS("sSG"); sBAR = SS("sBAR"); sO3 = SS("sO3")
        sOUT = SS("sOUT"); sIN = SS("sIN"); sFV = SS("sFV"); sFA = SS("sFA")
        sOU = [SS("sOU0"), SS("sOU1")]

        block = st.enter_context(nc.Block())
        ENGS = ("tensor", "vector", "scalar", "gpsimd", "sync")

        def on(eng_name, fn):
            getattr(block, eng_name)(fn)

        def vfence(v):
            sFV.inc(v.drain())
            v.wait_ge(sFV.h, sFV.n)

        def afence(sc):
            sFA.inc(sc.drain())
            sc.wait_ge(sFA.h, sFA.n)

        def barrier():
            def f(sync):
                wait(sync, sOUT, sOU[0], sOU[1], sSG, sXNH, sSZ, sX, sTR, sLD)
            on("sync", f)
            target = sBAR.n + 5
            for e in ENGS:
                def bf(eng, e=e):
                    sBAR.inc(eng.nop() if e == "sync" else eng.drain())
                on(e, bf)
            for e in ENGS:
                on(e, lambda eng: eng.wait_ge(sBAR.h, target))
            assert sBAR.n == target

        def dbg_dump(name, src_ap, *wait_sems):
            if not debug:
                return
            def f(sync):
                wait(sync, *wait_sems)
                sOUT.inc(sync.dma_start(out=dbg[name], in_=src_ap), 16)
            on("sync", f)

        def f(g):
            g.memset(ceps[:, 0:1], EPS_LN)
            g.memset(ceps[:, 1:2], 1.0)
            g.memset(ones32[:], 1.0)
            sIN.inc(g.memset(onesbf[:], 1.0))
            for kt in range(NKT):
                sLD.inc(g.dma_start(out=winsb[:, kt * CCH:kt * CCH + DIN],
                                    in_=win["f"][kt * 128:(kt + 1) * 128, 0:DIN]), 16)
            sLD.inc(g.dma_start(out=wfm[:], in_=convw["f"]), 16)
            sLD.inc(g.dma_start(out=cbfm[:], in_=convb["f"]), 16)
            sLD.inc(g.dma_start(out=nwsb[:], in_=nwrep["f"]), 16)
            sLD.inc(g.dma_start(out=drsb[:], in_=drep["f"]), 16)
            sLD.inc(g.dma_start(out=dtbsb[:], in_=dtb4["f"]), 16)
            sLD.inc(g.dma_start(out=alogsb[:], in_=alog4["f"]), 16)
            for kt in range(NKT):
                sLD.inc(g.dma_start(out=windt[:, kt * DTC:(kt + 1) * DTC],
                                    in_=win["f"][kt * 128:(kt + 1) * 128, DIN + CCH:DPROJ]), 16)
        on("gpsimd", f)

        def f(sync):
            sTR.inc(sync.dma_start(out=if32[:], in_=if32_in), 16)
            sTR.inc(sync.dma_start(out=ibf[:], in_=ibf_in), 16)
        on("sync", f)

        for di, (d, src) in enumerate([("f", x_in), ("b", xrev_in)]):
            for k in range(NCH):
                u = k % 2

                def f(sync, k=k, u=u, src=src):
                    if sXN.n >= 4:
                        sync.wait_ge(sXN.h, sXN.n - 2)
                    sX.inc(sync.dma_start(out=sx[u], in_=src[k * Q:(k + 1) * Q, :]), 16)
                on("sync", f)

                def f(sc, u=u):
                    sc.wait_ge(sX.h, sX.n)
                    wait(sc, sIN, sXNH)
                    if debug:
                        wait(sc, sOUT)
                    sc.activation(tbuf[:], sx[u], AF.Copy, accum_out=stat[:, 0:1])
                    sSQ.inc(sc.activation(xn0, sx[u], AF.Square,
                                          accum_out=stat[:, 1:2]))
                on("scalar", f)

                def f(v):
                    v.wait_ge(sSQ.h, sSQ.n)
                    v.tensor_scalar_mul(stat[:, 2:3], stat[:, 0:1], 1.0 / DM)
                    v.tensor_scalar_mul(stat[:, 3:4], stat[:, 1:2], 1.0 / DM)
                    v.tensor_scalar(stat[:, 4:5], stat[:, 0:1], stat[:, 0:1],
                                    1.0 / (DM * DM), op0=ALU.mult, op1=ALU.mult)
                    vfence(v)
                    sXN.inc(v.tensor_tensor(stat[:, 5:6], stat[:, 3:4], stat[:, 4:5],
                                            op=ALU.subtract))
                on("vector", f)

                def f(sc):
                    sc.wait_ge(sXN.h, sXN.n)
                    sc.activation(stat[:, 6:7], stat[:, 5:6], AF.Ln, bias=EPS_LN)
                    afence(sc)
                    sSQ.inc(sc.activation(stat[:, 7:8], stat[:, 6:7], AF.Exp, scale=-0.5))
                on("scalar", f)

                def f(v, u=u):
                    v.wait_ge(sSQ.h, sSQ.n)
                    wait(v, sXNH)
                    if debug:
                        wait(v, sOUT)
                    sXN.inc(v.tensor_scalar(xn0, sx[u], stat[:, 2:3], stat[:, 7:8],
                                            op0=ALU.subtract, op1=ALU.mult))
                on("vector", f)

                def f(sync, d=d, k=k):
                    sync.wait_ge(sXN.h, sXN.n)
                    sXNH.inc(sync.dma_start(out=xnh[d][k * Q:(k + 1) * Q, :], in_=xn0), 16)
                on("sync", f)

                if d == "f" and k == 0:
                    dbg_dump("t_xn", xn0, sXN)

        barrier()

        def load_xnfm(d):
            def f(sync, d=d):
                wait(sync, sXNH)
                for ft in range(NKT):
                    sTR.inc(sync.dma_start_transpose(
                        xnfm[:, ft * L:(ft + 1) * L],
                        xnh[d][:, ft * 128:(ft + 1) * 128]), 16)
            on("sync", f)

        def load_dir_small(d):
            def f(g, d=d):
                sLD.inc(g.dma_start(out=wfm[:], in_=convw[d]), 16)
                sLD.inc(g.dma_start(out=cbfm[:], in_=convb[d]), 16)
                sLD.inc(g.dma_start(out=nwsb[:], in_=nwrep[d]), 16)
                sLD.inc(g.dma_start(out=drsb[:], in_=drep[d]), 16)
                sLD.inc(g.dma_start(out=dtbsb[:], in_=dtb4[d]), 16)
                sLD.inc(g.dma_start(out=alogsb[:], in_=alog4[d]), 16)
                for kt in range(NKT):
                    sLD.inc(g.dma_start(out=windt[:, kt * DTC:(kt + 1) * DTC],
                                        in_=win[d][kt * 128:(kt + 1) * 128, DIN + CCH:DPROJ]), 16)
            on("gpsimd", f)

        def load_win_z(d):
            def f(g, d=d):
                for kt in range(NKT):
                    sLD.inc(g.dma_start(out=winsb[:, kt * CCH:kt * CCH + DIN],
                                        in_=win[d][kt * 128:(kt + 1) * 128, 0:DIN]), 16)
            on("gpsimd", f)

        def load_win_xbc(d):
            def f(g, d=d):
                for kt in range(NKT):
                    sLD.inc(g.dma_start(out=winsb[:, kt * CCH:kt * CCH + CCH],
                                        in_=win[d][kt * 128:(kt + 1) * 128, DIN:DIN + CCH]), 16)
            on("gpsimd", f)

        def load_wout(d):
            def f(g, d=d):
                for j in range(16):
                    sLD.inc(g.dma_start(out=woutsb[:, j * DM:(j + 1) * DM],
                                        in_=wout[d][j * 128:(j + 1) * 128, :]), 16)
            on("gpsimd", f)

        def emit_phA(d, di):
            ld_t = sLD.n
            tr_tt = sTR.n
            tr_t = sTR.n

            def f(t):
                t.wait_ge(sLD.h, ld_t)
                t.wait_ge(sTR.h, tr_t)
                if pha_parts < 1:
                    sDT.inc(t.drain())
                    return
                for q in range(4):
                    for kt in range(NKT):
                        t.matmul(pYA[q * 32:(q + 1) * 32, 0:512],
                                 windt[:, kt * DTC:(kt + 1) * DTC],
                                 xnfm[:, kt * L + q * 512:kt * L + (q + 1) * 512],
                                 start=(kt == 0), stop=(kt == NKT - 1),
                                 tile_position=(0, q * 32))
                sDT.inc(t.drain())
            on("tensor", f)

            for k in range(NCH if pha_parts >= 2 else 0):
                def f(t, k=k):
                    for g in range(4):
                        gi = sWG.n
                        if gi >= 2 and pha_parts >= 3:
                            t.wait_ge(sWE.h, gi - 1)
                        for kt in range(NKT):
                            t.matmul(pwin[:, (g % 2) * 512:(g % 2 + 1) * 512],
                                     xnfm[:, kt * L + k * Q:kt * L + (k + 1) * Q],
                                     winsb[:, kt * CCH + g * 512:kt * CCH + (g + 1) * 512],
                                     start=(kt == 0), stop=(kt == NKT - 1))
                        sWG.inc(t.drain())
                on("tensor", f)

                if pha_parts < 3:
                    continue

                def f(sc):
                    base = sWG.n - 4
                    wait(sc, sSZ)
                    if debug:
                        wait(sc, sOUT)
                    for g in range(4):
                        sc.wait_ge(sWG.h, base + g + 1)
                        sWE.inc(sc.activation(szb[0][:, g * 512:(g + 1) * 512],
                                              pwin[:, (g % 2) * 512:(g % 2 + 1) * 512],
                                              AF.Silu))
                on("scalar", f)

                if pha_parts < 4:
                    continue

                def f(sync, k=k):
                    wait(sync, sWE)
                    sSZ.inc(sync.dma_start(out=szh[k * Q:(k + 1) * Q, :], in_=szb[0][:]), 16)
                on("sync", f)

                if d == "f" and k == 0:
                    dbg_dump("t_sz", szb[0][:], sWE)

            if pha_parts < 5:
                return
            def f(v):
                wait(v, sDT)
                v.tensor_scalar_add(clt2[:], pYA[:, 0:512], dtbsb[:])
                sDT.inc(v.tensor_scalar(clt1[:], pYA[:, 0:512], dtbsb[:], 0.0,
                                        op0=ALU.add, op1=ALU.max))
            on("vector", f)

            def f(sc):
                wait(sc, sDT)
                sc.activation(s4[:], clt2[:], AF.Abs)
                afence(sc)
                sc.activation(s4[:], s4[:], AF.Exp, scale=-1.0)
                afence(sc)
                sc.activation(s4[:], s4[:], AF.Ln, bias=1.0)
                sDT.inc(sc.activation(acol[:], alogsb[:], AF.Exp))
            on("scalar", f)

            def f(v):
                wait(v, sDT)
                v.tensor_tensor(dtsp4[:], clt1[:], s4[:], op=ALU.add)
                vfence(v)
                sDT.inc(v.tensor_scalar_mul(clt2[:], dtsp4[:], acol[:]))
            on("vector", f)

            def f(sc):
                wait(sc, sDT)
                sDT.inc(sc.activation(a4[:], clt2[:], AF.Exp, scale=-1.0))
            on("scalar", f)

            if pha_parts < 6:
                return
            def f(v):
                wait(v, sDT)
                for k in range(NCH):
                    qq, cc = k // 4, (k % 4) * 128
                    v.tensor_tensor_scan(
                        s4[qq * 32:(qq + 1) * 32, cc:cc + 128],
                        clt2[qq * 32:(qq + 1) * 32, cc:cc + 128],
                        clt2[qq * 32:(qq + 1) * 32, cc:cc + 128],
                        0.0, op0=ALU.add, op1=ALU.bypass)
                sDT.inc(v.drain())
            on("vector", f)

            def f(sc):
                wait(sc, sDT)
                sDT.inc(sc.activation(es4[:], s4[:], AF.Exp, scale=-1.0))
            on("scalar", f)

            if pha_parts < 7:
                return
            for k in range(NCH):
                qq, cc = k // 4, (k % 4) * 128

                def f(t, k=k, qq=qq, cc=cc):
                    wait(t, sDT, sBE)
                    for col, srcm in ((0, dtsp4), (32, es4), (64, a4)):
                        t.matmul(psmall[:, col:col + 32],
                                 srcm[qq * 32:(qq + 1) * 32, cc:cc + 128],
                                 if32[qq * 32:(qq + 1) * 32, qq * 32:(qq + 1) * 32],
                                 start=True, stop=True, tile_position=(qq * 32, 0))
                    sBC.inc(t.drain())
                on("tensor", f)

                def f(t, k=k, qq=qq, cc=cc):
                    t.matmul(psmall[0:1, 96:128],
                             es4[qq * 32:(qq + 1) * 32, cc + 127:cc + 128],
                             if32[qq * 32:(qq + 1) * 32, qq * 32:(qq + 1) * 32],
                             start=True, stop=True, tile_position=(qq * 32, 0))
                    sBC.inc(t.drain())
                on("tensor", f)

                def f(sc, k=k):
                    wait(sc, sBC)
                    sc.copy(dts[:, k * 96:k * 96 + 96], psmall[:, 0:96])
                    sBE.inc(sc.copy(wcols[0:1, 0:32], psmall[0:1, 96:128]))
                on("scalar", f)

                def f(t, k=k):
                    wait(t, sBE)
                    t.matmul(psmall[:, 64:96], ones32[:], wcols[0:1, 0:32],
                             start=True, stop=True)
                    sBC.inc(t.drain())
                on("tensor", f)

                def f(v, k=k):
                    wait(v, sBC)
                    sBE.inc(v.tensor_copy(eslr[:, k * 32:(k + 1) * 32],
                                          psmall[0:64, 64:96]))
                on("vector", f)

            if d == "f":
                dbg_dump("t_dtsp", dtsp4[:], sDT, sBE)
                dbg_dump("t_a4", a4[:], sDT)
                dbg_dump("t_ets", dts[:, 0:64], sBE)

        def emit_phB(d, di):
            ld_t = sLD.n

            def f(g):
                wait(g, sHB, sY)
                g.memset(Hm[:], 0.0)
                g.memset(Hbf[:], 0.0)
                sHB.inc(g.memset(xbcw[:].rearrange("p (c w) -> p c w", c=17)[:, :, 0:3], 0.0))
            on("gpsimd", f)

            def f(sync):
                wait(sync, sSZ)
                sSZ.inc(sync.dma_start(out=szb[0][:], in_=szh[0:Q, :]), 16)
            on("sync", f)

            gsz = [512, 512, 512, 512, 128]
            for k in range(NCH):
                u = k % 2
                cs = k * Q

                if k + 1 < NCH:
                    def f(sync, k=k):
                        wait(sync, sEV)
                        sSZ.inc(sync.dma_start(out=szb[(k + 1) % 2][:],
                                               in_=szh[(k + 1) * Q:(k + 2) * Q, :]), 16)
                    on("sync", f)

                def f(t, k=k, cs=cs):
                    t.wait_ge(sLD.h, ld_t)
                    off = 0
                    for g, n in enumerate(gsz):
                        gi = sWG.n
                        if gi >= 2:
                            t.wait_ge(sWE.h, gi - 1)
                        for kt in range(NKT):
                            t.matmul(pwin[:, (g % 2) * 512:(g % 2) * 512 + n],
                                     xnfm[:, kt * L + cs:kt * L + cs + Q],
                                     winsb[:, kt * CCH + off:kt * CCH + off + n],
                                     start=(kt == 0), stop=(kt == NKT - 1))
                        sWG.inc(t.drain())
                        off += n
                on("tensor", f)

                def f(sc):
                    base = sWG.n - 5
                    off = 0
                    for g, n in enumerate(gsz):
                        sc.wait_ge(sWG.h, base + g + 1)
                        sWE.inc(sc.copy(xbc[:, off:off + n],
                                        pwin[:, (g % 2) * 512:(g % 2) * 512 + n]))
                        off += n
                on("scalar", f)

                tile_rounds = [(0, 4), (4, 4), (8, 4), (12, 4), (16, 1)]
                for r0, rn in tile_rounds:
                    def f(t, r0=r0, rn=rn):
                        wait(t, sWE, sQE, sAE)
                        for i in range(rn):
                            ct = r0 + i
                            t.matmul(pQT[:, i * 128:(i + 1) * 128],
                                     xbc[:, ct * 128:(ct + 1) * 128], ibf[:],
                                     start=True, stop=True)
                        sQT.inc(t.drain())
                    on("tensor", f)

                    def f(v, r0=r0, rn=rn):
                        wait(v, sQT, sCD, sCV)
                        o = xbcw[:].rearrange("p (c w) -> p c w", c=17)[:, r0:r0 + rn, 3:131]
                        i_ = pQT[:, 0:rn * 128].rearrange("p (c w) -> p c w", c=rn)
                        sQE.inc(v.tensor_copy(o, i_))
                    on("vector", f)

                def f(v):
                    wait(v, sQE, sCS, sHB)
                    w3 = xbcw[:].rearrange("p (c w) -> p c w", c=17)
                    for ct in range(17):
                        v.tensor_scalar(cpc[:, ct * 128:(ct + 1) * 128],
                                        w3[:, ct, 0:128],
                                        wfm[:, ct:ct + 1], cbfm[:, ct:ct + 1],
                                        op0=ALU.mult, op1=ALU.add)
                    for j in (1, 2, 3):
                        vfence(v)
                        for ct in range(17):
                            acc = cpc[:, ct * 128:(ct + 1) * 128]
                            v.scalar_tensor_tensor(acc, w3[:, ct, j:j + 128],
                                                   wfm[:, j * 17 + ct:j * 17 + ct + 1],
                                                   acc, op0=ALU.mult, op1=ALU.add)
                    sCD.inc(v.drain())
                on("vector", f)

                def f(g):
                    sCV.inc(g.drain())
                on("gpsimd", f)

                def f(v, k=k):
                    wait(v, sCD, sCV)
                    w3 = xbcw[:].rearrange("p (c w) -> p c w", c=17)
                    sQE.inc(v.tensor_copy(w3[:, :, 0:3], w3[:, :, 128:131]))
                on("vector", f)

                def f(sc):
                    wait(sc, sCD, sCV)
                    sCS.inc(sc.activation(cpc[:], cpc[:], AF.Silu))
                on("scalar", f)

                for r0, rn in tile_rounds:
                    def f(t, r0=r0, rn=rn):
                        wait(t, sCS, sQE)
                        for i in range(rn):
                            ct = r0 + i
                            t.matmul(pQT[:, i * 128:(i + 1) * 128],
                                     cpc[:, ct * 128:(ct + 1) * 128], ibf[:],
                                     start=True, stop=True)
                        sQT.inc(t.drain())
                    on("tensor", f)

                    def f(v, r0=r0, rn=rn):
                        wait(v, sQT)
                        if debug:
                            wait(v, sOUT)
                        o = xsbc[:, r0 * 128:(r0 + rn) * 128]
                        sQE.inc(v.tensor_copy(o, pQT[:, 0:rn * 128]))
                    on("vector", f)

                def f(v):
                    wait(v, sQE)
                    sCS.inc(v.drain())
                on("vector", f)

                if d == "f" and k == 0:
                    dbg_dump("t_xsbc", xsbc[:], sCS)

                def f(t):
                    wait(t, sCS, sGE)
                    t.matmul(psmall[0:64, 0:128], xsbc[:, DIN:DIN + 64], ibf[:],
                             start=True, stop=True)
                    t.matmul(psmall[0:64, 128:256], xsbc[:, DIN + 64:DIN + 128], ibf[:],
                             start=True, stop=True)
                    sBC.inc(t.drain())
                on("tensor", f)

                def f(sc):
                    wait(sc, sBC)
                    sc.copy(sbf[:], psmall[0:64, 0:128])
                    sBE.inc(sc.copy(scf[:], psmall[0:64, 128:256]))
                on("scalar", f)

                def f(t):
                    wait(t, sBE)
                    t.matmul(psmall[:, 256:384], sbf[:], scf[:], start=True, stop=True)
                    sGW.inc(t.drain())
                on("tensor", f)

                def f(v):
                    wait(v, sGW)
                    sGE.inc(v.tensor_copy(sg[:], psmall[:, 256:384]))
                on("vector", f)

                for half in range(2):
                    h0 = half * 16
                    for piece in range(4):
                        hh0 = h0 + piece * 4

                        def f(t, k=k, hh0=hh0):
                            wait(t, sSC, sQE, sAE)
                            for i in range(4):
                                h = hh0 + i
                                acolb = dts[:, k * 96 + 64 + h:k * 96 + 64 + h + 1] \
                                    .broadcast_to((128, 128))
                                t.matmul(pQT[:, i * 128:(i + 1) * 128], acolb, if32[:],
                                         start=True, stop=True)
                            sAR.inc(t.drain())
                        on("tensor", f)

                        def f(sc, piece=piece):
                            wait(sc, sAR)
                            sAE.inc(sc.copy(arep[:, (piece % 2) * 512:(piece % 2 + 1) * 512],
                                            pQT[:]))
                        on("scalar", f)

                        def f(v, hh0=hh0, piece=piece, half=half):
                            wait(v, sAE)
                            if debug:
                                wait(v, sOUT)
                            if half == 1 and piece == 0:
                                wait(v, sY)
                            for i in range(4):
                                h = hh0 + i
                                hh = h % 16
                                v.tensor_tensor_scan(
                                    Ebuf[:, hh * 128:(hh + 1) * 128],
                                    arep[:, (piece % 2) * 512 + i * 128:
                                         (piece % 2) * 512 + (i + 1) * 128],
                                    if32[:], 0.0, op0=ALU.mult, op1=ALU.add)
                            sSC.inc(v.drain())
                        on("vector", f)

                    def f(v, k=k, h0=h0):
                        wait(v, sSC)
                        for i in range(16):
                            h = h0 + i
                            v.tensor_copy(wcols[:, h:h + 1],
                                          Ebuf[:, i * 128 + 127:i * 128 + 128])
                        sWC.inc(v.drain())
                    on("vector", f)

                    def f(v):
                        wait(v, sGE, sWC)
                        e3 = Ebuf[:].rearrange("p (h t) -> p h t", h=16)
                        gb = sg[:].unsqueeze(1).broadcast_to((128, 16, 128))
                        sMT.inc(v.tensor_tensor(e3, e3, gb, op=ALU.mult))
                    on("vector", f)

                    if d == "f" and k == 0 and half == 0:
                        dbg_dump("t_mt", Ebuf[:], sMT)

                    def f(sc, k=k, h0=h0, half=half):
                        wait(sc, sCS)
                        if half == 1:
                            wait(sc, sST)
                        if debug:
                            wait(sc, sOUT)
                        for i in range(16):
                            h = h0 + i
                            sc.activation(xhat[:, i * 64:(i + 1) * 64],
                                          xsbc[:, h * 64:(h + 1) * 64], AF.Copy,
                                          scale=dts[:, k * 96 + h:k * 96 + h + 1])
                        sXH.inc(sc.drain())
                    on("scalar", f)

                    def f(sc, k=k, h0=h0, half=half):
                        wait(sc, sWC)
                        if half == 1:
                            wait(sc, sST)
                        if debug:
                            wait(sc, sOUT)
                        for i in range(16):
                            h = h0 + i
                            sc.activation(bwb[:, i * 64:(i + 1) * 64],
                                          xsbc[:, DIN:DIN + 64], AF.Copy,
                                          scale=wcols[:, h:h + 1])
                        sBW.inc(sc.drain())
                    on("scalar", f)

                    if d == "f" and k == 0 and half == 1:
                        dbg_dump("t_wc", wcols[:], sWC)
                        dbg_dump("t_bw", bwb[:], sBW)

                    if d == "f" and k == 0 and half == 0:
                        dbg_dump("t_xh", xhat[:], sXH)

                    def f(t, h0=h0, half=half):
                        wait(t, sMT, sXH, sHB, sEV, sHU, sOD)
                        for i in range(16):
                            t.matmul(pYA[:, i * 64:(i + 1) * 64],
                                     Ebuf[:, i * 128:(i + 1) * 128],
                                     xhat[:, i * 64:(i + 1) * 64], start=True, stop=True)
                        for i in range(16):
                            h = h0 + i
                            t.matmul(pYB[:, i * 64:(i + 1) * 64], scf[:],
                                     Hbf[:, h * 64:(h + 1) * 64],
                                     start=True, stop=True)
                        sY.inc(t.drain())
                    on("tensor", f)

                    def f(v, h0=h0):
                        wait(v, sCS, sSQ)
                        sEV.inc(v.tensor_tensor(dxsb[:], xsbc[:, h0 * 64:h0 * 64 + 1024],
                                                drsb[:, h0 * 64:h0 * 64 + 1024], op=ALU.mult))
                    on("vector", f)

                    def f(v, k=k, h0=h0, u=u):
                        wait(v, sY, sSZ, sSG, sEV)
                        if debug:
                            wait(v, sOUT)
                        v.tensor_tensor(tbuf[:], pYA[:], qb[:], op=ALU.add)
                        es = dts[:, k * 96 + 32 + h0:k * 96 + 32 + h0 + 16] \
                            .unsqueeze(-1).broadcast_to((128, 16, 64))
                        pyb3 = pYB[:].rearrange("p (h q) -> p h q", h=16)
                        vfence(v)
                        v.tensor_tensor(qb[:].rearrange("p (h q) -> p h q", h=16),
                                        pyb3, es, op=ALU.mult)
                        vfence(v)
                        v.tensor_tensor(qb[:], qb[:], tbuf[:], op=ALU.add)
                        vfence(v)
                        v.tensor_tensor(qb[:], qb[:], szb[u][:, h0 * 64:h0 * 64 + 1024],
                                        op=ALU.mult)
                        vfence(v)
                        sEV.inc(v.tensor_tensor(qhb[:, h0 * 64:h0 * 64 + 1024], qb[:],
                                                nwsb[:, h0 * 64:h0 * 64 + 1024], op=ALU.mult))
                    on("vector", f)

                    if d == "f" and k == 0 and half == 1:
                        dbg_dump("t_q", qb[:], sEV)

                    def f(sc, half=half):
                        wait(sc, sEV)
                        if debug:
                            wait(sc, sOUT)
                        sSQ.inc(sc.activation(qb[:], qb[:], AF.Square,
                                              accum_out=ssqt[:, half:half + 1]))
                    on("scalar", f)

                    def f(t, h0=h0, half=half):
                        wait(t, sBW, sEV)
                        tgt = pYA if half == 0 else pYB
                        for i in range(16):
                            t.matmul(tgt[0:64, i * 64:(i + 1) * 64],
                                     bwb[:, i * 64:(i + 1) * 64],
                                     xhat[:, i * 64:(i + 1) * 64],
                                     start=True, stop=True)
                        sST.inc(t.drain())
                    on("tensor", f)

                    def f(v, k=k, h0=h0, half=half):
                        wait(v, sST)
                        if debug:
                            wait(v, sOUT)
                        tgt = pYA if half == 0 else pYB
                        el = eslr[:, k * 32 + h0:k * 32 + h0 + 16].unsqueeze(-1) \
                            .broadcast_to((64, 16, 64))
                        hview = Hm[0:64, h0 * 64:h0 * 64 + 1024] \
                            .rearrange("p (h q) -> p h q", h=16)
                        v.tensor_tensor(hview, hview, el, op=ALU.mult)
                        vfence(v)
                        sHU.inc(v.tensor_tensor(Hm[0:64, h0 * 64:h0 * 64 + 1024],
                                                Hm[0:64, h0 * 64:h0 * 64 + 1024],
                                                tgt[0:64, :], op=ALU.add))
                    on("vector", f)

                def f(v, di=di, k=k):
                    wait(v, sSQ)
                    sEV.inc(v.tensor_tensor(ssqall[:, di * NCH + k:di * NCH + k + 1],
                                            ssqt[:, 0:1], ssqt[:, 1:2], op=ALU.add))
                on("vector", f)

                def f(sc):
                    wait(sc, sHU)
                    sHB.inc(sc.copy(Hbf[:], Hm[:]))
                on("scalar", f)

                if d == "f" and k == 0:
                    dbg_dump("t_H", Hm[:], sHU)
                    dbg_dump("t_qh", qhb[:], sEV)

                for r in range(4):
                    def f(t, r=r):
                        wait(t, sEV, sQE, sAE)
                        for i in range(4):
                            j = r * 4 + i
                            t.matmul(pQT[:, i * 128:(i + 1) * 128],
                                     qhb[:, j * 128:(j + 1) * 128], ibf[:],
                                     start=True, stop=True)
                        sQT.inc(t.drain())
                    on("tensor", f)

                    def f(v):
                        wait(v, sQT)
                        sQE.inc(v.tensor_copy(qfmb[:], pQT[:]))
                    on("vector", f)

                    def f(t, r=r):
                        wait(t, sQE)
                        if r == 0:
                            wait(t, sOD, sHU)
                        for i in range(4):
                            j = r * 4 + i
                            for nn in range(2):
                                t.matmul(pYB[:, nn * 512:(nn + 1) * 512],
                                         qfmb[:, i * 128:(i + 1) * 128],
                                         woutsb[:, j * DM + nn * 512:j * DM + (nn + 1) * 512],
                                         start=(j == 0), stop=(j == 15))
                        sWO.inc(t.drain())
                    on("tensor", f)

                def f(v):
                    wait(v, sWO)
                    if debug:
                        wait(v, sOUT)
                    sOD.inc(v.tensor_copy(odb[:], pYB[:]))
                on("vector", f)

                def f(sync, d=d, k=k):
                    wait(sync, sOD)
                    sSG.inc(sync.dma_start(out=odh[d][k * Q:(k + 1) * Q, :], in_=odb[:]), 16)
                on("sync", f)

                if d == "f" and k == 0:
                    dbg_dump("t_od", odb[:], sOD)

        dirs = []
        if stages >= 2:
            dirs.append(("f", 0))
        if stages >= 4:
            dirs.append(("b", 1))
        for d, di in dirs:
            if d == "b":
                load_dir_small(d)
                load_win_z(d)
            load_xnfm(d)
            barrier()
            emit_phA(d, di)
            barrier()
            if stages >= 3 or d == "b":
                load_win_xbc(d)
                load_wout(d)
                barrier()
                emit_phB(d, di)
            barrier()

        if stages < 5:
            def f(sync):
                wait(sync, sOUT, sOU[0], sOU[1], sSG, sXNH, sSZ, sLD, sTR, sX)
            on("sync", f)
            return nc
        outfm = winsb
        wosb = woutsb

        def f(g):
            wait(g, sSG)
            for j in range(16):
                sLD.inc(g.dma_start(out=wosb[:, j * DM:(j + 1) * DM],
                                    in_=wo_in[j * 128:(j + 1) * 128, :]), 16)
            for j in range(NKT):
                sLD.inc(g.dma_start(out=wtsb[:, j * DM:(j + 1) * DM],
                                    in_=wt_in[j * 128:(j + 1) * 128, :]), 16)
            for kt in range(NKT):
                sLD.inc(g.dma_start(out=nwsb[:, kt * 32:(kt + 1) * 32],
                                    in_=vr_in[kt * 128:(kt + 1) * 128, :]), 16)
            sLD.inc(g.dma_start(out=drsb[0:32, 0:1024], in_=ct_in), 16)
            sLD.inc(g.dma_start(out=szb[0][:, 0:DM], in_=bo_in), 16)
            sLD.inc(g.dma_start(out=bwb[0:1, 0:1024], in_=ffb_in), 16)
        on("gpsimd", f)

        def f(sync):
            wait(sync, sSG)
            for ft in range(NKT):
                sTR.inc(sync.dma_start_transpose(
                    outfm[:, ft * 2048:(ft + 1) * 2048],
                    odh["f"][:, ft * 128:(ft + 1) * 128]), 16)
        on("sync", f)

        def f(sc):
            wait(sc, sEV)
            sc.activation(rall[:], ssqall[:], AF.Ln, bias=EPS_RMS, scale=1.0 / DIN)
            afence(sc)
            sO3.inc(sc.activation(rall[:], rall[:], AF.Exp, scale=-0.5))
        on("scalar", f)

        barrier()

        ph3_ld = sLD.n
        ph3_tr = sTR.n

        for k in range(NCH):
            u = k % 2
            kb = NCH - 1 - k

            def f(sync, k=k, u=u, kb=kb):
                wait(sync, sOU[u])
                if sQT.n >= 2:
                    sync.wait_ge(sQT.h, sQT.n - 2)
                sX.inc(sync.dma_start(out=sx[u], in_=x_in[k * Q:(k + 1) * Q, :]), 16)
                sX.inc(sync.dma_start(out=odb[:], in_=odh["b"][kb * Q:(kb + 1) * Q, :]), 16)
            on("sync", f)

            def f(v, kb=kb):
                wait(v, sX, sO3)
                sO3.inc(v.tensor_scalar_mul(odb[:], odb[:],
                                            rall[:, NCH + kb:NCH + kb + 1]))
            on("vector", f)

            for r in range(2):
                def f(t, r=r):
                    wait(t, sO3, sQE)
                    t.wait_ge(sLD.h, ph3_ld)
                    t.wait_ge(sTR.h, ph3_tr)
                    for i in range(4):
                        j = r * 4 + i
                        t.matmul(pQT[:, i * 128:(i + 1) * 128],
                                 odb[:, j * 128:(j + 1) * 128], ibf[:, 127::-1],
                                 start=True, stop=True)
                    sQT.inc(t.drain())
                on("tensor", f)

                def f(v, r=r):
                    wait(v, sQT)
                    sQE.inc(v.tensor_copy(qhb[:, r * 512:(r + 1) * 512], pQT[:]))
                on("vector", f)

            def f(t, k=k):
                wait(t, sQE, sO3)
                for mt in range(8):
                    for nn in range(2):
                        t.matmul(pYA[:, nn * 512:(nn + 1) * 512],
                                 outfm[:, mt * 2048 + k * Q:mt * 2048 + (k + 1) * Q],
                                 wosb[:, mt * DM + nn * 512:mt * DM + (nn + 1) * 512],
                                 start=(mt == 0), stop=(mt == 7))
                for mt in range(8):
                    for nn in range(2):
                        t.matmul(pYB[:, nn * 512:(nn + 1) * 512],
                                 qhb[:, mt * 128:(mt + 1) * 128],
                                 wosb[:, (8 + mt) * DM + nn * 512:(8 + mt) * DM + (nn + 1) * 512],
                                 start=(mt == 0), stop=(mt == 7))
                sWO.inc(t.drain())
            on("tensor", f)

            def f(v, k=k, u=u, kb=kb):
                wait(v, sWO)
                if debug:
                    wait(v, sOUT)
                v.scalar_tensor_tensor(sx[u], pYA[:], rall[:, k:k + 1], sx[u],
                                       op0=ALU.mult, op1=ALU.add)
                vfence(v)
                v.tensor_tensor(sx[u], pYB[:], sx[u], op=ALU.add)
                vfence(v)
                sO3.inc(v.tensor_tensor(sx[u], szb[0][:, 0:DM], sx[u], op=ALU.add))
            on("vector", f)

            if k == 0:
                dbg_dump("t_x2", sx[u], sO3)

            def f(sc, u=u):
                wait(sc, sO3)
                if debug:
                    wait(sc, sOUT)
                sc.activation(qb[:], sx[u], AF.Copy, accum_out=stat[:, 0:1])
                sSQ.inc(sc.activation(xn0, sx[u], AF.Square,
                                      accum_out=stat[:, 1:2]))
            on("scalar", f)

            def f(v):
                wait(v, sSQ)
                v.tensor_scalar_mul(stat[:, 2:3], stat[:, 0:1], 1.0 / DM)
                v.tensor_scalar_mul(stat[:, 3:4], stat[:, 1:2], 1.0 / DM)
                v.tensor_scalar(stat[:, 4:5], stat[:, 0:1], stat[:, 0:1],
                                1.0 / (DM * DM), op0=ALU.mult, op1=ALU.mult)
                vfence(v)
                sO3.inc(v.tensor_tensor(stat[:, 5:6], stat[:, 3:4], stat[:, 4:5],
                                        op=ALU.subtract))
            on("vector", f)

            def f(sc):
                wait(sc, sO3)
                sc.activation(stat[:, 6:7], stat[:, 5:6], AF.Ln, bias=EPS_LN)
                afence(sc)
                sSQ.inc(sc.activation(stat[:, 7:8], stat[:, 6:7], AF.Exp, scale=-0.5))
            on("scalar", f)

            def f(v, u=u):
                wait(v, sSQ, sQT)
                sO3.inc(v.tensor_scalar(xn0, sx[u], stat[:, 2:3], stat[:, 7:8],
                                        op0=ALU.subtract, op1=ALU.mult))
            on("vector", f)

            for r in range(2):
                def f(t, r=r):
                    wait(t, sO3, sQE)
                    for i in range(4):
                        j = r * 4 + i
                        t.matmul(pQT[:, i * 128:(i + 1) * 128],
                                 qhb[:, DM + j * 128:DM + (j + 1) * 128], ibf[:],
                                 start=True, stop=True)
                    sQT.inc(t.drain())
                on("tensor", f)

                def f(v, r=r):
                    wait(v, sQT)
                    sQE.inc(v.tensor_copy(qb[:, r * 512:(r + 1) * 512], pQT[:]))
                on("vector", f)

            def f(t):
                wait(t, sQE, sBE)
                for kt in range(NKT):
                    t.matmul(psmall[0:32, 0:128],
                             nwsb[:, kt * 32:(kt + 1) * 32],
                             qb[:, kt * 128:(kt + 1) * 128],
                             start=(kt == 0), stop=(kt == NKT - 1))
                sBC.inc(t.drain())
            on("tensor", f)

            def f(sc):
                wait(sc, sBC)
                sBE.inc(sc.copy(scf[0:32, 0:128], psmall[0:32, 0:128]))
            on("scalar", f)

            def f(t):
                wait(t, sBE, sO3)
                for kt in range(NKT):
                    for nn in range(2):
                        t.matmul(pYA[:, nn * 512:(nn + 1) * 512],
                                 qb[:, kt * 128:(kt + 1) * 128],
                                 wtsb[:, kt * DM + nn * 512:kt * DM + (nn + 1) * 512],
                                 start=(kt == 0), stop=False)
                for nn in range(2):
                    t.matmul(pYA[:, nn * 512:(nn + 1) * 512], scf[0:32, 0:128],
                             drsb[0:32, nn * 512:(nn + 1) * 512],
                             start=False, stop=False)
                for nn in range(2):
                    t.matmul(pYA[:, nn * 512:(nn + 1) * 512], onesbf[:],
                             bwb[0:1, nn * 512:(nn + 1) * 512],
                             start=False, stop=True)
                sWO.inc(t.drain())
            on("tensor", f)

            def f(v, u=u):
                wait(v, sWO)
                sO3.inc(v.tensor_tensor(sx[u], pYA[:], sx[u], op=ALU.add))
            on("vector", f)

            def f(sync, k=k, u=u):
                wait(sync, sO3)
                sOU[u].inc(sync.dma_start(out=y_out[k * Q:(k + 1) * Q, :], in_=sx[u]), 16)
            on("sync", f)

        def f(sync):
            wait(sync, sOUT, sOU[0], sOU[1])
        on("sync", f)

    return nc


def prep_host(inputs):
    g = lambda k: np.ascontiguousarray(np.asarray(inputs[k], dtype=np.float32))
    m = {}
    for d in "fb":
        m[f"win_{d}"] = g(f"L0_attn_{d}Win")
        m[f"wout_{d}"] = g(f"L0_attn_{d}Wout")
        cw = g(f"L0_attn_{d}convw")[:, 0, :]
        m[f"convw_{d}"] = np.ascontiguousarray(
            cw.reshape(4, 17, 128).transpose(2, 0, 1).reshape(128, 68))
        m[f"convb_{d}"] = np.ascontiguousarray(
            g(f"L0_attn_{d}convb").reshape(17, 128).T)
        m[f"dtb4_{d}"] = np.ascontiguousarray(np.tile(g(f"L0_attn_{d}dtb"), 4)[:, None])
        m[f"alog4_{d}"] = np.ascontiguousarray(np.tile(g(f"L0_attn_{d}Alog"), 4)[:, None])
        m[f"drep_{d}"] = np.ascontiguousarray(np.broadcast_to(
            np.repeat(g(f"L0_attn_{d}D"), P)[None, :], (128, DIN)))
        m[f"nwrep_{d}"] = np.ascontiguousarray(np.broadcast_to(
            g(f"L0_attn_{d}normw")[None, :], (128, DIN)))
    m["wo"] = g("L0_attn_Wo")
    m["bo"] = np.ascontiguousarray(np.broadcast_to(g("L0_attn_bo")[None, :], (128, DM)))
    m["wt"] = np.ascontiguousarray(g("L0_ffn_W").T)
    m["ffb"] = g("L0_ffn_b")[None, :]
    m["vr"] = g("L0_ffn_V_r")
    m["ct"] = np.ascontiguousarray(g("L0_ffn_C").T)
    m["if32"] = np.eye(128, dtype=np.float32)
    m["ibf"] = np.eye(128, dtype=ml_dtypes.bfloat16)
    return m


def kernel(**inputs):
    nc = build(debug=False)
    shared = prep_host(inputs)
    x = np.ascontiguousarray(np.asarray(inputs["L0_attn_x"], dtype=np.float32))
    in_maps = []
    for b in range(8):
        im = dict(shared)
        im["x"] = np.ascontiguousarray(x[b])
        im["xrev"] = np.ascontiguousarray(x[b][::-1])
        in_maps.append(im)
    res = run_bass_kernel_spmd(nc, in_maps, core_ids=list(range(8)))
    out = np.stack([r["y"] for r in res.results], axis=0)
    return out.astype(np.float32)


# version 06
# speedup vs baseline: 1.0366x; 1.0366x over previous
import contextlib
import numpy as np
import ml_dtypes
import concourse.bass as bass
import concourse.mybir as mybir
from concourse.bass_utils import run_bass_kernel_spmd

F32 = mybir.dt.float32
BF16 = mybir.dt.bfloat16
AF = mybir.ActivationFunctionType
ALU = mybir.AluOpType

L = 2048
DM = 1024
Q = 128
NCH = L // Q
NH = 32
P = 64
NST = 64
DIN = 2048
CCH = DIN + 2 * NST
DTC = NH
DPROJ = 2 * DIN + 2 * NST + NH
NKT = DM // 128
EPS_LN = 1e-5
EPS_RMS = 1e-5


class Sem:
    def __init__(self, stack, nc, name):
        self.h = stack.enter_context(nc.semaphore(name))
        self.n = 0

    def inc(self, instr, k=1):
        instr.then_inc(self.h, k)
        self.n += k
        return instr


def wait(eng, *sems):
    for s in sems:
        if s.n:
            eng.wait_ge(s.h, s.n)


def build(debug=False, stages=5, pha_parts=9):
    nc = bass.Bass("TRN2", target_bir_lowering=False)
    dram = {}

    def din(name, shape, dt=F32):
        dram[name] = nc.dram_tensor(name, shape, dt, kind="ExternalInput").ap()
        return dram[name]

    def dout(name, shape, dt=F32):
        dram[name] = nc.dram_tensor(name, shape, dt, kind="ExternalOutput").ap()
        return dram[name]

    def dint(name, shape, dt=BF16):
        dram[name] = nc.dram_tensor(name, shape, dt, kind="Internal").ap()
        return dram[name]

    x_in = din("x", [L, DM])
    xrev_in = din("xrev", [L, DM])
    win = {d: din(f"win_{d}", [DM, DPROJ]) for d in "fb"}
    wout = {d: din(f"wout_{d}", [DIN, DM]) for d in "fb"}
    convw = {d: din(f"convw_{d}", [128, 4 * 17]) for d in "fb"}
    convb = {d: din(f"convb_{d}", [128, 17]) for d in "fb"}
    dtb4 = {d: din(f"dtb4_{d}", [128, 1]) for d in "fb"}
    alog4 = {d: din(f"alog4_{d}", [128, 1]) for d in "fb"}
    drep = {d: din(f"drep_{d}", [128, DIN]) for d in "fb"}
    nwrep = {d: din(f"nwrep_{d}", [128, DIN]) for d in "fb"}
    wo_in = din("wo", [DIN, DM])
    bo_in = din("bo", [128, DM])
    wt_in = din("wt", [DM, DM])
    ffb_in = din("ffb", [1, DM])
    vr_in = din("vr", [DM, 32])
    ct_in = din("ct", [32, DM])
    if32_in = din("if32", [128, 128])
    ibf_in = din("ibf", [128, 128], BF16)

    y_out = dout("y", [L, DM])

    xnh = {"f": dint("xnh", [L, DM]), "b": dint("xnrevh", [L, DM])}
    szh = dint("szh", [L, DIN])
    odh = {"f": dint("odhf", [L, DM]), "b": dint("odhb", [L, DM])}

    dbg = {}
    if debug:
        for nm, shp, dt in [
            ("t_xn", [128, DM], BF16), ("t_sz", [128, DIN], BF16),
            ("t_xsbc", [128, CCH], BF16), ("t_dtsp", [128, 512], F32),
            ("t_a4", [128, 512], F32), ("t_ets", [128, 64], F32),
            ("t_mt", [128, 2048], BF16), ("t_xh", [128, 1024], BF16),
            ("t_q", [128, 1024], BF16), ("t_qh", [128, DIN], BF16),
            ("t_wc", [128, 32], F32), ("t_bw", [128, 1024], BF16),
            ("t_od", [128, DM], BF16), ("t_H", [64, 2048], F32),
            ("t_x2", [128, DM], F32),
        ]:
            dbg[nm] = dout(nm, shp, dt)

    st = contextlib.ExitStack()
    with st:
        sb = lambda name, shape, dt: st.enter_context(nc.sbuf_tensor(name, shape, dt))
        xnfm = sb("xnfm", [128, NKT * L], BF16)
        winsb = sb("winsb", [128, NKT * CCH], BF16)
        windt = sb("windt", [128, NKT * DTC], BF16)
        woutsb = sb("woutsb", [128, 16 * DM], BF16)
        wtsb = sb("wtsb", [128, NKT * DM], BF16)
        wfm = sb("wfm", [128, 4 * 17], F32)
        cbfm = sb("cbfm", [128, 17], F32)
        nwsb = sb("nwsb", [128, DIN], BF16)
        drsb = sb("drsb", [128, DIN], BF16)
        dtbsb = sb("dtbsb", [128, 1], F32)
        alogsb = sb("alogsb", [128, 1], F32)
        acol = sb("acol", [128, 1], F32)
        if32 = sb("if32_s", [128, 128], F32)
        ibf = sb("ibf_s", [128, 128], BF16)
        ones32 = sb("ones32", [1, 128], F32)
        onesbf = sb("onesbf", [1, 128], BF16)
        ssqall = sb("ssqall", [128, 2 * NCH], F32)
        rall = sb("rall", [128, 2 * NCH], F32)
        dtsp4 = sb("dtsp4", [128, 512], F32)
        s4 = sb("s4", [128, 512], F32)
        dts = sb("dts", [128, NCH * 96], F32)
        eslr = sb("eslr", [64, NCH * 32], F32)
        clt1 = sb("clt1", [128, 512], F32)
        clt2 = sb("clt2", [128, 512], F32)
        a4 = clt1
        es4 = s4
        xbc = sb("xbc", [128, CCH], BF16)
        xbcw = sb("xbcw", [128, 17 * 131], BF16)
        cpa = sb("cpa", [128, CCH], BF16)
        cpc = sb("cpc", [128, CCH], BF16)
        szb = [sb(f"szb{i}", [128, DIN], BF16) for i in range(2)]
        arep = sb("arep", [128, 1024], F32)
        Ebuf = sb("Ebuf", [128, 2048], BF16)
        xhat = sb("xhat", [128, 1024], BF16)
        bwb = sb("bwb", [128, 1024], BF16)
        wcols = sb("wcols", [128, 32], F32)
        qb = sb("qb", [128, 1024], BF16)
        qhb = sb("qhb", [128, 2048], BF16)
        qfmb = sb("qfmb", [128, 512], BF16)
        Hm = sb("Hm", [64, 2048], F32)
        Hbf = sb("Hbf", [64, 2048], BF16)
        odb = sb("odb", [128, DM], BF16)
        sbf = sb("sbf", [64, 128], BF16)
        scf = sb("scf", [64, 128], BF16)
        sg = sb("sg", [128, 128], BF16)
        tbuf = sb("tbuf", [128, 1024], BF16)
        ssqt = sb("ssqt", [128, 2], F32)
        stat = sb("stat", [128, 8], F32)
        ceps = sb("ceps", [128, 2], F32)
        nc.const_aps.register(F32, EPS_LN, ceps[:, 0:1])
        nc.const_aps.register(F32, 1.0, ceps[:, 1:2])

        sx = [arep[:], Ebuf[:].bitcast(F32)]
        xn0 = qhb[:, DM:2 * DM]
        odb = tbuf
        xsbc = cpa
        dxsb = qb

        ps = lambda name, shape: st.enter_context(nc.psum_tensor(name, shape, F32))
        pwin = ps("pwin", [128, 1024])
        psmall = ps("psmall", [128, 512])
        pYA = ps("pYA", [128, 1024])
        pYB = ps("pYB", [128, 1024])
        pQT = ps("pQT", [128, 512])

        SS = lambda name: Sem(st, nc, name)
        sLD = SS("sLD"); sX = SS("sX"); sXN = SS("sXN"); sXNH = SS("sXNH")
        sTR = SS("sTR"); sWG = SS("sWG"); sWE = SS("sWE"); sSZ = SS("sSZ")
        sDT = SS("sDT"); sCV = SS("sCV"); sCD = SS("sCD"); sCS = SS("sCS")
        sBC = SS("sBC"); sBE = SS("sBE"); sGW = SS("sGW"); sGE = SS("sGE")
        sAR = SS("sAR"); sAE = SS("sAE"); sSC = SS("sSC"); sWC = SS("sWC")
        sMT = SS("sMT"); sXH = SS("sXH"); sBW = SS("sBW"); sY = SS("sY")
        sEV = SS("sEV"); sSQ = SS("sSQ"); sST = SS("sST"); sHU = SS("sHU")
        sHB = SS("sHB"); sQT = SS("sQT"); sQE = SS("sQE"); sWO = SS("sWO")
        sOD = SS("sOD"); sSG = SS("sSG"); sBAR = SS("sBAR"); sO3 = SS("sO3")
        sOUT = SS("sOUT"); sIN = SS("sIN"); sFV = SS("sFV"); sFA = SS("sFA")
        sOU = [SS("sOU0"), SS("sOU1")]

        block = st.enter_context(nc.Block())
        ENGS = ("tensor", "vector", "scalar", "gpsimd", "sync")

        def on(eng_name, fn):
            getattr(block, eng_name)(fn)

        def vfence(v):
            sFV.inc(v.engine_nop())
            v.wait_ge(sFV.h, sFV.n)

        def afence(sc):
            sFA.inc(sc.copy(ceps[0:1, 1:2], ceps[0:1, 1:2]))
            sc.wait_ge(sFA.h, sFA.n)

        def barrier():
            def f(sync):
                wait(sync, sOUT, sOU[0], sOU[1], sSG, sXNH, sSZ, sX, sTR, sLD)
            on("sync", f)
            target = sBAR.n + 5
            for e in ENGS:
                def bf(eng, e=e):
                    sBAR.inc(eng.nop() if e == "sync" else eng.drain())
                on(e, bf)
            for e in ENGS:
                on(e, lambda eng: eng.wait_ge(sBAR.h, target))
            assert sBAR.n == target

        def dbg_dump(name, src_ap, *wait_sems):
            if not debug:
                return
            def f(sync):
                wait(sync, *wait_sems)
                sOUT.inc(sync.dma_start(out=dbg[name], in_=src_ap), 16)
            on("sync", f)

        def f(g):
            g.memset(ceps[:, 0:1], EPS_LN)
            g.memset(ceps[:, 1:2], 1.0)
            g.memset(ones32[:], 1.0)
            sIN.inc(g.memset(onesbf[:], 1.0))
            for kt in range(NKT):
                sLD.inc(g.dma_start(out=winsb[:, kt * CCH:kt * CCH + DIN],
                                    in_=win["f"][kt * 128:(kt + 1) * 128, 0:DIN]), 16)
            sLD.inc(g.dma_start(out=wfm[:], in_=convw["f"]), 16)
            sLD.inc(g.dma_start(out=cbfm[:], in_=convb["f"]), 16)
            sLD.inc(g.dma_start(out=nwsb[:], in_=nwrep["f"]), 16)
            sLD.inc(g.dma_start(out=drsb[:], in_=drep["f"]), 16)
            sLD.inc(g.dma_start(out=dtbsb[:], in_=dtb4["f"]), 16)
            sLD.inc(g.dma_start(out=alogsb[:], in_=alog4["f"]), 16)
            for kt in range(NKT):
                sLD.inc(g.dma_start(out=windt[:, kt * DTC:(kt + 1) * DTC],
                                    in_=win["f"][kt * 128:(kt + 1) * 128, DIN + CCH:DPROJ]), 16)
        on("gpsimd", f)

        def f(sync):
            sTR.inc(sync.dma_start(out=if32[:], in_=if32_in), 16)
            sTR.inc(sync.dma_start(out=ibf[:], in_=ibf_in), 16)
        on("sync", f)

        for di, (d, src) in enumerate([("f", x_in), ("b", xrev_in)]):
            for k in range(NCH):
                u = k % 2

                def f(sync, k=k, u=u, src=src):
                    if sXN.n >= 4:
                        sync.wait_ge(sXN.h, sXN.n - 2)
                    sX.inc(sync.dma_start(out=sx[u], in_=src[k * Q:(k + 1) * Q, :]), 16)
                on("sync", f)

                def f(sc, u=u):
                    sc.wait_ge(sX.h, sX.n)
                    wait(sc, sIN, sXNH)
                    if debug:
                        wait(sc, sOUT)
                    sc.activation(tbuf[:], sx[u], AF.Copy, accum_out=stat[:, 0:1])
                    sSQ.inc(sc.activation(xn0, sx[u], AF.Square,
                                          accum_out=stat[:, 1:2]))
                on("scalar", f)

                def f(v):
                    v.wait_ge(sSQ.h, sSQ.n)
                    v.tensor_scalar_mul(stat[:, 2:3], stat[:, 0:1], 1.0 / DM)
                    v.tensor_scalar_mul(stat[:, 3:4], stat[:, 1:2], 1.0 / DM)
                    v.tensor_scalar(stat[:, 4:5], stat[:, 0:1], stat[:, 0:1],
                                    1.0 / (DM * DM), op0=ALU.mult, op1=ALU.mult)
                    vfence(v)
                    sXN.inc(v.tensor_tensor(stat[:, 5:6], stat[:, 3:4], stat[:, 4:5],
                                            op=ALU.subtract))
                on("vector", f)

                def f(sc):
                    sc.wait_ge(sXN.h, sXN.n)
                    sc.activation(stat[:, 6:7], stat[:, 5:6], AF.Ln, bias=EPS_LN)
                    afence(sc)
                    sSQ.inc(sc.activation(stat[:, 7:8], stat[:, 6:7], AF.Exp, scale=-0.5))
                on("scalar", f)

                def f(v, u=u):
                    v.wait_ge(sSQ.h, sSQ.n)
                    wait(v, sXNH)
                    if debug:
                        wait(v, sOUT)
                    sXN.inc(v.tensor_scalar(xn0, sx[u], stat[:, 2:3], stat[:, 7:8],
                                            op0=ALU.subtract, op1=ALU.mult))
                on("vector", f)

                def f(sync, d=d, k=k):
                    sync.wait_ge(sXN.h, sXN.n)
                    sXNH.inc(sync.dma_start(out=xnh[d][k * Q:(k + 1) * Q, :], in_=xn0), 16)
                on("sync", f)

                if d == "f" and k == 0:
                    dbg_dump("t_xn", xn0, sXN)

        barrier()

        def load_xnfm(d):
            def f(sync, d=d):
                wait(sync, sXNH)
                for ft in range(NKT):
                    sTR.inc(sync.dma_start_transpose(
                        xnfm[:, ft * L:(ft + 1) * L],
                        xnh[d][:, ft * 128:(ft + 1) * 128]), 16)
            on("sync", f)

        def load_dir_small(d):
            def f(g, d=d):
                sLD.inc(g.dma_start(out=wfm[:], in_=convw[d]), 16)
                sLD.inc(g.dma_start(out=cbfm[:], in_=convb[d]), 16)
                sLD.inc(g.dma_start(out=nwsb[:], in_=nwrep[d]), 16)
                sLD.inc(g.dma_start(out=drsb[:], in_=drep[d]), 16)
                sLD.inc(g.dma_start(out=dtbsb[:], in_=dtb4[d]), 16)
                sLD.inc(g.dma_start(out=alogsb[:], in_=alog4[d]), 16)
                for kt in range(NKT):
                    sLD.inc(g.dma_start(out=windt[:, kt * DTC:(kt + 1) * DTC],
                                        in_=win[d][kt * 128:(kt + 1) * 128, DIN + CCH:DPROJ]), 16)
            on("gpsimd", f)

        def load_win_z(d):
            def f(g, d=d):
                for kt in range(NKT):
                    sLD.inc(g.dma_start(out=winsb[:, kt * CCH:kt * CCH + DIN],
                                        in_=win[d][kt * 128:(kt + 1) * 128, 0:DIN]), 16)
            on("gpsimd", f)

        def load_win_xbc(d):
            def f(g, d=d):
                for kt in range(NKT):
                    sLD.inc(g.dma_start(out=winsb[:, kt * CCH:kt * CCH + CCH],
                                        in_=win[d][kt * 128:(kt + 1) * 128, DIN:DIN + CCH]), 16)
            on("gpsimd", f)

        def load_wout(d):
            def f(g, d=d):
                for j in range(16):
                    sLD.inc(g.dma_start(out=woutsb[:, j * DM:(j + 1) * DM],
                                        in_=wout[d][j * 128:(j + 1) * 128, :]), 16)
            on("gpsimd", f)

        def emit_phA(d, di):
            ld_t = sLD.n
            tr_tt = sTR.n
            tr_t = sTR.n

            def f(t):
                t.wait_ge(sLD.h, ld_t)
                t.wait_ge(sTR.h, tr_t)
                if pha_parts < 1:
                    sDT.inc(t.drain())
                    return
                for q in range(4):
                    for kt in range(NKT):
                        li = t.matmul(pYA[q * 32:(q + 1) * 32, 0:512],
                                      windt[:, kt * DTC:(kt + 1) * DTC],
                                      xnfm[:, kt * L + q * 512:kt * L + (q + 1) * 512],
                                      start=(kt == 0), stop=(kt == NKT - 1),
                                      tile_position=(0, q * 32))
                sDT.inc(li)
            on("tensor", f)

            for k in range(NCH if pha_parts >= 2 else 0):
                def f(t, k=k):
                    for g in range(4):
                        gi = sWG.n
                        if gi >= 2 and pha_parts >= 3:
                            t.wait_ge(sWE.h, gi - 1)
                        for kt in range(NKT):
                            li = t.matmul(pwin[:, (g % 2) * 512:(g % 2 + 1) * 512],
                                          xnfm[:, kt * L + k * Q:kt * L + (k + 1) * Q],
                                          winsb[:, kt * CCH + g * 512:kt * CCH + (g + 1) * 512],
                                          start=(kt == 0), stop=(kt == NKT - 1))
                        sWG.inc(li)
                on("tensor", f)

                if pha_parts < 3:
                    continue

                def f(sc):
                    base = sWG.n - 4
                    wait(sc, sSZ)
                    if debug:
                        wait(sc, sOUT)
                    for g in range(4):
                        sc.wait_ge(sWG.h, base + g + 1)
                        sWE.inc(sc.activation(szb[0][:, g * 512:(g + 1) * 512],
                                              pwin[:, (g % 2) * 512:(g % 2 + 1) * 512],
                                              AF.Silu))
                on("scalar", f)

                if pha_parts < 4:
                    continue

                def f(sync, k=k):
                    wait(sync, sWE)
                    sSZ.inc(sync.dma_start(out=szh[k * Q:(k + 1) * Q, :], in_=szb[0][:]), 16)
                on("sync", f)

                if d == "f" and k == 0:
                    dbg_dump("t_sz", szb[0][:], sWE)

            if pha_parts < 5:
                return
            def f(v):
                wait(v, sDT)
                v.tensor_scalar_add(clt2[:], pYA[:, 0:512], dtbsb[:])
                sDT.inc(v.tensor_scalar(clt1[:], pYA[:, 0:512], dtbsb[:], 0.0,
                                        op0=ALU.add, op1=ALU.max))
            on("vector", f)

            def f(sc):
                wait(sc, sDT)
                sc.activation(s4[:], clt2[:], AF.Abs)
                afence(sc)
                sc.activation(s4[:], s4[:], AF.Exp, scale=-1.0)
                afence(sc)
                sc.activation(s4[:], s4[:], AF.Ln, bias=1.0)
                sDT.inc(sc.activation(acol[:], alogsb[:], AF.Exp))
            on("scalar", f)

            def f(v):
                wait(v, sDT)
                v.tensor_tensor(dtsp4[:], clt1[:], s4[:], op=ALU.add)
                vfence(v)
                sDT.inc(v.tensor_scalar_mul(clt2[:], dtsp4[:], acol[:]))
            on("vector", f)

            def f(sc):
                wait(sc, sDT)
                sDT.inc(sc.activation(a4[:], clt2[:], AF.Exp, scale=-1.0))
            on("scalar", f)

            if pha_parts < 6:
                return
            def f(v):
                wait(v, sDT)
                for k in range(NCH):
                    qq, cc = k // 4, (k % 4) * 128
                    v.tensor_tensor_scan(
                        s4[qq * 32:(qq + 1) * 32, cc:cc + 128],
                        clt2[qq * 32:(qq + 1) * 32, cc:cc + 128],
                        clt2[qq * 32:(qq + 1) * 32, cc:cc + 128],
                        0.0, op0=ALU.add, op1=ALU.bypass)
                sDT.inc(v.engine_nop())
            on("vector", f)

            def f(sc):
                wait(sc, sDT)
                sDT.inc(sc.activation(es4[:], s4[:], AF.Exp, scale=-1.0))
            on("scalar", f)

            if pha_parts < 7:
                return
            for k in range(NCH):
                qq, cc = k // 4, (k % 4) * 128

                def f(t, k=k, qq=qq, cc=cc):
                    wait(t, sDT, sBE)
                    for col, srcm in ((0, dtsp4), (32, es4), (64, a4)):
                        li = t.matmul(psmall[:, col:col + 32],
                                      srcm[qq * 32:(qq + 1) * 32, cc:cc + 128],
                                      if32[qq * 32:(qq + 1) * 32, qq * 32:(qq + 1) * 32],
                                      start=True, stop=True, tile_position=(qq * 32, 0))
                    sBC.inc(li)
                on("tensor", f)

                def f(t, k=k, qq=qq, cc=cc):
                    sBC.inc(t.matmul(psmall[0:1, 96:128],
                                     es4[qq * 32:(qq + 1) * 32, cc + 127:cc + 128],
                                     if32[qq * 32:(qq + 1) * 32, qq * 32:(qq + 1) * 32],
                                     start=True, stop=True, tile_position=(qq * 32, 0)))
                on("tensor", f)

                def f(sc, k=k):
                    wait(sc, sBC)
                    sc.copy(dts[:, k * 96:k * 96 + 96], psmall[:, 0:96])
                    sBE.inc(sc.copy(wcols[0:1, 0:32], psmall[0:1, 96:128]))
                on("scalar", f)

                def f(t, k=k):
                    wait(t, sBE)
                    sBC.inc(t.matmul(psmall[:, 64:96], ones32[:], wcols[0:1, 0:32],
                                     start=True, stop=True))
                on("tensor", f)

                def f(v, k=k):
                    wait(v, sBC)
                    sBE.inc(v.tensor_copy(eslr[:, k * 32:(k + 1) * 32],
                                          psmall[0:64, 64:96]))
                on("vector", f)

            if d == "f":
                dbg_dump("t_dtsp", dtsp4[:], sDT, sBE)
                dbg_dump("t_a4", a4[:], sDT)
                dbg_dump("t_ets", dts[:, 0:64], sBE)

        def emit_phB(d, di):
            ld_t = sLD.n

            def f(g):
                wait(g, sHB, sY)
                g.memset(Hm[:], 0.0)
                g.memset(Hbf[:], 0.0)
                sHB.inc(g.memset(xbcw[:].rearrange("p (c w) -> p c w", c=17)[:, :, 0:3], 0.0))
            on("gpsimd", f)

            def f(sync):
                wait(sync, sSZ)
                sSZ.inc(sync.dma_start(out=szb[0][:], in_=szh[0:Q, :]), 16)
            on("sync", f)

            gsz = [512, 512, 512, 512, 128]
            for k in range(NCH):
                u = k % 2
                cs = k * Q

                if k + 1 < NCH:
                    def f(sync, k=k):
                        wait(sync, sEV)
                        sSZ.inc(sync.dma_start(out=szb[(k + 1) % 2][:],
                                               in_=szh[(k + 1) * Q:(k + 2) * Q, :]), 16)
                    on("sync", f)

                def f(t, k=k, cs=cs):
                    t.wait_ge(sLD.h, ld_t)
                    off = 0
                    for g, n in enumerate(gsz):
                        gi = sWG.n
                        if gi >= 2:
                            t.wait_ge(sWE.h, gi - 1)
                        for kt in range(NKT):
                            li = t.matmul(pwin[:, (g % 2) * 512:(g % 2) * 512 + n],
                                          xnfm[:, kt * L + cs:kt * L + cs + Q],
                                          winsb[:, kt * CCH + off:kt * CCH + off + n],
                                          start=(kt == 0), stop=(kt == NKT - 1))
                        sWG.inc(li)
                        off += n
                on("tensor", f)

                def f(sc):
                    base = sWG.n - 5
                    off = 0
                    for g, n in enumerate(gsz):
                        sc.wait_ge(sWG.h, base + g + 1)
                        sWE.inc(sc.copy(xbc[:, off:off + n],
                                        pwin[:, (g % 2) * 512:(g % 2) * 512 + n]))
                        off += n
                on("scalar", f)

                tile_rounds = [(0, 4), (4, 4), (8, 4), (12, 4), (16, 1)]
                for r0, rn in tile_rounds:
                    def f(t, r0=r0, rn=rn):
                        wait(t, sWE, sQE, sAE)
                        for i in range(rn):
                            ct = r0 + i
                            li = t.matmul(pQT[:, i * 128:(i + 1) * 128],
                                          xbc[:, ct * 128:(ct + 1) * 128], ibf[:],
                                          start=True, stop=True)
                        sQT.inc(li)
                    on("tensor", f)

                    def f(v, r0=r0, rn=rn):
                        wait(v, sQT, sCD, sCV)
                        o = xbcw[:].rearrange("p (c w) -> p c w", c=17)[:, r0:r0 + rn, 3:131]
                        i_ = pQT[:, 0:rn * 128].rearrange("p (c w) -> p c w", c=rn)
                        sQE.inc(v.tensor_copy(o, i_))
                    on("vector", f)

                def f(v):
                    wait(v, sQE, sCS, sHB)
                    w3 = xbcw[:].rearrange("p (c w) -> p c w", c=17)
                    for ct in range(17):
                        v.tensor_scalar(cpc[:, ct * 128:(ct + 1) * 128],
                                        w3[:, ct, 0:128],
                                        wfm[:, ct:ct + 1], cbfm[:, ct:ct + 1],
                                        op0=ALU.mult, op1=ALU.add)
                    for j in (1, 2, 3):
                        vfence(v)
                        for ct in range(17):
                            acc = cpc[:, ct * 128:(ct + 1) * 128]
                            v.scalar_tensor_tensor(acc, w3[:, ct, j:j + 128],
                                                   wfm[:, j * 17 + ct:j * 17 + ct + 1],
                                                   acc, op0=ALU.mult, op1=ALU.add)
                    sCD.inc(v.engine_nop())
                on("vector", f)



                def f(v, k=k):
                    wait(v, sCD, sCV)
                    w3 = xbcw[:].rearrange("p (c w) -> p c w", c=17)
                    sQE.inc(v.tensor_copy(w3[:, :, 0:3], w3[:, :, 128:131]))
                on("vector", f)

                def f(sc):
                    wait(sc, sCD, sCV)
                    sCS.inc(sc.activation(cpc[:], cpc[:], AF.Silu))
                on("scalar", f)

                for r0, rn in tile_rounds:
                    def f(t, r0=r0, rn=rn):
                        wait(t, sCS, sQE)
                        for i in range(rn):
                            ct = r0 + i
                            li = t.matmul(pQT[:, i * 128:(i + 1) * 128],
                                          cpc[:, ct * 128:(ct + 1) * 128], ibf[:],
                                          start=True, stop=True)
                        sQT.inc(li)
                    on("tensor", f)

                    def f(v, r0=r0, rn=rn):
                        wait(v, sQT)
                        if debug:
                            wait(v, sOUT)
                        o = xsbc[:, r0 * 128:(r0 + rn) * 128]
                        sQE.inc(v.tensor_copy(o, pQT[:, 0:rn * 128]))
                    on("vector", f)

                def f(v):
                    wait(v, sQE)
                    sCS.inc(v.engine_nop())
                on("vector", f)

                if d == "f" and k == 0:
                    dbg_dump("t_xsbc", xsbc[:], sCS)

                def f(t):
                    wait(t, sCS, sGE)
                    t.matmul(psmall[0:64, 0:128], xsbc[:, DIN:DIN + 64], ibf[:],
                             start=True, stop=True)
                    sBC.inc(t.matmul(psmall[0:64, 128:256], xsbc[:, DIN + 64:DIN + 128],
                                     ibf[:], start=True, stop=True))
                on("tensor", f)

                def f(sc):
                    wait(sc, sBC)
                    sc.copy(sbf[:], psmall[0:64, 0:128])
                    sBE.inc(sc.copy(scf[:], psmall[0:64, 128:256]))
                on("scalar", f)

                def f(t):
                    wait(t, sBE)
                    sGW.inc(t.matmul(psmall[:, 256:384], sbf[:], scf[:],
                                     start=True, stop=True))
                on("tensor", f)

                def f(v):
                    wait(v, sGW)
                    sGE.inc(v.tensor_copy(sg[:], psmall[:, 256:384]))
                on("vector", f)

                for half in range(2):
                    h0 = half * 16
                    for piece in range(4):
                        hh0 = h0 + piece * 4

                        def f(t, k=k, hh0=hh0):
                            wait(t, sSC, sQE, sAE)
                            for i in range(4):
                                h = hh0 + i
                                acolb = dts[:, k * 96 + 64 + h:k * 96 + 64 + h + 1] \
                                    .broadcast_to((128, 128))
                                li = t.matmul(pQT[:, i * 128:(i + 1) * 128], acolb, if32[:],
                                              start=True, stop=True)
                            sAR.inc(li)
                        on("tensor", f)

                        def f(sc, piece=piece):
                            wait(sc, sAR)
                            sAE.inc(sc.copy(arep[:, (piece % 2) * 512:(piece % 2 + 1) * 512],
                                            pQT[:]))
                        on("scalar", f)

                        def f(v, hh0=hh0, piece=piece, half=half):
                            wait(v, sAE)
                            if debug:
                                wait(v, sOUT)
                            if half == 1 and piece == 0:
                                wait(v, sY)
                            for i in range(4):
                                h = hh0 + i
                                hh = h % 16
                                v.tensor_tensor_scan(
                                    Ebuf[:, hh * 128:(hh + 1) * 128],
                                    arep[:, (piece % 2) * 512 + i * 128:
                                         (piece % 2) * 512 + (i + 1) * 128],
                                    if32[:], 0.0, op0=ALU.mult, op1=ALU.add)
                            sSC.inc(v.engine_nop())
                        on("vector", f)

                    def f(v, k=k, h0=h0):
                        wait(v, sSC)
                        for i in range(16):
                            h = h0 + i
                            v.tensor_copy(wcols[:, h:h + 1],
                                          Ebuf[:, i * 128 + 127:i * 128 + 128])
                        sWC.inc(v.engine_nop())
                    on("vector", f)

                    def f(v):
                        wait(v, sGE, sWC)
                        e3 = Ebuf[:].rearrange("p (h t) -> p h t", h=16)
                        gb = sg[:].unsqueeze(1).broadcast_to((128, 16, 128))
                        sMT.inc(v.tensor_tensor(e3, e3, gb, op=ALU.mult))
                    on("vector", f)

                    if d == "f" and k == 0 and half == 0:
                        dbg_dump("t_mt", Ebuf[:], sMT)

                    def f(sc, k=k, h0=h0, half=half):
                        wait(sc, sCS)
                        if half == 1:
                            wait(sc, sST)
                        if debug:
                            wait(sc, sOUT)
                        for i in range(16):
                            h = h0 + i
                            li = sc.activation(xhat[:, i * 64:(i + 1) * 64],
                                               xsbc[:, h * 64:(h + 1) * 64], AF.Copy,
                                               scale=dts[:, k * 96 + h:k * 96 + h + 1])
                        sXH.inc(li)
                    on("scalar", f)

                    def f(sc, k=k, h0=h0, half=half):
                        wait(sc, sWC)
                        if half == 1:
                            wait(sc, sST)
                        if debug:
                            wait(sc, sOUT)
                        for i in range(16):
                            h = h0 + i
                            li = sc.activation(bwb[:, i * 64:(i + 1) * 64],
                                               xsbc[:, DIN:DIN + 64], AF.Copy,
                                               scale=wcols[:, h:h + 1])
                        sBW.inc(li)
                    on("scalar", f)

                    if d == "f" and k == 0 and half == 1:
                        dbg_dump("t_wc", wcols[:], sWC)
                        dbg_dump("t_bw", bwb[:], sBW)

                    if d == "f" and k == 0 and half == 0:
                        dbg_dump("t_xh", xhat[:], sXH)

                    def f(t, h0=h0, half=half):
                        wait(t, sMT, sXH, sHB, sEV, sHU, sOD)
                        for i in range(16):
                            t.matmul(pYA[:, i * 64:(i + 1) * 64],
                                     Ebuf[:, i * 128:(i + 1) * 128],
                                     xhat[:, i * 64:(i + 1) * 64], start=True, stop=True)
                        for i in range(16):
                            h = h0 + i
                            li = t.matmul(pYB[:, i * 64:(i + 1) * 64], scf[:],
                                          Hbf[:, h * 64:(h + 1) * 64],
                                          start=True, stop=True)
                        sY.inc(li)
                    on("tensor", f)

                    def f(v, h0=h0):
                        wait(v, sCS, sSQ)
                        sEV.inc(v.tensor_tensor(dxsb[:], xsbc[:, h0 * 64:h0 * 64 + 1024],
                                                drsb[:, h0 * 64:h0 * 64 + 1024], op=ALU.mult))
                    on("vector", f)

                    def f(v, k=k, h0=h0, u=u):
                        wait(v, sY, sSZ, sSG, sEV)
                        if debug:
                            wait(v, sOUT)
                        v.tensor_tensor(tbuf[:], pYA[:], qb[:], op=ALU.add)
                        es = dts[:, k * 96 + 32 + h0:k * 96 + 32 + h0 + 16] \
                            .unsqueeze(-1).broadcast_to((128, 16, 64))
                        pyb3 = pYB[:].rearrange("p (h q) -> p h q", h=16)
                        vfence(v)
                        v.tensor_tensor(qb[:].rearrange("p (h q) -> p h q", h=16),
                                        pyb3, es, op=ALU.mult)
                        vfence(v)
                        v.tensor_tensor(qb[:], qb[:], tbuf[:], op=ALU.add)
                        vfence(v)
                        v.tensor_tensor(qb[:], qb[:], szb[u][:, h0 * 64:h0 * 64 + 1024],
                                        op=ALU.mult)
                        vfence(v)
                        sEV.inc(v.tensor_tensor(qhb[:, h0 * 64:h0 * 64 + 1024], qb[:],
                                                nwsb[:, h0 * 64:h0 * 64 + 1024], op=ALU.mult))
                    on("vector", f)

                    if d == "f" and k == 0 and half == 1:
                        dbg_dump("t_q", qb[:], sEV)

                    def f(sc, half=half):
                        wait(sc, sEV)
                        if debug:
                            wait(sc, sOUT)
                        sSQ.inc(sc.activation(qb[:], qb[:], AF.Square,
                                              accum_out=ssqt[:, half:half + 1]))
                    on("scalar", f)

                    def f(t, h0=h0, half=half):
                        wait(t, sBW, sEV)
                        tgt = pYA if half == 0 else pYB
                        for i in range(16):
                            li = t.matmul(tgt[0:64, i * 64:(i + 1) * 64],
                                          bwb[:, i * 64:(i + 1) * 64],
                                          xhat[:, i * 64:(i + 1) * 64],
                                          start=True, stop=True)
                        sST.inc(li)
                    on("tensor", f)

                    def f(v, k=k, h0=h0, half=half):
                        wait(v, sST)
                        if debug:
                            wait(v, sOUT)
                        tgt = pYA if half == 0 else pYB
                        el = eslr[:, k * 32 + h0:k * 32 + h0 + 16].unsqueeze(-1) \
                            .broadcast_to((64, 16, 64))
                        hview = Hm[0:64, h0 * 64:h0 * 64 + 1024] \
                            .rearrange("p (h q) -> p h q", h=16)
                        v.tensor_tensor(hview, hview, el, op=ALU.mult)
                        vfence(v)
                        sHU.inc(v.tensor_tensor(Hm[0:64, h0 * 64:h0 * 64 + 1024],
                                                Hm[0:64, h0 * 64:h0 * 64 + 1024],
                                                tgt[0:64, :], op=ALU.add))
                    on("vector", f)

                def f(v, di=di, k=k):
                    wait(v, sSQ)
                    sEV.inc(v.tensor_tensor(ssqall[:, di * NCH + k:di * NCH + k + 1],
                                            ssqt[:, 0:1], ssqt[:, 1:2], op=ALU.add))
                on("vector", f)

                def f(sc):
                    wait(sc, sHU)
                    sHB.inc(sc.copy(Hbf[:], Hm[:]))
                on("scalar", f)

                if d == "f" and k == 0:
                    dbg_dump("t_H", Hm[:], sHU)
                    dbg_dump("t_qh", qhb[:], sEV)

                for r in range(4):
                    def f(t, r=r):
                        wait(t, sEV, sQE, sAE)
                        for i in range(4):
                            j = r * 4 + i
                            li = t.matmul(pQT[:, i * 128:(i + 1) * 128],
                                          qhb[:, j * 128:(j + 1) * 128], ibf[:],
                                          start=True, stop=True)
                        sQT.inc(li)
                    on("tensor", f)

                    def f(v):
                        wait(v, sQT)
                        sQE.inc(v.tensor_copy(qfmb[:], pQT[:]))
                    on("vector", f)

                    def f(t, r=r):
                        wait(t, sQE)
                        if r == 0:
                            wait(t, sOD, sHU)
                        for i in range(4):
                            j = r * 4 + i
                            for nn in range(2):
                                li = t.matmul(pYB[:, nn * 512:(nn + 1) * 512],
                                              qfmb[:, i * 128:(i + 1) * 128],
                                              woutsb[:, j * DM + nn * 512:j * DM + (nn + 1) * 512],
                                              start=(j == 0), stop=(j == 15))
                        sWO.inc(li)
                    on("tensor", f)

                def f(v):
                    wait(v, sWO)
                    if debug:
                        wait(v, sOUT)
                    sOD.inc(v.tensor_copy(odb[:], pYB[:]))
                on("vector", f)

                def f(sync, d=d, k=k):
                    wait(sync, sOD)
                    sSG.inc(sync.dma_start(out=odh[d][k * Q:(k + 1) * Q, :], in_=odb[:]), 16)
                on("sync", f)

                if d == "f" and k == 0:
                    dbg_dump("t_od", odb[:], sOD)

        dirs = []
        if stages >= 2:
            dirs.append(("f", 0))
        if stages >= 4:
            dirs.append(("b", 1))
        for d, di in dirs:
            if d == "b":
                load_dir_small(d)
                load_win_z(d)
            load_xnfm(d)
            barrier()
            emit_phA(d, di)
            barrier()
            if stages >= 3 or d == "b":
                load_win_xbc(d)
                load_wout(d)
                barrier()
                emit_phB(d, di)
            barrier()

        if stages < 5:
            def f(sync):
                wait(sync, sOUT, sOU[0], sOU[1], sSG, sXNH, sSZ, sLD, sTR, sX)
            on("sync", f)
            return nc
        outfm = winsb
        wosb = woutsb

        def f(g):
            wait(g, sSG)
            for j in range(16):
                sLD.inc(g.dma_start(out=wosb[:, j * DM:(j + 1) * DM],
                                    in_=wo_in[j * 128:(j + 1) * 128, :]), 16)
            for j in range(NKT):
                sLD.inc(g.dma_start(out=wtsb[:, j * DM:(j + 1) * DM],
                                    in_=wt_in[j * 128:(j + 1) * 128, :]), 16)
            for kt in range(NKT):
                sLD.inc(g.dma_start(out=nwsb[:, kt * 32:(kt + 1) * 32],
                                    in_=vr_in[kt * 128:(kt + 1) * 128, :]), 16)
            sLD.inc(g.dma_start(out=drsb[0:32, 0:1024], in_=ct_in), 16)
            sLD.inc(g.dma_start(out=szb[0][:, 0:DM], in_=bo_in), 16)
            sLD.inc(g.dma_start(out=bwb[0:1, 0:1024], in_=ffb_in), 16)
        on("gpsimd", f)

        def f(sync):
            wait(sync, sSG)
            for ft in range(NKT):
                sTR.inc(sync.dma_start_transpose(
                    outfm[:, ft * 2048:(ft + 1) * 2048],
                    odh["f"][:, ft * 128:(ft + 1) * 128]), 16)
        on("sync", f)

        def f(sc):
            wait(sc, sEV)
            sc.activation(rall[:], ssqall[:], AF.Ln, bias=EPS_RMS, scale=1.0 / DIN)
            afence(sc)
            sO3.inc(sc.activation(rall[:], rall[:], AF.Exp, scale=-0.5))
        on("scalar", f)

        barrier()

        ph3_ld = sLD.n
        ph3_tr = sTR.n

        for k in range(NCH):
            u = k % 2
            kb = NCH - 1 - k

            def f(sync, k=k, u=u, kb=kb):
                wait(sync, sOU[u])
                if sQT.n >= 2:
                    sync.wait_ge(sQT.h, sQT.n - 2)
                sX.inc(sync.dma_start(out=sx[u], in_=x_in[k * Q:(k + 1) * Q, :]), 16)
                sX.inc(sync.dma_start(out=odb[:], in_=odh["b"][kb * Q:(kb + 1) * Q, :]), 16)
            on("sync", f)

            def f(v, kb=kb):
                wait(v, sX, sO3)
                sO3.inc(v.tensor_scalar_mul(odb[:], odb[:],
                                            rall[:, NCH + kb:NCH + kb + 1]))
            on("vector", f)

            for r in range(2):
                def f(t, r=r):
                    wait(t, sO3, sQE)
                    t.wait_ge(sLD.h, ph3_ld)
                    t.wait_ge(sTR.h, ph3_tr)
                    for i in range(4):
                        j = r * 4 + i
                        li = t.matmul(pQT[:, i * 128:(i + 1) * 128],
                                      odb[:, j * 128:(j + 1) * 128], ibf[:, 127::-1],
                                      start=True, stop=True)
                    sQT.inc(li)
                on("tensor", f)

                def f(v, r=r):
                    wait(v, sQT)
                    sQE.inc(v.tensor_copy(qhb[:, r * 512:(r + 1) * 512], pQT[:]))
                on("vector", f)

            def f(t, k=k):
                wait(t, sQE, sO3)
                for mt in range(8):
                    for nn in range(2):
                        t.matmul(pYA[:, nn * 512:(nn + 1) * 512],
                                 outfm[:, mt * 2048 + k * Q:mt * 2048 + (k + 1) * Q],
                                 wosb[:, mt * DM + nn * 512:mt * DM + (nn + 1) * 512],
                                 start=(mt == 0), stop=(mt == 7))
                for mt in range(8):
                    for nn in range(2):
                        li = t.matmul(pYB[:, nn * 512:(nn + 1) * 512],
                                      qhb[:, mt * 128:(mt + 1) * 128],
                                      wosb[:, (8 + mt) * DM + nn * 512:(8 + mt) * DM + (nn + 1) * 512],
                                      start=(mt == 0), stop=(mt == 7))
                sWO.inc(li)
            on("tensor", f)

            def f(v, k=k, u=u, kb=kb):
                wait(v, sWO)
                if debug:
                    wait(v, sOUT)
                v.scalar_tensor_tensor(sx[u], pYA[:], rall[:, k:k + 1], sx[u],
                                       op0=ALU.mult, op1=ALU.add)
                vfence(v)
                v.tensor_tensor(sx[u], pYB[:], sx[u], op=ALU.add)
                vfence(v)
                sO3.inc(v.tensor_tensor(sx[u], szb[0][:, 0:DM], sx[u], op=ALU.add))
            on("vector", f)

            if k == 0:
                dbg_dump("t_x2", sx[u], sO3)

            def f(sc, u=u):
                wait(sc, sO3)
                if debug:
                    wait(sc, sOUT)
                sc.activation(qb[:], sx[u], AF.Copy, accum_out=stat[:, 0:1])
                sSQ.inc(sc.activation(xn0, sx[u], AF.Square,
                                      accum_out=stat[:, 1:2]))
            on("scalar", f)

            def f(v):
                wait(v, sSQ)
                v.tensor_scalar_mul(stat[:, 2:3], stat[:, 0:1], 1.0 / DM)
                v.tensor_scalar_mul(stat[:, 3:4], stat[:, 1:2], 1.0 / DM)
                v.tensor_scalar(stat[:, 4:5], stat[:, 0:1], stat[:, 0:1],
                                1.0 / (DM * DM), op0=ALU.mult, op1=ALU.mult)
                vfence(v)
                sO3.inc(v.tensor_tensor(stat[:, 5:6], stat[:, 3:4], stat[:, 4:5],
                                        op=ALU.subtract))
            on("vector", f)

            def f(sc):
                wait(sc, sO3)
                sc.activation(stat[:, 6:7], stat[:, 5:6], AF.Ln, bias=EPS_LN)
                afence(sc)
                sSQ.inc(sc.activation(stat[:, 7:8], stat[:, 6:7], AF.Exp, scale=-0.5))
            on("scalar", f)

            def f(v, u=u):
                wait(v, sSQ, sQT)
                sO3.inc(v.tensor_scalar(xn0, sx[u], stat[:, 2:3], stat[:, 7:8],
                                        op0=ALU.subtract, op1=ALU.mult))
            on("vector", f)

            for r in range(2):
                def f(t, r=r):
                    wait(t, sO3, sQE)
                    for i in range(4):
                        j = r * 4 + i
                        li = t.matmul(pQT[:, i * 128:(i + 1) * 128],
                                      qhb[:, DM + j * 128:DM + (j + 1) * 128], ibf[:],
                                      start=True, stop=True)
                    sQT.inc(li)
                on("tensor", f)

                def f(v, r=r):
                    wait(v, sQT)
                    sQE.inc(v.tensor_copy(qb[:, r * 512:(r + 1) * 512], pQT[:]))
                on("vector", f)

            def f(t):
                wait(t, sQE, sBE)
                for kt in range(NKT):
                    li = t.matmul(psmall[0:32, 0:128],
                                  nwsb[:, kt * 32:(kt + 1) * 32],
                                  qb[:, kt * 128:(kt + 1) * 128],
                                  start=(kt == 0), stop=(kt == NKT - 1))
                sBC.inc(li)
            on("tensor", f)

            def f(sc):
                wait(sc, sBC)
                sBE.inc(sc.copy(scf[0:32, 0:128], psmall[0:32, 0:128]))
            on("scalar", f)

            def f(t):
                wait(t, sBE, sO3)
                for kt in range(NKT):
                    for nn in range(2):
                        t.matmul(pYA[:, nn * 512:(nn + 1) * 512],
                                 qb[:, kt * 128:(kt + 1) * 128],
                                 wtsb[:, kt * DM + nn * 512:kt * DM + (nn + 1) * 512],
                                 start=(kt == 0), stop=False)
                for nn in range(2):
                    t.matmul(pYA[:, nn * 512:(nn + 1) * 512], scf[0:32, 0:128],
                             drsb[0:32, nn * 512:(nn + 1) * 512],
                             start=False, stop=False)
                for nn in range(2):
                    li = t.matmul(pYA[:, nn * 512:(nn + 1) * 512], onesbf[:],
                                  bwb[0:1, nn * 512:(nn + 1) * 512],
                                  start=False, stop=True)
                sWO.inc(li)
            on("tensor", f)

            def f(v, u=u):
                wait(v, sWO)
                sO3.inc(v.tensor_tensor(sx[u], pYA[:], sx[u], op=ALU.add))
            on("vector", f)

            def f(sync, k=k, u=u):
                wait(sync, sO3)
                sOU[u].inc(sync.dma_start(out=y_out[k * Q:(k + 1) * Q, :], in_=sx[u]), 16)
            on("sync", f)

        def f(sync):
            wait(sync, sOUT, sOU[0], sOU[1])
        on("sync", f)

    return nc


def prep_host(inputs):
    g = lambda k: np.ascontiguousarray(np.asarray(inputs[k], dtype=np.float32))
    m = {}
    for d in "fb":
        m[f"win_{d}"] = g(f"L0_attn_{d}Win")
        m[f"wout_{d}"] = g(f"L0_attn_{d}Wout")
        cw = g(f"L0_attn_{d}convw")[:, 0, :]
        m[f"convw_{d}"] = np.ascontiguousarray(
            cw.reshape(4, 17, 128).transpose(2, 0, 1).reshape(128, 68))
        m[f"convb_{d}"] = np.ascontiguousarray(
            g(f"L0_attn_{d}convb").reshape(17, 128).T)
        m[f"dtb4_{d}"] = np.ascontiguousarray(np.tile(g(f"L0_attn_{d}dtb"), 4)[:, None])
        m[f"alog4_{d}"] = np.ascontiguousarray(np.tile(g(f"L0_attn_{d}Alog"), 4)[:, None])
        m[f"drep_{d}"] = np.ascontiguousarray(np.broadcast_to(
            np.repeat(g(f"L0_attn_{d}D"), P)[None, :], (128, DIN)))
        m[f"nwrep_{d}"] = np.ascontiguousarray(np.broadcast_to(
            g(f"L0_attn_{d}normw")[None, :], (128, DIN)))
    m["wo"] = g("L0_attn_Wo")
    m["bo"] = np.ascontiguousarray(np.broadcast_to(g("L0_attn_bo")[None, :], (128, DM)))
    m["wt"] = np.ascontiguousarray(g("L0_ffn_W").T)
    m["ffb"] = g("L0_ffn_b")[None, :]
    m["vr"] = g("L0_ffn_V_r")
    m["ct"] = np.ascontiguousarray(g("L0_ffn_C").T)
    m["if32"] = np.eye(128, dtype=np.float32)
    m["ibf"] = np.eye(128, dtype=ml_dtypes.bfloat16)
    return m


def kernel(**inputs):
    nc = build(debug=False)
    shared = prep_host(inputs)
    x = np.ascontiguousarray(np.asarray(inputs["L0_attn_x"], dtype=np.float32))
    in_maps = []
    for b in range(8):
        im = dict(shared)
        im["x"] = np.ascontiguousarray(x[b])
        im["xrev"] = np.ascontiguousarray(x[b][::-1])
        in_maps.append(im)
    res = run_bass_kernel_spmd(nc, in_maps, core_ids=list(range(8)))
    out = np.stack([r["y"] for r in res.results], axis=0)
    return out.astype(np.float32)


# version 07
# speedup vs baseline: 1.1036x; 1.0646x over previous
import contextlib
import numpy as np
import ml_dtypes
import concourse.bass as bass
import concourse.mybir as mybir
from concourse.bass_utils import run_bass_kernel_spmd

F32 = mybir.dt.float32
BF16 = mybir.dt.bfloat16
AF = mybir.ActivationFunctionType
ALU = mybir.AluOpType

L = 2048
DM = 1024
Q = 128
NCH = L // Q
NH = 32
P = 64
NST = 64
DIN = 2048
CCH = DIN + 2 * NST
DTC = NH
DPROJ = 2 * DIN + 2 * NST + NH
NKT = DM // 128
EPS_LN = 1e-5
EPS_RMS = 1e-5


class Sem:
    def __init__(self, stack, nc, name):
        self.h = stack.enter_context(nc.semaphore(name))
        self.n = 0

    def inc(self, instr, k=1):
        instr.then_inc(self.h, k)
        self.n += k
        return instr


def wait(eng, *sems):
    for s in sems:
        if s.n:
            eng.wait_ge(s.h, s.n)


def build(debug=False, stages=5, pha_parts=9):
    nc = bass.Bass("TRN2", target_bir_lowering=False)
    dram = {}

    def din(name, shape, dt=F32):
        dram[name] = nc.dram_tensor(name, shape, dt, kind="ExternalInput").ap()
        return dram[name]

    def dout(name, shape, dt=F32):
        dram[name] = nc.dram_tensor(name, shape, dt, kind="ExternalOutput").ap()
        return dram[name]

    def dint(name, shape, dt=BF16):
        dram[name] = nc.dram_tensor(name, shape, dt, kind="Internal").ap()
        return dram[name]

    x_in = din("x", [L, DM])
    xrev_in = din("xrev", [L, DM])
    win = {d: din(f"win_{d}", [DM, DPROJ]) for d in "fb"}
    wout = {d: din(f"wout_{d}", [DIN, DM]) for d in "fb"}
    convw = {d: din(f"convw_{d}", [128, 4 * 17]) for d in "fb"}
    convb = {d: din(f"convb_{d}", [128, 17]) for d in "fb"}
    dtb4 = {d: din(f"dtb4_{d}", [128, 1]) for d in "fb"}
    alog4 = {d: din(f"alog4_{d}", [128, 1]) for d in "fb"}
    drep = {d: din(f"drep_{d}", [128, DIN]) for d in "fb"}
    nwrep = {d: din(f"nwrep_{d}", [128, DIN]) for d in "fb"}
    wo_in = din("wo", [DIN, DM])
    bo_in = din("bo", [128, DM])
    wt_in = din("wt", [DM, DM])
    ffb_in = din("ffb", [1, DM])
    vr_in = din("vr", [DM, 32])
    ct_in = din("ct", [32, DM])
    if32_in = din("if32", [128, 128])
    ibf_in = din("ibf", [128, 128], BF16)

    y_out = dout("y", [L, DM])

    xnh = {"f": dint("xnh", [L, DM]), "b": dint("xnrevh", [L, DM])}
    szh = dint("szh", [L, DIN])
    odh = {"f": dint("odhf", [L, DM]), "b": dint("odhb", [L, DM])}

    dbg = {}
    if debug:
        for nm, shp, dt in [
            ("t_xn", [128, DM], BF16), ("t_sz", [128, DIN], BF16),
            ("t_xsbc", [128, CCH], BF16), ("t_dtsp", [128, 512], F32),
            ("t_a4", [128, 512], F32), ("t_ets", [128, 64], F32),
            ("t_mt", [128, 2048], BF16), ("t_xh", [128, 1024], BF16),
            ("t_q", [128, 1024], BF16), ("t_qh", [128, DIN], BF16),
            ("t_wc", [128, 32], F32), ("t_bw", [128, 1024], BF16),
            ("t_od", [128, DM], BF16), ("t_H", [64, 2048], F32),
            ("t_x2", [128, DM], F32),
        ]:
            dbg[nm] = dout(nm, shp, dt)

    st = contextlib.ExitStack()
    with st:
        sb = lambda name, shape, dt: st.enter_context(nc.sbuf_tensor(name, shape, dt))
        xnfm = sb("xnfm", [128, NKT * L], BF16)
        winsb = sb("winsb", [128, NKT * CCH], BF16)
        windt = sb("windt", [128, NKT * DTC], BF16)
        woutsb = sb("woutsb", [128, 16 * DM], BF16)
        wtsb = sb("wtsb", [128, NKT * DM], BF16)
        wfm = sb("wfm", [128, 4 * 17], F32)
        cbfm = sb("cbfm", [128, 17], F32)
        nwsb = sb("nwsb", [128, DIN], BF16)
        drsb = sb("drsb", [128, DIN], BF16)
        dtbsb = sb("dtbsb", [128, 1], F32)
        alogsb = sb("alogsb", [128, 1], F32)
        acol = sb("acol", [128, 1], F32)
        if32 = sb("if32_s", [128, 128], F32)
        ibf = sb("ibf_s", [128, 128], BF16)
        ones32 = sb("ones32", [1, 128], F32)
        onesbf = sb("onesbf", [1, 128], BF16)
        ssqall = sb("ssqall", [128, 2 * NCH], F32)
        rall = sb("rall", [128, 2 * NCH], F32)
        dtsp4 = sb("dtsp4", [128, 512], F32)
        s4 = sb("s4", [128, 512], F32)
        dts = sb("dts", [128, NCH * 96], F32)
        eslr = sb("eslr", [64, NCH * 32], F32)
        clt1 = sb("clt1", [128, 512], F32)
        clt2 = sb("clt2", [128, 512], F32)
        a4 = clt1
        es4 = s4
        xbc = sb("xbc", [128, CCH], BF16)
        xbcw = sb("xbcw", [128, 17 * 131], BF16)
        cpa = sb("cpa", [128, CCH], BF16)
        cpc = sb("cpc", [128, CCH], BF16)
        szb = [sb(f"szb{i}", [128, DIN], BF16) for i in range(2)]
        arep = sb("arep", [128, 1024], F32)
        Ebuf = sb("Ebuf", [128, 2048], BF16)
        xhat = sb("xhat", [128, 1024], BF16)
        bwb = sb("bwb", [128, 1024], BF16)
        wcols = sb("wcols", [128, 32], F32)
        qb = sb("qb", [128, 1024], BF16)
        qhb = sb("qhb", [128, 2048], BF16)
        qfmb = sb("qfmb", [128, 512], BF16)
        Hm = sb("Hm", [64, 2048], F32)
        Hbf = sb("Hbf", [64, 2048], BF16)
        odb = sb("odb", [128, DM], BF16)
        sbf = sb("sbf", [64, 128], BF16)
        scf = sb("scf", [64, 128], BF16)
        sg = sb("sg", [128, 128], BF16)
        tbuf = sb("tbuf", [128, 1024], BF16)
        ssqt = sb("ssqt", [128, 2], F32)
        stat = sb("stat", [128, 8], F32)
        ceps = sb("ceps", [128, 2], F32)
        nc.const_aps.register(F32, EPS_LN, ceps[:, 0:1])
        nc.const_aps.register(F32, 1.0, ceps[:, 1:2])

        sx = [arep[:], Ebuf[:].bitcast(F32)]
        xn0 = qhb[:, DM:2 * DM]
        odb = tbuf
        xsbc = cpa
        dxsb = qb

        ps = lambda name, shape: st.enter_context(nc.psum_tensor(name, shape, F32))
        pwin = ps("pwin", [128, 1024])
        psmall = ps("psmall", [128, 512])
        pYA = ps("pYA", [128, 1024])
        pYB = ps("pYB", [128, 1024])
        pQT = ps("pQT", [128, 512])

        SS = lambda name: Sem(st, nc, name)
        sLD = SS("sLD"); sX = SS("sX"); sXN = SS("sXN"); sXNH = SS("sXNH")
        sTR = SS("sTR"); sWG = SS("sWG"); sWE = SS("sWE"); sSZ = SS("sSZ")
        sDT = SS("sDT"); sCV = SS("sCV"); sCD = SS("sCD"); sCS = SS("sCS")
        sBC = SS("sBC"); sBE = SS("sBE"); sGW = SS("sGW"); sGE = SS("sGE")
        sAR = SS("sAR"); sAE = SS("sAE"); sSC = SS("sSC"); sWC = SS("sWC")
        sMT = SS("sMT"); sXH = SS("sXH"); sBW = SS("sBW"); sY = SS("sY")
        sEV = SS("sEV"); sSQ = SS("sSQ"); sST = SS("sST"); sHU = SS("sHU")
        sHB = SS("sHB"); sQT = SS("sQT"); sQE = SS("sQE"); sWO = SS("sWO")
        sOD = SS("sOD"); sSG = SS("sSG"); sBAR = SS("sBAR"); sO3 = SS("sO3")
        sOUT = SS("sOUT"); sIN = SS("sIN"); sFV = SS("sFV"); sFA = SS("sFA")
        sOU = [SS("sOU0"), SS("sOU1")]

        block = st.enter_context(nc.Block())
        ENGS = ("tensor", "vector", "scalar", "gpsimd", "sync")

        def on(eng_name, fn):
            getattr(block, eng_name)(fn)

        def vfence(v):
            sFV.inc(v.engine_nop())
            v.wait_ge(sFV.h, sFV.n)

        def afence(sc):
            sFA.inc(sc.copy(ceps[0:1, 1:2], ceps[0:1, 1:2]))
            sc.wait_ge(sFA.h, sFA.n)

        def barrier():
            def f(sync):
                wait(sync, sOUT, sOU[0], sOU[1], sSG, sXNH, sSZ, sX, sTR, sLD)
            on("sync", f)
            target = sBAR.n + 5
            for e in ENGS:
                def bf(eng, e=e):
                    sBAR.inc(eng.nop() if e == "sync" else eng.drain())
                on(e, bf)
            for e in ENGS:
                on(e, lambda eng: eng.wait_ge(sBAR.h, target))
            assert sBAR.n == target

        def dbg_dump(name, src_ap, *wait_sems):
            if not debug:
                return
            def f(sync):
                wait(sync, *wait_sems)
                sOUT.inc(sync.dma_start(out=dbg[name], in_=src_ap), 16)
            on("sync", f)

        def f(g):
            g.memset(ceps[:, 0:1], EPS_LN)
            g.memset(ceps[:, 1:2], 1.0)
            g.memset(ones32[:], 1.0)
            sIN.inc(g.memset(onesbf[:], 1.0))
            for kt in range(NKT):
                sLD.inc(g.dma_start(out=winsb[:, kt * CCH:kt * CCH + DIN],
                                    in_=win["f"][kt * 128:(kt + 1) * 128, 0:DIN]), 16)
            sLD.inc(g.dma_start(out=wfm[:], in_=convw["f"]), 16)
            sLD.inc(g.dma_start(out=cbfm[:], in_=convb["f"]), 16)
            sLD.inc(g.dma_start(out=nwsb[:], in_=nwrep["f"]), 16)
            sLD.inc(g.dma_start(out=drsb[:], in_=drep["f"]), 16)
            sLD.inc(g.dma_start(out=dtbsb[:], in_=dtb4["f"]), 16)
            sLD.inc(g.dma_start(out=alogsb[:], in_=alog4["f"]), 16)
            for kt in range(NKT):
                sLD.inc(g.dma_start(out=windt[:, kt * DTC:(kt + 1) * DTC],
                                    in_=win["f"][kt * 128:(kt + 1) * 128, DIN + CCH:DPROJ]), 16)
        on("gpsimd", f)

        def f(sync):
            sTR.inc(sync.dma_start(out=if32[:], in_=if32_in), 16)
            sTR.inc(sync.dma_start(out=ibf[:], in_=ibf_in), 16)
        on("sync", f)

        for di, (d, src) in enumerate([("f", x_in), ("b", xrev_in)]):
            for k in range(NCH):
                u = k % 2

                def f(sync, k=k, u=u, src=src):
                    if sXN.n >= 4:
                        sync.wait_ge(sXN.h, sXN.n - 2)
                    sX.inc(sync.dma_start(out=sx[u], in_=src[k * Q:(k + 1) * Q, :]), 16)
                on("sync", f)

                def f(sc, u=u):
                    sc.wait_ge(sX.h, sX.n)
                    wait(sc, sIN, sXNH)
                    if debug:
                        wait(sc, sOUT)
                    sc.activation(tbuf[:], sx[u], AF.Copy, accum_out=stat[:, 0:1])
                    sSQ.inc(sc.activation(xn0, sx[u], AF.Square,
                                          accum_out=stat[:, 1:2]))
                on("scalar", f)

                def f(v):
                    v.wait_ge(sSQ.h, sSQ.n)
                    v.tensor_scalar_mul(stat[:, 2:3], stat[:, 0:1], 1.0 / DM)
                    v.tensor_scalar_mul(stat[:, 3:4], stat[:, 1:2], 1.0 / DM)
                    v.tensor_scalar(stat[:, 4:5], stat[:, 0:1], stat[:, 0:1],
                                    1.0 / (DM * DM), op0=ALU.mult, op1=ALU.mult)
                    vfence(v)
                    sXN.inc(v.tensor_tensor(stat[:, 5:6], stat[:, 3:4], stat[:, 4:5],
                                            op=ALU.subtract))
                on("vector", f)

                def f(sc):
                    sc.wait_ge(sXN.h, sXN.n)
                    sc.activation(stat[:, 6:7], stat[:, 5:6], AF.Ln, bias=EPS_LN)
                    afence(sc)
                    sSQ.inc(sc.activation(stat[:, 7:8], stat[:, 6:7], AF.Exp, scale=-0.5))
                on("scalar", f)

                def f(v, u=u):
                    v.wait_ge(sSQ.h, sSQ.n)
                    wait(v, sXNH)
                    if debug:
                        wait(v, sOUT)
                    sXN.inc(v.tensor_scalar(xn0, sx[u], stat[:, 2:3], stat[:, 7:8],
                                            op0=ALU.subtract, op1=ALU.mult))
                on("vector", f)

                def f(sync, d=d, k=k):
                    sync.wait_ge(sXN.h, sXN.n)
                    sXNH.inc(sync.dma_start(out=xnh[d][k * Q:(k + 1) * Q, :], in_=xn0), 16)
                on("sync", f)

                if d == "f" and k == 0:
                    dbg_dump("t_xn", xn0, sXN)

        barrier()

        def load_xnfm(d):
            def f(sync, d=d):
                wait(sync, sXNH)
                for ft in range(NKT):
                    sTR.inc(sync.dma_start_transpose(
                        xnfm[:, ft * L:(ft + 1) * L],
                        xnh[d][:, ft * 128:(ft + 1) * 128]), 16)
            on("sync", f)

        def load_dir_small(d):
            def f(g, d=d):
                sLD.inc(g.dma_start(out=wfm[:], in_=convw[d]), 16)
                sLD.inc(g.dma_start(out=cbfm[:], in_=convb[d]), 16)
                sLD.inc(g.dma_start(out=nwsb[:], in_=nwrep[d]), 16)
                sLD.inc(g.dma_start(out=drsb[:], in_=drep[d]), 16)
                sLD.inc(g.dma_start(out=dtbsb[:], in_=dtb4[d]), 16)
                sLD.inc(g.dma_start(out=alogsb[:], in_=alog4[d]), 16)
                for kt in range(NKT):
                    sLD.inc(g.dma_start(out=windt[:, kt * DTC:(kt + 1) * DTC],
                                        in_=win[d][kt * 128:(kt + 1) * 128, DIN + CCH:DPROJ]), 16)
            on("gpsimd", f)

        def load_win_z(d):
            def f(g, d=d):
                for kt in range(NKT):
                    sLD.inc(g.dma_start(out=winsb[:, kt * CCH:kt * CCH + DIN],
                                        in_=win[d][kt * 128:(kt + 1) * 128, 0:DIN]), 16)
            on("gpsimd", f)

        def load_win_xbc(d):
            def f(g, d=d):
                for kt in range(NKT):
                    sLD.inc(g.dma_start(out=winsb[:, kt * CCH:kt * CCH + CCH],
                                        in_=win[d][kt * 128:(kt + 1) * 128, DIN:DIN + CCH]), 16)
            on("gpsimd", f)

        def load_wout(d):
            def f(g, d=d):
                for j in range(16):
                    sLD.inc(g.dma_start(out=woutsb[:, j * DM:(j + 1) * DM],
                                        in_=wout[d][j * 128:(j + 1) * 128, :]), 16)
            on("gpsimd", f)

        def emit_phA(d, di):
            ld_t = sLD.n
            tr_tt = sTR.n
            tr_t = sTR.n

            def f(t):
                t.wait_ge(sLD.h, ld_t)
                t.wait_ge(sTR.h, tr_t)
                if pha_parts < 1:
                    sDT.inc(t.drain())
                    return
                for q in range(4):
                    for kt in range(NKT):
                        li = t.matmul(pYA[q * 32:(q + 1) * 32, 0:512],
                                      windt[:, kt * DTC:(kt + 1) * DTC],
                                      xnfm[:, kt * L + q * 512:kt * L + (q + 1) * 512],
                                      start=(kt == 0), stop=(kt == NKT - 1),
                                      tile_position=(0, q * 32))
                sDT.inc(li)
            on("tensor", f)

            for k in range(NCH if pha_parts >= 2 else 0):
                def f(t, k=k):
                    for g in range(4):
                        gi = sWG.n
                        if gi >= 2 and pha_parts >= 3:
                            t.wait_ge(sWE.h, gi - 1)
                        for kt in range(NKT):
                            li = t.matmul(pwin[:, (g % 2) * 512:(g % 2 + 1) * 512],
                                          xnfm[:, kt * L + k * Q:kt * L + (k + 1) * Q],
                                          winsb[:, kt * CCH + g * 512:kt * CCH + (g + 1) * 512],
                                          start=(kt == 0), stop=(kt == NKT - 1))
                        sWG.inc(li)
                on("tensor", f)

                if pha_parts < 3:
                    continue

                def f(sc):
                    base = sWG.n - 4
                    wait(sc, sSZ)
                    if debug:
                        wait(sc, sOUT)
                    for g in range(4):
                        sc.wait_ge(sWG.h, base + g + 1)
                        sWE.inc(sc.activation(szb[0][:, g * 512:(g + 1) * 512],
                                              pwin[:, (g % 2) * 512:(g % 2 + 1) * 512],
                                              AF.Silu))
                on("scalar", f)

                if pha_parts < 4:
                    continue

                def f(sync, k=k):
                    wait(sync, sWE)
                    sSZ.inc(sync.dma_start(out=szh[k * Q:(k + 1) * Q, :], in_=szb[0][:]), 16)
                on("sync", f)

                if d == "f" and k == 0:
                    dbg_dump("t_sz", szb[0][:], sWE)

            if pha_parts < 5:
                return
            def f(v):
                wait(v, sDT)
                v.tensor_scalar_add(clt2[:], pYA[:, 0:512], dtbsb[:])
                sDT.inc(v.tensor_scalar(clt1[:], pYA[:, 0:512], dtbsb[:], 0.0,
                                        op0=ALU.add, op1=ALU.max))
            on("vector", f)

            def f(sc):
                wait(sc, sDT)
                sc.activation(s4[:], clt2[:], AF.Abs)
                afence(sc)
                sc.activation(s4[:], s4[:], AF.Exp, scale=-1.0)
                afence(sc)
                sc.activation(s4[:], s4[:], AF.Ln, bias=1.0)
                sDT.inc(sc.activation(acol[:], alogsb[:], AF.Exp))
            on("scalar", f)

            def f(v):
                wait(v, sDT)
                v.tensor_tensor(dtsp4[:], clt1[:], s4[:], op=ALU.add)
                vfence(v)
                sDT.inc(v.tensor_scalar_mul(clt2[:], dtsp4[:], acol[:]))
            on("vector", f)

            def f(sc):
                wait(sc, sDT)
                sDT.inc(sc.activation(a4[:], clt2[:], AF.Exp, scale=-1.0))
            on("scalar", f)

            if pha_parts < 6:
                return
            def f(v):
                wait(v, sDT)
                for k in range(NCH):
                    qq, cc = k // 4, (k % 4) * 128
                    v.tensor_tensor_scan(
                        s4[qq * 32:(qq + 1) * 32, cc:cc + 128],
                        clt2[qq * 32:(qq + 1) * 32, cc:cc + 128],
                        clt2[qq * 32:(qq + 1) * 32, cc:cc + 128],
                        0.0, op0=ALU.add, op1=ALU.bypass)
                sDT.inc(v.engine_nop())
            on("vector", f)

            def f(sc):
                wait(sc, sDT)
                sDT.inc(sc.activation(es4[:], s4[:], AF.Exp, scale=-1.0))
            on("scalar", f)

            if pha_parts < 7:
                return
            for k in range(NCH):
                qq, cc = k // 4, (k % 4) * 128

                def f(t, k=k, qq=qq, cc=cc):
                    wait(t, sDT, sBE)
                    for col, srcm in ((0, dtsp4), (32, es4), (64, a4)):
                        li = t.matmul(psmall[:, col:col + 32],
                                      srcm[qq * 32:(qq + 1) * 32, cc:cc + 128],
                                      if32[qq * 32:(qq + 1) * 32, qq * 32:(qq + 1) * 32],
                                      start=True, stop=True, tile_position=(qq * 32, 0))
                    sBC.inc(li)
                on("tensor", f)

                def f(t, k=k, qq=qq, cc=cc):
                    sBC.inc(t.matmul(psmall[0:1, 96:128],
                                     es4[qq * 32:(qq + 1) * 32, cc + 127:cc + 128],
                                     if32[qq * 32:(qq + 1) * 32, qq * 32:(qq + 1) * 32],
                                     start=True, stop=True, tile_position=(qq * 32, 0)))
                on("tensor", f)

                def f(sc, k=k):
                    wait(sc, sBC)
                    sc.copy(dts[:, k * 96:k * 96 + 96], psmall[:, 0:96])
                    sBE.inc(sc.copy(wcols[0:1, 0:32], psmall[0:1, 96:128]))
                on("scalar", f)

                def f(t, k=k):
                    wait(t, sBE)
                    sBC.inc(t.matmul(psmall[:, 64:96], ones32[:], wcols[0:1, 0:32],
                                     start=True, stop=True))
                on("tensor", f)

                def f(v, k=k):
                    wait(v, sBC)
                    sBE.inc(v.tensor_copy(eslr[:, k * 32:(k + 1) * 32],
                                          psmall[0:64, 64:96]))
                on("vector", f)

            if d == "f":
                dbg_dump("t_dtsp", dtsp4[:], sDT, sBE)
                dbg_dump("t_a4", a4[:], sDT)
                dbg_dump("t_ets", dts[:, 0:64], sBE)

        def emit_phB(d, di):
            ld_t = sLD.n

            def f(g):
                wait(g, sHB, sY)
                g.memset(Hm[:], 0.0)
                g.memset(Hbf[:], 0.0)
                sHB.inc(g.memset(xbcw[:].rearrange("p (c w) -> p c w", c=17)[:, :, 0:3], 0.0))
            on("gpsimd", f)

            def f(sync):
                wait(sync, sSZ)
                sSZ.inc(sync.dma_start(out=szb[0][:], in_=szh[0:Q, :]), 16)
            on("sync", f)

            gsz = [512, 512, 512, 512, 128]
            for k in range(NCH):
                u = k % 2
                cs = k * Q

                if k + 1 < NCH:
                    def f(sync, k=k):
                        wait(sync, sEV)
                        sSZ.inc(sync.dma_start(out=szb[(k + 1) % 2][:],
                                               in_=szh[(k + 1) * Q:(k + 2) * Q, :]), 16)
                    on("sync", f)

                def f(t, k=k, cs=cs):
                    t.wait_ge(sLD.h, ld_t)
                    off = 0
                    for g, n in enumerate(gsz):
                        gi = sWG.n
                        if gi >= 2:
                            t.wait_ge(sWE.h, gi - 1)
                        for kt in range(NKT):
                            li = t.matmul(pwin[:, (g % 2) * 512:(g % 2) * 512 + n],
                                          xnfm[:, kt * L + cs:kt * L + cs + Q],
                                          winsb[:, kt * CCH + off:kt * CCH + off + n],
                                          start=(kt == 0), stop=(kt == NKT - 1))
                        sWG.inc(li)
                        off += n
                on("tensor", f)

                def f(v):
                    base = sWG.n - 5
                    off = 0
                    for g, n in enumerate(gsz):
                        v.wait_ge(sWG.h, base + g + 1)
                        sWE.inc(v.tensor_copy(xbc[:, off:off + n],
                                              pwin[:, (g % 2) * 512:(g % 2) * 512 + n]))
                        off += n
                on("vector", f)

                tile_rounds = [(0, 4), (4, 4), (8, 4), (12, 4), (16, 1)]
                for r0, rn in tile_rounds:
                    def f(t, r0=r0, rn=rn):
                        wait(t, sWE, sQE, sAE)
                        for i in range(rn):
                            ct = r0 + i
                            li = t.matmul(pQT[:, i * 128:(i + 1) * 128],
                                          xbc[:, ct * 128:(ct + 1) * 128], ibf[:],
                                          start=True, stop=True)
                        sQT.inc(li)
                    on("tensor", f)

                    def f(v, r0=r0, rn=rn):
                        wait(v, sQT, sCD, sCV)
                        o = xbcw[:].rearrange("p (c w) -> p c w", c=17)[:, r0:r0 + rn, 3:131]
                        i_ = pQT[:, 0:rn * 128].rearrange("p (c w) -> p c w", c=rn)
                        sQE.inc(v.tensor_copy(o, i_))
                    on("vector", f)

                def f(v):
                    wait(v, sQE, sCS, sHB)
                    w3 = xbcw[:].rearrange("p (c w) -> p c w", c=17)
                    for ct in range(17):
                        v.tensor_scalar(cpc[:, ct * 128:(ct + 1) * 128],
                                        w3[:, ct, 0:128],
                                        wfm[:, ct:ct + 1], cbfm[:, ct:ct + 1],
                                        op0=ALU.mult, op1=ALU.add)
                    for j in (1, 2, 3):
                        vfence(v)
                        for ct in range(17):
                            acc = cpc[:, ct * 128:(ct + 1) * 128]
                            v.scalar_tensor_tensor(acc, w3[:, ct, j:j + 128],
                                                   wfm[:, j * 17 + ct:j * 17 + ct + 1],
                                                   acc, op0=ALU.mult, op1=ALU.add)
                    sCD.inc(v.engine_nop())
                on("vector", f)



                def f(v, k=k):
                    wait(v, sCD, sCV)
                    w3 = xbcw[:].rearrange("p (c w) -> p c w", c=17)
                    sQE.inc(v.tensor_copy(w3[:, :, 0:3], w3[:, :, 128:131]))
                on("vector", f)

                def f(sc):
                    wait(sc, sCD, sCV)
                    sCS.inc(sc.activation(cpc[:], cpc[:], AF.Silu))
                on("scalar", f)

                for r0, rn in tile_rounds:
                    def f(t, r0=r0, rn=rn):
                        wait(t, sCS, sQE)
                        for i in range(rn):
                            ct = r0 + i
                            li = t.matmul(pQT[:, i * 128:(i + 1) * 128],
                                          cpc[:, ct * 128:(ct + 1) * 128], ibf[:],
                                          start=True, stop=True)
                        sQT.inc(li)
                    on("tensor", f)

                    def f(v, r0=r0, rn=rn):
                        wait(v, sQT)
                        if debug:
                            wait(v, sOUT)
                        o = xsbc[:, r0 * 128:(r0 + rn) * 128]
                        sQE.inc(v.tensor_copy(o, pQT[:, 0:rn * 128]))
                    on("vector", f)

                def f(v):
                    wait(v, sQE)
                    sCS.inc(v.engine_nop())
                on("vector", f)

                if d == "f" and k == 0:
                    dbg_dump("t_xsbc", xsbc[:], sCS)

                def f(t):
                    wait(t, sCS, sGE)
                    t.matmul(psmall[0:64, 0:128], xsbc[:, DIN:DIN + 64], ibf[:],
                             start=True, stop=True)
                    sBC.inc(t.matmul(psmall[0:64, 128:256], xsbc[:, DIN + 64:DIN + 128],
                                     ibf[:], start=True, stop=True))
                on("tensor", f)

                def f(v):
                    wait(v, sBC)
                    v.tensor_copy(sbf[:], psmall[0:64, 0:128])
                    sBE.inc(v.tensor_copy(scf[:], psmall[0:64, 128:256]))
                on("vector", f)

                def f(t):
                    wait(t, sBE)
                    sGW.inc(t.matmul(psmall[:, 256:384], sbf[:], scf[:],
                                     start=True, stop=True))
                on("tensor", f)

                def f(v):
                    wait(v, sGW)
                    sGE.inc(v.tensor_copy(sg[:], psmall[:, 256:384]))
                on("vector", f)

                for half in range(2):
                    h0 = half * 16
                    for piece in range(4):
                        hh0 = h0 + piece * 4

                        def f(t, k=k, hh0=hh0):
                            wait(t, sSC, sQE, sAE)
                            for i in range(4):
                                h = hh0 + i
                                acolb = dts[:, k * 96 + 64 + h:k * 96 + 64 + h + 1] \
                                    .broadcast_to((128, 128))
                                li = t.matmul(pQT[:, i * 128:(i + 1) * 128], acolb, if32[:],
                                              start=True, stop=True)
                            sAR.inc(li)
                        on("tensor", f)

                        def f(v, piece=piece):
                            wait(v, sAR)
                            sAE.inc(v.tensor_copy(arep[:, (piece % 2) * 512:(piece % 2 + 1) * 512],
                                                  pQT[:]))
                        on("vector", f)

                        def f(v, hh0=hh0, piece=piece, half=half):
                            wait(v, sAE)
                            if debug:
                                wait(v, sOUT)
                            if half == 1 and piece == 0:
                                wait(v, sY)
                            for i in range(4):
                                h = hh0 + i
                                hh = h % 16
                                v.tensor_tensor_scan(
                                    Ebuf[:, hh * 128:(hh + 1) * 128],
                                    arep[:, (piece % 2) * 512 + i * 128:
                                         (piece % 2) * 512 + (i + 1) * 128],
                                    if32[:], 0.0, op0=ALU.mult, op1=ALU.add)
                            sSC.inc(v.engine_nop())
                        on("vector", f)

                    def f(v, k=k, h0=h0):
                        wait(v, sSC)
                        for i in range(16):
                            h = h0 + i
                            v.tensor_copy(wcols[:, h:h + 1],
                                          Ebuf[:, i * 128 + 127:i * 128 + 128])
                        sWC.inc(v.engine_nop())
                    on("vector", f)

                    def f(v):
                        wait(v, sGE, sWC)
                        e3 = Ebuf[:].rearrange("p (h t) -> p h t", h=16)
                        gb = sg[:].unsqueeze(1).broadcast_to((128, 16, 128))
                        sMT.inc(v.tensor_tensor(e3, e3, gb, op=ALU.mult))
                    on("vector", f)

                    if d == "f" and k == 0 and half == 0:
                        dbg_dump("t_mt", Ebuf[:], sMT)

                    def f(v, k=k, h0=h0, half=half):
                        wait(v, sCS)
                        if half == 1:
                            wait(v, sST)
                        if debug:
                            wait(v, sOUT)
                        for i in range(16):
                            h = h0 + i
                            li = v.tensor_scalar_mul(xhat[:, i * 64:(i + 1) * 64],
                                                     xsbc[:, h * 64:(h + 1) * 64],
                                                     dts[:, k * 96 + h:k * 96 + h + 1])
                        sXH.inc(li)
                    on("vector", f)

                    def f(v, k=k, h0=h0, half=half):
                        wait(v, sWC)
                        if half == 1:
                            wait(v, sST)
                        if debug:
                            wait(v, sOUT)
                        for i in range(16):
                            h = h0 + i
                            li = v.tensor_scalar_mul(bwb[:, i * 64:(i + 1) * 64],
                                                     xsbc[:, DIN:DIN + 64],
                                                     wcols[:, h:h + 1])
                        sBW.inc(li)
                    on("vector", f)

                    if d == "f" and k == 0 and half == 1:
                        dbg_dump("t_wc", wcols[:], sWC)
                        dbg_dump("t_bw", bwb[:], sBW)

                    if d == "f" and k == 0 and half == 0:
                        dbg_dump("t_xh", xhat[:], sXH)

                    def f(t, h0=h0, half=half):
                        wait(t, sMT, sXH, sHB, sEV, sHU, sOD)
                        for i in range(16):
                            t.matmul(pYA[:, i * 64:(i + 1) * 64],
                                     Ebuf[:, i * 128:(i + 1) * 128],
                                     xhat[:, i * 64:(i + 1) * 64], start=True, stop=True)
                        for i in range(16):
                            h = h0 + i
                            li = t.matmul(pYB[:, i * 64:(i + 1) * 64], scf[:],
                                          Hbf[:, h * 64:(h + 1) * 64],
                                          start=True, stop=True)
                        sY.inc(li)
                    on("tensor", f)

                    def f(v, h0=h0):
                        wait(v, sCS, sSQ)
                        sEV.inc(v.tensor_tensor(dxsb[:], xsbc[:, h0 * 64:h0 * 64 + 1024],
                                                drsb[:, h0 * 64:h0 * 64 + 1024], op=ALU.mult))
                    on("vector", f)

                    def f(v, k=k, h0=h0, u=u):
                        wait(v, sY, sSZ, sSG, sEV)
                        if debug:
                            wait(v, sOUT)
                        v.tensor_tensor(tbuf[:], pYA[:], qb[:], op=ALU.add)
                        es = dts[:, k * 96 + 32 + h0:k * 96 + 32 + h0 + 16] \
                            .unsqueeze(-1).broadcast_to((128, 16, 64))
                        pyb3 = pYB[:].rearrange("p (h q) -> p h q", h=16)
                        vfence(v)
                        v.tensor_tensor(qb[:].rearrange("p (h q) -> p h q", h=16),
                                        pyb3, es, op=ALU.mult)
                        vfence(v)
                        v.tensor_tensor(qb[:], qb[:], tbuf[:], op=ALU.add)
                        vfence(v)
                        v.tensor_tensor(qb[:], qb[:], szb[u][:, h0 * 64:h0 * 64 + 1024],
                                        op=ALU.mult)
                        vfence(v)
                        sEV.inc(v.tensor_tensor(qhb[:, h0 * 64:h0 * 64 + 1024], qb[:],
                                                nwsb[:, h0 * 64:h0 * 64 + 1024], op=ALU.mult))
                    on("vector", f)

                    if d == "f" and k == 0 and half == 1:
                        dbg_dump("t_q", qb[:], sEV)

                    def f(v, half=half):
                        wait(v, sEV)
                        if debug:
                            wait(v, sOUT)
                        sSQ.inc(v.scalar_tensor_tensor(qb[:], qb[:], 0.0, qb[:],
                                                       op0=ALU.add, op1=ALU.mult,
                                                       accum_out=ssqt[:, half:half + 1]))
                    on("vector", f)

                    def f(t, h0=h0, half=half):
                        wait(t, sBW, sEV)
                        tgt = pYA if half == 0 else pYB
                        for i in range(16):
                            li = t.matmul(tgt[0:64, i * 64:(i + 1) * 64],
                                          bwb[:, i * 64:(i + 1) * 64],
                                          xhat[:, i * 64:(i + 1) * 64],
                                          start=True, stop=True)
                        sST.inc(li)
                    on("tensor", f)

                    def f(v, k=k, h0=h0, half=half):
                        wait(v, sST)
                        if debug:
                            wait(v, sOUT)
                        tgt = pYA if half == 0 else pYB
                        el = eslr[:, k * 32 + h0:k * 32 + h0 + 16].unsqueeze(-1) \
                            .broadcast_to((64, 16, 64))
                        hview = Hm[0:64, h0 * 64:h0 * 64 + 1024] \
                            .rearrange("p (h q) -> p h q", h=16)
                        v.tensor_tensor(hview, hview, el, op=ALU.mult)
                        vfence(v)
                        sHU.inc(v.tensor_tensor(Hm[0:64, h0 * 64:h0 * 64 + 1024],
                                                Hm[0:64, h0 * 64:h0 * 64 + 1024],
                                                tgt[0:64, :], op=ALU.add))
                    on("vector", f)

                def f(v, di=di, k=k):
                    wait(v, sSQ)
                    sEV.inc(v.tensor_tensor(ssqall[:, di * NCH + k:di * NCH + k + 1],
                                            ssqt[:, 0:1], ssqt[:, 1:2], op=ALU.add))
                on("vector", f)

                def f(v):
                    wait(v, sHU)
                    sHB.inc(v.tensor_copy(Hbf[:], Hm[:]))
                on("vector", f)

                if d == "f" and k == 0:
                    dbg_dump("t_H", Hm[:], sHU)
                    dbg_dump("t_qh", qhb[:], sEV)

                for r in range(4):
                    def f(t, r=r):
                        wait(t, sEV, sQE, sAE)
                        for i in range(4):
                            j = r * 4 + i
                            li = t.matmul(pQT[:, i * 128:(i + 1) * 128],
                                          qhb[:, j * 128:(j + 1) * 128], ibf[:],
                                          start=True, stop=True)
                        sQT.inc(li)
                    on("tensor", f)

                    def f(v):
                        wait(v, sQT)
                        sQE.inc(v.tensor_copy(qfmb[:], pQT[:]))
                    on("vector", f)

                    def f(t, r=r):
                        wait(t, sQE)
                        if r == 0:
                            wait(t, sOD, sHU)
                        for i in range(4):
                            j = r * 4 + i
                            for nn in range(2):
                                li = t.matmul(pYB[:, nn * 512:(nn + 1) * 512],
                                              qfmb[:, i * 128:(i + 1) * 128],
                                              woutsb[:, j * DM + nn * 512:j * DM + (nn + 1) * 512],
                                              start=(j == 0), stop=(j == 15))
                        sWO.inc(li)
                    on("tensor", f)

                def f(v):
                    wait(v, sWO)
                    if debug:
                        wait(v, sOUT)
                    sOD.inc(v.tensor_copy(odb[:], pYB[:]))
                on("vector", f)

                def f(sync, d=d, k=k):
                    wait(sync, sOD)
                    sSG.inc(sync.dma_start(out=odh[d][k * Q:(k + 1) * Q, :], in_=odb[:]), 16)
                on("sync", f)

                if d == "f" and k == 0:
                    dbg_dump("t_od", odb[:], sOD)

        dirs = []
        if stages >= 2:
            dirs.append(("f", 0))
        if stages >= 4:
            dirs.append(("b", 1))
        for d, di in dirs:
            if d == "b":
                load_dir_small(d)
                load_win_z(d)
            load_xnfm(d)
            barrier()
            emit_phA(d, di)
            barrier()
            if stages >= 3 or d == "b":
                load_win_xbc(d)
                load_wout(d)
                barrier()
                emit_phB(d, di)
            barrier()

        if stages < 5:
            def f(sync):
                wait(sync, sOUT, sOU[0], sOU[1], sSG, sXNH, sSZ, sLD, sTR, sX)
            on("sync", f)
            return nc
        outfm = winsb
        wosb = woutsb

        def f(g):
            wait(g, sSG)
            for j in range(16):
                sLD.inc(g.dma_start(out=wosb[:, j * DM:(j + 1) * DM],
                                    in_=wo_in[j * 128:(j + 1) * 128, :]), 16)
            for j in range(NKT):
                sLD.inc(g.dma_start(out=wtsb[:, j * DM:(j + 1) * DM],
                                    in_=wt_in[j * 128:(j + 1) * 128, :]), 16)
            for kt in range(NKT):
                sLD.inc(g.dma_start(out=nwsb[:, kt * 32:(kt + 1) * 32],
                                    in_=vr_in[kt * 128:(kt + 1) * 128, :]), 16)
            sLD.inc(g.dma_start(out=drsb[0:32, 0:1024], in_=ct_in), 16)
            sLD.inc(g.dma_start(out=szb[0][:, 0:DM], in_=bo_in), 16)
            sLD.inc(g.dma_start(out=bwb[0:1, 0:1024], in_=ffb_in), 16)
        on("gpsimd", f)

        def f(sync):
            wait(sync, sSG)
            for ft in range(NKT):
                sTR.inc(sync.dma_start_transpose(
                    outfm[:, ft * 2048:(ft + 1) * 2048],
                    odh["f"][:, ft * 128:(ft + 1) * 128]), 16)
        on("sync", f)

        def f(sc):
            wait(sc, sEV)
            sc.activation(rall[:], ssqall[:], AF.Ln, bias=EPS_RMS, scale=1.0 / DIN)
            afence(sc)
            sO3.inc(sc.activation(rall[:], rall[:], AF.Exp, scale=-0.5))
        on("scalar", f)

        barrier()

        ph3_ld = sLD.n
        ph3_tr = sTR.n

        for k in range(NCH):
            u = k % 2
            kb = NCH - 1 - k

            def f(sync, k=k, u=u, kb=kb):
                wait(sync, sOU[u])
                if sQT.n >= 2:
                    sync.wait_ge(sQT.h, sQT.n - 2)
                sX.inc(sync.dma_start(out=sx[u], in_=x_in[k * Q:(k + 1) * Q, :]), 16)
                sX.inc(sync.dma_start(out=odb[:], in_=odh["b"][kb * Q:(kb + 1) * Q, :]), 16)
            on("sync", f)

            def f(v, kb=kb):
                wait(v, sX, sO3)
                sO3.inc(v.tensor_scalar_mul(odb[:], odb[:],
                                            rall[:, NCH + kb:NCH + kb + 1]))
            on("vector", f)

            for r in range(2):
                def f(t, r=r):
                    wait(t, sO3, sQE)
                    t.wait_ge(sLD.h, ph3_ld)
                    t.wait_ge(sTR.h, ph3_tr)
                    for i in range(4):
                        j = r * 4 + i
                        li = t.matmul(pQT[:, i * 128:(i + 1) * 128],
                                      odb[:, j * 128:(j + 1) * 128], ibf[:, 127::-1],
                                      start=True, stop=True)
                    sQT.inc(li)
                on("tensor", f)

                def f(v, r=r):
                    wait(v, sQT)
                    sQE.inc(v.tensor_copy(qhb[:, r * 512:(r + 1) * 512], pQT[:]))
                on("vector", f)

            def f(t, k=k):
                wait(t, sQE, sO3)
                for mt in range(8):
                    for nn in range(2):
                        t.matmul(pYA[:, nn * 512:(nn + 1) * 512],
                                 outfm[:, mt * 2048 + k * Q:mt * 2048 + (k + 1) * Q],
                                 wosb[:, mt * DM + nn * 512:mt * DM + (nn + 1) * 512],
                                 start=(mt == 0), stop=(mt == 7))
                for mt in range(8):
                    for nn in range(2):
                        li = t.matmul(pYB[:, nn * 512:(nn + 1) * 512],
                                      qhb[:, mt * 128:(mt + 1) * 128],
                                      wosb[:, (8 + mt) * DM + nn * 512:(8 + mt) * DM + (nn + 1) * 512],
                                      start=(mt == 0), stop=(mt == 7))
                sWO.inc(li)
            on("tensor", f)

            def f(v, k=k, u=u, kb=kb):
                wait(v, sWO)
                if debug:
                    wait(v, sOUT)
                v.scalar_tensor_tensor(sx[u], pYA[:], rall[:, k:k + 1], sx[u],
                                       op0=ALU.mult, op1=ALU.add)
                vfence(v)
                v.tensor_tensor(sx[u], pYB[:], sx[u], op=ALU.add)
                vfence(v)
                sO3.inc(v.tensor_tensor(sx[u], szb[0][:, 0:DM], sx[u], op=ALU.add))
            on("vector", f)

            if k == 0:
                dbg_dump("t_x2", sx[u], sO3)

            def f(sc, u=u):
                wait(sc, sO3)
                if debug:
                    wait(sc, sOUT)
                sc.activation(qb[:], sx[u], AF.Copy, accum_out=stat[:, 0:1])
                sSQ.inc(sc.activation(xn0, sx[u], AF.Square,
                                      accum_out=stat[:, 1:2]))
            on("scalar", f)

            def f(v):
                wait(v, sSQ)
                v.tensor_scalar_mul(stat[:, 2:3], stat[:, 0:1], 1.0 / DM)
                v.tensor_scalar_mul(stat[:, 3:4], stat[:, 1:2], 1.0 / DM)
                v.tensor_scalar(stat[:, 4:5], stat[:, 0:1], stat[:, 0:1],
                                1.0 / (DM * DM), op0=ALU.mult, op1=ALU.mult)
                vfence(v)
                sO3.inc(v.tensor_tensor(stat[:, 5:6], stat[:, 3:4], stat[:, 4:5],
                                        op=ALU.subtract))
            on("vector", f)

            def f(sc):
                wait(sc, sO3)
                sc.activation(stat[:, 6:7], stat[:, 5:6], AF.Ln, bias=EPS_LN)
                afence(sc)
                sSQ.inc(sc.activation(stat[:, 7:8], stat[:, 6:7], AF.Exp, scale=-0.5))
            on("scalar", f)

            def f(v, u=u):
                wait(v, sSQ, sQT)
                sO3.inc(v.tensor_scalar(xn0, sx[u], stat[:, 2:3], stat[:, 7:8],
                                        op0=ALU.subtract, op1=ALU.mult))
            on("vector", f)

            for r in range(2):
                def f(t, r=r):
                    wait(t, sO3, sQE)
                    for i in range(4):
                        j = r * 4 + i
                        li = t.matmul(pQT[:, i * 128:(i + 1) * 128],
                                      qhb[:, DM + j * 128:DM + (j + 1) * 128], ibf[:],
                                      start=True, stop=True)
                    sQT.inc(li)
                on("tensor", f)

                def f(v, r=r):
                    wait(v, sQT)
                    sQE.inc(v.tensor_copy(qb[:, r * 512:(r + 1) * 512], pQT[:]))
                on("vector", f)

            def f(t):
                wait(t, sQE, sBE)
                for kt in range(NKT):
                    li = t.matmul(psmall[0:32, 0:128],
                                  nwsb[:, kt * 32:(kt + 1) * 32],
                                  qb[:, kt * 128:(kt + 1) * 128],
                                  start=(kt == 0), stop=(kt == NKT - 1))
                sBC.inc(li)
            on("tensor", f)

            def f(sc):
                wait(sc, sBC)
                sBE.inc(sc.copy(scf[0:32, 0:128], psmall[0:32, 0:128]))
            on("scalar", f)

            def f(t):
                wait(t, sBE, sO3)
                for kt in range(NKT):
                    for nn in range(2):
                        t.matmul(pYA[:, nn * 512:(nn + 1) * 512],
                                 qb[:, kt * 128:(kt + 1) * 128],
                                 wtsb[:, kt * DM + nn * 512:kt * DM + (nn + 1) * 512],
                                 start=(kt == 0), stop=False)
                for nn in range(2):
                    t.matmul(pYA[:, nn * 512:(nn + 1) * 512], scf[0:32, 0:128],
                             drsb[0:32, nn * 512:(nn + 1) * 512],
                             start=False, stop=False)
                for nn in range(2):
                    li = t.matmul(pYA[:, nn * 512:(nn + 1) * 512], onesbf[:],
                                  bwb[0:1, nn * 512:(nn + 1) * 512],
                                  start=False, stop=True)
                sWO.inc(li)
            on("tensor", f)

            def f(v, u=u):
                wait(v, sWO)
                sO3.inc(v.tensor_tensor(sx[u], pYA[:], sx[u], op=ALU.add))
            on("vector", f)

            def f(sync, k=k, u=u):
                wait(sync, sO3)
                sOU[u].inc(sync.dma_start(out=y_out[k * Q:(k + 1) * Q, :], in_=sx[u]), 16)
            on("sync", f)

        def f(sync):
            wait(sync, sOUT, sOU[0], sOU[1])
        on("sync", f)

    return nc


def prep_host(inputs):
    g = lambda k: np.ascontiguousarray(np.asarray(inputs[k], dtype=np.float32))
    m = {}
    for d in "fb":
        m[f"win_{d}"] = g(f"L0_attn_{d}Win")
        m[f"wout_{d}"] = g(f"L0_attn_{d}Wout")
        cw = g(f"L0_attn_{d}convw")[:, 0, :]
        m[f"convw_{d}"] = np.ascontiguousarray(
            cw.reshape(4, 17, 128).transpose(2, 0, 1).reshape(128, 68))
        m[f"convb_{d}"] = np.ascontiguousarray(
            g(f"L0_attn_{d}convb").reshape(17, 128).T)
        m[f"dtb4_{d}"] = np.ascontiguousarray(np.tile(g(f"L0_attn_{d}dtb"), 4)[:, None])
        m[f"alog4_{d}"] = np.ascontiguousarray(np.tile(g(f"L0_attn_{d}Alog"), 4)[:, None])
        m[f"drep_{d}"] = np.ascontiguousarray(np.broadcast_to(
            np.repeat(g(f"L0_attn_{d}D"), P)[None, :], (128, DIN)))
        m[f"nwrep_{d}"] = np.ascontiguousarray(np.broadcast_to(
            g(f"L0_attn_{d}normw")[None, :], (128, DIN)))
    m["wo"] = g("L0_attn_Wo")
    m["bo"] = np.ascontiguousarray(np.broadcast_to(g("L0_attn_bo")[None, :], (128, DM)))
    m["wt"] = np.ascontiguousarray(g("L0_ffn_W").T)
    m["ffb"] = g("L0_ffn_b")[None, :]
    m["vr"] = g("L0_ffn_V_r")
    m["ct"] = np.ascontiguousarray(g("L0_ffn_C").T)
    m["if32"] = np.eye(128, dtype=np.float32)
    m["ibf"] = np.eye(128, dtype=ml_dtypes.bfloat16)
    return m


def kernel(**inputs):
    nc = build(debug=False)
    shared = prep_host(inputs)
    x = np.ascontiguousarray(np.asarray(inputs["L0_attn_x"], dtype=np.float32))
    in_maps = []
    for b in range(8):
        im = dict(shared)
        im["x"] = np.ascontiguousarray(x[b])
        im["xrev"] = np.ascontiguousarray(x[b][::-1])
        in_maps.append(im)
    res = run_bass_kernel_spmd(nc, in_maps, core_ids=list(range(8)))
    out = np.stack([r["y"] for r in res.results], axis=0)
    return out.astype(np.float32)


# version 08
# speedup vs baseline: 1.1133x; 1.0088x over previous
import contextlib
import numpy as np
import ml_dtypes
import concourse.bass as bass
import concourse.mybir as mybir
from concourse.bass_utils import run_bass_kernel_spmd

F32 = mybir.dt.float32
BF16 = mybir.dt.bfloat16
AF = mybir.ActivationFunctionType
ALU = mybir.AluOpType

L = 2048
DM = 1024
Q = 128
NCH = L // Q
NH = 32
P = 64
NST = 64
DIN = 2048
CCH = DIN + 2 * NST
DTC = NH
DPROJ = 2 * DIN + 2 * NST + NH
NKT = DM // 128
EPS_LN = 1e-5
EPS_RMS = 1e-5


class Sem:
    def __init__(self, stack, nc, name):
        self.h = stack.enter_context(nc.semaphore(name))
        self.n = 0

    def inc(self, instr, k=1):
        instr.then_inc(self.h, k)
        self.n += k
        return instr


def wait(eng, *sems):
    for s in sems:
        if s.n:
            eng.wait_ge(s.h, s.n)


def build(debug=False, stages=5, pha_parts=9):
    nc = bass.Bass("TRN2", target_bir_lowering=False)
    dram = {}

    def din(name, shape, dt=F32):
        dram[name] = nc.dram_tensor(name, shape, dt, kind="ExternalInput").ap()
        return dram[name]

    def dout(name, shape, dt=F32):
        dram[name] = nc.dram_tensor(name, shape, dt, kind="ExternalOutput").ap()
        return dram[name]

    def dint(name, shape, dt=BF16):
        dram[name] = nc.dram_tensor(name, shape, dt, kind="Internal").ap()
        return dram[name]

    x_in = din("x", [L, DM])
    xrev_in = din("xrev", [L, DM])
    win = {d: din(f"win_{d}", [DM, DPROJ]) for d in "fb"}
    wout = {d: din(f"wout_{d}", [DIN, DM]) for d in "fb"}
    convw = {d: din(f"convw_{d}", [128, 4 * 17]) for d in "fb"}
    convb = {d: din(f"convb_{d}", [128, 17]) for d in "fb"}
    dtb4 = {d: din(f"dtb4_{d}", [128, 1]) for d in "fb"}
    alog4 = {d: din(f"alog4_{d}", [128, 1]) for d in "fb"}
    drep = {d: din(f"drep_{d}", [128, DIN]) for d in "fb"}
    nwrep = {d: din(f"nwrep_{d}", [128, DIN]) for d in "fb"}
    wo_in = din("wo", [DIN, DM])
    bo_in = din("bo", [128, DM])
    wt_in = din("wt", [DM, DM])
    ffb_in = din("ffb", [1, DM])
    vr_in = din("vr", [DM, 32])
    ct_in = din("ct", [32, DM])
    if32_in = din("if32", [128, 128])
    ibf_in = din("ibf", [128, 128], BF16)

    y_out = dout("y", [L, DM])

    xnh = {"f": dint("xnh", [L, DM]), "b": dint("xnrevh", [L, DM])}
    szh = dint("szh", [L, DIN])
    odh = {"f": dint("odhf", [L, DM]), "b": dint("odhb", [L, DM])}

    dbg = {}
    if debug:
        for nm, shp, dt in [
            ("t_xn", [128, DM], BF16), ("t_sz", [128, DIN], BF16),
            ("t_xsbc", [128, CCH], BF16), ("t_dtsp", [128, 512], F32),
            ("t_a4", [128, 512], F32), ("t_ets", [128, 64], F32),
            ("t_mt", [128, 2048], BF16), ("t_xh", [128, 1024], BF16),
            ("t_q", [128, 1024], BF16), ("t_qh", [128, DIN], BF16),
            ("t_wc", [128, 32], F32), ("t_bw", [128, 1024], BF16),
            ("t_od", [128, DM], BF16), ("t_H", [64, 2048], F32),
            ("t_x2", [128, DM], F32),
        ]:
            dbg[nm] = dout(nm, shp, dt)

    st = contextlib.ExitStack()
    with st:
        sb = lambda name, shape, dt: st.enter_context(nc.sbuf_tensor(name, shape, dt))
        xnfm = sb("xnfm", [128, NKT * L], BF16)
        winsb = sb("winsb", [128, NKT * CCH], BF16)
        windt = sb("windt", [128, NKT * DTC], BF16)
        woutsb = sb("woutsb", [128, 16 * DM], BF16)
        wtsb = sb("wtsb", [128, NKT * DM], BF16)
        wfm = sb("wfm", [128, 4 * 17], F32)
        cbfm = sb("cbfm", [128, 17], F32)
        nwsb = sb("nwsb", [128, DIN], BF16)
        drsb = sb("drsb", [128, DIN], BF16)
        dtbsb = sb("dtbsb", [128, 1], F32)
        alogsb = sb("alogsb", [128, 1], F32)
        acol = sb("acol", [128, 1], F32)
        if32 = sb("if32_s", [128, 128], F32)
        ibf = sb("ibf_s", [128, 128], BF16)
        ones32 = sb("ones32", [1, 128], F32)
        onesbf = sb("onesbf", [1, 128], BF16)
        ssqall = sb("ssqall", [128, 2 * NCH], F32)
        rall = sb("rall", [128, 2 * NCH], F32)
        dtsp4 = sb("dtsp4", [128, 512], F32)
        s4 = sb("s4", [128, 512], F32)
        dts = sb("dts", [128, NCH * 96], F32)
        eslr = sb("eslr", [64, NCH * 32], F32)
        clt1 = sb("clt1", [128, 512], F32)
        clt2 = sb("clt2", [128, 512], F32)
        a4 = clt1
        es4 = s4
        xbc = sb("xbc", [128, CCH], BF16)
        xbcw = sb("xbcw", [128, 17 * 131], BF16)
        cpa = sb("cpa", [128, CCH], BF16)
        cpc = sb("cpc", [128, CCH], BF16)
        szb = [sb(f"szb{i}", [128, DIN], BF16) for i in range(2)]
        arep = sb("arep", [128, 1024], F32)
        Ebuf = sb("Ebuf", [128, 2048], BF16)
        xhat = sb("xhat", [128, 1024], BF16)
        bwb = sb("bwb", [128, 1024], BF16)
        wcols = sb("wcols", [128, 32], F32)
        qb = sb("qb", [128, 1024], BF16)
        qhb = sb("qhb", [128, 2048], BF16)
        qfmb = sb("qfmb", [128, 512], BF16)
        Hm = sb("Hm", [64, 2048], F32)
        Hbf = sb("Hbf", [64, 2048], BF16)
        odb = sb("odb", [128, DM], BF16)
        sbf = sb("sbf", [64, 128], BF16)
        scf = sb("scf", [64, 128], BF16)
        sg = sb("sg", [128, 128], BF16)
        tbuf = sb("tbuf", [128, 1024], BF16)
        ssqt = sb("ssqt", [128, 2], F32)
        stat = sb("stat", [128, 8], F32)
        ceps = sb("ceps", [128, 2], F32)
        nc.const_aps.register(F32, EPS_LN, ceps[:, 0:1])
        nc.const_aps.register(F32, 1.0, ceps[:, 1:2])

        sx = [arep[:], Ebuf[:].bitcast(F32)]
        xn0 = qhb[:, DM:2 * DM]
        odb = tbuf
        xsbc = cpa
        dxsb = qb

        ps = lambda name, shape: st.enter_context(nc.psum_tensor(name, shape, F32))
        pwin = ps("pwin", [128, 1024])
        psmall = ps("psmall", [128, 512])
        pYA = ps("pYA", [128, 1024])
        pYB = ps("pYB", [128, 1024])
        pQT = ps("pQT", [128, 512])

        SS = lambda name: Sem(st, nc, name)
        sLD = SS("sLD"); sX = SS("sX"); sXN = SS("sXN"); sXNH = SS("sXNH")
        sTR = SS("sTR"); sWG = SS("sWG"); sWE = SS("sWE"); sSZ = SS("sSZ")
        sDT = SS("sDT"); sCV = SS("sCV"); sCD = SS("sCD"); sCS = SS("sCS")
        sBC = SS("sBC"); sBE = SS("sBE"); sGW = SS("sGW"); sGE = SS("sGE")
        sAR = SS("sAR"); sAE = SS("sAE"); sSC = SS("sSC"); sWC = SS("sWC")
        sMT = SS("sMT"); sXH = SS("sXH"); sBW = SS("sBW"); sY = SS("sY")
        sEV = SS("sEV"); sSQ = SS("sSQ"); sST = SS("sST"); sHU = SS("sHU")
        sHB = SS("sHB"); sQT = SS("sQT"); sQE = SS("sQE"); sWO = SS("sWO")
        sOD = SS("sOD"); sSG = SS("sSG"); sBAR = SS("sBAR"); sO3 = SS("sO3")
        sOUT = SS("sOUT"); sIN = SS("sIN"); sFV = SS("sFV"); sFA = SS("sFA")
        sOU = [SS("sOU0"), SS("sOU1")]

        block = st.enter_context(nc.Block())
        ENGS = ("tensor", "vector", "scalar", "gpsimd", "sync")

        def on(eng_name, fn):
            getattr(block, eng_name)(fn)

        def vfence(v):
            sFV.inc(v.engine_nop())
            v.wait_ge(sFV.h, sFV.n)

        def afence(sc):
            sFA.inc(sc.copy(ceps[0:1, 1:2], ceps[0:1, 1:2]))
            sc.wait_ge(sFA.h, sFA.n)

        def barrier():
            def f(sync):
                wait(sync, sOUT, sOU[0], sOU[1], sSG, sXNH, sSZ, sX, sTR, sLD)
            on("sync", f)
            target = sBAR.n + 5
            for e in ENGS:
                def bf(eng, e=e):
                    sBAR.inc(eng.nop() if e == "sync" else eng.drain())
                on(e, bf)
            for e in ENGS:
                on(e, lambda eng: eng.wait_ge(sBAR.h, target))
            assert sBAR.n == target

        def dbg_dump(name, src_ap, *wait_sems):
            if not debug:
                return
            def f(sync):
                wait(sync, *wait_sems)
                sOUT.inc(sync.dma_start(out=dbg[name], in_=src_ap), 16)
            on("sync", f)

        def f(g):
            g.memset(ceps[:, 0:1], EPS_LN)
            g.memset(ceps[:, 1:2], 1.0)
            g.memset(ones32[:], 1.0)
            sIN.inc(g.memset(onesbf[:], 1.0))
            for kt in range(NKT):
                sLD.inc(g.dma_start(out=winsb[:, kt * CCH:kt * CCH + DIN],
                                    in_=win["f"][kt * 128:(kt + 1) * 128, 0:DIN]), 16)
            sLD.inc(g.dma_start(out=wfm[:], in_=convw["f"]), 16)
            sLD.inc(g.dma_start(out=cbfm[:], in_=convb["f"]), 16)
            sLD.inc(g.dma_start(out=nwsb[:], in_=nwrep["f"]), 16)
            sLD.inc(g.dma_start(out=drsb[:], in_=drep["f"]), 16)
            sLD.inc(g.dma_start(out=dtbsb[:], in_=dtb4["f"]), 16)
            sLD.inc(g.dma_start(out=alogsb[:], in_=alog4["f"]), 16)
            for kt in range(NKT):
                sLD.inc(g.dma_start(out=windt[:, kt * DTC:(kt + 1) * DTC],
                                    in_=win["f"][kt * 128:(kt + 1) * 128, DIN + CCH:DPROJ]), 16)
        on("gpsimd", f)

        def f(sync):
            sTR.inc(sync.dma_start(out=if32[:], in_=if32_in), 16)
            sTR.inc(sync.dma_start(out=ibf[:], in_=ibf_in), 16)
        on("sync", f)

        for di, (d, src) in enumerate([("f", x_in), ("b", xrev_in)]):
            for k in range(NCH):
                u = k % 2

                def f(sync, k=k, u=u, src=src):
                    if sXN.n >= 4:
                        sync.wait_ge(sXN.h, sXN.n - 2)
                    sX.inc(sync.dma_start(out=sx[u], in_=src[k * Q:(k + 1) * Q, :]), 16)
                on("sync", f)

                def f(sc, u=u):
                    sc.wait_ge(sX.h, sX.n)
                    wait(sc, sIN, sXNH)
                    if debug:
                        wait(sc, sOUT)
                    sc.activation(tbuf[:], sx[u], AF.Copy, accum_out=stat[:, 0:1])
                    sSQ.inc(sc.activation(xn0, sx[u], AF.Square,
                                          accum_out=stat[:, 1:2]))
                on("scalar", f)

                def f(v):
                    v.wait_ge(sSQ.h, sSQ.n)
                    v.tensor_scalar_mul(stat[:, 2:3], stat[:, 0:1], 1.0 / DM)
                    v.tensor_scalar_mul(stat[:, 3:4], stat[:, 1:2], 1.0 / DM)
                    v.tensor_scalar(stat[:, 4:5], stat[:, 0:1], stat[:, 0:1],
                                    1.0 / (DM * DM), op0=ALU.mult, op1=ALU.mult)
                    vfence(v)
                    sXN.inc(v.tensor_tensor(stat[:, 5:6], stat[:, 3:4], stat[:, 4:5],
                                            op=ALU.subtract))
                on("vector", f)

                def f(sc):
                    sc.wait_ge(sXN.h, sXN.n)
                    sc.activation(stat[:, 6:7], stat[:, 5:6], AF.Ln, bias=EPS_LN)
                    afence(sc)
                    sSQ.inc(sc.activation(stat[:, 7:8], stat[:, 6:7], AF.Exp, scale=-0.5))
                on("scalar", f)

                def f(v, u=u):
                    v.wait_ge(sSQ.h, sSQ.n)
                    wait(v, sXNH)
                    if debug:
                        wait(v, sOUT)
                    sXN.inc(v.tensor_scalar(xn0, sx[u], stat[:, 2:3], stat[:, 7:8],
                                            op0=ALU.subtract, op1=ALU.mult))
                on("vector", f)

                def f(sync, d=d, k=k):
                    sync.wait_ge(sXN.h, sXN.n)
                    sXNH.inc(sync.dma_start(out=xnh[d][k * Q:(k + 1) * Q, :], in_=xn0), 16)
                on("sync", f)

                if d == "f" and k == 0:
                    dbg_dump("t_xn", xn0, sXN)

        barrier()

        def load_xnfm(d):
            def f(sync, d=d):
                wait(sync, sXNH)
                for ft in range(NKT):
                    sTR.inc(sync.dma_start_transpose(
                        xnfm[:, ft * L:(ft + 1) * L],
                        xnh[d][:, ft * 128:(ft + 1) * 128]), 16)
            on("sync", f)

        def load_dir_small(d):
            def f(g, d=d):
                sLD.inc(g.dma_start(out=wfm[:], in_=convw[d]), 16)
                sLD.inc(g.dma_start(out=cbfm[:], in_=convb[d]), 16)
                sLD.inc(g.dma_start(out=nwsb[:], in_=nwrep[d]), 16)
                sLD.inc(g.dma_start(out=drsb[:], in_=drep[d]), 16)
                sLD.inc(g.dma_start(out=dtbsb[:], in_=dtb4[d]), 16)
                sLD.inc(g.dma_start(out=alogsb[:], in_=alog4[d]), 16)
                for kt in range(NKT):
                    sLD.inc(g.dma_start(out=windt[:, kt * DTC:(kt + 1) * DTC],
                                        in_=win[d][kt * 128:(kt + 1) * 128, DIN + CCH:DPROJ]), 16)
            on("gpsimd", f)

        def load_win_z(d):
            def f(g, d=d):
                for kt in range(NKT):
                    sLD.inc(g.dma_start(out=winsb[:, kt * CCH:kt * CCH + DIN],
                                        in_=win[d][kt * 128:(kt + 1) * 128, 0:DIN]), 16)
            on("gpsimd", f)

        def load_win_xbc(d):
            def f(g, d=d):
                for kt in range(NKT):
                    sLD.inc(g.dma_start(out=winsb[:, kt * CCH:kt * CCH + CCH],
                                        in_=win[d][kt * 128:(kt + 1) * 128, DIN:DIN + CCH]), 16)
            on("gpsimd", f)

        def load_wout(d):
            def f(g, d=d):
                for j in range(16):
                    sLD.inc(g.dma_start(out=woutsb[:, j * DM:(j + 1) * DM],
                                        in_=wout[d][j * 128:(j + 1) * 128, :]), 16)
            on("gpsimd", f)

        def emit_phA(d, di):
            ld_t = sLD.n
            tr_tt = sTR.n
            tr_t = sTR.n

            def f(t):
                t.wait_ge(sLD.h, ld_t)
                t.wait_ge(sTR.h, tr_t)
                if pha_parts < 1:
                    sDT.inc(t.drain())
                    return
                for q in range(4):
                    for kt in range(NKT):
                        li = t.matmul(pYA[q * 32:(q + 1) * 32, 0:512],
                                      windt[:, kt * DTC:(kt + 1) * DTC],
                                      xnfm[:, kt * L + q * 512:kt * L + (q + 1) * 512],
                                      start=(kt == 0), stop=(kt == NKT - 1),
                                      tile_position=(0, q * 32))
                sDT.inc(li)
            on("tensor", f)

            for k in range(NCH if pha_parts >= 2 else 0):
                def f(t, k=k):
                    for g in range(4):
                        gi = sWG.n
                        if gi >= 2 and pha_parts >= 3:
                            t.wait_ge(sWE.h, gi - 1)
                        for kt in range(NKT):
                            li = t.matmul(pwin[:, (g % 2) * 512:(g % 2 + 1) * 512],
                                          xnfm[:, kt * L + k * Q:kt * L + (k + 1) * Q],
                                          winsb[:, kt * CCH + g * 512:kt * CCH + (g + 1) * 512],
                                          start=(kt == 0), stop=(kt == NKT - 1))
                        sWG.inc(li)
                on("tensor", f)

                if pha_parts < 3:
                    continue

                def f(sc):
                    base = sWG.n - 4
                    wait(sc, sSZ)
                    if debug:
                        wait(sc, sOUT)
                    for g in range(4):
                        sc.wait_ge(sWG.h, base + g + 1)
                        sWE.inc(sc.activation(szb[0][:, g * 512:(g + 1) * 512],
                                              pwin[:, (g % 2) * 512:(g % 2 + 1) * 512],
                                              AF.Silu))
                on("scalar", f)

                if pha_parts < 4:
                    continue

                def f(sync, k=k):
                    wait(sync, sWE)
                    sSZ.inc(sync.dma_start(out=szh[k * Q:(k + 1) * Q, :], in_=szb[0][:]), 16)
                on("sync", f)

                if d == "f" and k == 0:
                    dbg_dump("t_sz", szb[0][:], sWE)

            if pha_parts < 5:
                return
            def f(v):
                wait(v, sDT)
                v.tensor_scalar_add(clt2[:], pYA[:, 0:512], dtbsb[:])
                sDT.inc(v.tensor_scalar(clt1[:], pYA[:, 0:512], dtbsb[:], 0.0,
                                        op0=ALU.add, op1=ALU.max))
            on("vector", f)

            def f(sc):
                wait(sc, sDT)
                sc.activation(s4[:], clt2[:], AF.Abs)
                afence(sc)
                sc.activation(s4[:], s4[:], AF.Exp, scale=-1.0)
                afence(sc)
                sc.activation(s4[:], s4[:], AF.Ln, bias=1.0)
                sDT.inc(sc.activation(acol[:], alogsb[:], AF.Exp))
            on("scalar", f)

            def f(v):
                wait(v, sDT)
                v.tensor_tensor(dtsp4[:], clt1[:], s4[:], op=ALU.add)
                vfence(v)
                sDT.inc(v.tensor_scalar_mul(clt2[:], dtsp4[:], acol[:]))
            on("vector", f)

            def f(sc):
                wait(sc, sDT)
                sDT.inc(sc.activation(a4[:], clt2[:], AF.Exp, scale=-1.0))
            on("scalar", f)

            if pha_parts < 6:
                return
            def f(v):
                wait(v, sDT)
                for k in range(NCH):
                    qq, cc = k // 4, (k % 4) * 128
                    v.tensor_tensor_scan(
                        s4[qq * 32:(qq + 1) * 32, cc:cc + 128],
                        clt2[qq * 32:(qq + 1) * 32, cc:cc + 128],
                        clt2[qq * 32:(qq + 1) * 32, cc:cc + 128],
                        0.0, op0=ALU.add, op1=ALU.bypass)
                sDT.inc(v.engine_nop())
            on("vector", f)

            def f(sc):
                wait(sc, sDT)
                sDT.inc(sc.activation(es4[:], s4[:], AF.Exp, scale=-1.0))
            on("scalar", f)

            if pha_parts < 7:
                return
            for k in range(NCH):
                qq, cc = k // 4, (k % 4) * 128

                def f(t, k=k, qq=qq, cc=cc):
                    wait(t, sDT, sBE)
                    for col, srcm in ((0, dtsp4), (32, es4), (64, a4)):
                        li = t.matmul(psmall[:, col:col + 32],
                                      srcm[qq * 32:(qq + 1) * 32, cc:cc + 128],
                                      if32[qq * 32:(qq + 1) * 32, qq * 32:(qq + 1) * 32],
                                      start=True, stop=True, tile_position=(qq * 32, 0))
                    sBC.inc(li)
                on("tensor", f)

                def f(t, k=k, qq=qq, cc=cc):
                    sBC.inc(t.matmul(psmall[0:1, 96:128],
                                     es4[qq * 32:(qq + 1) * 32, cc + 127:cc + 128],
                                     if32[qq * 32:(qq + 1) * 32, qq * 32:(qq + 1) * 32],
                                     start=True, stop=True, tile_position=(qq * 32, 0)))
                on("tensor", f)

                def f(v, k=k):
                    wait(v, sBC)
                    v.tensor_copy(dts[:, k * 96:k * 96 + 96], psmall[:, 0:96])
                    sBE.inc(v.tensor_copy(wcols[0:1, 0:32], psmall[0:1, 96:128]))
                on("vector", f)

                def f(t, k=k):
                    wait(t, sBE)
                    sBC.inc(t.matmul(psmall[:, 64:96], ones32[:], wcols[0:1, 0:32],
                                     start=True, stop=True))
                on("tensor", f)

                def f(v, k=k):
                    wait(v, sBC)
                    sBE.inc(v.tensor_copy(eslr[:, k * 32:(k + 1) * 32],
                                          psmall[0:64, 64:96]))
                on("vector", f)

            if d == "f":
                dbg_dump("t_dtsp", dtsp4[:], sDT, sBE)
                dbg_dump("t_a4", a4[:], sDT)
                dbg_dump("t_ets", dts[:, 0:64], sBE)

        def emit_phB(d, di):
            ld_t = sLD.n

            def f(g):
                wait(g, sHB, sY)
                g.memset(Hm[:], 0.0)
                g.memset(Hbf[:], 0.0)
                sHB.inc(g.memset(xbcw[:].rearrange("p (c w) -> p c w", c=17)[:, :, 0:3], 0.0))
            on("gpsimd", f)

            def f(sync):
                wait(sync, sSZ)
                sSZ.inc(sync.dma_start(out=szb[0][:], in_=szh[0:Q, :]), 16)
            on("sync", f)

            gsz = [512, 512, 512, 512, 128]
            for k in range(NCH):
                u = k % 2
                cs = k * Q

                if k + 1 < NCH:
                    def f(sync, k=k):
                        wait(sync, sEV)
                        sSZ.inc(sync.dma_start(out=szb[(k + 1) % 2][:],
                                               in_=szh[(k + 1) * Q:(k + 2) * Q, :]), 16)
                    on("sync", f)

                def f(t, k=k, cs=cs):
                    t.wait_ge(sLD.h, ld_t)
                    off = 0
                    for g, n in enumerate(gsz):
                        gi = sWG.n
                        if gi >= 2:
                            t.wait_ge(sWE.h, gi - 1)
                        for kt in range(NKT):
                            li = t.matmul(pwin[:, (g % 2) * 512:(g % 2) * 512 + n],
                                          xnfm[:, kt * L + cs:kt * L + cs + Q],
                                          winsb[:, kt * CCH + off:kt * CCH + off + n],
                                          start=(kt == 0), stop=(kt == NKT - 1))
                        sWG.inc(li)
                        off += n
                on("tensor", f)

                def f(v):
                    base = sWG.n - 5
                    off = 0
                    for g, n in enumerate(gsz):
                        v.wait_ge(sWG.h, base + g + 1)
                        sWE.inc(v.tensor_copy(xbc[:, off:off + n],
                                              pwin[:, (g % 2) * 512:(g % 2) * 512 + n]))
                        off += n
                on("vector", f)

                tile_rounds = [(0, 4), (4, 4), (8, 4), (12, 4), (16, 1)]
                for r0, rn in tile_rounds:
                    def f(t, r0=r0, rn=rn):
                        wait(t, sWE, sQE, sAE)
                        for i in range(rn):
                            ct = r0 + i
                            li = t.matmul(pQT[:, i * 128:(i + 1) * 128],
                                          xbc[:, ct * 128:(ct + 1) * 128], ibf[:],
                                          start=True, stop=True)
                        sQT.inc(li)
                    on("tensor", f)

                    def f(v, r0=r0, rn=rn):
                        wait(v, sQT, sCD, sCV)
                        o = xbcw[:].rearrange("p (c w) -> p c w", c=17)[:, r0:r0 + rn, 3:131]
                        i_ = pQT[:, 0:rn * 128].rearrange("p (c w) -> p c w", c=rn)
                        sQE.inc(v.tensor_copy(o, i_))
                    on("vector", f)

                def f(v):
                    wait(v, sQE, sCS, sHB)
                    w3 = xbcw[:].rearrange("p (c w) -> p c w", c=17)
                    for ct in range(17):
                        v.tensor_scalar(cpc[:, ct * 128:(ct + 1) * 128],
                                        w3[:, ct, 0:128],
                                        wfm[:, ct:ct + 1], cbfm[:, ct:ct + 1],
                                        op0=ALU.mult, op1=ALU.add)
                    for j in (1, 2, 3):
                        vfence(v)
                        for ct in range(17):
                            acc = cpc[:, ct * 128:(ct + 1) * 128]
                            v.scalar_tensor_tensor(acc, w3[:, ct, j:j + 128],
                                                   wfm[:, j * 17 + ct:j * 17 + ct + 1],
                                                   acc, op0=ALU.mult, op1=ALU.add)
                    sCD.inc(v.engine_nop())
                on("vector", f)



                def f(v, k=k):
                    wait(v, sCD, sCV)
                    w3 = xbcw[:].rearrange("p (c w) -> p c w", c=17)
                    sQE.inc(v.tensor_copy(w3[:, :, 0:3], w3[:, :, 128:131]))
                on("vector", f)

                def f(sc):
                    wait(sc, sCD, sCV)
                    sCS.inc(sc.activation(cpc[:], cpc[:], AF.Silu))
                on("scalar", f)

                for r0, rn in tile_rounds:
                    def f(t, r0=r0, rn=rn):
                        wait(t, sCS, sQE)
                        for i in range(rn):
                            ct = r0 + i
                            li = t.matmul(pQT[:, i * 128:(i + 1) * 128],
                                          cpc[:, ct * 128:(ct + 1) * 128], ibf[:],
                                          start=True, stop=True)
                        sQT.inc(li)
                    on("tensor", f)

                    def f(v, r0=r0, rn=rn):
                        wait(v, sQT)
                        if debug:
                            wait(v, sOUT)
                        o = xsbc[:, r0 * 128:(r0 + rn) * 128]
                        sQE.inc(v.tensor_copy(o, pQT[:, 0:rn * 128]))
                    on("vector", f)

                def f(v):
                    wait(v, sQE)
                    sCS.inc(v.engine_nop())
                on("vector", f)

                if d == "f" and k == 0:
                    dbg_dump("t_xsbc", xsbc[:], sCS)

                def f(t):
                    wait(t, sCS, sGE)
                    t.matmul(psmall[0:64, 0:128], xsbc[:, DIN:DIN + 64], ibf[:],
                             start=True, stop=True)
                    sBC.inc(t.matmul(psmall[0:64, 128:256], xsbc[:, DIN + 64:DIN + 128],
                                     ibf[:], start=True, stop=True))
                on("tensor", f)

                def f(v):
                    wait(v, sBC)
                    v.tensor_copy(sbf[:], psmall[0:64, 0:128])
                    sBE.inc(v.tensor_copy(scf[:], psmall[0:64, 128:256]))
                on("vector", f)

                def f(t):
                    wait(t, sBE)
                    sGW.inc(t.matmul(psmall[:, 256:384], sbf[:], scf[:],
                                     start=True, stop=True))
                on("tensor", f)

                def f(v):
                    wait(v, sGW)
                    sGE.inc(v.tensor_copy(sg[:], psmall[:, 256:384]))
                on("vector", f)

                for half in range(2):
                    h0 = half * 16
                    for piece in range(4):
                        hh0 = h0 + piece * 4

                        def f(t, k=k, hh0=hh0):
                            wait(t, sSC, sQE, sAE)
                            for i in range(4):
                                h = hh0 + i
                                acolb = dts[:, k * 96 + 64 + h:k * 96 + 64 + h + 1] \
                                    .broadcast_to((128, 128))
                                li = t.matmul(pQT[:, i * 128:(i + 1) * 128], acolb, if32[:],
                                              start=True, stop=True)
                            sAR.inc(li)
                        on("tensor", f)

                        def f(v, piece=piece):
                            wait(v, sAR)
                            sAE.inc(v.tensor_copy(arep[:, (piece % 2) * 512:(piece % 2 + 1) * 512],
                                                  pQT[:]))
                        on("vector", f)

                        def f(v, hh0=hh0, piece=piece, half=half):
                            wait(v, sAE)
                            if debug:
                                wait(v, sOUT)
                            if half == 1 and piece == 0:
                                wait(v, sY)
                            for i in range(4):
                                h = hh0 + i
                                hh = h % 16
                                v.tensor_tensor_scan(
                                    Ebuf[:, hh * 128:(hh + 1) * 128],
                                    arep[:, (piece % 2) * 512 + i * 128:
                                         (piece % 2) * 512 + (i + 1) * 128],
                                    if32[:], 0.0, op0=ALU.mult, op1=ALU.add)
                            sSC.inc(v.engine_nop())
                        on("vector", f)

                    def f(v, k=k, h0=h0):
                        wait(v, sSC)
                        for i in range(16):
                            h = h0 + i
                            v.tensor_copy(wcols[:, h:h + 1],
                                          Ebuf[:, i * 128 + 127:i * 128 + 128])
                        sWC.inc(v.engine_nop())
                    on("vector", f)

                    def f(v):
                        wait(v, sGE, sWC)
                        e3 = Ebuf[:].rearrange("p (h t) -> p h t", h=16)
                        gb = sg[:].unsqueeze(1).broadcast_to((128, 16, 128))
                        sMT.inc(v.tensor_tensor(e3, e3, gb, op=ALU.mult))
                    on("vector", f)

                    if d == "f" and k == 0 and half == 0:
                        dbg_dump("t_mt", Ebuf[:], sMT)

                    def f(v, k=k, h0=h0, half=half):
                        wait(v, sCS)
                        if half == 1:
                            wait(v, sST)
                        if debug:
                            wait(v, sOUT)
                        for i in range(16):
                            h = h0 + i
                            li = v.tensor_scalar_mul(xhat[:, i * 64:(i + 1) * 64],
                                                     xsbc[:, h * 64:(h + 1) * 64],
                                                     dts[:, k * 96 + h:k * 96 + h + 1])
                        sXH.inc(li)
                    on("vector", f)

                    def f(v, k=k, h0=h0, half=half):
                        wait(v, sWC)
                        if half == 1:
                            wait(v, sST)
                        if debug:
                            wait(v, sOUT)
                        for i in range(16):
                            h = h0 + i
                            li = v.tensor_scalar_mul(bwb[:, i * 64:(i + 1) * 64],
                                                     xsbc[:, DIN:DIN + 64],
                                                     wcols[:, h:h + 1])
                        sBW.inc(li)
                    on("vector", f)

                    if d == "f" and k == 0 and half == 1:
                        dbg_dump("t_wc", wcols[:], sWC)
                        dbg_dump("t_bw", bwb[:], sBW)

                    if d == "f" and k == 0 and half == 0:
                        dbg_dump("t_xh", xhat[:], sXH)

                    def f(t, h0=h0, half=half):
                        wait(t, sMT, sXH, sHB, sEV, sHU, sOD)
                        for i in range(16):
                            t.matmul(pYA[:, i * 64:(i + 1) * 64],
                                     Ebuf[:, i * 128:(i + 1) * 128],
                                     xhat[:, i * 64:(i + 1) * 64], start=True, stop=True)
                        for i in range(16):
                            h = h0 + i
                            li = t.matmul(pYB[:, i * 64:(i + 1) * 64], scf[:],
                                          Hbf[:, h * 64:(h + 1) * 64],
                                          start=True, stop=True)
                        sY.inc(li)
                    on("tensor", f)

                    def f(v, h0=h0):
                        wait(v, sCS, sSQ)
                        sEV.inc(v.tensor_tensor(dxsb[:], xsbc[:, h0 * 64:h0 * 64 + 1024],
                                                drsb[:, h0 * 64:h0 * 64 + 1024], op=ALU.mult))
                    on("vector", f)

                    def f(v, k=k, h0=h0, u=u):
                        wait(v, sY, sSZ, sSG, sEV)
                        if debug:
                            wait(v, sOUT)
                        v.tensor_tensor(tbuf[:], pYA[:], qb[:], op=ALU.add)
                        es = dts[:, k * 96 + 32 + h0:k * 96 + 32 + h0 + 16] \
                            .unsqueeze(-1).broadcast_to((128, 16, 64))
                        pyb3 = pYB[:].rearrange("p (h q) -> p h q", h=16)
                        vfence(v)
                        v.tensor_tensor(qb[:].rearrange("p (h q) -> p h q", h=16),
                                        pyb3, es, op=ALU.mult)
                        vfence(v)
                        v.tensor_tensor(qb[:], qb[:], tbuf[:], op=ALU.add)
                        vfence(v)
                        v.tensor_tensor(qb[:], qb[:], szb[u][:, h0 * 64:h0 * 64 + 1024],
                                        op=ALU.mult)
                        vfence(v)
                        sEV.inc(v.tensor_tensor(qhb[:, h0 * 64:h0 * 64 + 1024], qb[:],
                                                nwsb[:, h0 * 64:h0 * 64 + 1024], op=ALU.mult))
                    on("vector", f)

                    if d == "f" and k == 0 and half == 1:
                        dbg_dump("t_q", qb[:], sEV)

                    def f(v, half=half):
                        wait(v, sEV)
                        if debug:
                            wait(v, sOUT)
                        sSQ.inc(v.scalar_tensor_tensor(qb[:], qb[:], 0.0, qb[:],
                                                       op0=ALU.add, op1=ALU.mult,
                                                       accum_out=ssqt[:, half:half + 1]))
                    on("vector", f)

                    def f(t, h0=h0, half=half):
                        wait(t, sBW, sEV)
                        tgt = pYA if half == 0 else pYB
                        for i in range(16):
                            li = t.matmul(tgt[0:64, i * 64:(i + 1) * 64],
                                          bwb[:, i * 64:(i + 1) * 64],
                                          xhat[:, i * 64:(i + 1) * 64],
                                          start=True, stop=True)
                        sST.inc(li)
                    on("tensor", f)

                    def f(v, k=k, h0=h0, half=half):
                        wait(v, sST)
                        if debug:
                            wait(v, sOUT)
                        tgt = pYA if half == 0 else pYB
                        el = eslr[:, k * 32 + h0:k * 32 + h0 + 16].unsqueeze(-1) \
                            .broadcast_to((64, 16, 64))
                        hview = Hm[0:64, h0 * 64:h0 * 64 + 1024] \
                            .rearrange("p (h q) -> p h q", h=16)
                        v.tensor_tensor(hview, hview, el, op=ALU.mult)
                        vfence(v)
                        sHU.inc(v.tensor_tensor(Hm[0:64, h0 * 64:h0 * 64 + 1024],
                                                Hm[0:64, h0 * 64:h0 * 64 + 1024],
                                                tgt[0:64, :], op=ALU.add))
                    on("vector", f)

                def f(v, di=di, k=k):
                    wait(v, sSQ)
                    sEV.inc(v.tensor_tensor(ssqall[:, di * NCH + k:di * NCH + k + 1],
                                            ssqt[:, 0:1], ssqt[:, 1:2], op=ALU.add))
                on("vector", f)

                def f(v):
                    wait(v, sHU)
                    sHB.inc(v.tensor_copy(Hbf[:], Hm[:]))
                on("vector", f)

                if d == "f" and k == 0:
                    dbg_dump("t_H", Hm[:], sHU)
                    dbg_dump("t_qh", qhb[:], sEV)

                for r in range(4):
                    def f(t, r=r):
                        wait(t, sEV, sQE, sAE)
                        for i in range(4):
                            j = r * 4 + i
                            li = t.matmul(pQT[:, i * 128:(i + 1) * 128],
                                          qhb[:, j * 128:(j + 1) * 128], ibf[:],
                                          start=True, stop=True)
                        sQT.inc(li)
                    on("tensor", f)

                    def f(v):
                        wait(v, sQT)
                        sQE.inc(v.tensor_copy(qfmb[:], pQT[:]))
                    on("vector", f)

                    def f(t, r=r):
                        wait(t, sQE)
                        if r == 0:
                            wait(t, sOD, sHU)
                        for i in range(4):
                            j = r * 4 + i
                            for nn in range(2):
                                li = t.matmul(pYB[:, nn * 512:(nn + 1) * 512],
                                              qfmb[:, i * 128:(i + 1) * 128],
                                              woutsb[:, j * DM + nn * 512:j * DM + (nn + 1) * 512],
                                              start=(j == 0), stop=(j == 15))
                        sWO.inc(li)
                    on("tensor", f)

                def f(v):
                    wait(v, sWO)
                    if debug:
                        wait(v, sOUT)
                    sOD.inc(v.tensor_copy(odb[:], pYB[:]))
                on("vector", f)

                def f(sync, d=d, k=k):
                    wait(sync, sOD)
                    sSG.inc(sync.dma_start(out=odh[d][k * Q:(k + 1) * Q, :], in_=odb[:]), 16)
                on("sync", f)

                if d == "f" and k == 0:
                    dbg_dump("t_od", odb[:], sOD)

        dirs = []
        if stages >= 2:
            dirs.append(("f", 0))
        if stages >= 4:
            dirs.append(("b", 1))
        for d, di in dirs:
            if d == "b":
                load_dir_small(d)
                load_win_z(d)
            load_xnfm(d)
            barrier()
            emit_phA(d, di)
            barrier()
            if stages >= 3 or d == "b":
                load_win_xbc(d)
                load_wout(d)
                barrier()
                emit_phB(d, di)
            barrier()

        if stages < 5:
            def f(sync):
                wait(sync, sOUT, sOU[0], sOU[1], sSG, sXNH, sSZ, sLD, sTR, sX)
            on("sync", f)
            return nc
        outfm = winsb
        wosb = woutsb

        def f(g):
            wait(g, sSG)
            for j in range(16):
                sLD.inc(g.dma_start(out=wosb[:, j * DM:(j + 1) * DM],
                                    in_=wo_in[j * 128:(j + 1) * 128, :]), 16)
            for j in range(NKT):
                sLD.inc(g.dma_start(out=wtsb[:, j * DM:(j + 1) * DM],
                                    in_=wt_in[j * 128:(j + 1) * 128, :]), 16)
            for kt in range(NKT):
                sLD.inc(g.dma_start(out=nwsb[:, kt * 32:(kt + 1) * 32],
                                    in_=vr_in[kt * 128:(kt + 1) * 128, :]), 16)
            sLD.inc(g.dma_start(out=drsb[0:32, 0:1024], in_=ct_in), 16)
            sLD.inc(g.dma_start(out=szb[0][:, 0:DM], in_=bo_in), 16)
            sLD.inc(g.dma_start(out=bwb[0:1, 0:1024], in_=ffb_in), 16)
        on("gpsimd", f)

        def f(sync):
            wait(sync, sSG)
            for ft in range(NKT):
                sTR.inc(sync.dma_start_transpose(
                    outfm[:, ft * 2048:(ft + 1) * 2048],
                    odh["f"][:, ft * 128:(ft + 1) * 128]), 16)
        on("sync", f)

        def f(sc):
            wait(sc, sEV)
            sc.activation(rall[:], ssqall[:], AF.Ln, bias=EPS_RMS, scale=1.0 / DIN)
            afence(sc)
            sO3.inc(sc.activation(rall[:], rall[:], AF.Exp, scale=-0.5))
        on("scalar", f)

        barrier()

        ph3_ld = sLD.n
        ph3_tr = sTR.n

        for k in range(NCH):
            u = k % 2
            kb = NCH - 1 - k

            def f(sync, k=k, u=u, kb=kb):
                wait(sync, sOU[u])
                if sQT.n >= 2:
                    sync.wait_ge(sQT.h, sQT.n - 2)
                sX.inc(sync.dma_start(out=sx[u], in_=x_in[k * Q:(k + 1) * Q, :]), 16)
                sX.inc(sync.dma_start(out=odb[:], in_=odh["b"][kb * Q:(kb + 1) * Q, :]), 16)
            on("sync", f)

            def f(v, kb=kb):
                wait(v, sX, sO3)
                sO3.inc(v.tensor_scalar_mul(odb[:], odb[:],
                                            rall[:, NCH + kb:NCH + kb + 1]))
            on("vector", f)

            for r in range(2):
                def f(t, r=r):
                    wait(t, sO3, sQE)
                    t.wait_ge(sLD.h, ph3_ld)
                    t.wait_ge(sTR.h, ph3_tr)
                    for i in range(4):
                        j = r * 4 + i
                        li = t.matmul(pQT[:, i * 128:(i + 1) * 128],
                                      odb[:, j * 128:(j + 1) * 128], ibf[:, 127::-1],
                                      start=True, stop=True)
                    sQT.inc(li)
                on("tensor", f)

                def f(v, r=r):
                    wait(v, sQT)
                    sQE.inc(v.tensor_copy(qhb[:, r * 512:(r + 1) * 512], pQT[:]))
                on("vector", f)

            def f(t, k=k):
                wait(t, sQE, sO3)
                for mt in range(8):
                    for nn in range(2):
                        t.matmul(pYA[:, nn * 512:(nn + 1) * 512],
                                 outfm[:, mt * 2048 + k * Q:mt * 2048 + (k + 1) * Q],
                                 wosb[:, mt * DM + nn * 512:mt * DM + (nn + 1) * 512],
                                 start=(mt == 0), stop=(mt == 7))
                for mt in range(8):
                    for nn in range(2):
                        li = t.matmul(pYB[:, nn * 512:(nn + 1) * 512],
                                      qhb[:, mt * 128:(mt + 1) * 128],
                                      wosb[:, (8 + mt) * DM + nn * 512:(8 + mt) * DM + (nn + 1) * 512],
                                      start=(mt == 0), stop=(mt == 7))
                sWO.inc(li)
            on("tensor", f)

            def f(v, k=k, u=u, kb=kb):
                wait(v, sWO)
                if debug:
                    wait(v, sOUT)
                v.scalar_tensor_tensor(sx[u], pYA[:], rall[:, k:k + 1], sx[u],
                                       op0=ALU.mult, op1=ALU.add)
                vfence(v)
                v.tensor_tensor(sx[u], pYB[:], sx[u], op=ALU.add)
                vfence(v)
                sO3.inc(v.tensor_tensor(sx[u], szb[0][:, 0:DM], sx[u], op=ALU.add))
            on("vector", f)

            if k == 0:
                dbg_dump("t_x2", sx[u], sO3)

            def f(v, u=u):
                wait(v, sO3)
                if debug:
                    wait(v, sOUT)
                v.scalar_tensor_tensor(qb[:], sx[u], 0.0, sx[u],
                                       op0=ALU.add, op1=ALU.bypass,
                                       accum_out=stat[:, 0:1])
                sSQ.inc(v.scalar_tensor_tensor(xn0, sx[u], 0.0, sx[u],
                                               op0=ALU.add, op1=ALU.mult,
                                               accum_out=stat[:, 1:2]))
            on("vector", f)

            def f(v):
                wait(v, sSQ)
                v.tensor_scalar_mul(stat[:, 2:3], stat[:, 0:1], 1.0 / DM)
                v.tensor_scalar_mul(stat[:, 3:4], stat[:, 1:2], 1.0 / DM)
                v.tensor_scalar(stat[:, 4:5], stat[:, 0:1], stat[:, 0:1],
                                1.0 / (DM * DM), op0=ALU.mult, op1=ALU.mult)
                vfence(v)
                sO3.inc(v.tensor_tensor(stat[:, 5:6], stat[:, 3:4], stat[:, 4:5],
                                        op=ALU.subtract))
            on("vector", f)

            def f(sc):
                wait(sc, sO3)
                sc.activation(stat[:, 6:7], stat[:, 5:6], AF.Ln, bias=EPS_LN)
                afence(sc)
                sSQ.inc(sc.activation(stat[:, 7:8], stat[:, 6:7], AF.Exp, scale=-0.5))
            on("scalar", f)

            def f(v, u=u):
                wait(v, sSQ, sQT)
                sO3.inc(v.tensor_scalar(xn0, sx[u], stat[:, 2:3], stat[:, 7:8],
                                        op0=ALU.subtract, op1=ALU.mult))
            on("vector", f)

            for r in range(2):
                def f(t, r=r):
                    wait(t, sO3, sQE)
                    for i in range(4):
                        j = r * 4 + i
                        li = t.matmul(pQT[:, i * 128:(i + 1) * 128],
                                      qhb[:, DM + j * 128:DM + (j + 1) * 128], ibf[:],
                                      start=True, stop=True)
                    sQT.inc(li)
                on("tensor", f)

                def f(v, r=r):
                    wait(v, sQT)
                    sQE.inc(v.tensor_copy(qb[:, r * 512:(r + 1) * 512], pQT[:]))
                on("vector", f)

            def f(t):
                wait(t, sQE, sBE)
                for kt in range(NKT):
                    li = t.matmul(psmall[0:32, 0:128],
                                  nwsb[:, kt * 32:(kt + 1) * 32],
                                  qb[:, kt * 128:(kt + 1) * 128],
                                  start=(kt == 0), stop=(kt == NKT - 1))
                sBC.inc(li)
            on("tensor", f)

            def f(v):
                wait(v, sBC)
                sBE.inc(v.tensor_copy(scf[0:32, 0:128], psmall[0:32, 0:128]))
            on("vector", f)

            def f(t):
                wait(t, sBE, sO3)
                for kt in range(NKT):
                    for nn in range(2):
                        t.matmul(pYA[:, nn * 512:(nn + 1) * 512],
                                 qb[:, kt * 128:(kt + 1) * 128],
                                 wtsb[:, kt * DM + nn * 512:kt * DM + (nn + 1) * 512],
                                 start=(kt == 0), stop=False)
                for nn in range(2):
                    t.matmul(pYA[:, nn * 512:(nn + 1) * 512], scf[0:32, 0:128],
                             drsb[0:32, nn * 512:(nn + 1) * 512],
                             start=False, stop=False)
                for nn in range(2):
                    li = t.matmul(pYA[:, nn * 512:(nn + 1) * 512], onesbf[:],
                                  bwb[0:1, nn * 512:(nn + 1) * 512],
                                  start=False, stop=True)
                sWO.inc(li)
            on("tensor", f)

            def f(v, u=u):
                wait(v, sWO)
                sO3.inc(v.tensor_tensor(sx[u], pYA[:], sx[u], op=ALU.add))
            on("vector", f)

            def f(sync, k=k, u=u):
                wait(sync, sO3)
                sOU[u].inc(sync.dma_start(out=y_out[k * Q:(k + 1) * Q, :], in_=sx[u]), 16)
            on("sync", f)

        def f(sync):
            wait(sync, sOUT, sOU[0], sOU[1])
        on("sync", f)

    return nc


def prep_host(inputs):
    g = lambda k: np.ascontiguousarray(np.asarray(inputs[k], dtype=np.float32))
    m = {}
    for d in "fb":
        m[f"win_{d}"] = g(f"L0_attn_{d}Win")
        m[f"wout_{d}"] = g(f"L0_attn_{d}Wout")
        cw = g(f"L0_attn_{d}convw")[:, 0, :]
        m[f"convw_{d}"] = np.ascontiguousarray(
            cw.reshape(4, 17, 128).transpose(2, 0, 1).reshape(128, 68))
        m[f"convb_{d}"] = np.ascontiguousarray(
            g(f"L0_attn_{d}convb").reshape(17, 128).T)
        m[f"dtb4_{d}"] = np.ascontiguousarray(np.tile(g(f"L0_attn_{d}dtb"), 4)[:, None])
        m[f"alog4_{d}"] = np.ascontiguousarray(np.tile(g(f"L0_attn_{d}Alog"), 4)[:, None])
        m[f"drep_{d}"] = np.ascontiguousarray(np.broadcast_to(
            np.repeat(g(f"L0_attn_{d}D"), P)[None, :], (128, DIN)))
        m[f"nwrep_{d}"] = np.ascontiguousarray(np.broadcast_to(
            g(f"L0_attn_{d}normw")[None, :], (128, DIN)))
    m["wo"] = g("L0_attn_Wo")
    m["bo"] = np.ascontiguousarray(np.broadcast_to(g("L0_attn_bo")[None, :], (128, DM)))
    m["wt"] = np.ascontiguousarray(g("L0_ffn_W").T)
    m["ffb"] = g("L0_ffn_b")[None, :]
    m["vr"] = g("L0_ffn_V_r")
    m["ct"] = np.ascontiguousarray(g("L0_ffn_C").T)
    m["if32"] = np.eye(128, dtype=np.float32)
    m["ibf"] = np.eye(128, dtype=ml_dtypes.bfloat16)
    return m


def kernel(**inputs):
    nc = build(debug=False)
    shared = prep_host(inputs)
    x = np.ascontiguousarray(np.asarray(inputs["L0_attn_x"], dtype=np.float32))
    in_maps = []
    for b in range(8):
        im = dict(shared)
        im["x"] = np.ascontiguousarray(x[b])
        im["xrev"] = np.ascontiguousarray(x[b][::-1])
        in_maps.append(im)
    res = run_bass_kernel_spmd(nc, in_maps, core_ids=list(range(8)))
    out = np.stack([r["y"] for r in res.results], axis=0)
    return out.astype(np.float32)


# version 09
# speedup vs baseline: 1.1366x; 1.0210x over previous
import contextlib
import numpy as np
import ml_dtypes
import concourse.bass as bass
import concourse.mybir as mybir
from concourse.bass_utils import run_bass_kernel_spmd

F32 = mybir.dt.float32
BF16 = mybir.dt.bfloat16
AF = mybir.ActivationFunctionType
ALU = mybir.AluOpType

L = 2048
DM = 1024
Q = 128
NCH = L // Q
NH = 32
P = 64
NST = 64
DIN = 2048
CCH = DIN + 2 * NST
DTC = NH
DPROJ = 2 * DIN + 2 * NST + NH
NKT = DM // 128
EPS_LN = 1e-5
EPS_RMS = 1e-5


class Sem:
    def __init__(self, stack, nc, name):
        self.h = stack.enter_context(nc.semaphore(name))
        self.n = 0

    def inc(self, instr, k=1):
        instr.then_inc(self.h, k)
        self.n += k
        return instr


def wait(eng, *sems):
    for s in sems:
        if s.n:
            eng.wait_ge(s.h, s.n)


def build(debug=False, stages=5, pha_parts=9):
    nc = bass.Bass("TRN2", target_bir_lowering=False)
    dram = {}

    def din(name, shape, dt=F32):
        dram[name] = nc.dram_tensor(name, shape, dt, kind="ExternalInput").ap()
        return dram[name]

    def dout(name, shape, dt=F32):
        dram[name] = nc.dram_tensor(name, shape, dt, kind="ExternalOutput").ap()
        return dram[name]

    def dint(name, shape, dt=BF16):
        dram[name] = nc.dram_tensor(name, shape, dt, kind="Internal").ap()
        return dram[name]

    x_in = din("x", [L, DM])
    xrev_in = din("xrev", [L, DM])
    win = {d: din(f"win_{d}", [DM, DPROJ]) for d in "fb"}
    wout = {d: din(f"wout_{d}", [DIN, DM]) for d in "fb"}
    convw = {d: din(f"convw_{d}", [128, 4 * 17]) for d in "fb"}
    convb = {d: din(f"convb_{d}", [128, 17]) for d in "fb"}
    dtb4 = {d: din(f"dtb4_{d}", [128, 1]) for d in "fb"}
    alog4 = {d: din(f"alog4_{d}", [128, 1]) for d in "fb"}
    drep = {d: din(f"drep_{d}", [128, DIN]) for d in "fb"}
    nwrep = {d: din(f"nwrep_{d}", [128, DIN]) for d in "fb"}
    wo_in = din("wo", [DIN, DM])
    bo_in = din("bo", [128, DM])
    wt_in = din("wt", [DM, DM])
    ffb_in = din("ffb", [1, DM])
    vr_in = din("vr", [DM, 32])
    ct_in = din("ct", [32, DM])
    if32_in = din("if32", [128, 128])
    ibf_in = din("ibf", [128, 128], BF16)

    y_out = dout("y", [L, DM])

    xnh = {"f": dint("xnh", [L, DM]), "b": dint("xnrevh", [L, DM])}
    szh = dint("szh", [L, DIN])
    odh = {"f": dint("odhf", [L, DM]), "b": dint("odhb", [L, DM])}

    dbg = {}
    if debug:
        for nm, shp, dt in [
            ("t_xn", [128, DM], BF16), ("t_sz", [128, DIN], BF16),
            ("t_xsbc", [128, CCH], BF16), ("t_dtsp", [128, 512], F32),
            ("t_a4", [128, 512], F32), ("t_ets", [128, 64], F32),
            ("t_mt", [128, 2048], BF16), ("t_xh", [128, 1024], BF16),
            ("t_q", [128, 1024], BF16), ("t_qh", [128, DIN], BF16),
            ("t_wc", [128, 32], F32), ("t_bw", [128, 1024], BF16),
            ("t_od", [128, DM], BF16), ("t_H", [64, 2048], F32),
            ("t_x2", [128, DM], F32),
        ]:
            dbg[nm] = dout(nm, shp, dt)

    st = contextlib.ExitStack()
    with st:
        sb = lambda name, shape, dt: st.enter_context(nc.sbuf_tensor(name, shape, dt))
        xnfm = sb("xnfm", [128, NKT * L], BF16)
        winsb = sb("winsb", [128, NKT * CCH], BF16)
        windt = sb("windt", [128, NKT * DTC], BF16)
        woutsb = sb("woutsb", [128, 16 * DM], BF16)
        wtsb = sb("wtsb", [128, NKT * DM], BF16)
        wfm = sb("wfm", [128, 4 * 17], F32)
        cbfm = sb("cbfm", [128, 17], F32)
        nwsb = sb("nwsb", [128, DIN], BF16)
        drsb = sb("drsb", [128, DIN], BF16)
        dtbsb = sb("dtbsb", [128, 1], F32)
        alogsb = sb("alogsb", [128, 1], F32)
        acol = sb("acol", [128, 1], F32)
        if32 = sb("if32_s", [128, 128], F32)
        ibf = sb("ibf_s", [128, 128], BF16)
        ones32 = sb("ones32", [1, 128], F32)
        onesbf = sb("onesbf", [1, 128], BF16)
        ssqall = sb("ssqall", [128, 2 * NCH], F32)
        rall = sb("rall", [128, 2 * NCH], F32)
        dtsp4 = sb("dtsp4", [128, 512], F32)
        s4 = sb("s4", [128, 512], F32)
        dts = sb("dts", [128, NCH * 96], F32)
        eslr = sb("eslr", [64, NCH * 32], F32)
        clt1 = sb("clt1", [128, 512], F32)
        clt2 = sb("clt2", [128, 512], F32)
        a4 = clt1
        es4 = s4
        xbc = sb("xbc", [128, CCH], BF16)
        xbcw = sb("xbcw", [128, 17 * 131], BF16)
        cpa = sb("cpa", [128, CCH], BF16)
        cpc = sb("cpc", [128, CCH], BF16)
        szb = [sb(f"szb{i}", [128, DIN], BF16) for i in range(2)]
        arep = sb("arep", [128, 1024], F32)
        Ebuf = sb("Ebuf", [128, 2048], BF16)
        xhat = sb("xhat", [128, 1024], BF16)
        bwb = sb("bwb", [128, 1024], BF16)
        wcols = sb("wcols", [128, 32], F32)
        qb = sb("qb", [128, 1024], BF16)
        qhb = sb("qhb", [128, 2048], BF16)
        qfmb = sb("qfmb", [128, 512], BF16)
        Hm = sb("Hm", [64, 2048], F32)
        Hbf = sb("Hbf", [64, 2048], BF16)
        odb = sb("odb", [128, DM], BF16)
        sbf = sb("sbf", [64, 128], BF16)
        scf = sb("scf", [64, 128], BF16)
        sg = sb("sg", [128, 128], BF16)
        tbuf = sb("tbuf", [128, 1024], BF16)
        ssqt = sb("ssqt", [128, 2], F32)
        stat = sb("stat", [128, 8], F32)
        ceps = sb("ceps", [128, 2], F32)
        nc.const_aps.register(F32, EPS_LN, ceps[:, 0:1])
        nc.const_aps.register(F32, 1.0, ceps[:, 1:2])

        sx = [arep[:], Ebuf[:].bitcast(F32)]
        xn0 = qhb[:, DM:2 * DM]
        odb = tbuf
        xsbc = cpa
        dxsb = qb

        ps = lambda name, shape: st.enter_context(nc.psum_tensor(name, shape, F32))
        pwin = ps("pwin", [128, 1024])
        psmall = ps("psmall", [128, 512])
        pYA = ps("pYA", [128, 1024])
        pYB = ps("pYB", [128, 1024])
        pQT = ps("pQT", [128, 512])

        SS = lambda name: Sem(st, nc, name)
        sLD = SS("sLD"); sX = SS("sX"); sXN = SS("sXN"); sXNH = SS("sXNH")
        sTR = SS("sTR"); sWG = SS("sWG"); sWE = SS("sWE"); sSZ = SS("sSZ")
        sDT = SS("sDT"); sCV = SS("sCV"); sCD = SS("sCD"); sCS = SS("sCS")
        sBC = SS("sBC"); sBE = SS("sBE"); sGW = SS("sGW"); sGE = SS("sGE")
        sAR = SS("sAR"); sAE = SS("sAE"); sSC = SS("sSC"); sWC = SS("sWC")
        sMT = SS("sMT"); sXH = SS("sXH"); sBW = SS("sBW"); sY = SS("sY")
        sEV = SS("sEV"); sSQ = SS("sSQ"); sST = SS("sST"); sHU = SS("sHU")
        sHB = SS("sHB"); sQT = SS("sQT"); sQE = SS("sQE"); sWO = SS("sWO")
        sOD = SS("sOD"); sSG = SS("sSG"); sBAR = SS("sBAR"); sO3 = SS("sO3")
        sOUT = SS("sOUT"); sIN = SS("sIN"); sFV = SS("sFV"); sFA = SS("sFA")
        sOU = [SS("sOU0"), SS("sOU1")]

        block = st.enter_context(nc.Block())
        ENGS = ("tensor", "vector", "scalar", "gpsimd", "sync")

        def on(eng_name, fn):
            getattr(block, eng_name)(fn)

        def vfence(v):
            sFV.inc(v.engine_nop())
            v.wait_ge(sFV.h, sFV.n)

        def afence(sc):
            sFA.inc(sc.copy(ceps[0:1, 1:2], ceps[0:1, 1:2]))
            sc.wait_ge(sFA.h, sFA.n)

        def barrier():
            def f(sync):
                wait(sync, sOUT, sOU[0], sOU[1], sSG, sXNH, sSZ, sX, sTR, sLD)
            on("sync", f)
            target = sBAR.n + 5
            for e in ENGS:
                def bf(eng, e=e):
                    sBAR.inc(eng.nop() if e == "sync" else eng.drain())
                on(e, bf)
            for e in ENGS:
                on(e, lambda eng: eng.wait_ge(sBAR.h, target))
            assert sBAR.n == target

        def dbg_dump(name, src_ap, *wait_sems):
            if not debug:
                return
            def f(sync):
                wait(sync, *wait_sems)
                sOUT.inc(sync.dma_start(out=dbg[name], in_=src_ap), 16)
            on("sync", f)

        def f(g):
            g.memset(ceps[:, 0:1], EPS_LN)
            g.memset(ceps[:, 1:2], 1.0)
            g.memset(ones32[:], 1.0)
            sIN.inc(g.memset(onesbf[:], 1.0))
            for kt in range(NKT):
                sLD.inc(g.dma_start(out=winsb[:, kt * CCH:kt * CCH + DIN],
                                    in_=win["f"][kt * 128:(kt + 1) * 128, 0:DIN]), 16)
            sLD.inc(g.dma_start(out=wfm[:], in_=convw["f"]), 16)
            sLD.inc(g.dma_start(out=cbfm[:], in_=convb["f"]), 16)
            sLD.inc(g.dma_start(out=nwsb[:], in_=nwrep["f"]), 16)
            sLD.inc(g.dma_start(out=drsb[:], in_=drep["f"]), 16)
            sLD.inc(g.dma_start(out=dtbsb[:], in_=dtb4["f"]), 16)
            sLD.inc(g.dma_start(out=alogsb[:], in_=alog4["f"]), 16)
            for kt in range(NKT):
                sLD.inc(g.dma_start(out=windt[:, kt * DTC:(kt + 1) * DTC],
                                    in_=win["f"][kt * 128:(kt + 1) * 128, DIN + CCH:DPROJ]), 16)
        on("gpsimd", f)

        def f(sync):
            sTR.inc(sync.dma_start(out=if32[:], in_=if32_in), 16)
            sTR.inc(sync.dma_start(out=ibf[:], in_=ibf_in), 16)
        on("sync", f)

        sumall = dts[:, 0:32]
        sqall = dts[:, 32:64]
        meanall = dts[:, 64:96]
        rstdall = dts[:, 96:128]
        varall = dts[:, 128:160]

        for di, (d, src) in enumerate([("f", x_in), ("b", xrev_in)]):
            for k in range(NCH):
                c = di * NCH + k
                u = c % 2

                def f(sync, k=k, u=u, src=src):
                    if sXN.n >= 2:
                        sync.wait_ge(sXN.h, sXN.n - 1)
                    sX.inc(sync.dma_start(out=sx[u], in_=src[k * Q:(k + 1) * Q, :]), 16)
                on("sync", f)

                def f(v, c=c, u=u):
                    v.wait_ge(sX.h, sX.n)
                    if c == 0:
                        wait(v, sIN)
                    v.scalar_tensor_tensor(tbuf[:], sx[u], 0.0, sx[u],
                                           op0=ALU.add, op1=ALU.bypass,
                                           accum_out=sumall[:, c:c + 1])
                    sXN.inc(v.scalar_tensor_tensor(xn0, sx[u], 0.0, sx[u],
                                                   op0=ALU.add, op1=ALU.mult,
                                                   accum_out=sqall[:, c:c + 1]))
                on("vector", f)

        def f(v):
            wait(v, sXN)
            v.tensor_scalar_mul(meanall[:], sumall[:], 1.0 / DM)
            v.tensor_scalar_mul(varall[:], sqall[:], 1.0 / DM)
            vfence(v)
            v.tensor_tensor(sqall[:], meanall[:], meanall[:], op=ALU.mult)
            vfence(v)
            sXN.inc(v.tensor_tensor(varall[:], varall[:], sqall[:], op=ALU.subtract))
        on("vector", f)

        def f(sc):
            wait(sc, sXN)
            sc.activation(varall[:], varall[:], AF.Ln, bias=EPS_LN)
            afence(sc)
            sSQ.inc(sc.activation(rstdall[:], varall[:], AF.Exp, scale=-0.5))
        on("scalar", f)

        for di, (d, src) in enumerate([("f", x_in), ("b", xrev_in)]):
            for k in range(NCH):
                c = di * NCH + k
                u = c % 2
                xnb = qhb[:, u * DM:(u + 1) * DM]

                def f(sync, k=k, u=u, src=src):
                    if sXN.n >= 2:
                        sync.wait_ge(sXN.h, sXN.n - 1)
                    sX.inc(sync.dma_start(out=sx[u], in_=src[k * Q:(k + 1) * Q, :]), 16)
                on("sync", f)

                def f(v, c=c, u=u, xnb=xnb):
                    v.wait_ge(sX.h, sX.n)
                    if c < 2:
                        wait(v, sSQ)
                    if sXNH.n >= 32:
                        v.wait_ge(sXNH.h, sXNH.n - 16)
                    if debug:
                        wait(v, sOUT)
                    sXN.inc(v.tensor_scalar(xnb, sx[u], meanall[:, c:c + 1],
                                            rstdall[:, c:c + 1],
                                            op0=ALU.subtract, op1=ALU.mult))
                on("vector", f)

                def f(sync, d=d, k=k, xnb=xnb):
                    sync.wait_ge(sXN.h, sXN.n)
                    sXNH.inc(sync.dma_start(out=xnh[d][k * Q:(k + 1) * Q, :], in_=xnb), 16)
                on("sync", f)

                if d == "f" and k == 0:
                    dbg_dump("t_xn", xnb, sXN)

        barrier()

        def load_xnfm(d):
            def f(sync, d=d):
                wait(sync, sXNH)
                for ft in range(NKT):
                    sTR.inc(sync.dma_start_transpose(
                        xnfm[:, ft * L:(ft + 1) * L],
                        xnh[d][:, ft * 128:(ft + 1) * 128]), 16)
            on("sync", f)

        def load_dir_small(d):
            def f(g, d=d):
                sLD.inc(g.dma_start(out=wfm[:], in_=convw[d]), 16)
                sLD.inc(g.dma_start(out=cbfm[:], in_=convb[d]), 16)
                sLD.inc(g.dma_start(out=nwsb[:], in_=nwrep[d]), 16)
                sLD.inc(g.dma_start(out=drsb[:], in_=drep[d]), 16)
                sLD.inc(g.dma_start(out=dtbsb[:], in_=dtb4[d]), 16)
                sLD.inc(g.dma_start(out=alogsb[:], in_=alog4[d]), 16)
                for kt in range(NKT):
                    sLD.inc(g.dma_start(out=windt[:, kt * DTC:(kt + 1) * DTC],
                                        in_=win[d][kt * 128:(kt + 1) * 128, DIN + CCH:DPROJ]), 16)
            on("gpsimd", f)

        def load_win_z(d):
            def f(g, d=d):
                for kt in range(NKT):
                    sLD.inc(g.dma_start(out=winsb[:, kt * CCH:kt * CCH + DIN],
                                        in_=win[d][kt * 128:(kt + 1) * 128, 0:DIN]), 16)
            on("gpsimd", f)

        def load_win_xbc(d):
            def f(g, d=d):
                for kt in range(NKT):
                    sLD.inc(g.dma_start(out=winsb[:, kt * CCH:kt * CCH + CCH],
                                        in_=win[d][kt * 128:(kt + 1) * 128, DIN:DIN + CCH]), 16)
            on("gpsimd", f)

        def load_wout(d):
            def f(g, d=d):
                for j in range(16):
                    sLD.inc(g.dma_start(out=woutsb[:, j * DM:(j + 1) * DM],
                                        in_=wout[d][j * 128:(j + 1) * 128, :]), 16)
            on("gpsimd", f)

        def emit_phA(d, di):
            ld_t = sLD.n
            tr_tt = sTR.n
            tr_t = sTR.n

            def f(t):
                t.wait_ge(sLD.h, ld_t)
                t.wait_ge(sTR.h, tr_t)
                if pha_parts < 1:
                    sDT.inc(t.drain())
                    return
                for q in range(4):
                    for kt in range(NKT):
                        li = t.matmul(pYA[q * 32:(q + 1) * 32, 0:512],
                                      windt[:, kt * DTC:(kt + 1) * DTC],
                                      xnfm[:, kt * L + q * 512:kt * L + (q + 1) * 512],
                                      start=(kt == 0), stop=(kt == NKT - 1),
                                      tile_position=(0, q * 32))
                sDT.inc(li)
            on("tensor", f)

            for k in range(NCH if pha_parts >= 2 else 0):
                def f(t, k=k):
                    for g in range(4):
                        gi = sWG.n
                        if gi >= 2 and pha_parts >= 3:
                            t.wait_ge(sWE.h, gi - 1)
                        for kt in range(NKT):
                            li = t.matmul(pwin[:, (g % 2) * 512:(g % 2 + 1) * 512],
                                          xnfm[:, kt * L + k * Q:kt * L + (k + 1) * Q],
                                          winsb[:, kt * CCH + g * 512:kt * CCH + (g + 1) * 512],
                                          start=(kt == 0), stop=(kt == NKT - 1))
                        sWG.inc(li)
                on("tensor", f)

                if pha_parts < 3:
                    continue

                def f(sc):
                    base = sWG.n - 4
                    wait(sc, sSZ)
                    if debug:
                        wait(sc, sOUT)
                    for g in range(4):
                        sc.wait_ge(sWG.h, base + g + 1)
                        sWE.inc(sc.activation(szb[0][:, g * 512:(g + 1) * 512],
                                              pwin[:, (g % 2) * 512:(g % 2 + 1) * 512],
                                              AF.Silu))
                on("scalar", f)

                if pha_parts < 4:
                    continue

                def f(sync, k=k):
                    wait(sync, sWE)
                    sSZ.inc(sync.dma_start(out=szh[k * Q:(k + 1) * Q, :], in_=szb[0][:]), 16)
                on("sync", f)

                if d == "f" and k == 0:
                    dbg_dump("t_sz", szb[0][:], sWE)

            if pha_parts < 5:
                return
            def f(v):
                wait(v, sDT)
                v.tensor_scalar_add(clt2[:], pYA[:, 0:512], dtbsb[:])
                sDT.inc(v.tensor_scalar(clt1[:], pYA[:, 0:512], dtbsb[:], 0.0,
                                        op0=ALU.add, op1=ALU.max))
            on("vector", f)

            def f(sc):
                wait(sc, sDT)
                sc.activation(s4[:], clt2[:], AF.Abs)
                afence(sc)
                sc.activation(s4[:], s4[:], AF.Exp, scale=-1.0)
                afence(sc)
                sc.activation(s4[:], s4[:], AF.Ln, bias=1.0)
                sDT.inc(sc.activation(acol[:], alogsb[:], AF.Exp))
            on("scalar", f)

            def f(v):
                wait(v, sDT)
                v.tensor_tensor(dtsp4[:], clt1[:], s4[:], op=ALU.add)
                vfence(v)
                sDT.inc(v.tensor_scalar_mul(clt2[:], dtsp4[:], acol[:]))
            on("vector", f)

            def f(sc):
                wait(sc, sDT)
                sDT.inc(sc.activation(a4[:], clt2[:], AF.Exp, scale=-1.0))
            on("scalar", f)

            if pha_parts < 6:
                return
            def f(v):
                wait(v, sDT)
                for k in range(NCH):
                    qq, cc = k // 4, (k % 4) * 128
                    v.tensor_tensor_scan(
                        s4[qq * 32:(qq + 1) * 32, cc:cc + 128],
                        clt2[qq * 32:(qq + 1) * 32, cc:cc + 128],
                        clt2[qq * 32:(qq + 1) * 32, cc:cc + 128],
                        0.0, op0=ALU.add, op1=ALU.bypass)
                sDT.inc(v.engine_nop())
            on("vector", f)

            def f(sc):
                wait(sc, sDT)
                sDT.inc(sc.activation(es4[:], s4[:], AF.Exp, scale=-1.0))
            on("scalar", f)

            if pha_parts < 7:
                return
            for k in range(NCH):
                qq, cc = k // 4, (k % 4) * 128

                def f(t, k=k, qq=qq, cc=cc):
                    wait(t, sDT, sBE)
                    for col, srcm in ((0, dtsp4), (32, es4), (64, a4)):
                        li = t.matmul(psmall[:, col:col + 32],
                                      srcm[qq * 32:(qq + 1) * 32, cc:cc + 128],
                                      if32[qq * 32:(qq + 1) * 32, qq * 32:(qq + 1) * 32],
                                      start=True, stop=True, tile_position=(qq * 32, 0))
                    sBC.inc(li)
                on("tensor", f)

                def f(t, k=k, qq=qq, cc=cc):
                    sBC.inc(t.matmul(psmall[0:1, 96:128],
                                     es4[qq * 32:(qq + 1) * 32, cc + 127:cc + 128],
                                     if32[qq * 32:(qq + 1) * 32, qq * 32:(qq + 1) * 32],
                                     start=True, stop=True, tile_position=(qq * 32, 0)))
                on("tensor", f)

                def f(v, k=k):
                    wait(v, sBC)
                    v.tensor_copy(dts[:, k * 96:k * 96 + 96], psmall[:, 0:96])
                    sBE.inc(v.tensor_copy(wcols[0:1, 0:32], psmall[0:1, 96:128]))
                on("vector", f)

                def f(t, k=k):
                    wait(t, sBE)
                    sBC.inc(t.matmul(psmall[:, 64:96], ones32[:], wcols[0:1, 0:32],
                                     start=True, stop=True))
                on("tensor", f)

                def f(v, k=k):
                    wait(v, sBC)
                    sBE.inc(v.tensor_copy(eslr[:, k * 32:(k + 1) * 32],
                                          psmall[0:64, 64:96]))
                on("vector", f)

            if d == "f":
                dbg_dump("t_dtsp", dtsp4[:], sDT, sBE)
                dbg_dump("t_a4", a4[:], sDT)
                dbg_dump("t_ets", dts[:, 0:64], sBE)

        def emit_phB(d, di):
            ld_t = sLD.n

            def f(g):
                wait(g, sHB, sY)
                g.memset(Hm[:], 0.0)
                g.memset(Hbf[:], 0.0)
                sHB.inc(g.memset(xbcw[:].rearrange("p (c w) -> p c w", c=17)[:, :, 0:3], 0.0))
            on("gpsimd", f)

            def f(sync):
                wait(sync, sSZ)
                sSZ.inc(sync.dma_start(out=szb[0][:], in_=szh[0:Q, :]), 16)
            on("sync", f)

            gsz = [512, 512, 512, 512, 128]
            for k in range(NCH):
                u = k % 2
                cs = k * Q

                if k + 1 < NCH:
                    def f(sync, k=k):
                        wait(sync, sEV)
                        sSZ.inc(sync.dma_start(out=szb[(k + 1) % 2][:],
                                               in_=szh[(k + 1) * Q:(k + 2) * Q, :]), 16)
                    on("sync", f)

                def f(t, k=k, cs=cs):
                    t.wait_ge(sLD.h, ld_t)
                    off = 0
                    for g, n in enumerate(gsz):
                        gi = sWG.n
                        if gi >= 2:
                            t.wait_ge(sWE.h, gi - 1)
                        for kt in range(NKT):
                            li = t.matmul(pwin[:, (g % 2) * 512:(g % 2) * 512 + n],
                                          xnfm[:, kt * L + cs:kt * L + cs + Q],
                                          winsb[:, kt * CCH + off:kt * CCH + off + n],
                                          start=(kt == 0), stop=(kt == NKT - 1))
                        sWG.inc(li)
                        off += n
                on("tensor", f)

                def f(v):
                    base = sWG.n - 5
                    off = 0
                    for g, n in enumerate(gsz):
                        v.wait_ge(sWG.h, base + g + 1)
                        sWE.inc(v.tensor_copy(xbc[:, off:off + n],
                                              pwin[:, (g % 2) * 512:(g % 2) * 512 + n]))
                        off += n
                on("vector", f)

                tile_rounds = [(0, 4), (4, 4), (8, 4), (12, 4), (16, 1)]
                for r0, rn in tile_rounds:
                    def f(t, r0=r0, rn=rn):
                        wait(t, sWE, sQE, sAE)
                        for i in range(rn):
                            ct = r0 + i
                            li = t.matmul(pQT[:, i * 128:(i + 1) * 128],
                                          xbc[:, ct * 128:(ct + 1) * 128], ibf[:],
                                          start=True, stop=True)
                        sQT.inc(li)
                    on("tensor", f)

                    def f(v, r0=r0, rn=rn):
                        wait(v, sQT, sCD, sCV)
                        o = xbcw[:].rearrange("p (c w) -> p c w", c=17)[:, r0:r0 + rn, 3:131]
                        i_ = pQT[:, 0:rn * 128].rearrange("p (c w) -> p c w", c=rn)
                        sQE.inc(v.tensor_copy(o, i_))
                    on("vector", f)

                def f(v):
                    wait(v, sQE, sCS, sHB)
                    w3 = xbcw[:].rearrange("p (c w) -> p c w", c=17)
                    for ct in range(17):
                        v.tensor_scalar(cpc[:, ct * 128:(ct + 1) * 128],
                                        w3[:, ct, 0:128],
                                        wfm[:, ct:ct + 1], cbfm[:, ct:ct + 1],
                                        op0=ALU.mult, op1=ALU.add)
                    for j in (1, 2, 3):
                        vfence(v)
                        for ct in range(17):
                            acc = cpc[:, ct * 128:(ct + 1) * 128]
                            v.scalar_tensor_tensor(acc, w3[:, ct, j:j + 128],
                                                   wfm[:, j * 17 + ct:j * 17 + ct + 1],
                                                   acc, op0=ALU.mult, op1=ALU.add)
                    sCD.inc(v.engine_nop())
                on("vector", f)



                def f(v, k=k):
                    wait(v, sCD, sCV)
                    w3 = xbcw[:].rearrange("p (c w) -> p c w", c=17)
                    sQE.inc(v.tensor_copy(w3[:, :, 0:3], w3[:, :, 128:131]))
                on("vector", f)

                def f(sc):
                    wait(sc, sCD, sCV)
                    sCS.inc(sc.activation(cpc[:], cpc[:], AF.Silu))
                on("scalar", f)

                for r0, rn in tile_rounds:
                    def f(t, r0=r0, rn=rn):
                        wait(t, sCS, sQE)
                        for i in range(rn):
                            ct = r0 + i
                            li = t.matmul(pQT[:, i * 128:(i + 1) * 128],
                                          cpc[:, ct * 128:(ct + 1) * 128], ibf[:],
                                          start=True, stop=True)
                        sQT.inc(li)
                    on("tensor", f)

                    def f(v, r0=r0, rn=rn):
                        wait(v, sQT)
                        if debug:
                            wait(v, sOUT)
                        o = xsbc[:, r0 * 128:(r0 + rn) * 128]
                        sQE.inc(v.tensor_copy(o, pQT[:, 0:rn * 128]))
                    on("vector", f)

                def f(v):
                    wait(v, sQE)
                    sCS.inc(v.engine_nop())
                on("vector", f)

                if d == "f" and k == 0:
                    dbg_dump("t_xsbc", xsbc[:], sCS)

                def f(t):
                    wait(t, sCS, sGE)
                    t.matmul(psmall[0:64, 0:128], xsbc[:, DIN:DIN + 64], ibf[:],
                             start=True, stop=True)
                    sBC.inc(t.matmul(psmall[0:64, 128:256], xsbc[:, DIN + 64:DIN + 128],
                                     ibf[:], start=True, stop=True))
                on("tensor", f)

                def f(v):
                    wait(v, sBC)
                    v.tensor_copy(sbf[:], psmall[0:64, 0:128])
                    sBE.inc(v.tensor_copy(scf[:], psmall[0:64, 128:256]))
                on("vector", f)

                def f(t):
                    wait(t, sBE)
                    sGW.inc(t.matmul(psmall[:, 256:384], sbf[:], scf[:],
                                     start=True, stop=True))
                on("tensor", f)

                def f(v):
                    wait(v, sGW)
                    sGE.inc(v.tensor_copy(sg[:], psmall[:, 256:384]))
                on("vector", f)

                for half in range(2):
                    h0 = half * 16
                    for piece in range(4):
                        hh0 = h0 + piece * 4

                        def f(t, k=k, hh0=hh0):
                            wait(t, sSC, sQE, sAE)
                            for i in range(4):
                                h = hh0 + i
                                acolb = dts[:, k * 96 + 64 + h:k * 96 + 64 + h + 1] \
                                    .broadcast_to((128, 128))
                                li = t.matmul(pQT[:, i * 128:(i + 1) * 128], acolb, if32[:],
                                              start=True, stop=True)
                            sAR.inc(li)
                        on("tensor", f)

                        def f(v, piece=piece):
                            wait(v, sAR)
                            sAE.inc(v.tensor_copy(arep[:, (piece % 2) * 512:(piece % 2 + 1) * 512],
                                                  pQT[:]))
                        on("vector", f)

                        def f(v, hh0=hh0, piece=piece, half=half):
                            wait(v, sAE)
                            if debug:
                                wait(v, sOUT)
                            if half == 1 and piece == 0:
                                wait(v, sY)
                            for i in range(4):
                                h = hh0 + i
                                hh = h % 16
                                v.tensor_tensor_scan(
                                    Ebuf[:, hh * 128:(hh + 1) * 128],
                                    arep[:, (piece % 2) * 512 + i * 128:
                                         (piece % 2) * 512 + (i + 1) * 128],
                                    if32[:], 0.0, op0=ALU.mult, op1=ALU.add)
                            sSC.inc(v.engine_nop())
                        on("vector", f)

                    def f(v, k=k, h0=h0):
                        wait(v, sSC)
                        for i in range(16):
                            h = h0 + i
                            v.tensor_copy(wcols[:, h:h + 1],
                                          Ebuf[:, i * 128 + 127:i * 128 + 128])
                        sWC.inc(v.engine_nop())
                    on("vector", f)

                    def f(v):
                        wait(v, sGE, sWC)
                        e3 = Ebuf[:].rearrange("p (h t) -> p h t", h=16)
                        gb = sg[:].unsqueeze(1).broadcast_to((128, 16, 128))
                        sMT.inc(v.tensor_tensor(e3, e3, gb, op=ALU.mult))
                    on("vector", f)

                    if d == "f" and k == 0 and half == 0:
                        dbg_dump("t_mt", Ebuf[:], sMT)

                    def f(v, k=k, h0=h0, half=half):
                        wait(v, sCS)
                        if half == 1:
                            wait(v, sST)
                        if debug:
                            wait(v, sOUT)
                        for i in range(16):
                            h = h0 + i
                            li = v.tensor_scalar_mul(xhat[:, i * 64:(i + 1) * 64],
                                                     xsbc[:, h * 64:(h + 1) * 64],
                                                     dts[:, k * 96 + h:k * 96 + h + 1])
                        sXH.inc(li)
                    on("vector", f)

                    def f(v, k=k, h0=h0, half=half):
                        wait(v, sWC)
                        if half == 1:
                            wait(v, sST)
                        if debug:
                            wait(v, sOUT)
                        for i in range(16):
                            h = h0 + i
                            li = v.tensor_scalar_mul(bwb[:, i * 64:(i + 1) * 64],
                                                     xsbc[:, DIN:DIN + 64],
                                                     wcols[:, h:h + 1])
                        sBW.inc(li)
                    on("vector", f)

                    if d == "f" and k == 0 and half == 1:
                        dbg_dump("t_wc", wcols[:], sWC)
                        dbg_dump("t_bw", bwb[:], sBW)

                    if d == "f" and k == 0 and half == 0:
                        dbg_dump("t_xh", xhat[:], sXH)

                    def f(t, h0=h0, half=half):
                        wait(t, sMT, sXH, sHB, sEV, sHU, sOD)
                        for i in range(16):
                            t.matmul(pYA[:, i * 64:(i + 1) * 64],
                                     Ebuf[:, i * 128:(i + 1) * 128],
                                     xhat[:, i * 64:(i + 1) * 64], start=True, stop=True)
                        for i in range(16):
                            h = h0 + i
                            li = t.matmul(pYB[:, i * 64:(i + 1) * 64], scf[:],
                                          Hbf[:, h * 64:(h + 1) * 64],
                                          start=True, stop=True)
                        sY.inc(li)
                    on("tensor", f)

                    def f(v, h0=h0):
                        wait(v, sCS, sSQ)
                        sEV.inc(v.tensor_tensor(dxsb[:], xsbc[:, h0 * 64:h0 * 64 + 1024],
                                                drsb[:, h0 * 64:h0 * 64 + 1024], op=ALU.mult))
                    on("vector", f)

                    def f(v, k=k, h0=h0, u=u):
                        wait(v, sY, sSZ, sSG, sEV)
                        if debug:
                            wait(v, sOUT)
                        v.tensor_tensor(tbuf[:], pYA[:], qb[:], op=ALU.add)
                        es = dts[:, k * 96 + 32 + h0:k * 96 + 32 + h0 + 16] \
                            .unsqueeze(-1).broadcast_to((128, 16, 64))
                        pyb3 = pYB[:].rearrange("p (h q) -> p h q", h=16)
                        vfence(v)
                        v.tensor_tensor(qb[:].rearrange("p (h q) -> p h q", h=16),
                                        pyb3, es, op=ALU.mult)
                        vfence(v)
                        v.tensor_tensor(qb[:], qb[:], tbuf[:], op=ALU.add)
                        vfence(v)
                        v.tensor_tensor(qb[:], qb[:], szb[u][:, h0 * 64:h0 * 64 + 1024],
                                        op=ALU.mult)
                        vfence(v)
                        sEV.inc(v.tensor_tensor(qhb[:, h0 * 64:h0 * 64 + 1024], qb[:],
                                                nwsb[:, h0 * 64:h0 * 64 + 1024], op=ALU.mult))
                    on("vector", f)

                    if d == "f" and k == 0 and half == 1:
                        dbg_dump("t_q", qb[:], sEV)

                    def f(v, half=half):
                        wait(v, sEV)
                        if debug:
                            wait(v, sOUT)
                        sSQ.inc(v.scalar_tensor_tensor(qb[:], qb[:], 0.0, qb[:],
                                                       op0=ALU.add, op1=ALU.mult,
                                                       accum_out=ssqt[:, half:half + 1]))
                    on("vector", f)

                    def f(t, h0=h0, half=half):
                        wait(t, sBW, sEV)
                        tgt = pYA if half == 0 else pYB
                        for i in range(16):
                            li = t.matmul(tgt[0:64, i * 64:(i + 1) * 64],
                                          bwb[:, i * 64:(i + 1) * 64],
                                          xhat[:, i * 64:(i + 1) * 64],
                                          start=True, stop=True)
                        sST.inc(li)
                    on("tensor", f)

                    def f(v, k=k, h0=h0, half=half):
                        wait(v, sST)
                        if debug:
                            wait(v, sOUT)
                        tgt = pYA if half == 0 else pYB
                        el = eslr[:, k * 32 + h0:k * 32 + h0 + 16].unsqueeze(-1) \
                            .broadcast_to((64, 16, 64))
                        hview = Hm[0:64, h0 * 64:h0 * 64 + 1024] \
                            .rearrange("p (h q) -> p h q", h=16)
                        v.tensor_tensor(hview, hview, el, op=ALU.mult)
                        vfence(v)
                        sHU.inc(v.tensor_tensor(Hm[0:64, h0 * 64:h0 * 64 + 1024],
                                                Hm[0:64, h0 * 64:h0 * 64 + 1024],
                                                tgt[0:64, :], op=ALU.add))
                    on("vector", f)

                def f(v, di=di, k=k):
                    wait(v, sSQ)
                    sEV.inc(v.tensor_tensor(ssqall[:, di * NCH + k:di * NCH + k + 1],
                                            ssqt[:, 0:1], ssqt[:, 1:2], op=ALU.add))
                on("vector", f)

                def f(v):
                    wait(v, sHU)
                    sHB.inc(v.tensor_copy(Hbf[:], Hm[:]))
                on("vector", f)

                if d == "f" and k == 0:
                    dbg_dump("t_H", Hm[:], sHU)
                    dbg_dump("t_qh", qhb[:], sEV)

                for r in range(4):
                    def f(t, r=r):
                        wait(t, sEV, sQE, sAE)
                        for i in range(4):
                            j = r * 4 + i
                            li = t.matmul(pQT[:, i * 128:(i + 1) * 128],
                                          qhb[:, j * 128:(j + 1) * 128], ibf[:],
                                          start=True, stop=True)
                        sQT.inc(li)
                    on("tensor", f)

                    def f(v):
                        wait(v, sQT)
                        sQE.inc(v.tensor_copy(qfmb[:], pQT[:]))
                    on("vector", f)

                    def f(t, r=r):
                        wait(t, sQE)
                        if r == 0:
                            wait(t, sOD, sHU)
                        for i in range(4):
                            j = r * 4 + i
                            for nn in range(2):
                                li = t.matmul(pYB[:, nn * 512:(nn + 1) * 512],
                                              qfmb[:, i * 128:(i + 1) * 128],
                                              woutsb[:, j * DM + nn * 512:j * DM + (nn + 1) * 512],
                                              start=(j == 0), stop=(j == 15))
                        sWO.inc(li)
                    on("tensor", f)

                def f(v):
                    wait(v, sWO)
                    if debug:
                        wait(v, sOUT)
                    sOD.inc(v.tensor_copy(odb[:], pYB[:]))
                on("vector", f)

                def f(sync, d=d, k=k):
                    wait(sync, sOD)
                    sSG.inc(sync.dma_start(out=odh[d][k * Q:(k + 1) * Q, :], in_=odb[:]), 16)
                on("sync", f)

                if d == "f" and k == 0:
                    dbg_dump("t_od", odb[:], sOD)

        dirs = []
        if stages >= 2:
            dirs.append(("f", 0))
        if stages >= 4:
            dirs.append(("b", 1))
        for d, di in dirs:
            if d == "b":
                load_dir_small(d)
                load_win_z(d)
            load_xnfm(d)
            barrier()
            emit_phA(d, di)
            barrier()
            if stages >= 3 or d == "b":
                load_win_xbc(d)
                load_wout(d)
                barrier()
                emit_phB(d, di)
            barrier()

        if stages < 5:
            def f(sync):
                wait(sync, sOUT, sOU[0], sOU[1], sSG, sXNH, sSZ, sLD, sTR, sX)
            on("sync", f)
            return nc
        outfm = winsb
        wosb = woutsb

        def f(g):
            wait(g, sSG)
            for j in range(16):
                sLD.inc(g.dma_start(out=wosb[:, j * DM:(j + 1) * DM],
                                    in_=wo_in[j * 128:(j + 1) * 128, :]), 16)
            for j in range(NKT):
                sLD.inc(g.dma_start(out=wtsb[:, j * DM:(j + 1) * DM],
                                    in_=wt_in[j * 128:(j + 1) * 128, :]), 16)
            for kt in range(NKT):
                sLD.inc(g.dma_start(out=nwsb[:, kt * 32:(kt + 1) * 32],
                                    in_=vr_in[kt * 128:(kt + 1) * 128, :]), 16)
            sLD.inc(g.dma_start(out=drsb[0:32, 0:1024], in_=ct_in), 16)
            sLD.inc(g.dma_start(out=szb[0][:, 0:DM], in_=bo_in), 16)
            sLD.inc(g.dma_start(out=bwb[0:1, 0:1024], in_=ffb_in), 16)
        on("gpsimd", f)

        def f(sync):
            wait(sync, sSG)
            for ft in range(NKT):
                sTR.inc(sync.dma_start_transpose(
                    outfm[:, ft * 2048:(ft + 1) * 2048],
                    odh["f"][:, ft * 128:(ft + 1) * 128]), 16)
        on("sync", f)

        def f(sc):
            wait(sc, sEV)
            sc.activation(rall[:], ssqall[:], AF.Ln, bias=EPS_RMS, scale=1.0 / DIN)
            afence(sc)
            sO3.inc(sc.activation(rall[:], rall[:], AF.Exp, scale=-0.5))
        on("scalar", f)

        barrier()

        ph3_ld = sLD.n
        ph3_tr = sTR.n

        for k in range(NCH):
            u = k % 2
            kb = NCH - 1 - k

            def f(sync, k=k, u=u, kb=kb):
                wait(sync, sOU[u])
                if sQT.n >= 2:
                    sync.wait_ge(sQT.h, sQT.n - 2)
                sX.inc(sync.dma_start(out=sx[u], in_=x_in[k * Q:(k + 1) * Q, :]), 16)
                sX.inc(sync.dma_start(out=odb[:], in_=odh["b"][kb * Q:(kb + 1) * Q, :]), 16)
            on("sync", f)

            def f(v, kb=kb):
                wait(v, sX, sO3)
                sO3.inc(v.tensor_scalar_mul(odb[:], odb[:],
                                            rall[:, NCH + kb:NCH + kb + 1]))
            on("vector", f)

            for r in range(2):
                def f(t, r=r):
                    wait(t, sO3, sQE)
                    t.wait_ge(sLD.h, ph3_ld)
                    t.wait_ge(sTR.h, ph3_tr)
                    for i in range(4):
                        j = r * 4 + i
                        li = t.matmul(pQT[:, i * 128:(i + 1) * 128],
                                      odb[:, j * 128:(j + 1) * 128], ibf[:, 127::-1],
                                      start=True, stop=True)
                    sQT.inc(li)
                on("tensor", f)

                def f(v, r=r):
                    wait(v, sQT)
                    sQE.inc(v.tensor_copy(qhb[:, r * 512:(r + 1) * 512], pQT[:]))
                on("vector", f)

            def f(t, k=k):
                wait(t, sQE, sO3)
                for mt in range(8):
                    for nn in range(2):
                        t.matmul(pYA[:, nn * 512:(nn + 1) * 512],
                                 outfm[:, mt * 2048 + k * Q:mt * 2048 + (k + 1) * Q],
                                 wosb[:, mt * DM + nn * 512:mt * DM + (nn + 1) * 512],
                                 start=(mt == 0), stop=(mt == 7))
                for mt in range(8):
                    for nn in range(2):
                        li = t.matmul(pYB[:, nn * 512:(nn + 1) * 512],
                                      qhb[:, mt * 128:(mt + 1) * 128],
                                      wosb[:, (8 + mt) * DM + nn * 512:(8 + mt) * DM + (nn + 1) * 512],
                                      start=(mt == 0), stop=(mt == 7))
                sWO.inc(li)
            on("tensor", f)

            def f(v, k=k, u=u, kb=kb):
                wait(v, sWO)
                if debug:
                    wait(v, sOUT)
                v.scalar_tensor_tensor(sx[u], pYA[:], rall[:, k:k + 1], sx[u],
                                       op0=ALU.mult, op1=ALU.add)
                vfence(v)
                v.tensor_tensor(sx[u], pYB[:], sx[u], op=ALU.add)
                vfence(v)
                sO3.inc(v.tensor_tensor(sx[u], szb[0][:, 0:DM], sx[u], op=ALU.add))
            on("vector", f)

            if k == 0:
                dbg_dump("t_x2", sx[u], sO3)

            def f(v, u=u):
                wait(v, sO3)
                if debug:
                    wait(v, sOUT)
                v.scalar_tensor_tensor(qb[:], sx[u], 0.0, sx[u],
                                       op0=ALU.add, op1=ALU.bypass,
                                       accum_out=stat[:, 0:1])
                sSQ.inc(v.scalar_tensor_tensor(xn0, sx[u], 0.0, sx[u],
                                               op0=ALU.add, op1=ALU.mult,
                                               accum_out=stat[:, 1:2]))
            on("vector", f)

            def f(v):
                wait(v, sSQ)
                v.tensor_scalar_mul(stat[:, 2:3], stat[:, 0:1], 1.0 / DM)
                v.tensor_scalar_mul(stat[:, 3:4], stat[:, 1:2], 1.0 / DM)
                v.tensor_scalar(stat[:, 4:5], stat[:, 0:1], stat[:, 0:1],
                                1.0 / (DM * DM), op0=ALU.mult, op1=ALU.mult)
                vfence(v)
                sO3.inc(v.tensor_tensor(stat[:, 5:6], stat[:, 3:4], stat[:, 4:5],
                                        op=ALU.subtract))
            on("vector", f)

            def f(sc):
                wait(sc, sO3)
                sc.activation(stat[:, 6:7], stat[:, 5:6], AF.Ln, bias=EPS_LN)
                afence(sc)
                sSQ.inc(sc.activation(stat[:, 7:8], stat[:, 6:7], AF.Exp, scale=-0.5))
            on("scalar", f)

            def f(v, u=u):
                wait(v, sSQ, sQT)
                sO3.inc(v.tensor_scalar(xn0, sx[u], stat[:, 2:3], stat[:, 7:8],
                                        op0=ALU.subtract, op1=ALU.mult))
            on("vector", f)

            for r in range(2):
                def f(t, r=r):
                    wait(t, sO3, sQE)
                    for i in range(4):
                        j = r * 4 + i
                        li = t.matmul(pQT[:, i * 128:(i + 1) * 128],
                                      qhb[:, DM + j * 128:DM + (j + 1) * 128], ibf[:],
                                      start=True, stop=True)
                    sQT.inc(li)
                on("tensor", f)

                def f(v, r=r):
                    wait(v, sQT)
                    sQE.inc(v.tensor_copy(qb[:, r * 512:(r + 1) * 512], pQT[:]))
                on("vector", f)

            def f(t):
                wait(t, sQE, sBE)
                for kt in range(NKT):
                    li = t.matmul(psmall[0:32, 0:128],
                                  nwsb[:, kt * 32:(kt + 1) * 32],
                                  qb[:, kt * 128:(kt + 1) * 128],
                                  start=(kt == 0), stop=(kt == NKT - 1))
                sBC.inc(li)
            on("tensor", f)

            def f(v):
                wait(v, sBC)
                sBE.inc(v.tensor_copy(scf[0:32, 0:128], psmall[0:32, 0:128]))
            on("vector", f)

            def f(t):
                wait(t, sBE, sO3)
                for kt in range(NKT):
                    for nn in range(2):
                        t.matmul(pYA[:, nn * 512:(nn + 1) * 512],
                                 qb[:, kt * 128:(kt + 1) * 128],
                                 wtsb[:, kt * DM + nn * 512:kt * DM + (nn + 1) * 512],
                                 start=(kt == 0), stop=False)
                for nn in range(2):
                    t.matmul(pYA[:, nn * 512:(nn + 1) * 512], scf[0:32, 0:128],
                             drsb[0:32, nn * 512:(nn + 1) * 512],
                             start=False, stop=False)
                for nn in range(2):
                    li = t.matmul(pYA[:, nn * 512:(nn + 1) * 512], onesbf[:],
                                  bwb[0:1, nn * 512:(nn + 1) * 512],
                                  start=False, stop=True)
                sWO.inc(li)
            on("tensor", f)

            def f(v, u=u):
                wait(v, sWO)
                sO3.inc(v.tensor_tensor(sx[u], pYA[:], sx[u], op=ALU.add))
            on("vector", f)

            def f(sync, k=k, u=u):
                wait(sync, sO3)
                sOU[u].inc(sync.dma_start(out=y_out[k * Q:(k + 1) * Q, :], in_=sx[u]), 16)
            on("sync", f)

        def f(sync):
            wait(sync, sOUT, sOU[0], sOU[1])
        on("sync", f)

    return nc


def prep_host(inputs):
    g = lambda k: np.ascontiguousarray(np.asarray(inputs[k], dtype=np.float32))
    m = {}
    for d in "fb":
        m[f"win_{d}"] = g(f"L0_attn_{d}Win")
        m[f"wout_{d}"] = g(f"L0_attn_{d}Wout")
        cw = g(f"L0_attn_{d}convw")[:, 0, :]
        m[f"convw_{d}"] = np.ascontiguousarray(
            cw.reshape(4, 17, 128).transpose(2, 0, 1).reshape(128, 68))
        m[f"convb_{d}"] = np.ascontiguousarray(
            g(f"L0_attn_{d}convb").reshape(17, 128).T)
        m[f"dtb4_{d}"] = np.ascontiguousarray(np.tile(g(f"L0_attn_{d}dtb"), 4)[:, None])
        m[f"alog4_{d}"] = np.ascontiguousarray(np.tile(g(f"L0_attn_{d}Alog"), 4)[:, None])
        m[f"drep_{d}"] = np.ascontiguousarray(np.broadcast_to(
            np.repeat(g(f"L0_attn_{d}D"), P)[None, :], (128, DIN)))
        m[f"nwrep_{d}"] = np.ascontiguousarray(np.broadcast_to(
            g(f"L0_attn_{d}normw")[None, :], (128, DIN)))
    m["wo"] = g("L0_attn_Wo")
    m["bo"] = np.ascontiguousarray(np.broadcast_to(g("L0_attn_bo")[None, :], (128, DM)))
    m["wt"] = np.ascontiguousarray(g("L0_ffn_W").T)
    m["ffb"] = g("L0_ffn_b")[None, :]
    m["vr"] = g("L0_ffn_V_r")
    m["ct"] = np.ascontiguousarray(g("L0_ffn_C").T)
    m["if32"] = np.eye(128, dtype=np.float32)
    m["ibf"] = np.eye(128, dtype=ml_dtypes.bfloat16)
    return m


def kernel(**inputs):
    nc = build(debug=False)
    shared = prep_host(inputs)
    x = np.ascontiguousarray(np.asarray(inputs["L0_attn_x"], dtype=np.float32))
    in_maps = []
    for b in range(8):
        im = dict(shared)
        im["x"] = np.ascontiguousarray(x[b])
        im["xrev"] = np.ascontiguousarray(x[b][::-1])
        in_maps.append(im)
    res = run_bass_kernel_spmd(nc, in_maps, core_ids=list(range(8)))
    out = np.stack([r["y"] for r in res.results], axis=0)
    return out.astype(np.float32)


# version 10
# speedup vs baseline: 1.2084x; 1.0631x over previous
import contextlib
import numpy as np
import ml_dtypes
import concourse.bass as bass
import concourse.mybir as mybir
from concourse.bass_utils import run_bass_kernel_spmd

F32 = mybir.dt.float32
BF16 = mybir.dt.bfloat16
AF = mybir.ActivationFunctionType
ALU = mybir.AluOpType

L = 2048
DM = 1024
Q = 128
NCH = L // Q
NH = 32
P = 64
NST = 64
DIN = 2048
CCH = DIN + 2 * NST
DTC = NH
DPROJ = 2 * DIN + 2 * NST + NH
NKT = DM // 128
EPS_LN = 1e-5
EPS_RMS = 1e-5


class Sem:
    def __init__(self, stack, nc, name):
        self.h = stack.enter_context(nc.semaphore(name))
        self.n = 0

    def inc(self, instr, k=1):
        instr.then_inc(self.h, k)
        self.n += k
        return instr


def wait(eng, *sems):
    for s in sems:
        if s.n:
            eng.wait_ge(s.h, s.n)


def build(debug=False, stages=5, pha_parts=9):
    nc = bass.Bass("TRN2", target_bir_lowering=False)
    dram = {}

    def din(name, shape, dt=F32):
        dram[name] = nc.dram_tensor(name, shape, dt, kind="ExternalInput").ap()
        return dram[name]

    def dout(name, shape, dt=F32):
        dram[name] = nc.dram_tensor(name, shape, dt, kind="ExternalOutput").ap()
        return dram[name]

    def dint(name, shape, dt=BF16):
        dram[name] = nc.dram_tensor(name, shape, dt, kind="Internal").ap()
        return dram[name]

    x_in = din("x", [L, DM])
    xrev_in = din("xrev", [L, DM])
    win = {d: din(f"win_{d}", [DM, DPROJ]) for d in "fb"}
    wout = {d: din(f"wout_{d}", [DIN, DM]) for d in "fb"}
    convw = {d: din(f"convw_{d}", [128, 4 * 17]) for d in "fb"}
    convb = {d: din(f"convb_{d}", [128, 17]) for d in "fb"}
    dtb4 = {d: din(f"dtb4_{d}", [128, 1]) for d in "fb"}
    alog4 = {d: din(f"alog4_{d}", [128, 1]) for d in "fb"}
    drep = {d: din(f"drep_{d}", [128, DIN]) for d in "fb"}
    nwrep = {d: din(f"nwrep_{d}", [128, DIN]) for d in "fb"}
    wo_in = din("wo", [DIN, DM])
    bo_in = din("bo", [128, DM])
    wt_in = din("wt", [DM, DM])
    ffb_in = din("ffb", [1, DM])
    vr_in = din("vr", [DM, 32])
    ct_in = din("ct", [32, DM])
    if32_in = din("if32", [128, 128])
    ibf_in = din("ibf", [128, 128], BF16)

    y_out = dout("y", [L, DM])

    xnh = {"f": dint("xnh", [L, DM]), "b": dint("xnrevh", [L, DM])}
    szh = dint("szh", [L, DIN])
    odh = {"f": dint("odhf", [L, DM]), "b": dint("odhb", [L, DM])}

    dbg = {}
    if debug:
        for nm, shp, dt in [
            ("t_xn", [128, DM], BF16), ("t_sz", [128, DIN], BF16),
            ("t_xsbc", [128, CCH], BF16), ("t_dtsp", [128, 512], F32),
            ("t_a4", [128, 512], F32), ("t_ets", [128, 64], F32),
            ("t_mt", [128, 2048], BF16), ("t_xh", [128, 1024], BF16),
            ("t_q", [128, 1024], BF16), ("t_qh", [128, DIN], BF16),
            ("t_wc", [128, 32], F32), ("t_bw", [128, 1024], BF16),
            ("t_od", [128, DM], BF16), ("t_H", [64, 2048], F32),
            ("t_x2", [128, DM], F32),
        ]:
            dbg[nm] = dout(nm, shp, dt)

    st = contextlib.ExitStack()
    with st:
        sb = lambda name, shape, dt: st.enter_context(nc.sbuf_tensor(name, shape, dt))
        xnfm = sb("xnfm", [128, NKT * L], BF16)
        winsb = sb("winsb", [128, NKT * CCH], BF16)
        windt = sb("windt", [128, NKT * DTC], BF16)
        woutsb = sb("woutsb", [128, 16 * DM], BF16)
        wtsb = sb("wtsb", [128, NKT * DM], BF16)
        wfm = sb("wfm", [128, 4 * 17], F32)
        cbfm = sb("cbfm", [128, 17], F32)
        nwsb = sb("nwsb", [128, DIN], BF16)
        drsb = sb("drsb", [128, DIN], BF16)
        dtbsb = sb("dtbsb", [128, 1], F32)
        alogsb = sb("alogsb", [128, 1], F32)
        acol = sb("acol", [128, 1], F32)
        if32 = sb("if32_s", [128, 128], F32)
        ibf = sb("ibf_s", [128, 128], BF16)
        ones32 = sb("ones32", [1, 128], F32)
        onesbf = sb("onesbf", [1, 128], BF16)
        ssqall = sb("ssqall", [128, 2 * NCH], F32)
        rall = sb("rall", [128, 2 * NCH], F32)
        xsbc1 = sb("xsbc1", [128, CCH], BF16)
        dtsp4 = xsbc1[:, 0:1024].bitcast(F32)
        s4 = xsbc1[:, 1024:2048].bitcast(F32)
        dts = sb("dts", [128, NCH * 96], F32)
        eslr = sb("eslr", [64, NCH * 32], F32)
        clt1 = sb("clt1", [128, 512], F32)
        clt2 = sb("clt2", [128, 512], F32)
        a4 = clt1
        es4 = s4
        xbc = sb("xbc", [128, CCH], BF16)
        xbcw = sb("xbcw", [128, 17 * 131], BF16)
        cpa = sb("cpa", [128, CCH], BF16)
        cpc = sb("cpc", [128, CCH], BF16)
        szb = [sb(f"szb{i}", [128, DIN], BF16) for i in range(2)]
        arep = sb("arep", [128, 1024], F32)
        Ebuf = sb("Ebuf", [128, 2048], BF16)
        xhat = sb("xhat", [128, 1024], BF16)
        bwb = sb("bwb", [128, 1024], BF16)
        wcols = sb("wcols", [128, 32], F32)
        qb = sb("qb", [128, 1024], BF16)
        qhb = sb("qhb", [128, 2048], BF16)
        qfmb = sb("qfmb", [128, 512], BF16)
        Hm = sb("Hm", [64, 2048], F32)
        Hbf = sb("Hbf", [64, 2048], BF16)
        odb = sb("odb", [128, DM], BF16)
        sbf = sb("sbf", [64, 128], BF16)
        scf = sb("scf", [64, 128], BF16)
        sg = sb("sg", [128, 128], BF16)
        tbuf = sb("tbuf", [128, 1024], BF16)
        ssqt = sb("ssqt", [128, 2], F32)
        stat = sb("stat", [128, 8], F32)
        ceps = sb("ceps", [128, 2], F32)
        nc.const_aps.register(F32, EPS_LN, ceps[:, 0:1])
        nc.const_aps.register(F32, 1.0, ceps[:, 1:2])

        sx = [arep[:], Ebuf[:].bitcast(F32)]
        xn0 = qhb[:, DM:2 * DM]
        odb = tbuf
        xsbc2 = [cpa, xsbc1]
        dxsb = qb

        ps = lambda name, shape: st.enter_context(nc.psum_tensor(name, shape, F32))
        pwin = ps("pwin", [128, 1024])
        psmall = ps("psmall", [128, 512])
        pYA = ps("pYA", [128, 1024])
        pYB = ps("pYB", [128, 1024])
        pQT = ps("pQT", [128, 512])

        SS = lambda name: Sem(st, nc, name)
        sLD = SS("sLD"); sX = SS("sX"); sXN = SS("sXN"); sXNH = SS("sXNH")
        sTR = SS("sTR"); sWG = SS("sWG"); sWE = SS("sWE"); sSZ = SS("sSZ")
        sDT = SS("sDT"); sCV = SS("sCV"); sCD = SS("sCD"); sCS = SS("sCS")
        sBC = SS("sBC"); sBE = SS("sBE"); sGW = SS("sGW"); sGE = SS("sGE")
        sAR = SS("sAR"); sAE = SS("sAE"); sSC = SS("sSC"); sWC = SS("sWC")
        sMT = SS("sMT"); sXH = SS("sXH"); sBW = SS("sBW"); sY = SS("sY")
        sEV = SS("sEV"); sSQ = SS("sSQ"); sST = SS("sST"); sHU = SS("sHU")
        sHB = SS("sHB"); sQT = SS("sQT"); sQE = SS("sQE"); sWO = SS("sWO")
        sOD = SS("sOD"); sSG = SS("sSG"); sBAR = SS("sBAR"); sO3 = SS("sO3")
        sOUT = SS("sOUT"); sIN = SS("sIN"); sFV = SS("sFV"); sFA = SS("sFA")
        sOU = [SS("sOU0"), SS("sOU1")]
        sFT = SS("sFT"); sFE = SS("sFE")

        block = st.enter_context(nc.Block())
        ENGS = ("tensor", "vector", "scalar", "gpsimd", "sync")

        def on(eng_name, fn):
            getattr(block, eng_name)(fn)

        def vfence(v):
            sFV.inc(v.engine_nop())
            v.wait_ge(sFV.h, sFV.n)

        def afence(sc):
            sFA.inc(sc.copy(ceps[0:1, 1:2], ceps[0:1, 1:2]))
            sc.wait_ge(sFA.h, sFA.n)

        def barrier():
            def f(sync):
                wait(sync, sOUT, sOU[0], sOU[1], sSG, sXNH, sSZ, sX, sTR, sLD)
            on("sync", f)
            target = sBAR.n + 5
            for e in ENGS:
                def bf(eng, e=e):
                    sBAR.inc(eng.nop() if e == "sync" else eng.drain())
                on(e, bf)
            for e in ENGS:
                on(e, lambda eng: eng.wait_ge(sBAR.h, target))
            assert sBAR.n == target

        def dbg_dump(name, src_ap, *wait_sems):
            if not debug:
                return
            def f(sync):
                wait(sync, *wait_sems)
                sOUT.inc(sync.dma_start(out=dbg[name], in_=src_ap), 16)
            on("sync", f)

        def f(g):
            g.memset(ceps[:, 0:1], EPS_LN)
            g.memset(ceps[:, 1:2], 1.0)
            g.memset(ones32[:], 1.0)
            sIN.inc(g.memset(onesbf[:], 1.0))
            for kt in range(NKT):
                sLD.inc(g.dma_start(out=winsb[:, kt * CCH:kt * CCH + DIN],
                                    in_=win["f"][kt * 128:(kt + 1) * 128, 0:DIN]), 16)
            sLD.inc(g.dma_start(out=wfm[:], in_=convw["f"]), 16)
            sLD.inc(g.dma_start(out=cbfm[:], in_=convb["f"]), 16)
            sLD.inc(g.dma_start(out=nwsb[:], in_=nwrep["f"]), 16)
            sLD.inc(g.dma_start(out=drsb[:], in_=drep["f"]), 16)
            sLD.inc(g.dma_start(out=dtbsb[:], in_=dtb4["f"]), 16)
            sLD.inc(g.dma_start(out=alogsb[:], in_=alog4["f"]), 16)
            for kt in range(NKT):
                sLD.inc(g.dma_start(out=windt[:, kt * DTC:(kt + 1) * DTC],
                                    in_=win["f"][kt * 128:(kt + 1) * 128, DIN + CCH:DPROJ]), 16)
        on("gpsimd", f)

        def f(sync):
            sTR.inc(sync.dma_start(out=if32[:], in_=if32_in), 16)
            sTR.inc(sync.dma_start(out=ibf[:], in_=ibf_in), 16)
        on("sync", f)

        sumall = dts[:, 0:32]
        sqall = dts[:, 32:64]
        meanall = dts[:, 64:96]
        rstdall = dts[:, 96:128]
        varall = dts[:, 128:160]

        for di, (d, src) in enumerate([("f", x_in), ("b", xrev_in)]):
            for k in range(NCH):
                c = di * NCH + k
                u = c % 2

                def f(sync, k=k, u=u, src=src):
                    if sXN.n >= 2:
                        sync.wait_ge(sXN.h, sXN.n - 1)
                    sX.inc(sync.dma_start(out=sx[u], in_=src[k * Q:(k + 1) * Q, :]), 16)
                on("sync", f)

                def f(v, c=c, u=u):
                    v.wait_ge(sX.h, sX.n)
                    if c == 0:
                        wait(v, sIN)
                    v.scalar_tensor_tensor(tbuf[:], sx[u], 0.0, sx[u],
                                           op0=ALU.add, op1=ALU.bypass,
                                           accum_out=sumall[:, c:c + 1])
                    sXN.inc(v.scalar_tensor_tensor(xn0, sx[u], 0.0, sx[u],
                                                   op0=ALU.add, op1=ALU.mult,
                                                   accum_out=sqall[:, c:c + 1]))
                on("vector", f)

        def f(v):
            wait(v, sXN)
            v.tensor_scalar_mul(meanall[:], sumall[:], 1.0 / DM)
            v.tensor_scalar_mul(varall[:], sqall[:], 1.0 / DM)
            vfence(v)
            v.tensor_tensor(sqall[:], meanall[:], meanall[:], op=ALU.mult)
            vfence(v)
            sXN.inc(v.tensor_tensor(varall[:], varall[:], sqall[:], op=ALU.subtract))
        on("vector", f)

        def f(sc):
            wait(sc, sXN)
            sc.activation(varall[:], varall[:], AF.Ln, bias=EPS_LN)
            afence(sc)
            sSQ.inc(sc.activation(rstdall[:], varall[:], AF.Exp, scale=-0.5))
        on("scalar", f)

        for di, (d, src) in enumerate([("f", x_in), ("b", xrev_in)]):
            for k in range(NCH):
                c = di * NCH + k
                u = c % 2
                xnb = qhb[:, u * DM:(u + 1) * DM]

                def f(sync, k=k, u=u, src=src):
                    if sXN.n >= 2:
                        sync.wait_ge(sXN.h, sXN.n - 1)
                    sX.inc(sync.dma_start(out=sx[u], in_=src[k * Q:(k + 1) * Q, :]), 16)
                on("sync", f)

                def f(v, c=c, u=u, xnb=xnb):
                    v.wait_ge(sX.h, sX.n)
                    if c < 2:
                        wait(v, sSQ)
                    if sXNH.n >= 32:
                        v.wait_ge(sXNH.h, sXNH.n - 16)
                    if debug:
                        wait(v, sOUT)
                    sXN.inc(v.tensor_scalar(xnb, sx[u], meanall[:, c:c + 1],
                                            rstdall[:, c:c + 1],
                                            op0=ALU.subtract, op1=ALU.mult))
                on("vector", f)

                def f(sync, d=d, k=k, xnb=xnb):
                    sync.wait_ge(sXN.h, sXN.n)
                    sXNH.inc(sync.dma_start(out=xnh[d][k * Q:(k + 1) * Q, :], in_=xnb), 16)
                on("sync", f)

                if d == "f" and k == 0:
                    dbg_dump("t_xn", xnb, sXN)

        barrier()

        def load_xnfm(d):
            def f(sync, d=d):
                wait(sync, sXNH)
                for ft in range(NKT):
                    sTR.inc(sync.dma_start_transpose(
                        xnfm[:, ft * L:(ft + 1) * L],
                        xnh[d][:, ft * 128:(ft + 1) * 128]), 16)
            on("sync", f)

        def load_dir_small(d):
            def f(g, d=d):
                sLD.inc(g.dma_start(out=wfm[:], in_=convw[d]), 16)
                sLD.inc(g.dma_start(out=cbfm[:], in_=convb[d]), 16)
                sLD.inc(g.dma_start(out=nwsb[:], in_=nwrep[d]), 16)
                sLD.inc(g.dma_start(out=drsb[:], in_=drep[d]), 16)
                sLD.inc(g.dma_start(out=dtbsb[:], in_=dtb4[d]), 16)
                sLD.inc(g.dma_start(out=alogsb[:], in_=alog4[d]), 16)
                for kt in range(NKT):
                    sLD.inc(g.dma_start(out=windt[:, kt * DTC:(kt + 1) * DTC],
                                        in_=win[d][kt * 128:(kt + 1) * 128, DIN + CCH:DPROJ]), 16)
            on("gpsimd", f)

        def load_win_z(d):
            def f(g, d=d):
                for kt in range(NKT):
                    sLD.inc(g.dma_start(out=winsb[:, kt * CCH:kt * CCH + DIN],
                                        in_=win[d][kt * 128:(kt + 1) * 128, 0:DIN]), 16)
            on("gpsimd", f)

        def load_win_xbc(d):
            def f(g, d=d):
                for kt in range(NKT):
                    sLD.inc(g.dma_start(out=winsb[:, kt * CCH:kt * CCH + CCH],
                                        in_=win[d][kt * 128:(kt + 1) * 128, DIN:DIN + CCH]), 16)
            on("gpsimd", f)

        def load_wout(d):
            def f(g, d=d):
                for j in range(16):
                    sLD.inc(g.dma_start(out=woutsb[:, j * DM:(j + 1) * DM],
                                        in_=wout[d][j * 128:(j + 1) * 128, :]), 16)
            on("gpsimd", f)

        def emit_phA(d, di):
            ld_t = sLD.n
            tr_tt = sTR.n
            tr_t = sTR.n

            def f(t):
                t.wait_ge(sLD.h, ld_t)
                t.wait_ge(sTR.h, tr_t)
                if pha_parts < 1:
                    sDT.inc(t.drain())
                    return
                for q in range(4):
                    for kt in range(NKT):
                        li = t.matmul(pYA[q * 32:(q + 1) * 32, 0:512],
                                      windt[:, kt * DTC:(kt + 1) * DTC],
                                      xnfm[:, kt * L + q * 512:kt * L + (q + 1) * 512],
                                      start=(kt == 0), stop=(kt == NKT - 1),
                                      tile_position=(0, q * 32))
                sDT.inc(li)
            on("tensor", f)

            for k in range(NCH if pha_parts >= 2 else 0):
                def f(t, k=k):
                    for g in range(4):
                        gi = sWG.n
                        if gi >= 2 and pha_parts >= 3:
                            t.wait_ge(sWE.h, gi - 1)
                        for kt in range(NKT):
                            li = t.matmul(pwin[:, (g % 2) * 512:(g % 2 + 1) * 512],
                                          xnfm[:, kt * L + k * Q:kt * L + (k + 1) * Q],
                                          winsb[:, kt * CCH + g * 512:kt * CCH + (g + 1) * 512],
                                          start=(kt == 0), stop=(kt == NKT - 1))
                        sWG.inc(li)
                on("tensor", f)

                if pha_parts < 3:
                    continue

                def f(sc):
                    base = sWG.n - 4
                    wait(sc, sSZ)
                    if debug:
                        wait(sc, sOUT)
                    for g in range(4):
                        sc.wait_ge(sWG.h, base + g + 1)
                        sWE.inc(sc.activation(szb[0][:, g * 512:(g + 1) * 512],
                                              pwin[:, (g % 2) * 512:(g % 2 + 1) * 512],
                                              AF.Silu))
                on("scalar", f)

                if pha_parts < 4:
                    continue

                def f(sync, k=k):
                    wait(sync, sWE)
                    sSZ.inc(sync.dma_start(out=szh[k * Q:(k + 1) * Q, :], in_=szb[0][:]), 16)
                on("sync", f)

                if d == "f" and k == 0:
                    dbg_dump("t_sz", szb[0][:], sWE)

            if pha_parts < 5:
                return
            def f(v):
                wait(v, sDT)
                v.tensor_scalar_add(clt2[:], pYA[:, 0:512], dtbsb[:])
                sDT.inc(v.tensor_scalar(clt1[:], pYA[:, 0:512], dtbsb[:], 0.0,
                                        op0=ALU.add, op1=ALU.max))
            on("vector", f)

            def f(sc):
                wait(sc, sDT)
                sc.activation(s4[:], clt2[:], AF.Abs)
                afence(sc)
                sc.activation(s4[:], s4[:], AF.Exp, scale=-1.0)
                afence(sc)
                sc.activation(s4[:], s4[:], AF.Ln, bias=1.0)
                sDT.inc(sc.activation(acol[:], alogsb[:], AF.Exp))
            on("scalar", f)

            def f(v):
                wait(v, sDT)
                v.tensor_tensor(dtsp4[:], clt1[:], s4[:], op=ALU.add)
                vfence(v)
                sDT.inc(v.tensor_scalar_mul(clt2[:], dtsp4[:], acol[:]))
            on("vector", f)

            def f(sc):
                wait(sc, sDT)
                sDT.inc(sc.activation(a4[:], clt2[:], AF.Exp, scale=-1.0))
            on("scalar", f)

            if pha_parts < 6:
                return
            def f(v):
                wait(v, sDT)
                for k in range(NCH):
                    qq, cc = k // 4, (k % 4) * 128
                    v.tensor_tensor_scan(
                        s4[qq * 32:(qq + 1) * 32, cc:cc + 128],
                        clt2[qq * 32:(qq + 1) * 32, cc:cc + 128],
                        clt2[qq * 32:(qq + 1) * 32, cc:cc + 128],
                        0.0, op0=ALU.add, op1=ALU.bypass)
                sDT.inc(v.engine_nop())
            on("vector", f)

            def f(sc):
                wait(sc, sDT)
                sDT.inc(sc.activation(es4[:], s4[:], AF.Exp, scale=-1.0))
            on("scalar", f)

            if pha_parts < 7:
                return
            for k in range(NCH):
                qq, cc = k // 4, (k % 4) * 128

                def f(t, k=k, qq=qq, cc=cc):
                    wait(t, sDT, sBE)
                    for col, srcm in ((0, dtsp4), (32, es4), (64, a4)):
                        li = t.matmul(psmall[:, col:col + 32],
                                      srcm[qq * 32:(qq + 1) * 32, cc:cc + 128],
                                      if32[qq * 32:(qq + 1) * 32, qq * 32:(qq + 1) * 32],
                                      start=True, stop=True, tile_position=(qq * 32, 0))
                    sBC.inc(li)
                on("tensor", f)

                def f(t, k=k, qq=qq, cc=cc):
                    sBC.inc(t.matmul(psmall[0:1, 96:128],
                                     es4[qq * 32:(qq + 1) * 32, cc + 127:cc + 128],
                                     if32[qq * 32:(qq + 1) * 32, qq * 32:(qq + 1) * 32],
                                     start=True, stop=True, tile_position=(qq * 32, 0)))
                on("tensor", f)

                def f(v, k=k):
                    wait(v, sBC)
                    v.tensor_copy(dts[:, k * 96:k * 96 + 96], psmall[:, 0:96])
                    sBE.inc(v.tensor_copy(wcols[0:1, 0:32], psmall[0:1, 96:128]))
                on("vector", f)

                def f(t, k=k):
                    wait(t, sBE)
                    sBC.inc(t.matmul(psmall[:, 64:96], ones32[:], wcols[0:1, 0:32],
                                     start=True, stop=True))
                on("tensor", f)

                def f(v, k=k):
                    wait(v, sBC)
                    sBE.inc(v.tensor_copy(eslr[:, k * 32:(k + 1) * 32],
                                          psmall[0:64, 64:96]))
                on("vector", f)

            if d == "f":
                dbg_dump("t_dtsp", dtsp4[:], sDT, sBE)
                dbg_dump("t_a4", a4[:], sDT)
                dbg_dump("t_ets", dts[:, 0:64], sBE)

        def emit_phB(d, di):
            ld_t = sLD.n
            tgroups = [(0, 4), (4, 4), (8, 4), (12, 4), (16, 1)]
            gsz = [512, 512, 512, 512, 128]

            def f(g):
                wait(g, sHB, sY)
                g.memset(Hm[:], 0.0)
                g.memset(Hbf[:], 0.0)
                sHB.inc(g.memset(xbcw[:].rearrange("p (c w) -> p c w", c=17)[:, :, 0:3], 0.0))
            on("gpsimd", f)

            def front1(k):
                u = k % 2
                cs = k * Q

                def f(sync, k=k, u=u):
                    wait(sync, sEV)
                    sSZ.inc(sync.dma_start(out=szb[u][:], in_=szh[k * Q:(k + 1) * Q, :]), 16)
                on("sync", f)

                def f(t, cs=cs):
                    t.wait_ge(sLD.h, ld_t)
                    wait(t, sFE)
                    off = 0
                    for g, n in enumerate(gsz):
                        gi = sWG.n
                        if gi >= 2:
                            t.wait_ge(sWE.h, gi - 1)
                        for kt in range(NKT):
                            li = t.matmul(pwin[:, (g % 2) * 512:(g % 2) * 512 + n],
                                          xnfm[:, kt * L + cs:kt * L + cs + Q],
                                          winsb[:, kt * CCH + off:kt * CCH + off + n],
                                          start=(kt == 0), stop=(kt == NKT - 1))
                        sWG.inc(li)
                        off += n
                on("tensor", f)

                def f(v):
                    base = sWG.n - 5
                    wait(v, sFT)
                    off = 0
                    for g, n in enumerate(gsz):
                        v.wait_ge(sWG.h, base + g + 1)
                        sWE.inc(v.tensor_copy(xbc[:, off:off + n],
                                              pwin[:, (g % 2) * 512:(g % 2) * 512 + n]))
                        off += n
                on("vector", f)

                fe_b = sFE.n
                ft_b = sFT.n
                for g2, (r0, rn) in enumerate(tgroups):
                    def f(t, r0=r0, rn=rn, g2=g2, fe_b=fe_b):
                        if g2 == 0:
                            wait(t, sWE)
                        if g2 >= 2:
                            t.wait_ge(sFE.h, fe_b + g2 - 1)
                        for i in range(rn):
                            ct = r0 + i
                            b = (g2 % 2) * 512
                            li = t.matmul(pwin[:, b + i * 128:b + (i + 1) * 128],
                                          xbc[:, ct * 128:(ct + 1) * 128], ibf[:],
                                          start=True, stop=True)
                        sFT.inc(li)
                    on("tensor", f)

                    def f(v, r0=r0, rn=rn, g2=g2, ft_b=ft_b):
                        v.wait_ge(sFT.h, ft_b + g2 + 1)
                        o = xbcw[:].rearrange("p (c w) -> p c w", c=17)[:, r0:r0 + rn, 3:131]
                        b = (g2 % 2) * 512
                        i_ = pwin[:, b:b + rn * 128].rearrange("p (c w) -> p c w", c=rn)
                        sFE.inc(v.tensor_copy(o, i_))
                    on("vector", f)

                def f(v):
                    wait(v, sFT, sHB)
                    w3 = xbcw[:].rearrange("p (c w) -> p c w", c=17)
                    for ct in range(17):
                        v.tensor_scalar(cpc[:, ct * 128:(ct + 1) * 128],
                                        w3[:, ct, 0:128],
                                        wfm[:, ct:ct + 1], cbfm[:, ct:ct + 1],
                                        op0=ALU.mult, op1=ALU.add)
                    for j in (1, 2, 3):
                        vfence(v)
                        for ct in range(17):
                            acc = cpc[:, ct * 128:(ct + 1) * 128]
                            v.scalar_tensor_tensor(acc, w3[:, ct, j:j + 128],
                                                   wfm[:, j * 17 + ct:j * 17 + ct + 1],
                                                   acc, op0=ALU.mult, op1=ALU.add)
                    sCD.inc(v.engine_nop())
                on("vector", f)

                def f(v):
                    wait(v, sCD)
                    w3 = xbcw[:].rearrange("p (c w) -> p c w", c=17)
                    v.tensor_copy(w3[:, :, 0:3], w3[:, :, 128:131])
                on("vector", f)

                def f(sc):
                    wait(sc, sCD)
                    sCS.inc(sc.activation(cpc[:], cpc[:], AF.Silu))
                on("scalar", f)

            def front2(k):
                u = k % 2
                fe_b = sFE.n
                ft_b = sFT.n
                for g2, (r0, rn) in enumerate(tgroups):
                    def f(t, r0=r0, rn=rn, g2=g2, fe_b=fe_b):
                        if g2 == 0:
                            wait(t, sCS)
                            t.wait_ge(sFE.h, fe_b)
                        if g2 >= 2:
                            t.wait_ge(sFE.h, fe_b + g2 - 1)
                        for i in range(rn):
                            ct = r0 + i
                            b = (g2 % 2) * 512
                            li = t.matmul(pwin[:, b + i * 128:b + (i + 1) * 128],
                                          cpc[:, ct * 128:(ct + 1) * 128], ibf[:],
                                          start=True, stop=True)
                        sFT.inc(li)
                    on("tensor", f)

                    def f(v, r0=r0, rn=rn, g2=g2, ft_b=ft_b, u=u):
                        v.wait_ge(sFT.h, ft_b + g2 + 1)
                        if debug:
                            wait(v, sOUT)
                        b = (g2 % 2) * 512
                        sFE.inc(v.tensor_copy(xsbc2[u][:, r0 * 128:(r0 + rn) * 128],
                                              pwin[:, b:b + rn * 128]))
                    on("vector", f)

                if debug and d == "f" and k == 0:
                    dbg_dump("t_xsbc", xsbc2[u][:], sFE)

            def mid_half(k, half, u):
                h0 = half * 16

                if half == 0:
                    def f(t):
                        wait(t, sFE, sGE)
                        t.matmul(psmall[0:64, 0:128], xsbc2[u][:, DIN:DIN + 64], ibf[:],
                                 start=True, stop=True)
                        sBC.inc(t.matmul(psmall[0:64, 128:256],
                                         xsbc2[u][:, DIN + 64:DIN + 128],
                                         ibf[:], start=True, stop=True))
                    on("tensor", f)

                    def f(v):
                        wait(v, sBC, sY)
                        v.tensor_copy(sbf[:], psmall[0:64, 0:128])
                        sBE.inc(v.tensor_copy(scf[:], psmall[0:64, 128:256]))
                    on("vector", f)

                    def f(t):
                        wait(t, sBE)
                        sGW.inc(t.matmul(psmall[:, 256:384], sbf[:], scf[:],
                                         start=True, stop=True))
                    on("tensor", f)

                    def f(v):
                        wait(v, sGW)
                        sGE.inc(v.tensor_copy(sg[:], psmall[:, 256:384]))
                    on("vector", f)

                for piece in range(4):
                    hh0 = h0 + piece * 4

                    def f(t, k=k, hh0=hh0):
                        wait(t, sSC, sQE, sAE)
                        for i in range(4):
                            h = hh0 + i
                            acolb = dts[:, k * 96 + 64 + h:k * 96 + 64 + h + 1] \
                                .broadcast_to((128, 128))
                            li = t.matmul(pQT[:, i * 128:(i + 1) * 128], acolb, if32[:],
                                          start=True, stop=True)
                        sAR.inc(li)
                    on("tensor", f)

                    def f(v, piece=piece):
                        wait(v, sAR)
                        sAE.inc(v.tensor_copy(arep[:, (piece % 2) * 512:(piece % 2 + 1) * 512],
                                              pQT[:]))
                    on("vector", f)

                    def f(v, hh0=hh0, piece=piece):
                        wait(v, sAE)
                        if debug:
                            wait(v, sOUT)
                        if piece == 0:
                            wait(v, sY, sST)
                        for i in range(4):
                            h = hh0 + i
                            hh = h % 16
                            v.tensor_tensor_scan(
                                Ebuf[:, hh * 128:(hh + 1) * 128],
                                arep[:, (piece % 2) * 512 + i * 128:
                                     (piece % 2) * 512 + (i + 1) * 128],
                                if32[:], 0.0, op0=ALU.mult, op1=ALU.add)
                        sSC.inc(v.engine_nop())
                    on("vector", f)

                def f(v, k=k, h0=h0):
                    wait(v, sSC)
                    for i in range(16):
                        h = h0 + i
                        v.tensor_copy(wcols[:, h:h + 1],
                                      Ebuf[:, i * 128 + 127:i * 128 + 128])
                    sWC.inc(v.engine_nop())
                on("vector", f)

                def f(v):
                    wait(v, sGE, sWC)
                    e3 = Ebuf[:].rearrange("p (h t) -> p h t", h=16)
                    gb = sg[:].unsqueeze(1).broadcast_to((128, 16, 128))
                    sMT.inc(v.tensor_tensor(e3, e3, gb, op=ALU.mult))
                on("vector", f)

                if d == "f" and k == 0 and half == 0:
                    dbg_dump("t_mt", Ebuf[:], sMT)

                def f(v, k=k, h0=h0, u=u):
                    wait(v, sFE, sST)
                    if debug:
                        wait(v, sOUT)
                    for i in range(16):
                        h = h0 + i
                        li = v.tensor_scalar_mul(xhat[:, i * 64:(i + 1) * 64],
                                                 xsbc2[u][:, h * 64:(h + 1) * 64],
                                                 dts[:, k * 96 + h:k * 96 + h + 1])
                    sXH.inc(li)
                on("vector", f)

                def f(v, k=k, h0=h0, u=u):
                    wait(v, sWC, sST)
                    if debug:
                        wait(v, sOUT)
                    for i in range(16):
                        h = h0 + i
                        li = v.tensor_scalar_mul(bwb[:, i * 64:(i + 1) * 64],
                                                 xsbc2[u][:, DIN:DIN + 64],
                                                 wcols[:, h:h + 1])
                    sBW.inc(li)
                on("vector", f)

                if d == "f" and k == 0 and half == 1:
                    dbg_dump("t_wc", wcols[:], sWC)
                    dbg_dump("t_bw", bwb[:], sBW)
                if d == "f" and k == 0 and half == 0:
                    dbg_dump("t_xh", xhat[:], sXH)

            def back_half(k, half, u):
                h0 = half * 16

                def f(t, h0=h0):
                    wait(t, sMT, sXH, sHB, sEV, sHU, sOD)
                    for i in range(16):
                        t.matmul(pYA[:, i * 64:(i + 1) * 64],
                                 Ebuf[:, i * 128:(i + 1) * 128],
                                 xhat[:, i * 64:(i + 1) * 64], start=True, stop=True)
                    for i in range(16):
                        h = h0 + i
                        li = t.matmul(pYB[:, i * 64:(i + 1) * 64], scf[:],
                                      Hbf[:, h * 64:(h + 1) * 64],
                                      start=True, stop=True)
                    sY.inc(li)
                on("tensor", f)

                def f(v, h0=h0, u=u):
                    wait(v, sFE, sSQ)
                    sEV.inc(v.tensor_tensor(dxsb[:], xsbc2[u][:, h0 * 64:h0 * 64 + 1024],
                                            drsb[:, h0 * 64:h0 * 64 + 1024], op=ALU.mult))
                on("vector", f)

                def f(v, k=k, h0=h0, u=u):
                    wait(v, sY, sSZ, sSG, sEV)
                    if debug:
                        wait(v, sOUT)
                    v.tensor_tensor(tbuf[:], pYA[:], qb[:], op=ALU.add)
                    es = dts[:, k * 96 + 32 + h0:k * 96 + 32 + h0 + 16] \
                        .unsqueeze(-1).broadcast_to((128, 16, 64))
                    pyb3 = pYB[:].rearrange("p (h q) -> p h q", h=16)
                    vfence(v)
                    v.tensor_tensor(qb[:].rearrange("p (h q) -> p h q", h=16),
                                    pyb3, es, op=ALU.mult)
                    vfence(v)
                    v.tensor_tensor(qb[:], qb[:], tbuf[:], op=ALU.add)
                    vfence(v)
                    v.tensor_tensor(qb[:], qb[:], szb[u][:, h0 * 64:h0 * 64 + 1024],
                                    op=ALU.mult)
                    vfence(v)
                    sEV.inc(v.tensor_tensor(qhb[:, h0 * 64:h0 * 64 + 1024], qb[:],
                                            nwsb[:, h0 * 64:h0 * 64 + 1024], op=ALU.mult))
                on("vector", f)

                if d == "f" and k == 0 and half == 1:
                    dbg_dump("t_q", qb[:], sEV)

                def f(v, half=half):
                    wait(v, sEV)
                    if debug:
                        wait(v, sOUT)
                    sSQ.inc(v.scalar_tensor_tensor(qb[:], qb[:], 0.0, qb[:],
                                                   op0=ALU.add, op1=ALU.mult,
                                                   accum_out=ssqt[:, half:half + 1]))
                on("vector", f)

                def f(t, half=half):
                    wait(t, sBW, sEV)
                    tgt = pYA if half == 0 else pYB
                    for i in range(16):
                        li = t.matmul(tgt[0:64, i * 64:(i + 1) * 64],
                                      bwb[:, i * 64:(i + 1) * 64],
                                      xhat[:, i * 64:(i + 1) * 64],
                                      start=True, stop=True)
                    sST.inc(li)
                on("tensor", f)

                def f(v, k=k, h0=h0, half=half):
                    wait(v, sST)
                    if debug:
                        wait(v, sOUT)
                    tgt = pYA if half == 0 else pYB
                    el = eslr[:, k * 32 + h0:k * 32 + h0 + 16].unsqueeze(-1) \
                        .broadcast_to((64, 16, 64))
                    hview = Hm[0:64, h0 * 64:h0 * 64 + 1024] \
                        .rearrange("p (h q) -> p h q", h=16)
                    v.tensor_tensor(hview, hview, el, op=ALU.mult)
                    vfence(v)
                    sHU.inc(v.tensor_tensor(Hm[0:64, h0 * 64:h0 * 64 + 1024],
                                            Hm[0:64, h0 * 64:h0 * 64 + 1024],
                                            tgt[0:64, :], op=ALU.add))
                on("vector", f)

            def tail(k, u):
                def f(v, di=di, k=k):
                    wait(v, sSQ)
                    sEV.inc(v.tensor_tensor(ssqall[:, di * NCH + k:di * NCH + k + 1],
                                            ssqt[:, 0:1], ssqt[:, 1:2], op=ALU.add))
                on("vector", f)

                def f(v):
                    wait(v, sHU)
                    sHB.inc(v.tensor_copy(Hbf[:], Hm[:]))
                on("vector", f)

                if d == "f" and k == 0:
                    dbg_dump("t_H", Hm[:], sHU)
                    dbg_dump("t_qh", qhb[:], sEV)

                for r in range(4):
                    def f(t, r=r):
                        wait(t, sEV, sQE, sAE)
                        for i in range(4):
                            j = r * 4 + i
                            li = t.matmul(pQT[:, i * 128:(i + 1) * 128],
                                          qhb[:, j * 128:(j + 1) * 128], ibf[:],
                                          start=True, stop=True)
                        sQT.inc(li)
                    on("tensor", f)

                    def f(v):
                        wait(v, sQT)
                        sQE.inc(v.tensor_copy(qfmb[:], pQT[:]))
                    on("vector", f)

                    def f(t, r=r):
                        wait(t, sQE)
                        if r == 0:
                            wait(t, sOD, sHU)
                        for i in range(4):
                            j = r * 4 + i
                            for nn in range(2):
                                li = t.matmul(pYB[:, nn * 512:(nn + 1) * 512],
                                              qfmb[:, i * 128:(i + 1) * 128],
                                              woutsb[:, j * DM + nn * 512:j * DM + (nn + 1) * 512],
                                              start=(j == 0), stop=(j == 15))
                        sWO.inc(li)
                    on("tensor", f)

                def f(v):
                    wait(v, sWO)
                    if debug:
                        wait(v, sOUT)
                    sOD.inc(v.tensor_copy(odb[:], pYB[:]))
                on("vector", f)

                def f(sync, k=k):
                    wait(sync, sOD)
                    sSG.inc(sync.dma_start(out=odh[d][k * Q:(k + 1) * Q, :], in_=odb[:]), 16)
                on("sync", f)

                if d == "f" and k == 0:
                    dbg_dump("t_od", odb[:], sOD)

            front1(0)
            front2(0)
            for k in range(NCH):
                u = k % 2
                mid_half(k, 0, u)
                if k + 1 < NCH:
                    front1(k + 1)
                back_half(k, 0, u)
                mid_half(k, 1, u)
                back_half(k, 1, u)
                if k + 1 < NCH:
                    front2(k + 1)
                tail(k, u)

        dirs = []
        if stages >= 2:
            dirs.append(("f", 0))
        if stages >= 4:
            dirs.append(("b", 1))
        for d, di in dirs:
            if d == "b":
                load_dir_small(d)
                load_win_z(d)
            load_xnfm(d)
            barrier()
            emit_phA(d, di)
            barrier()
            if stages >= 3 or d == "b":
                load_win_xbc(d)
                load_wout(d)
                barrier()
                emit_phB(d, di)
            barrier()

        if stages < 5:
            def f(sync):
                wait(sync, sOUT, sOU[0], sOU[1], sSG, sXNH, sSZ, sLD, sTR, sX)
            on("sync", f)
            return nc
        outfm = winsb
        wosb = woutsb

        def f(g):
            wait(g, sSG)
            for j in range(16):
                sLD.inc(g.dma_start(out=wosb[:, j * DM:(j + 1) * DM],
                                    in_=wo_in[j * 128:(j + 1) * 128, :]), 16)
            for j in range(NKT):
                sLD.inc(g.dma_start(out=wtsb[:, j * DM:(j + 1) * DM],
                                    in_=wt_in[j * 128:(j + 1) * 128, :]), 16)
            for kt in range(NKT):
                sLD.inc(g.dma_start(out=nwsb[:, kt * 32:(kt + 1) * 32],
                                    in_=vr_in[kt * 128:(kt + 1) * 128, :]), 16)
            sLD.inc(g.dma_start(out=drsb[0:32, 0:1024], in_=ct_in), 16)
            sLD.inc(g.dma_start(out=szb[0][:, 0:DM], in_=bo_in), 16)
            sLD.inc(g.dma_start(out=bwb[0:1, 0:1024], in_=ffb_in), 16)
        on("gpsimd", f)

        def f(sync):
            wait(sync, sSG)
            for ft in range(NKT):
                sTR.inc(sync.dma_start_transpose(
                    outfm[:, ft * 2048:(ft + 1) * 2048],
                    odh["f"][:, ft * 128:(ft + 1) * 128]), 16)
        on("sync", f)

        def f(sc):
            wait(sc, sEV)
            sc.activation(rall[:], ssqall[:], AF.Ln, bias=EPS_RMS, scale=1.0 / DIN)
            afence(sc)
            sO3.inc(sc.activation(rall[:], rall[:], AF.Exp, scale=-0.5))
        on("scalar", f)

        barrier()

        ph3_ld = sLD.n
        ph3_tr = sTR.n

        for k in range(NCH):
            u = k % 2
            kb = NCH - 1 - k

            def f(sync, k=k, u=u, kb=kb):
                wait(sync, sOU[u])
                if sQT.n >= 2:
                    sync.wait_ge(sQT.h, sQT.n - 2)
                sX.inc(sync.dma_start(out=sx[u], in_=x_in[k * Q:(k + 1) * Q, :]), 16)
                sX.inc(sync.dma_start(out=odb[:], in_=odh["b"][kb * Q:(kb + 1) * Q, :]), 16)
            on("sync", f)

            def f(v, kb=kb):
                wait(v, sX, sO3)
                sO3.inc(v.tensor_scalar_mul(odb[:], odb[:],
                                            rall[:, NCH + kb:NCH + kb + 1]))
            on("vector", f)

            for r in range(2):
                def f(t, r=r):
                    wait(t, sO3, sQE)
                    t.wait_ge(sLD.h, ph3_ld)
                    t.wait_ge(sTR.h, ph3_tr)
                    for i in range(4):
                        j = r * 4 + i
                        li = t.matmul(pQT[:, i * 128:(i + 1) * 128],
                                      odb[:, j * 128:(j + 1) * 128], ibf[:, 127::-1],
                                      start=True, stop=True)
                    sQT.inc(li)
                on("tensor", f)

                def f(v, r=r):
                    wait(v, sQT)
                    sQE.inc(v.tensor_copy(qhb[:, r * 512:(r + 1) * 512], pQT[:]))
                on("vector", f)

            def f(t, k=k):
                wait(t, sQE, sO3)
                for mt in range(8):
                    for nn in range(2):
                        t.matmul(pYA[:, nn * 512:(nn + 1) * 512],
                                 outfm[:, mt * 2048 + k * Q:mt * 2048 + (k + 1) * Q],
                                 wosb[:, mt * DM + nn * 512:mt * DM + (nn + 1) * 512],
                                 start=(mt == 0), stop=(mt == 7))
                for mt in range(8):
                    for nn in range(2):
                        li = t.matmul(pYB[:, nn * 512:(nn + 1) * 512],
                                      qhb[:, mt * 128:(mt + 1) * 128],
                                      wosb[:, (8 + mt) * DM + nn * 512:(8 + mt) * DM + (nn + 1) * 512],
                                      start=(mt == 0), stop=(mt == 7))
                sWO.inc(li)
            on("tensor", f)

            def f(v, k=k, u=u, kb=kb):
                wait(v, sWO)
                if debug:
                    wait(v, sOUT)
                v.scalar_tensor_tensor(sx[u], pYA[:], rall[:, k:k + 1], sx[u],
                                       op0=ALU.mult, op1=ALU.add)
                vfence(v)
                v.tensor_tensor(sx[u], pYB[:], sx[u], op=ALU.add)
                vfence(v)
                sO3.inc(v.tensor_tensor(sx[u], szb[0][:, 0:DM], sx[u], op=ALU.add))
            on("vector", f)

            if k == 0:
                dbg_dump("t_x2", sx[u], sO3)

            def f(v, u=u):
                wait(v, sO3)
                if debug:
                    wait(v, sOUT)
                v.scalar_tensor_tensor(qb[:], sx[u], 0.0, sx[u],
                                       op0=ALU.add, op1=ALU.bypass,
                                       accum_out=stat[:, 0:1])
                sSQ.inc(v.scalar_tensor_tensor(xn0, sx[u], 0.0, sx[u],
                                               op0=ALU.add, op1=ALU.mult,
                                               accum_out=stat[:, 1:2]))
            on("vector", f)

            def f(v):
                wait(v, sSQ)
                v.tensor_scalar_mul(stat[:, 2:3], stat[:, 0:1], 1.0 / DM)
                v.tensor_scalar_mul(stat[:, 3:4], stat[:, 1:2], 1.0 / DM)
                v.tensor_scalar(stat[:, 4:5], stat[:, 0:1], stat[:, 0:1],
                                1.0 / (DM * DM), op0=ALU.mult, op1=ALU.mult)
                vfence(v)
                sO3.inc(v.tensor_tensor(stat[:, 5:6], stat[:, 3:4], stat[:, 4:5],
                                        op=ALU.subtract))
            on("vector", f)

            def f(sc):
                wait(sc, sO3)
                sc.activation(stat[:, 6:7], stat[:, 5:6], AF.Ln, bias=EPS_LN)
                afence(sc)
                sSQ.inc(sc.activation(stat[:, 7:8], stat[:, 6:7], AF.Exp, scale=-0.5))
            on("scalar", f)

            def f(v, u=u):
                wait(v, sSQ, sQT)
                sO3.inc(v.tensor_scalar(xn0, sx[u], stat[:, 2:3], stat[:, 7:8],
                                        op0=ALU.subtract, op1=ALU.mult))
            on("vector", f)

            for r in range(2):
                def f(t, r=r):
                    wait(t, sO3, sQE)
                    for i in range(4):
                        j = r * 4 + i
                        li = t.matmul(pQT[:, i * 128:(i + 1) * 128],
                                      qhb[:, DM + j * 128:DM + (j + 1) * 128], ibf[:],
                                      start=True, stop=True)
                    sQT.inc(li)
                on("tensor", f)

                def f(v, r=r):
                    wait(v, sQT)
                    sQE.inc(v.tensor_copy(qb[:, r * 512:(r + 1) * 512], pQT[:]))
                on("vector", f)

            def f(t):
                wait(t, sQE, sBE)
                for kt in range(NKT):
                    li = t.matmul(psmall[0:32, 0:128],
                                  nwsb[:, kt * 32:(kt + 1) * 32],
                                  qb[:, kt * 128:(kt + 1) * 128],
                                  start=(kt == 0), stop=(kt == NKT - 1))
                sBC.inc(li)
            on("tensor", f)

            def f(v):
                wait(v, sBC)
                sBE.inc(v.tensor_copy(scf[0:32, 0:128], psmall[0:32, 0:128]))
            on("vector", f)

            def f(t):
                wait(t, sBE, sO3)
                for kt in range(NKT):
                    for nn in range(2):
                        t.matmul(pYA[:, nn * 512:(nn + 1) * 512],
                                 qb[:, kt * 128:(kt + 1) * 128],
                                 wtsb[:, kt * DM + nn * 512:kt * DM + (nn + 1) * 512],
                                 start=(kt == 0), stop=False)
                for nn in range(2):
                    t.matmul(pYA[:, nn * 512:(nn + 1) * 512], scf[0:32, 0:128],
                             drsb[0:32, nn * 512:(nn + 1) * 512],
                             start=False, stop=False)
                for nn in range(2):
                    li = t.matmul(pYA[:, nn * 512:(nn + 1) * 512], onesbf[:],
                                  bwb[0:1, nn * 512:(nn + 1) * 512],
                                  start=False, stop=True)
                sWO.inc(li)
            on("tensor", f)

            def f(v, u=u):
                wait(v, sWO)
                sO3.inc(v.tensor_tensor(sx[u], pYA[:], sx[u], op=ALU.add))
            on("vector", f)

            def f(sync, k=k, u=u):
                wait(sync, sO3)
                sOU[u].inc(sync.dma_start(out=y_out[k * Q:(k + 1) * Q, :], in_=sx[u]), 16)
            on("sync", f)

        def f(sync):
            wait(sync, sOUT, sOU[0], sOU[1])
        on("sync", f)

    return nc


def prep_host(inputs):
    g = lambda k: np.ascontiguousarray(np.asarray(inputs[k], dtype=np.float32))
    m = {}
    for d in "fb":
        m[f"win_{d}"] = g(f"L0_attn_{d}Win")
        m[f"wout_{d}"] = g(f"L0_attn_{d}Wout")
        cw = g(f"L0_attn_{d}convw")[:, 0, :]
        m[f"convw_{d}"] = np.ascontiguousarray(
            cw.reshape(4, 17, 128).transpose(2, 0, 1).reshape(128, 68))
        m[f"convb_{d}"] = np.ascontiguousarray(
            g(f"L0_attn_{d}convb").reshape(17, 128).T)
        m[f"dtb4_{d}"] = np.ascontiguousarray(np.tile(g(f"L0_attn_{d}dtb"), 4)[:, None])
        m[f"alog4_{d}"] = np.ascontiguousarray(np.tile(g(f"L0_attn_{d}Alog"), 4)[:, None])
        m[f"drep_{d}"] = np.ascontiguousarray(np.broadcast_to(
            np.repeat(g(f"L0_attn_{d}D"), P)[None, :], (128, DIN)))
        m[f"nwrep_{d}"] = np.ascontiguousarray(np.broadcast_to(
            g(f"L0_attn_{d}normw")[None, :], (128, DIN)))
    m["wo"] = g("L0_attn_Wo")
    m["bo"] = np.ascontiguousarray(np.broadcast_to(g("L0_attn_bo")[None, :], (128, DM)))
    m["wt"] = np.ascontiguousarray(g("L0_ffn_W").T)
    m["ffb"] = g("L0_ffn_b")[None, :]
    m["vr"] = g("L0_ffn_V_r")
    m["ct"] = np.ascontiguousarray(g("L0_ffn_C").T)
    m["if32"] = np.eye(128, dtype=np.float32)
    m["ibf"] = np.eye(128, dtype=ml_dtypes.bfloat16)
    return m


def kernel(**inputs):
    nc = build(debug=False)
    shared = prep_host(inputs)
    x = np.ascontiguousarray(np.asarray(inputs["L0_attn_x"], dtype=np.float32))
    in_maps = []
    for b in range(8):
        im = dict(shared)
        im["x"] = np.ascontiguousarray(x[b])
        im["xrev"] = np.ascontiguousarray(x[b][::-1])
        in_maps.append(im)
    res = run_bass_kernel_spmd(nc, in_maps, core_ids=list(range(8)))
    out = np.stack([r["y"] for r in res.results], axis=0)
    return out.astype(np.float32)


# version 11
# speedup vs baseline: 1.2167x; 1.0069x over previous
import contextlib
import numpy as np
import ml_dtypes
import concourse.bass as bass
import concourse.mybir as mybir
from concourse.bass_utils import run_bass_kernel_spmd

F32 = mybir.dt.float32
BF16 = mybir.dt.bfloat16
AF = mybir.ActivationFunctionType
ALU = mybir.AluOpType

L = 2048
DM = 1024
Q = 128
NCH = L // Q
NH = 32
P = 64
NST = 64
DIN = 2048
CCH = DIN + 2 * NST
DTC = NH
DPROJ = 2 * DIN + 2 * NST + NH
NKT = DM // 128
EPS_LN = 1e-5
EPS_RMS = 1e-5


class Sem:
    def __init__(self, stack, nc, name):
        self.h = stack.enter_context(nc.semaphore(name))
        self.n = 0

    def inc(self, instr, k=1):
        instr.then_inc(self.h, k)
        self.n += k
        return instr


def wait(eng, *sems):
    for s in sems:
        if s.n:
            eng.wait_ge(s.h, s.n)


def build(debug=False, stages=5, pha_parts=9):
    nc = bass.Bass("TRN2", target_bir_lowering=False)
    dram = {}

    def din(name, shape, dt=F32):
        dram[name] = nc.dram_tensor(name, shape, dt, kind="ExternalInput").ap()
        return dram[name]

    def dout(name, shape, dt=F32):
        dram[name] = nc.dram_tensor(name, shape, dt, kind="ExternalOutput").ap()
        return dram[name]

    def dint(name, shape, dt=BF16):
        dram[name] = nc.dram_tensor(name, shape, dt, kind="Internal").ap()
        return dram[name]

    x_in = din("x", [L, DM])
    xrev_in = din("xrev", [L, DM])
    win = {d: din(f"win_{d}", [DM, DPROJ]) for d in "fb"}
    wout = {d: din(f"wout_{d}", [DIN, DM]) for d in "fb"}
    convw = {d: din(f"convw_{d}", [128, 4 * 17]) for d in "fb"}
    convb = {d: din(f"convb_{d}", [128, 17]) for d in "fb"}
    dtb4 = {d: din(f"dtb4_{d}", [128, 1]) for d in "fb"}
    alog4 = {d: din(f"alog4_{d}", [128, 1]) for d in "fb"}
    drep = {d: din(f"drep_{d}", [128, DIN]) for d in "fb"}
    nwrep = {d: din(f"nwrep_{d}", [128, DIN]) for d in "fb"}
    wo_in = din("wo", [DIN, DM])
    bo_in = din("bo", [128, DM])
    wt_in = din("wt", [DM, DM])
    ffb_in = din("ffb", [1, DM])
    vr_in = din("vr", [DM, 32])
    ct_in = din("ct", [32, DM])
    if32_in = din("if32", [128, 128])
    ibf_in = din("ibf", [128, 128], BF16)

    y_out = dout("y", [L, DM])

    xnh = {"f": dint("xnh", [L, DM]), "b": dint("xnrevh", [L, DM])}
    szh = dint("szh", [L, DIN])
    odh = {"f": dint("odhf", [L, DM]), "b": dint("odhb", [L, DM])}

    dbg = {}
    if debug:
        for nm, shp, dt in [
            ("t_xn", [128, DM], BF16), ("t_sz", [128, DIN], BF16),
            ("t_xsbc", [128, CCH], BF16), ("t_dtsp", [128, 512], F32),
            ("t_a4", [128, 512], F32), ("t_ets", [128, 64], F32),
            ("t_mt", [128, 2048], BF16), ("t_xh", [128, 1024], BF16),
            ("t_q", [128, 1024], BF16), ("t_qh", [128, DIN], BF16),
            ("t_wc", [128, 32], F32), ("t_bw", [128, 1024], BF16),
            ("t_od", [128, DM], BF16), ("t_H", [64, 2048], F32),
            ("t_x2", [128, DM], F32),
        ]:
            dbg[nm] = dout(nm, shp, dt)

    st = contextlib.ExitStack()
    with st:
        sb = lambda name, shape, dt: st.enter_context(nc.sbuf_tensor(name, shape, dt))
        xnfm = sb("xnfm", [128, NKT * L], BF16)
        winsb = sb("winsb", [128, NKT * CCH], BF16)
        windt = sb("windt", [128, NKT * DTC], BF16)
        woutsb = sb("woutsb", [128, 16 * DM], BF16)
        wtsb = sb("wtsb", [128, NKT * DM], BF16)
        wfm = sb("wfm", [128, 4 * 17], F32)
        cbfm = sb("cbfm", [128, 17], F32)
        nwsb = sb("nwsb", [128, DIN], BF16)
        drsb = sb("drsb", [128, DIN], BF16)
        dtbsb = sb("dtbsb", [128, 1], F32)
        alogsb = sb("alogsb", [128, 1], F32)
        acol = sb("acol", [128, 1], F32)
        if32 = sb("if32_s", [128, 128], F32)
        ibf = sb("ibf_s", [128, 128], BF16)
        ones32 = sb("ones32", [1, 128], F32)
        onesbf = sb("onesbf", [1, 128], BF16)
        ssqall = sb("ssqall", [128, 2 * NCH], F32)
        rall = sb("rall", [128, 2 * NCH], F32)
        xsbc1 = sb("xsbc1", [128, CCH], BF16)
        dtsp4 = xsbc1[:, 0:1024].bitcast(F32)
        s4 = xsbc1[:, 1024:2048].bitcast(F32)
        dts = sb("dts", [128, NCH * 96], F32)
        eslr = sb("eslr", [64, NCH * 32], F32)
        clt1 = sb("clt1", [128, 512], F32)
        clt2 = sb("clt2", [128, 512], F32)
        a4 = clt1
        es4 = s4
        xbc = sb("xbc", [128, CCH], BF16)
        xbcw = sb("xbcw", [128, 17 * 131], BF16)
        cpa = sb("cpa", [128, CCH], BF16)
        cpc = sb("cpc", [128, CCH], BF16)
        szb = [sb(f"szb{i}", [128, DIN], BF16) for i in range(2)]
        arep = sb("arep", [128, 1024], F32)
        Ebuf = sb("Ebuf", [128, 2048], BF16)
        xhat = sb("xhat", [128, 1024], BF16)
        bwb = sb("bwb", [128, 1024], BF16)
        wcols = sb("wcols", [128, 32], F32)
        qb = sb("qb", [128, 1024], BF16)
        qhb = sb("qhb", [128, 2048], BF16)
        qfmb = sb("qfmb", [128, 1024], BF16)
        Hm = sb("Hm", [64, 2048], F32)
        Hbf = sb("Hbf", [64, 2048], BF16)
        odb = sb("odb", [128, DM], BF16)
        sbf = sb("sbf", [64, 128], BF16)
        scf = sb("scf", [64, 128], BF16)
        sg = sb("sg", [128, 128], BF16)
        tbuf = sb("tbuf", [128, 1024], BF16)
        ssqt = sb("ssqt", [128, 2], F32)
        stat = sb("stat", [128, 8], F32)
        ceps = sb("ceps", [128, 2], F32)
        nc.const_aps.register(F32, EPS_LN, ceps[:, 0:1])
        nc.const_aps.register(F32, 1.0, ceps[:, 1:2])

        sx = [arep[:], Ebuf[:].bitcast(F32)]
        xn0 = qhb[:, DM:2 * DM]
        odb = tbuf
        xsbc2 = [cpa, xsbc1]
        dxsb = qb

        ps = lambda name, shape: st.enter_context(nc.psum_tensor(name, shape, F32))
        pwin = ps("pwin", [128, 1024])
        psmall = ps("psmall", [128, 512])
        pYA = ps("pYA", [128, 1024])
        pYB = ps("pYB", [128, 1024])
        pQT = ps("pQT", [128, 512])

        SS = lambda name: Sem(st, nc, name)
        sLD = SS("sLD"); sX = SS("sX"); sXN = SS("sXN"); sXNH = SS("sXNH")
        sTR = SS("sTR"); sWG = SS("sWG"); sWE = SS("sWE"); sSZ = SS("sSZ")
        sDT = SS("sDT"); sCV = SS("sCV"); sCD = SS("sCD"); sCS = SS("sCS")
        sBC = SS("sBC"); sBE = SS("sBE"); sGW = SS("sGW"); sGE = SS("sGE")
        sAR = SS("sAR"); sAE = SS("sAE"); sSC = SS("sSC"); sWC = SS("sWC")
        sMT = SS("sMT"); sXH = SS("sXH"); sBW = SS("sBW"); sY = SS("sY")
        sEV = SS("sEV"); sSQ = SS("sSQ"); sST = SS("sST"); sHU = SS("sHU")
        sHB = SS("sHB"); sQT = SS("sQT"); sQE = SS("sQE"); sWO = SS("sWO")
        sOD = SS("sOD"); sSG = SS("sSG"); sBAR = SS("sBAR"); sO3 = SS("sO3")
        sOUT = SS("sOUT"); sIN = SS("sIN"); sFV = SS("sFV"); sFA = SS("sFA")
        sOU = [SS("sOU0"), SS("sOU1")]
        sFT = SS("sFT"); sFE = SS("sFE")

        block = st.enter_context(nc.Block())
        ENGS = ("tensor", "vector", "scalar", "gpsimd", "sync")

        def on(eng_name, fn):
            getattr(block, eng_name)(fn)

        def vfence(v):
            sFV.inc(v.engine_nop())
            v.wait_ge(sFV.h, sFV.n)

        def afence(sc):
            sFA.inc(sc.copy(ceps[0:1, 1:2], ceps[0:1, 1:2]))
            sc.wait_ge(sFA.h, sFA.n)

        def barrier():
            def f(sync):
                wait(sync, sOUT, sOU[0], sOU[1], sSG, sXNH, sSZ, sX, sTR, sLD)
            on("sync", f)
            target = sBAR.n + 5
            for e in ENGS:
                def bf(eng, e=e):
                    sBAR.inc(eng.nop() if e == "sync" else eng.drain())
                on(e, bf)
            for e in ENGS:
                on(e, lambda eng: eng.wait_ge(sBAR.h, target))
            assert sBAR.n == target

        def dbg_dump(name, src_ap, *wait_sems):
            if not debug:
                return
            def f(sync):
                wait(sync, *wait_sems)
                sOUT.inc(sync.dma_start(out=dbg[name], in_=src_ap), 16)
            on("sync", f)

        def f(g):
            g.memset(ceps[:, 0:1], EPS_LN)
            g.memset(ceps[:, 1:2], 1.0)
            g.memset(ones32[:], 1.0)
            sIN.inc(g.memset(onesbf[:], 1.0))
            for kt in range(NKT):
                sLD.inc(g.dma_start(out=winsb[:, kt * CCH:kt * CCH + DIN],
                                    in_=win["f"][kt * 128:(kt + 1) * 128, 0:DIN]), 16)
            sLD.inc(g.dma_start(out=wfm[:], in_=convw["f"]), 16)
            sLD.inc(g.dma_start(out=cbfm[:], in_=convb["f"]), 16)
            sLD.inc(g.dma_start(out=nwsb[:], in_=nwrep["f"]), 16)
            sLD.inc(g.dma_start(out=drsb[:], in_=drep["f"]), 16)
            sLD.inc(g.dma_start(out=dtbsb[:], in_=dtb4["f"]), 16)
            sLD.inc(g.dma_start(out=alogsb[:], in_=alog4["f"]), 16)
            for kt in range(NKT):
                sLD.inc(g.dma_start(out=windt[:, kt * DTC:(kt + 1) * DTC],
                                    in_=win["f"][kt * 128:(kt + 1) * 128, DIN + CCH:DPROJ]), 16)
        on("gpsimd", f)

        def f(sync):
            sTR.inc(sync.dma_start(out=if32[:], in_=if32_in), 16)
            sTR.inc(sync.dma_start(out=ibf[:], in_=ibf_in), 16)
        on("sync", f)

        sumall = dts[:, 0:32]
        sqall = dts[:, 32:64]
        meanall = dts[:, 64:96]
        rstdall = dts[:, 96:128]
        varall = dts[:, 128:160]

        for di, (d, src) in enumerate([("f", x_in), ("b", xrev_in)]):
            for k in range(NCH):
                c = di * NCH + k
                u = c % 2

                def f(sync, k=k, u=u, src=src):
                    if sXN.n >= 2:
                        sync.wait_ge(sXN.h, sXN.n - 1)
                    sX.inc(sync.dma_start(out=sx[u], in_=src[k * Q:(k + 1) * Q, :]), 16)
                on("sync", f)

                def f(v, c=c, u=u):
                    v.wait_ge(sX.h, sX.n)
                    if c == 0:
                        wait(v, sIN)
                    v.scalar_tensor_tensor(tbuf[:], sx[u], 0.0, sx[u],
                                           op0=ALU.add, op1=ALU.bypass,
                                           accum_out=sumall[:, c:c + 1])
                    sXN.inc(v.scalar_tensor_tensor(xn0, sx[u], 0.0, sx[u],
                                                   op0=ALU.add, op1=ALU.mult,
                                                   accum_out=sqall[:, c:c + 1]))
                on("vector", f)

        def f(v):
            wait(v, sXN)
            v.tensor_scalar_mul(meanall[:], sumall[:], 1.0 / DM)
            v.tensor_scalar_mul(varall[:], sqall[:], 1.0 / DM)
            vfence(v)
            v.tensor_tensor(sqall[:], meanall[:], meanall[:], op=ALU.mult)
            vfence(v)
            sXN.inc(v.tensor_tensor(varall[:], varall[:], sqall[:], op=ALU.subtract))
        on("vector", f)

        def f(sc):
            wait(sc, sXN)
            sc.activation(varall[:], varall[:], AF.Ln, bias=EPS_LN)
            afence(sc)
            sSQ.inc(sc.activation(rstdall[:], varall[:], AF.Exp, scale=-0.5))
        on("scalar", f)

        for di, (d, src) in enumerate([("f", x_in), ("b", xrev_in)]):
            for k in range(NCH):
                c = di * NCH + k
                u = c % 2
                xnb = qhb[:, u * DM:(u + 1) * DM]

                def f(sync, k=k, u=u, src=src):
                    if sXN.n >= 2:
                        sync.wait_ge(sXN.h, sXN.n - 1)
                    sX.inc(sync.dma_start(out=sx[u], in_=src[k * Q:(k + 1) * Q, :]), 16)
                on("sync", f)

                def f(v, c=c, u=u, xnb=xnb):
                    v.wait_ge(sX.h, sX.n)
                    if c < 2:
                        wait(v, sSQ)
                    if sXNH.n >= 32:
                        v.wait_ge(sXNH.h, sXNH.n - 16)
                    if debug:
                        wait(v, sOUT)
                    sXN.inc(v.tensor_scalar(xnb, sx[u], meanall[:, c:c + 1],
                                            rstdall[:, c:c + 1],
                                            op0=ALU.subtract, op1=ALU.mult))
                on("vector", f)

                def f(sync, d=d, k=k, xnb=xnb):
                    sync.wait_ge(sXN.h, sXN.n)
                    sXNH.inc(sync.dma_start(out=xnh[d][k * Q:(k + 1) * Q, :], in_=xnb), 16)
                on("sync", f)

                if d == "f" and k == 0:
                    dbg_dump("t_xn", xnb, sXN)

        barrier()

        def load_xnfm(d):
            def f(sync, d=d):
                wait(sync, sXNH)
                for ft in range(NKT):
                    sTR.inc(sync.dma_start_transpose(
                        xnfm[:, ft * L:(ft + 1) * L],
                        xnh[d][:, ft * 128:(ft + 1) * 128]), 16)
            on("sync", f)

        def load_dir_small(d):
            def f(g, d=d):
                sLD.inc(g.dma_start(out=wfm[:], in_=convw[d]), 16)
                sLD.inc(g.dma_start(out=cbfm[:], in_=convb[d]), 16)
                sLD.inc(g.dma_start(out=nwsb[:], in_=nwrep[d]), 16)
                sLD.inc(g.dma_start(out=drsb[:], in_=drep[d]), 16)
                sLD.inc(g.dma_start(out=dtbsb[:], in_=dtb4[d]), 16)
                sLD.inc(g.dma_start(out=alogsb[:], in_=alog4[d]), 16)
                for kt in range(NKT):
                    sLD.inc(g.dma_start(out=windt[:, kt * DTC:(kt + 1) * DTC],
                                        in_=win[d][kt * 128:(kt + 1) * 128, DIN + CCH:DPROJ]), 16)
            on("gpsimd", f)

        def load_win_z(d):
            def f(g, d=d):
                for kt in range(NKT):
                    sLD.inc(g.dma_start(out=winsb[:, kt * CCH:kt * CCH + DIN],
                                        in_=win[d][kt * 128:(kt + 1) * 128, 0:DIN]), 16)
            on("gpsimd", f)

        def load_win_xbc(d):
            def f(g, d=d):
                for kt in range(NKT):
                    sLD.inc(g.dma_start(out=winsb[:, kt * CCH:kt * CCH + CCH],
                                        in_=win[d][kt * 128:(kt + 1) * 128, DIN:DIN + CCH]), 16)
            on("gpsimd", f)

        def load_wout(d):
            def f(g, d=d):
                for j in range(16):
                    sLD.inc(g.dma_start(out=woutsb[:, j * DM:(j + 1) * DM],
                                        in_=wout[d][j * 128:(j + 1) * 128, :]), 16)
            on("gpsimd", f)

        def emit_phA(d, di):
            ld_t = sLD.n
            tr_tt = sTR.n
            tr_t = sTR.n

            def f(t):
                t.wait_ge(sLD.h, ld_t)
                t.wait_ge(sTR.h, tr_t)
                if pha_parts < 1:
                    sDT.inc(t.drain())
                    return
                for q in range(4):
                    for kt in range(NKT):
                        li = t.matmul(pYA[q * 32:(q + 1) * 32, 0:512],
                                      windt[:, kt * DTC:(kt + 1) * DTC],
                                      xnfm[:, kt * L + q * 512:kt * L + (q + 1) * 512],
                                      start=(kt == 0), stop=(kt == NKT - 1),
                                      tile_position=(0, q * 32))
                sDT.inc(li)
            on("tensor", f)

            for k in range(NCH if pha_parts >= 2 else 0):
                def f(t, k=k):
                    for g in range(4):
                        gi = sWG.n
                        if gi >= 2 and pha_parts >= 3:
                            t.wait_ge(sWE.h, gi - 1)
                        for kt in range(NKT):
                            li = t.matmul(pwin[:, (g % 2) * 512:(g % 2 + 1) * 512],
                                          xnfm[:, kt * L + k * Q:kt * L + (k + 1) * Q],
                                          winsb[:, kt * CCH + g * 512:kt * CCH + (g + 1) * 512],
                                          start=(kt == 0), stop=(kt == NKT - 1))
                        sWG.inc(li)
                on("tensor", f)

                if pha_parts < 3:
                    continue

                def f(sc):
                    base = sWG.n - 4
                    wait(sc, sSZ)
                    if debug:
                        wait(sc, sOUT)
                    for g in range(4):
                        sc.wait_ge(sWG.h, base + g + 1)
                        sWE.inc(sc.activation(szb[0][:, g * 512:(g + 1) * 512],
                                              pwin[:, (g % 2) * 512:(g % 2 + 1) * 512],
                                              AF.Silu))
                on("scalar", f)

                if pha_parts < 4:
                    continue

                def f(sync, k=k):
                    wait(sync, sWE)
                    sSZ.inc(sync.dma_start(out=szh[k * Q:(k + 1) * Q, :], in_=szb[0][:]), 16)
                on("sync", f)

                if d == "f" and k == 0:
                    dbg_dump("t_sz", szb[0][:], sWE)

            if pha_parts < 5:
                return
            def f(v):
                wait(v, sDT)
                v.tensor_scalar_add(clt2[:], pYA[:, 0:512], dtbsb[:])
                sDT.inc(v.tensor_scalar(clt1[:], pYA[:, 0:512], dtbsb[:], 0.0,
                                        op0=ALU.add, op1=ALU.max))
            on("vector", f)

            def f(sc):
                wait(sc, sDT)
                sc.activation(s4[:], clt2[:], AF.Abs)
                afence(sc)
                sc.activation(s4[:], s4[:], AF.Exp, scale=-1.0)
                afence(sc)
                sc.activation(s4[:], s4[:], AF.Ln, bias=1.0)
                sDT.inc(sc.activation(acol[:], alogsb[:], AF.Exp))
            on("scalar", f)

            def f(v):
                wait(v, sDT)
                v.tensor_tensor(dtsp4[:], clt1[:], s4[:], op=ALU.add)
                vfence(v)
                sDT.inc(v.tensor_scalar_mul(clt2[:], dtsp4[:], acol[:]))
            on("vector", f)

            def f(sc):
                wait(sc, sDT)
                sDT.inc(sc.activation(a4[:], clt2[:], AF.Exp, scale=-1.0))
            on("scalar", f)

            if pha_parts < 6:
                return
            def f(v):
                wait(v, sDT)
                for k in range(NCH):
                    qq, cc = k // 4, (k % 4) * 128
                    v.tensor_tensor_scan(
                        s4[qq * 32:(qq + 1) * 32, cc:cc + 128],
                        clt2[qq * 32:(qq + 1) * 32, cc:cc + 128],
                        clt2[qq * 32:(qq + 1) * 32, cc:cc + 128],
                        0.0, op0=ALU.add, op1=ALU.bypass)
                sDT.inc(v.engine_nop())
            on("vector", f)

            def f(sc):
                wait(sc, sDT)
                sDT.inc(sc.activation(es4[:], s4[:], AF.Exp, scale=-1.0))
            on("scalar", f)

            if pha_parts < 7:
                return
            for k in range(NCH):
                qq, cc = k // 4, (k % 4) * 128

                def f(t, k=k, qq=qq, cc=cc):
                    wait(t, sDT, sBE)
                    for col, srcm in ((0, dtsp4), (32, es4), (64, a4)):
                        li = t.matmul(psmall[:, col:col + 32],
                                      srcm[qq * 32:(qq + 1) * 32, cc:cc + 128],
                                      if32[qq * 32:(qq + 1) * 32, qq * 32:(qq + 1) * 32],
                                      start=True, stop=True, tile_position=(qq * 32, 0))
                    sBC.inc(li)
                on("tensor", f)

                def f(t, k=k, qq=qq, cc=cc):
                    sBC.inc(t.matmul(psmall[0:1, 96:128],
                                     es4[qq * 32:(qq + 1) * 32, cc + 127:cc + 128],
                                     if32[qq * 32:(qq + 1) * 32, qq * 32:(qq + 1) * 32],
                                     start=True, stop=True, tile_position=(qq * 32, 0)))
                on("tensor", f)

                def f(v, k=k):
                    wait(v, sBC)
                    v.tensor_copy(dts[:, k * 96:k * 96 + 96], psmall[:, 0:96])
                    sBE.inc(v.tensor_copy(wcols[0:1, 0:32], psmall[0:1, 96:128]))
                on("vector", f)

                def f(t, k=k):
                    wait(t, sBE)
                    sBC.inc(t.matmul(psmall[:, 64:96], ones32[:], wcols[0:1, 0:32],
                                     start=True, stop=True))
                on("tensor", f)

                def f(v, k=k):
                    wait(v, sBC)
                    sBE.inc(v.tensor_copy(eslr[:, k * 32:(k + 1) * 32],
                                          psmall[0:64, 64:96]))
                on("vector", f)

            if d == "f":
                dbg_dump("t_dtsp", dtsp4[:], sDT, sBE)
                dbg_dump("t_a4", a4[:], sDT)
                dbg_dump("t_ets", dts[:, 0:64], sBE)

        def emit_phB(d, di):
            ld_t = sLD.n
            tgroups = [(0, 4), (4, 4), (8, 4), (12, 4), (16, 1)]
            gsz = [512, 512, 512, 512, 128]

            def f(g):
                wait(g, sHB, sY)
                g.memset(Hm[:], 0.0)
                g.memset(Hbf[:], 0.0)
                sHB.inc(g.memset(xbcw[:].rearrange("p (c w) -> p c w", c=17)[:, :, 0:3], 0.0))
            on("gpsimd", f)

            def front1(k):
                u = k % 2
                cs = k * Q

                def f(sync, k=k, u=u):
                    wait(sync, sEV)
                    sSZ.inc(sync.dma_start(out=szb[u][:], in_=szh[k * Q:(k + 1) * Q, :]), 16)
                on("sync", f)

                def f(t, cs=cs):
                    t.wait_ge(sLD.h, ld_t)
                    wait(t, sFE)
                    off = 0
                    for g, n in enumerate(gsz):
                        gi = sWG.n
                        if gi >= 2:
                            t.wait_ge(sWE.h, gi - 1)
                        for kt in range(NKT):
                            li = t.matmul(pwin[:, (g % 2) * 512:(g % 2) * 512 + n],
                                          xnfm[:, kt * L + cs:kt * L + cs + Q],
                                          winsb[:, kt * CCH + off:kt * CCH + off + n],
                                          start=(kt == 0), stop=(kt == NKT - 1))
                        sWG.inc(li)
                        off += n
                on("tensor", f)

                def f(v):
                    base = sWG.n - 5
                    wait(v, sFT)
                    off = 0
                    for g, n in enumerate(gsz):
                        v.wait_ge(sWG.h, base + g + 1)
                        sWE.inc(v.tensor_copy(xbc[:, off:off + n],
                                              pwin[:, (g % 2) * 512:(g % 2) * 512 + n]))
                        off += n
                on("vector", f)

                fe_b = sFE.n
                ft_b = sFT.n
                for g2, (r0, rn) in enumerate(tgroups):
                    def f(t, r0=r0, rn=rn, g2=g2, fe_b=fe_b):
                        if g2 == 0:
                            wait(t, sWE)
                        if g2 >= 2:
                            t.wait_ge(sFE.h, fe_b + g2 - 1)
                        for i in range(rn):
                            ct = r0 + i
                            b = (g2 % 2) * 512
                            li = t.matmul(pwin[:, b + i * 128:b + (i + 1) * 128],
                                          xbc[:, ct * 128:(ct + 1) * 128], ibf[:],
                                          start=True, stop=True)
                        sFT.inc(li)
                    on("tensor", f)

                    def f(v, r0=r0, rn=rn, g2=g2, ft_b=ft_b):
                        v.wait_ge(sFT.h, ft_b + g2 + 1)
                        o = xbcw[:].rearrange("p (c w) -> p c w", c=17)[:, r0:r0 + rn, 3:131]
                        b = (g2 % 2) * 512
                        i_ = pwin[:, b:b + rn * 128].rearrange("p (c w) -> p c w", c=rn)
                        sFE.inc(v.tensor_copy(o, i_))
                    on("vector", f)

                def f(v):
                    wait(v, sFT, sHB)
                    w3 = xbcw[:].rearrange("p (c w) -> p c w", c=17)
                    for ct in range(17):
                        v.tensor_scalar(cpc[:, ct * 128:(ct + 1) * 128],
                                        w3[:, ct, 0:128],
                                        wfm[:, ct:ct + 1], cbfm[:, ct:ct + 1],
                                        op0=ALU.mult, op1=ALU.add)
                    for j in (1, 2, 3):
                        vfence(v)
                        for ct in range(17):
                            acc = cpc[:, ct * 128:(ct + 1) * 128]
                            v.scalar_tensor_tensor(acc, w3[:, ct, j:j + 128],
                                                   wfm[:, j * 17 + ct:j * 17 + ct + 1],
                                                   acc, op0=ALU.mult, op1=ALU.add)
                    sCD.inc(v.engine_nop())
                on("vector", f)

                def f(v):
                    wait(v, sCD)
                    w3 = xbcw[:].rearrange("p (c w) -> p c w", c=17)
                    v.tensor_copy(w3[:, :, 0:3], w3[:, :, 128:131])
                on("vector", f)

                def f(sc):
                    wait(sc, sCD)
                    sCS.inc(sc.activation(cpc[:], cpc[:], AF.Silu))
                on("scalar", f)

            def front2(k):
                u = k % 2
                fe_b = sFE.n
                ft_b = sFT.n
                for g2, (r0, rn) in enumerate(tgroups):
                    def f(t, r0=r0, rn=rn, g2=g2, fe_b=fe_b):
                        if g2 == 0:
                            wait(t, sCS)
                            t.wait_ge(sFE.h, fe_b)
                        if g2 >= 2:
                            t.wait_ge(sFE.h, fe_b + g2 - 1)
                        for i in range(rn):
                            ct = r0 + i
                            b = (g2 % 2) * 512
                            li = t.matmul(pwin[:, b + i * 128:b + (i + 1) * 128],
                                          cpc[:, ct * 128:(ct + 1) * 128], ibf[:],
                                          start=True, stop=True)
                        sFT.inc(li)
                    on("tensor", f)

                    def f(v, r0=r0, rn=rn, g2=g2, ft_b=ft_b, u=u):
                        v.wait_ge(sFT.h, ft_b + g2 + 1)
                        if debug:
                            wait(v, sOUT)
                        b = (g2 % 2) * 512
                        sFE.inc(v.tensor_copy(xsbc2[u][:, r0 * 128:(r0 + rn) * 128],
                                              pwin[:, b:b + rn * 128]))
                    on("vector", f)

                if debug and d == "f" and k == 0:
                    dbg_dump("t_xsbc", xsbc2[u][:], sFE)

            def mid_half(k, half, u):
                h0 = half * 16

                if half == 0:
                    def f(t):
                        wait(t, sFE, sGE, sQE)
                        t.matmul(psmall[0:64, 0:128], xsbc2[u][:, DIN:DIN + 64], ibf[:],
                                 start=True, stop=True)
                        sBC.inc(t.matmul(psmall[0:64, 128:256],
                                         xsbc2[u][:, DIN + 64:DIN + 128],
                                         ibf[:], start=True, stop=True))
                    on("tensor", f)

                    def f(v):
                        wait(v, sBC, sY)
                        v.tensor_copy(sbf[:], psmall[0:64, 0:128])
                        sBE.inc(v.tensor_copy(scf[:], psmall[0:64, 128:256]))
                    on("vector", f)

                    def f(t):
                        wait(t, sBE)
                        sGW.inc(t.matmul(psmall[:, 256:384], sbf[:], scf[:],
                                         start=True, stop=True))
                    on("tensor", f)

                    def f(v):
                        wait(v, sGW)
                        sGE.inc(v.tensor_copy(sg[:], psmall[:, 256:384]))
                    on("vector", f)

                for piece in range(4):
                    hh0 = h0 + piece * 4

                    def f(t, k=k, hh0=hh0):
                        wait(t, sSC, sQE, sAE)
                        for i in range(4):
                            h = hh0 + i
                            acolb = dts[:, k * 96 + 64 + h:k * 96 + 64 + h + 1] \
                                .broadcast_to((128, 128))
                            li = t.matmul(pQT[:, i * 128:(i + 1) * 128], acolb, if32[:],
                                          start=True, stop=True)
                        sAR.inc(li)
                    on("tensor", f)

                    def f(v, piece=piece):
                        wait(v, sAR)
                        sAE.inc(v.tensor_copy(arep[:, (piece % 2) * 512:(piece % 2 + 1) * 512],
                                              pQT[:]))
                    on("vector", f)

                    def f(v, hh0=hh0, piece=piece):
                        wait(v, sAE)
                        if debug:
                            wait(v, sOUT)
                        if piece == 0:
                            wait(v, sY, sST)
                        for i in range(4):
                            h = hh0 + i
                            hh = h % 16
                            v.tensor_tensor_scan(
                                Ebuf[:, hh * 128:(hh + 1) * 128],
                                arep[:, (piece % 2) * 512 + i * 128:
                                     (piece % 2) * 512 + (i + 1) * 128],
                                if32[:], 0.0, op0=ALU.mult, op1=ALU.add)
                        sSC.inc(v.engine_nop())
                    on("vector", f)

                def f(v, h0=h0, u=u):
                    wait(v, sSC, sST)
                    if debug:
                        wait(v, sOUT)
                    e127 = Ebuf[:].rearrange("p (h t) -> p h t", h=16)[:, :, 127]
                    v.tensor_copy(wcols[:, h0:h0 + 16], e127)
                    vfence(v)
                    for i in range(16):
                        h = h0 + i
                        li = v.tensor_scalar_mul(bwb[:, i * 64:(i + 1) * 64],
                                                 xsbc2[u][:, DIN:DIN + 64],
                                                 wcols[:, h:h + 1])
                    sBW.inc(li)
                on("vector", f)

                def f(v):
                    wait(v, sGE)
                    e3 = Ebuf[:].rearrange("p (h t) -> p h t", h=16)
                    gb = sg[:].unsqueeze(1).broadcast_to((128, 16, 128))
                    sMT.inc(v.tensor_tensor(e3, e3, gb, op=ALU.mult))
                on("vector", f)

                if d == "f" and k == 0 and half == 0:
                    dbg_dump("t_mt", Ebuf[:], sMT)

                def f(v, k=k, h0=h0, u=u):
                    wait(v, sFE, sST)
                    if debug:
                        wait(v, sOUT)
                    for i in range(16):
                        h = h0 + i
                        li = v.tensor_scalar_mul(xhat[:, i * 64:(i + 1) * 64],
                                                 xsbc2[u][:, h * 64:(h + 1) * 64],
                                                 dts[:, k * 96 + h:k * 96 + h + 1])
                    sXH.inc(li)
                on("vector", f)

                if d == "f" and k == 0 and half == 1:
                    dbg_dump("t_bw", bwb[:], sBW)
                if d == "f" and k == 0 and half == 0:
                    dbg_dump("t_xh", xhat[:], sXH)

            def back_half(k, half, u):
                h0 = half * 16

                def f(t, h0=h0):
                    wait(t, sMT, sXH, sHB, sEV, sHU, sOD)
                    for i in range(16):
                        t.matmul(pYA[:, i * 64:(i + 1) * 64],
                                 Ebuf[:, i * 128:(i + 1) * 128],
                                 xhat[:, i * 64:(i + 1) * 64], start=True, stop=True)
                    for i in range(16):
                        h = h0 + i
                        li = t.matmul(pYB[:, i * 64:(i + 1) * 64], scf[:],
                                      Hbf[:, h * 64:(h + 1) * 64],
                                      start=True, stop=True)
                    sY.inc(li)
                on("tensor", f)

                def f(v, h0=h0, u=u):
                    wait(v, sFE, sSQ)
                    sEV.inc(v.tensor_tensor(dxsb[:], xsbc2[u][:, h0 * 64:h0 * 64 + 1024],
                                            drsb[:, h0 * 64:h0 * 64 + 1024], op=ALU.mult))
                on("vector", f)

                def f(v, k=k, h0=h0, u=u):
                    wait(v, sY, sSZ, sSG, sEV)
                    if debug:
                        wait(v, sOUT)
                    v.tensor_tensor(tbuf[:], pYA[:], qb[:], op=ALU.add)
                    es = dts[:, k * 96 + 32 + h0:k * 96 + 32 + h0 + 16] \
                        .unsqueeze(-1).broadcast_to((128, 16, 64))
                    pyb3 = pYB[:].rearrange("p (h q) -> p h q", h=16)
                    vfence(v)
                    v.tensor_tensor(qb[:].rearrange("p (h q) -> p h q", h=16),
                                    pyb3, es, op=ALU.mult)
                    vfence(v)
                    v.tensor_tensor(qb[:], qb[:], tbuf[:], op=ALU.add)
                    vfence(v)
                    v.tensor_tensor(qb[:], qb[:], szb[u][:, h0 * 64:h0 * 64 + 1024],
                                    op=ALU.mult)
                    vfence(v)
                    sEV.inc(v.tensor_tensor(qhb[:, h0 * 64:h0 * 64 + 1024], qb[:],
                                            nwsb[:, h0 * 64:h0 * 64 + 1024], op=ALU.mult))
                on("vector", f)

                if d == "f" and k == 0 and half == 1:
                    dbg_dump("t_q", qb[:], sEV)

                def f(v, half=half):
                    wait(v, sEV)
                    if debug:
                        wait(v, sOUT)
                    sSQ.inc(v.scalar_tensor_tensor(qb[:], qb[:], 0.0, qb[:],
                                                   op0=ALU.add, op1=ALU.mult,
                                                   accum_out=ssqt[:, half:half + 1]))
                on("vector", f)

                def f(t, half=half):
                    wait(t, sBW, sEV)
                    tgt = pYA if half == 0 else pYB
                    for i in range(16):
                        li = t.matmul(tgt[0:64, i * 64:(i + 1) * 64],
                                      bwb[:, i * 64:(i + 1) * 64],
                                      xhat[:, i * 64:(i + 1) * 64],
                                      start=True, stop=True)
                    sST.inc(li)
                on("tensor", f)

                def f(v, k=k, h0=h0, half=half):
                    wait(v, sST)
                    if debug:
                        wait(v, sOUT)
                    tgt = pYA if half == 0 else pYB
                    el = eslr[:, k * 32 + h0:k * 32 + h0 + 16].unsqueeze(-1) \
                        .broadcast_to((64, 16, 64))
                    hview = Hm[0:64, h0 * 64:h0 * 64 + 1024] \
                        .rearrange("p (h q) -> p h q", h=16)
                    v.tensor_tensor(hview, hview, el, op=ALU.mult)
                    vfence(v)
                    sHU.inc(v.tensor_tensor(Hm[0:64, h0 * 64:h0 * 64 + 1024],
                                            Hm[0:64, h0 * 64:h0 * 64 + 1024],
                                            tgt[0:64, :], op=ALU.add))
                on("vector", f)

            def tail(k, u):
                def f(v, di=di, k=k):
                    wait(v, sSQ)
                    sEV.inc(v.tensor_tensor(ssqall[:, di * NCH + k:di * NCH + k + 1],
                                            ssqt[:, 0:1], ssqt[:, 1:2], op=ALU.add))
                on("vector", f)

                def f(v):
                    wait(v, sHU)
                    sHB.inc(v.tensor_copy(Hbf[:], Hm[:]))
                on("vector", f)

                if d == "f" and k == 0:
                    dbg_dump("t_H", Hm[:], sHU)
                    dbg_dump("t_qh", qhb[:], sEV)

                qt_b = sQT.n
                qe_b = sQE.n
                for r in range(4):
                    pb = pQT if r % 2 == 0 else psmall
                    def f(t, r=r, pb=pb, qe_b=qe_b):
                        if r == 0:
                            wait(t, sEV, sQE, sAE, sGE)
                        if r >= 2:
                            t.wait_ge(sQE.h, qe_b + r - 1)
                        for i in range(4):
                            j = r * 4 + i
                            li = t.matmul(pb[:, i * 128:(i + 1) * 128],
                                          qhb[:, j * 128:(j + 1) * 128], ibf[:],
                                          start=True, stop=True)
                        sQT.inc(li)
                    on("tensor", f)

                    def f(v, r=r, pb=pb, qt_b=qt_b):
                        v.wait_ge(sQT.h, qt_b + r + 1)
                        sQE.inc(v.tensor_copy(qfmb[:, (r % 2) * 512:(r % 2 + 1) * 512],
                                              pb[:]))
                    on("vector", f)

                    def f(t, r=r, qe_b=qe_b):
                        t.wait_ge(sQE.h, qe_b + r + 1)
                        if r == 0:
                            wait(t, sOD, sHU)
                        for i in range(4):
                            j = r * 4 + i
                            for nn in range(2):
                                li = t.matmul(pYB[:, nn * 512:(nn + 1) * 512],
                                              qfmb[:, (r % 2) * 512 + i * 128:(r % 2) * 512 + (i + 1) * 128],
                                              woutsb[:, j * DM + nn * 512:j * DM + (nn + 1) * 512],
                                              start=(j == 0), stop=(j == 15))
                        sWO.inc(li)
                    on("tensor", f)

                def f(v):
                    wait(v, sWO)
                    if debug:
                        wait(v, sOUT)
                    sOD.inc(v.tensor_copy(odb[:], pYB[:]))
                on("vector", f)

                def f(sync, k=k):
                    wait(sync, sOD)
                    sSG.inc(sync.dma_start(out=odh[d][k * Q:(k + 1) * Q, :], in_=odb[:]), 16)
                on("sync", f)

                if d == "f" and k == 0:
                    dbg_dump("t_od", odb[:], sOD)

            front1(0)
            front2(0)
            for k in range(NCH):
                u = k % 2
                mid_half(k, 0, u)
                if k + 1 < NCH:
                    front1(k + 1)
                back_half(k, 0, u)
                mid_half(k, 1, u)
                back_half(k, 1, u)
                if k + 1 < NCH:
                    front2(k + 1)
                tail(k, u)

        dirs = []
        if stages >= 2:
            dirs.append(("f", 0))
        if stages >= 4:
            dirs.append(("b", 1))
        for d, di in dirs:
            if d == "b":
                load_dir_small(d)
                load_win_z(d)
            load_xnfm(d)
            barrier()
            emit_phA(d, di)
            barrier()
            if stages >= 3 or d == "b":
                load_win_xbc(d)
                load_wout(d)
                barrier()
                emit_phB(d, di)
            barrier()

        if stages < 5:
            def f(sync):
                wait(sync, sOUT, sOU[0], sOU[1], sSG, sXNH, sSZ, sLD, sTR, sX)
            on("sync", f)
            return nc
        outfm = winsb
        wosb = woutsb

        def f(g):
            wait(g, sSG)
            for j in range(16):
                sLD.inc(g.dma_start(out=wosb[:, j * DM:(j + 1) * DM],
                                    in_=wo_in[j * 128:(j + 1) * 128, :]), 16)
            for j in range(NKT):
                sLD.inc(g.dma_start(out=wtsb[:, j * DM:(j + 1) * DM],
                                    in_=wt_in[j * 128:(j + 1) * 128, :]), 16)
            for kt in range(NKT):
                sLD.inc(g.dma_start(out=nwsb[:, kt * 32:(kt + 1) * 32],
                                    in_=vr_in[kt * 128:(kt + 1) * 128, :]), 16)
            sLD.inc(g.dma_start(out=drsb[0:32, 0:1024], in_=ct_in), 16)
            sLD.inc(g.dma_start(out=szb[0][:, 0:DM], in_=bo_in), 16)
            sLD.inc(g.dma_start(out=bwb[0:1, 0:1024], in_=ffb_in), 16)
        on("gpsimd", f)

        def f(sync):
            wait(sync, sSG)
            for ft in range(NKT):
                sTR.inc(sync.dma_start_transpose(
                    outfm[:, ft * 2048:(ft + 1) * 2048],
                    odh["f"][:, ft * 128:(ft + 1) * 128]), 16)
        on("sync", f)

        def f(sc):
            wait(sc, sEV)
            sc.activation(rall[:], ssqall[:], AF.Ln, bias=EPS_RMS, scale=1.0 / DIN)
            afence(sc)
            sO3.inc(sc.activation(rall[:], rall[:], AF.Exp, scale=-0.5))
        on("scalar", f)

        barrier()

        ph3_ld = sLD.n
        ph3_tr = sTR.n

        for k in range(NCH):
            u = k % 2
            kb = NCH - 1 - k

            def f(sync, k=k, u=u, kb=kb):
                wait(sync, sOU[u])
                if sQT.n >= 2:
                    sync.wait_ge(sQT.h, sQT.n - 2)
                sX.inc(sync.dma_start(out=sx[u], in_=x_in[k * Q:(k + 1) * Q, :]), 16)
                sX.inc(sync.dma_start(out=odb[:], in_=odh["b"][kb * Q:(kb + 1) * Q, :]), 16)
            on("sync", f)

            def f(v, kb=kb):
                wait(v, sX, sO3)
                sO3.inc(v.tensor_scalar_mul(odb[:], odb[:],
                                            rall[:, NCH + kb:NCH + kb + 1]))
            on("vector", f)

            for r in range(2):
                def f(t, r=r):
                    wait(t, sO3, sQE)
                    t.wait_ge(sLD.h, ph3_ld)
                    t.wait_ge(sTR.h, ph3_tr)
                    for i in range(4):
                        j = r * 4 + i
                        li = t.matmul(pQT[:, i * 128:(i + 1) * 128],
                                      odb[:, j * 128:(j + 1) * 128], ibf[:, 127::-1],
                                      start=True, stop=True)
                    sQT.inc(li)
                on("tensor", f)

                def f(v, r=r):
                    wait(v, sQT)
                    sQE.inc(v.tensor_copy(qhb[:, r * 512:(r + 1) * 512], pQT[:]))
                on("vector", f)

            def f(t, k=k):
                wait(t, sQE, sO3)
                for mt in range(8):
                    for nn in range(2):
                        t.matmul(pYA[:, nn * 512:(nn + 1) * 512],
                                 outfm[:, mt * 2048 + k * Q:mt * 2048 + (k + 1) * Q],
                                 wosb[:, mt * DM + nn * 512:mt * DM + (nn + 1) * 512],
                                 start=(mt == 0), stop=(mt == 7))
                for mt in range(8):
                    for nn in range(2):
                        li = t.matmul(pYB[:, nn * 512:(nn + 1) * 512],
                                      qhb[:, mt * 128:(mt + 1) * 128],
                                      wosb[:, (8 + mt) * DM + nn * 512:(8 + mt) * DM + (nn + 1) * 512],
                                      start=(mt == 0), stop=(mt == 7))
                sWO.inc(li)
            on("tensor", f)

            def f(v, k=k, u=u, kb=kb):
                wait(v, sWO)
                if debug:
                    wait(v, sOUT)
                v.scalar_tensor_tensor(sx[u], pYA[:], rall[:, k:k + 1], sx[u],
                                       op0=ALU.mult, op1=ALU.add)
                vfence(v)
                v.tensor_tensor(sx[u], pYB[:], sx[u], op=ALU.add)
                vfence(v)
                sO3.inc(v.tensor_tensor(sx[u], szb[0][:, 0:DM], sx[u], op=ALU.add))
            on("vector", f)

            if k == 0:
                dbg_dump("t_x2", sx[u], sO3)

            def f(v, u=u):
                wait(v, sO3)
                if debug:
                    wait(v, sOUT)
                v.scalar_tensor_tensor(qb[:], sx[u], 0.0, sx[u],
                                       op0=ALU.add, op1=ALU.bypass,
                                       accum_out=stat[:, 0:1])
                sSQ.inc(v.scalar_tensor_tensor(xn0, sx[u], 0.0, sx[u],
                                               op0=ALU.add, op1=ALU.mult,
                                               accum_out=stat[:, 1:2]))
            on("vector", f)

            def f(v):
                wait(v, sSQ)
                v.tensor_scalar_mul(stat[:, 2:3], stat[:, 0:1], 1.0 / DM)
                v.tensor_scalar_mul(stat[:, 3:4], stat[:, 1:2], 1.0 / DM)
                v.tensor_scalar(stat[:, 4:5], stat[:, 0:1], stat[:, 0:1],
                                1.0 / (DM * DM), op0=ALU.mult, op1=ALU.mult)
                vfence(v)
                sO3.inc(v.tensor_tensor(stat[:, 5:6], stat[:, 3:4], stat[:, 4:5],
                                        op=ALU.subtract))
            on("vector", f)

            def f(sc):
                wait(sc, sO3)
                sc.activation(stat[:, 6:7], stat[:, 5:6], AF.Ln, bias=EPS_LN)
                afence(sc)
                sSQ.inc(sc.activation(stat[:, 7:8], stat[:, 6:7], AF.Exp, scale=-0.5))
            on("scalar", f)

            def f(v, u=u):
                wait(v, sSQ, sQT)
                sO3.inc(v.tensor_scalar(xn0, sx[u], stat[:, 2:3], stat[:, 7:8],
                                        op0=ALU.subtract, op1=ALU.mult))
            on("vector", f)

            for r in range(2):
                def f(t, r=r):
                    wait(t, sO3, sQE)
                    for i in range(4):
                        j = r * 4 + i
                        li = t.matmul(pQT[:, i * 128:(i + 1) * 128],
                                      qhb[:, DM + j * 128:DM + (j + 1) * 128], ibf[:],
                                      start=True, stop=True)
                    sQT.inc(li)
                on("tensor", f)

                def f(v, r=r):
                    wait(v, sQT)
                    sQE.inc(v.tensor_copy(qb[:, r * 512:(r + 1) * 512], pQT[:]))
                on("vector", f)

            def f(t):
                wait(t, sQE, sBE)
                for kt in range(NKT):
                    li = t.matmul(psmall[0:32, 0:128],
                                  nwsb[:, kt * 32:(kt + 1) * 32],
                                  qb[:, kt * 128:(kt + 1) * 128],
                                  start=(kt == 0), stop=(kt == NKT - 1))
                sBC.inc(li)
            on("tensor", f)

            def f(v):
                wait(v, sBC)
                sBE.inc(v.tensor_copy(scf[0:32, 0:128], psmall[0:32, 0:128]))
            on("vector", f)

            def f(t):
                wait(t, sBE, sO3)
                for kt in range(NKT):
                    for nn in range(2):
                        t.matmul(pYA[:, nn * 512:(nn + 1) * 512],
                                 qb[:, kt * 128:(kt + 1) * 128],
                                 wtsb[:, kt * DM + nn * 512:kt * DM + (nn + 1) * 512],
                                 start=(kt == 0), stop=False)
                for nn in range(2):
                    t.matmul(pYA[:, nn * 512:(nn + 1) * 512], scf[0:32, 0:128],
                             drsb[0:32, nn * 512:(nn + 1) * 512],
                             start=False, stop=False)
                for nn in range(2):
                    li = t.matmul(pYA[:, nn * 512:(nn + 1) * 512], onesbf[:],
                                  bwb[0:1, nn * 512:(nn + 1) * 512],
                                  start=False, stop=True)
                sWO.inc(li)
            on("tensor", f)

            def f(v, u=u):
                wait(v, sWO)
                sO3.inc(v.tensor_tensor(sx[u], pYA[:], sx[u], op=ALU.add))
            on("vector", f)

            def f(sync, k=k, u=u):
                wait(sync, sO3)
                sOU[u].inc(sync.dma_start(out=y_out[k * Q:(k + 1) * Q, :], in_=sx[u]), 16)
            on("sync", f)

        def f(sync):
            wait(sync, sOUT, sOU[0], sOU[1])
        on("sync", f)

    return nc


def prep_host(inputs):
    g = lambda k: np.ascontiguousarray(np.asarray(inputs[k], dtype=np.float32))
    m = {}
    for d in "fb":
        m[f"win_{d}"] = g(f"L0_attn_{d}Win")
        m[f"wout_{d}"] = g(f"L0_attn_{d}Wout")
        cw = g(f"L0_attn_{d}convw")[:, 0, :]
        m[f"convw_{d}"] = np.ascontiguousarray(
            cw.reshape(4, 17, 128).transpose(2, 0, 1).reshape(128, 68))
        m[f"convb_{d}"] = np.ascontiguousarray(
            g(f"L0_attn_{d}convb").reshape(17, 128).T)
        m[f"dtb4_{d}"] = np.ascontiguousarray(np.tile(g(f"L0_attn_{d}dtb"), 4)[:, None])
        m[f"alog4_{d}"] = np.ascontiguousarray(np.tile(g(f"L0_attn_{d}Alog"), 4)[:, None])
        m[f"drep_{d}"] = np.ascontiguousarray(np.broadcast_to(
            np.repeat(g(f"L0_attn_{d}D"), P)[None, :], (128, DIN)))
        m[f"nwrep_{d}"] = np.ascontiguousarray(np.broadcast_to(
            g(f"L0_attn_{d}normw")[None, :], (128, DIN)))
    m["wo"] = g("L0_attn_Wo")
    m["bo"] = np.ascontiguousarray(np.broadcast_to(g("L0_attn_bo")[None, :], (128, DM)))
    m["wt"] = np.ascontiguousarray(g("L0_ffn_W").T)
    m["ffb"] = g("L0_ffn_b")[None, :]
    m["vr"] = g("L0_ffn_V_r")
    m["ct"] = np.ascontiguousarray(g("L0_ffn_C").T)
    m["if32"] = np.eye(128, dtype=np.float32)
    m["ibf"] = np.eye(128, dtype=ml_dtypes.bfloat16)
    return m


def kernel(**inputs):
    nc = build(debug=False)
    shared = prep_host(inputs)
    x = np.ascontiguousarray(np.asarray(inputs["L0_attn_x"], dtype=np.float32))
    in_maps = []
    for b in range(8):
        im = dict(shared)
        im["x"] = np.ascontiguousarray(x[b])
        im["xrev"] = np.ascontiguousarray(x[b][::-1])
        in_maps.append(im)
    res = run_bass_kernel_spmd(nc, in_maps, core_ids=list(range(8)))
    out = np.stack([r["y"] for r in res.results], axis=0)
    return out.astype(np.float32)


# version 12
# speedup vs baseline: 1.2991x; 1.0677x over previous
import contextlib
import numpy as np
import ml_dtypes
import concourse.bass as bass
import concourse.mybir as mybir
from concourse.bass_utils import run_bass_kernel_spmd

F32 = mybir.dt.float32
BF16 = mybir.dt.bfloat16
AF = mybir.ActivationFunctionType
ALU = mybir.AluOpType

L = 2048
DM = 1024
Q = 128
NCH = L // Q
NH = 32
P = 64
NST = 64
DIN = 2048
CCH = DIN + 2 * NST
DTC = NH
DPROJ = 2 * DIN + 2 * NST + NH
NKT = DM // 128
EPS_LN = 1e-5
EPS_RMS = 1e-5


class Sem:
    def __init__(self, stack, nc, name):
        self.h = stack.enter_context(nc.semaphore(name))
        self.n = 0

    def inc(self, instr, k=1):
        instr.then_inc(self.h, k)
        self.n += k
        return instr


def wait(eng, *sems):
    for s in sems:
        if s.n:
            eng.wait_ge(s.h, s.n)


def build(debug=False, stages=5, pha_parts=9):
    nc = bass.Bass("TRN2", target_bir_lowering=False)
    dram = {}

    def din(name, shape, dt=F32):
        dram[name] = nc.dram_tensor(name, shape, dt, kind="ExternalInput").ap()
        return dram[name]

    def dout(name, shape, dt=F32):
        dram[name] = nc.dram_tensor(name, shape, dt, kind="ExternalOutput").ap()
        return dram[name]

    def dint(name, shape, dt=BF16):
        dram[name] = nc.dram_tensor(name, shape, dt, kind="Internal").ap()
        return dram[name]

    x_in = din("x", [L, DM])
    xrev_in = din("xrev", [L, DM])
    win = {d: din(f"win_{d}", [DM, DPROJ]) for d in "fb"}
    wout = {d: din(f"wout_{d}", [DIN, DM]) for d in "fb"}
    convw = {d: din(f"convw_{d}", [128, 4 * 17]) for d in "fb"}
    convb = {d: din(f"convb_{d}", [128, 17]) for d in "fb"}
    dtb4 = {d: din(f"dtb4_{d}", [128, 1]) for d in "fb"}
    alog4 = {d: din(f"alog4_{d}", [128, 1]) for d in "fb"}
    drep = {d: din(f"drep_{d}", [128, DIN]) for d in "fb"}
    nwrep = {d: din(f"nwrep_{d}", [128, DIN]) for d in "fb"}
    wo_in = din("wo", [DIN, DM])
    bo_in = din("bo", [128, DM])
    wt_in = din("wt", [DM, DM])
    ffb_in = din("ffb", [1, DM])
    vr_in = din("vr", [DM, 32])
    ct_in = din("ct", [32, DM])
    if32_in = din("if32", [128, 128])
    ibf_in = din("ibf", [128, 128], BF16)

    y_out = dout("y", [L, DM])

    xnh = {"f": dint("xnh", [L, DM]), "b": dint("xnrevh", [L, DM])}
    szh = dint("szh", [L, DIN])
    odh = {"f": dint("odhf", [L, DM]), "b": dint("odhb", [L, DM])}

    dbg = {}
    if debug:
        for nm, shp, dt in [
            ("t_xn", [128, DM], BF16), ("t_sz", [128, DIN], BF16),
            ("t_xsbc", [128, CCH], BF16), ("t_dtsp", [128, 512], F32),
            ("t_a4", [128, 512], F32), ("t_ets", [128, 64], F32),
            ("t_mt", [128, 2048], BF16), ("t_xh", [128, 1024], BF16),
            ("t_q", [128, 1024], BF16), ("t_qh", [128, DIN], BF16),
            ("t_wc", [128, 32], F32), ("t_bw", [128, 1024], BF16),
            ("t_od", [128, DM], BF16), ("t_H", [64, 2048], F32),
            ("t_x2", [128, DM], F32),
        ]:
            dbg[nm] = dout(nm, shp, dt)

    st = contextlib.ExitStack()
    with st:
        sb = lambda name, shape, dt: st.enter_context(nc.sbuf_tensor(name, shape, dt))
        xnfm = sb("xnfm", [128, NKT * L], BF16)
        winsb = sb("winsb", [128, NKT * CCH], BF16)
        windt = sb("windt", [128, NKT * DTC], BF16)
        woutsb = sb("woutsb", [128, 16 * DM], BF16)
        wtsb = sb("wtsb", [128, NKT * DM], BF16)
        wfm = sb("wfm", [128, 4 * 17], F32)
        cbfm = sb("cbfm", [128, 17], F32)
        nwsb = sb("nwsb", [128, DIN], BF16)
        drsb = sb("drsb", [128, DIN], BF16)
        dtbsb = sb("dtbsb", [128, 1], F32)
        alogsb = sb("alogsb", [128, 1], F32)
        acol = sb("acol", [128, 1], F32)
        if32 = sb("if32_s", [128, 128], F32)
        ibf = sb("ibf_s", [128, 128], BF16)
        ones32 = sb("ones32", [1, 128], F32)
        onesbf = sb("onesbf", [1, 128], BF16)
        ssqall = sb("ssqall", [128, 2 * NCH], F32)
        rall = sb("rall", [128, 2 * NCH], F32)
        xsbc1 = sb("xsbc1", [128, CCH], BF16)
        dtsp4 = xsbc1[:, 0:1024].bitcast(F32)
        s4 = xsbc1[:, 1024:2048].bitcast(F32)
        dts = sb("dts", [128, NCH * 96], F32)
        eslr = sb("eslr", [64, NCH * 32], F32)
        clt1 = sb("clt1", [128, 512], F32)
        clt2 = sb("clt2", [128, 512], F32)
        a4 = clt1
        es4 = s4
        xbc = sb("xbc", [128, CCH], BF16)
        xbcw = sb("xbcw", [128, 17 * 131], BF16)
        cpa = sb("cpa", [128, CCH], BF16)
        cpc = sb("cpc", [128, CCH], BF16)
        szb = [sb(f"szb{i}", [128, DIN], BF16) for i in range(2)]
        arep = sb("arep", [128, 1024], F32)
        Ebuf = sb("Ebuf", [128, 2048], BF16)
        xhat = sb("xhat", [128, 1024], BF16)
        bwb = sb("bwb", [128, 1024], BF16)
        wcols = sb("wcols", [128, 32], F32)
        qb = sb("qb", [128, 1024], BF16)
        qhb = sb("qhb", [128, 2048], BF16)
        qfmb = sb("qfmb", [128, 1024], BF16)
        Hm = sb("Hm", [64, 2048], F32)
        Hbf = sb("Hbf", [64, 2048], BF16)
        odb = sb("odb", [128, DM], BF16)
        sbf = sb("sbf", [64, 128], BF16)
        scf = sb("scf", [64, 128], BF16)
        sg = sb("sg", [128, 128], BF16)
        tbuf = sb("tbuf", [128, 1024], BF16)
        ssqt = sb("ssqt", [128, 2], F32)
        stat = sb("stat", [128, 8], F32)
        ceps = sb("ceps", [128, 2], F32)
        nc.const_aps.register(F32, EPS_LN, ceps[:, 0:1])
        nc.const_aps.register(F32, 1.0, ceps[:, 1:2])

        sx = [arep[:], Ebuf[:].bitcast(F32)]
        xn0 = qhb[:, DM:2 * DM]
        odb = tbuf
        xsbc2 = [cpa, xsbc1]
        dxsb = qb

        ps = lambda name, shape: st.enter_context(nc.psum_tensor(name, shape, F32))
        pwin = ps("pwin", [128, 1024])
        psmall = ps("psmall", [128, 512])
        pYA = ps("pYA", [128, 1024])
        pYB = ps("pYB", [128, 1024])
        pQT = ps("pQT", [128, 512])

        SS = lambda name: Sem(st, nc, name)
        sLD = SS("sLD"); sX = SS("sX"); sXN = SS("sXN"); sXNH = SS("sXNH")
        sTR = SS("sTR"); sWG = SS("sWG"); sWE = SS("sWE"); sSZ = SS("sSZ")
        sDT = SS("sDT"); sCV = SS("sCV"); sCD = SS("sCD"); sCS = SS("sCS")
        sBC = SS("sBC"); sBE = SS("sBE"); sGW = SS("sGW"); sGE = SS("sGE")
        sAR = SS("sAR"); sAE = SS("sAE"); sSC = SS("sSC"); sWC = SS("sWC")
        sMT = SS("sMT"); sXH = SS("sXH"); sBW = SS("sBW"); sY = SS("sY")
        sEV = SS("sEV"); sSQ = SS("sSQ"); sST = SS("sST"); sHU = SS("sHU")
        sHB = SS("sHB"); sQT = SS("sQT"); sQE = SS("sQE"); sWO = SS("sWO")
        sOD = SS("sOD"); sSG = SS("sSG"); sBAR = SS("sBAR"); sO3 = SS("sO3")
        sOUT = SS("sOUT"); sIN = SS("sIN"); sFV = SS("sFV"); sFA = SS("sFA")
        sOU = [SS("sOU0"), SS("sOU1")]
        sFT = SS("sFT"); sFE = SS("sFE")

        block = st.enter_context(nc.Block())
        ENGS = ("tensor", "vector", "scalar", "gpsimd", "sync")

        def on(eng_name, fn):
            getattr(block, eng_name)(fn)

        def vfence(v):
            sFV.inc(v.engine_nop())
            v.wait_ge(sFV.h, sFV.n)

        def afence(sc):
            sFA.inc(sc.copy(ceps[0:1, 1:2], ceps[0:1, 1:2]))
            sc.wait_ge(sFA.h, sFA.n)

        def barrier():
            def f(sync):
                wait(sync, sOUT, sOU[0], sOU[1], sSG, sXNH, sSZ, sX, sTR, sLD)
            on("sync", f)
            target = sBAR.n + 5
            for e in ENGS:
                def bf(eng, e=e):
                    sBAR.inc(eng.nop() if e == "sync" else eng.drain())
                on(e, bf)
            for e in ENGS:
                on(e, lambda eng: eng.wait_ge(sBAR.h, target))
            assert sBAR.n == target

        def dbg_dump(name, src_ap, *wait_sems):
            if not debug:
                return
            def f(sync):
                wait(sync, *wait_sems)
                sOUT.inc(sync.dma_start(out=dbg[name], in_=src_ap), 16)
            on("sync", f)

        def f(g):
            g.memset(ceps[:, 0:1], EPS_LN)
            g.memset(ceps[:, 1:2], 1.0)
            g.memset(ones32[:], 1.0)
            sIN.inc(g.memset(onesbf[:], 1.0))
            for kt in range(NKT):
                sLD.inc(g.dma_start(out=winsb[:, kt * CCH:kt * CCH + DIN],
                                    in_=win["f"][kt * 128:(kt + 1) * 128, 0:DIN]), 16)
            sLD.inc(g.dma_start(out=wfm[:], in_=convw["f"]), 16)
            sLD.inc(g.dma_start(out=cbfm[:], in_=convb["f"]), 16)
            sLD.inc(g.dma_start(out=nwsb[:], in_=nwrep["f"]), 16)
            sLD.inc(g.dma_start(out=drsb[:], in_=drep["f"]), 16)
            sLD.inc(g.dma_start(out=dtbsb[:], in_=dtb4["f"]), 16)
            sLD.inc(g.dma_start(out=alogsb[:], in_=alog4["f"]), 16)
            for kt in range(NKT):
                sLD.inc(g.dma_start(out=windt[:, kt * DTC:(kt + 1) * DTC],
                                    in_=win["f"][kt * 128:(kt + 1) * 128, DIN + CCH:DPROJ]), 16)
        on("gpsimd", f)

        def f(sync):
            sTR.inc(sync.dma_start(out=if32[:], in_=if32_in), 16)
            sTR.inc(sync.dma_start(out=ibf[:], in_=ibf_in), 16)
        on("sync", f)

        sumall = dts[:, 0:32]
        sqall = dts[:, 32:64]
        meanall = dts[:, 64:96]
        rstdall = dts[:, 96:128]
        varall = dts[:, 128:160]

        for di, (d, src) in enumerate([("f", x_in), ("b", xrev_in)]):
            for k in range(NCH):
                c = di * NCH + k
                u = c % 2

                def f(sync, k=k, u=u, src=src):
                    if sXN.n >= 2:
                        sync.wait_ge(sXN.h, sXN.n - 1)
                    sX.inc(sync.dma_start(out=sx[u], in_=src[k * Q:(k + 1) * Q, :]), 16)
                on("sync", f)

                def f(v, c=c, u=u):
                    v.wait_ge(sX.h, sX.n)
                    if c == 0:
                        wait(v, sIN)
                    v.scalar_tensor_tensor(tbuf[:], sx[u], 0.0, sx[u],
                                           op0=ALU.add, op1=ALU.bypass,
                                           accum_out=sumall[:, c:c + 1])
                    sXN.inc(v.scalar_tensor_tensor(xn0, sx[u], 0.0, sx[u],
                                                   op0=ALU.add, op1=ALU.mult,
                                                   accum_out=sqall[:, c:c + 1]))
                on("vector", f)

        def f(v):
            wait(v, sXN)
            v.tensor_scalar_mul(meanall[:], sumall[:], 1.0 / DM)
            v.tensor_scalar_mul(varall[:], sqall[:], 1.0 / DM)
            vfence(v)
            v.tensor_tensor(sqall[:], meanall[:], meanall[:], op=ALU.mult)
            vfence(v)
            sXN.inc(v.tensor_tensor(varall[:], varall[:], sqall[:], op=ALU.subtract))
        on("vector", f)

        def f(sc):
            wait(sc, sXN)
            sc.activation(varall[:], varall[:], AF.Ln, bias=EPS_LN)
            afence(sc)
            sSQ.inc(sc.activation(rstdall[:], varall[:], AF.Exp, scale=-0.5))
        on("scalar", f)

        for di, (d, src) in enumerate([("f", x_in), ("b", xrev_in)]):
            for k in range(NCH):
                c = di * NCH + k
                u = c % 2
                xnb = qhb[:, u * DM:(u + 1) * DM]

                def f(sync, k=k, u=u, src=src):
                    if sXN.n >= 2:
                        sync.wait_ge(sXN.h, sXN.n - 1)
                    sX.inc(sync.dma_start(out=sx[u], in_=src[k * Q:(k + 1) * Q, :]), 16)
                on("sync", f)

                def f(v, c=c, u=u, xnb=xnb):
                    v.wait_ge(sX.h, sX.n)
                    if c < 2:
                        wait(v, sSQ)
                    if sXNH.n >= 32:
                        v.wait_ge(sXNH.h, sXNH.n - 16)
                    if debug:
                        wait(v, sOUT)
                    sXN.inc(v.tensor_scalar(xnb, sx[u], meanall[:, c:c + 1],
                                            rstdall[:, c:c + 1],
                                            op0=ALU.subtract, op1=ALU.mult))
                on("vector", f)

                def f(sync, d=d, k=k, xnb=xnb):
                    sync.wait_ge(sXN.h, sXN.n)
                    sXNH.inc(sync.dma_start(out=xnh[d][k * Q:(k + 1) * Q, :], in_=xnb), 16)
                on("sync", f)

                if d == "f" and k == 0:
                    dbg_dump("t_xn", xnb, sXN)

        barrier()

        def load_xnfm(d):
            def f(sync, d=d):
                wait(sync, sXNH)
                for ft in range(NKT):
                    sTR.inc(sync.dma_start_transpose(
                        xnfm[:, ft * L:(ft + 1) * L],
                        xnh[d][:, ft * 128:(ft + 1) * 128]), 16)
            on("sync", f)

        def load_dir_small(d):
            def f(g, d=d):
                sLD.inc(g.dma_start(out=wfm[:], in_=convw[d]), 16)
                sLD.inc(g.dma_start(out=cbfm[:], in_=convb[d]), 16)
                sLD.inc(g.dma_start(out=nwsb[:], in_=nwrep[d]), 16)
                sLD.inc(g.dma_start(out=drsb[:], in_=drep[d]), 16)
                sLD.inc(g.dma_start(out=dtbsb[:], in_=dtb4[d]), 16)
                sLD.inc(g.dma_start(out=alogsb[:], in_=alog4[d]), 16)
                for kt in range(NKT):
                    sLD.inc(g.dma_start(out=windt[:, kt * DTC:(kt + 1) * DTC],
                                        in_=win[d][kt * 128:(kt + 1) * 128, DIN + CCH:DPROJ]), 16)
            on("gpsimd", f)

        def load_win_z(d):
            def f(g, d=d):
                for kt in range(NKT):
                    sLD.inc(g.dma_start(out=winsb[:, kt * CCH:kt * CCH + DIN],
                                        in_=win[d][kt * 128:(kt + 1) * 128, 0:DIN]), 16)
            on("gpsimd", f)

        def load_win_xbc(d):
            def f(g, d=d):
                for kt in range(NKT):
                    sLD.inc(g.dma_start(out=winsb[:, kt * CCH:kt * CCH + CCH],
                                        in_=win[d][kt * 128:(kt + 1) * 128, DIN:DIN + CCH]), 16)
            on("gpsimd", f)

        def load_wout(d):
            def f(g, d=d):
                for j in range(16):
                    sLD.inc(g.dma_start(out=woutsb[:, j * DM:(j + 1) * DM],
                                        in_=wout[d][j * 128:(j + 1) * 128, :]), 16)
            on("gpsimd", f)

        def emit_phA(d, di):
            ld_t = sLD.n
            tr_tt = sTR.n
            tr_t = sTR.n

            def f(t):
                t.wait_ge(sLD.h, ld_t)
                t.wait_ge(sTR.h, tr_t)
                if pha_parts < 1:
                    sDT.inc(t.drain())
                    return
                for q in range(4):
                    for kt in range(NKT):
                        li = t.matmul(pYA[q * 32:(q + 1) * 32, 0:512],
                                      windt[:, kt * DTC:(kt + 1) * DTC],
                                      xnfm[:, kt * L + q * 512:kt * L + (q + 1) * 512],
                                      start=(kt == 0), stop=(kt == NKT - 1),
                                      tile_position=(0, q * 32))
                sDT.inc(li)
            on("tensor", f)

            for k in range(NCH if pha_parts >= 2 else 0):
                def f(t, k=k):
                    for g in range(4):
                        gi = sWG.n
                        if gi >= 2 and pha_parts >= 3:
                            t.wait_ge(sWE.h, gi - 1)
                        for kt in range(NKT):
                            li = t.matmul(pwin[:, (g % 2) * 512:(g % 2 + 1) * 512],
                                          xnfm[:, kt * L + k * Q:kt * L + (k + 1) * Q],
                                          winsb[:, kt * CCH + g * 512:kt * CCH + (g + 1) * 512],
                                          start=(kt == 0), stop=(kt == NKT - 1))
                        sWG.inc(li)
                on("tensor", f)

                if pha_parts < 3:
                    continue

                def f(sc):
                    base = sWG.n - 4
                    wait(sc, sSZ)
                    if debug:
                        wait(sc, sOUT)
                    for g in range(4):
                        sc.wait_ge(sWG.h, base + g + 1)
                        sWE.inc(sc.activation(szb[0][:, g * 512:(g + 1) * 512],
                                              pwin[:, (g % 2) * 512:(g % 2 + 1) * 512],
                                              AF.Silu))
                on("scalar", f)

                if pha_parts < 4:
                    continue

                def f(sync, k=k):
                    wait(sync, sWE)
                    sSZ.inc(sync.dma_start(out=szh[k * Q:(k + 1) * Q, :], in_=szb[0][:]), 16)
                on("sync", f)

                if d == "f" and k == 0:
                    dbg_dump("t_sz", szb[0][:], sWE)

            if pha_parts < 5:
                return
            def f(v):
                wait(v, sDT)
                v.tensor_scalar_add(clt2[:], pYA[:, 0:512], dtbsb[:])
                sDT.inc(v.tensor_scalar(clt1[:], pYA[:, 0:512], dtbsb[:], 0.0,
                                        op0=ALU.add, op1=ALU.max))
            on("vector", f)

            def f(sc):
                wait(sc, sDT)
                sc.activation(s4[:], clt2[:], AF.Abs)
                afence(sc)
                sc.activation(s4[:], s4[:], AF.Exp, scale=-1.0)
                afence(sc)
                sc.activation(s4[:], s4[:], AF.Ln, bias=1.0)
                sDT.inc(sc.activation(acol[:], alogsb[:], AF.Exp))
            on("scalar", f)

            def f(v):
                wait(v, sDT)
                v.tensor_tensor(dtsp4[:], clt1[:], s4[:], op=ALU.add)
                vfence(v)
                sDT.inc(v.tensor_scalar_mul(clt2[:], dtsp4[:], acol[:]))
            on("vector", f)

            def f(sc):
                wait(sc, sDT)
                sDT.inc(sc.activation(a4[:], clt2[:], AF.Exp, scale=-1.0))
            on("scalar", f)

            if pha_parts < 6:
                return
            def f(v):
                wait(v, sDT)
                for k in range(NCH):
                    qq, cc = k // 4, (k % 4) * 128
                    v.tensor_tensor_scan(
                        s4[qq * 32:(qq + 1) * 32, cc:cc + 128],
                        clt2[qq * 32:(qq + 1) * 32, cc:cc + 128],
                        clt2[qq * 32:(qq + 1) * 32, cc:cc + 128],
                        0.0, op0=ALU.add, op1=ALU.bypass)
                sDT.inc(v.engine_nop())
            on("vector", f)

            def f(sc):
                wait(sc, sDT)
                sDT.inc(sc.activation(es4[:], s4[:], AF.Exp, scale=-1.0))
            on("scalar", f)

            if pha_parts < 7:
                return
            for k in range(NCH):
                qq, cc = k // 4, (k % 4) * 128

                def f(t, k=k, qq=qq, cc=cc):
                    wait(t, sDT, sBE)
                    for col, srcm in ((0, dtsp4), (32, es4), (64, a4)):
                        li = t.matmul(psmall[:, col:col + 32],
                                      srcm[qq * 32:(qq + 1) * 32, cc:cc + 128],
                                      if32[qq * 32:(qq + 1) * 32, qq * 32:(qq + 1) * 32],
                                      start=True, stop=True, tile_position=(qq * 32, 0))
                    sBC.inc(li)
                on("tensor", f)

                def f(t, k=k, qq=qq, cc=cc):
                    sBC.inc(t.matmul(psmall[0:1, 96:128],
                                     es4[qq * 32:(qq + 1) * 32, cc + 127:cc + 128],
                                     if32[qq * 32:(qq + 1) * 32, qq * 32:(qq + 1) * 32],
                                     start=True, stop=True, tile_position=(qq * 32, 0)))
                on("tensor", f)

                def f(v, k=k):
                    wait(v, sBC)
                    v.tensor_copy(dts[:, k * 96:k * 96 + 96], psmall[:, 0:96])
                    sBE.inc(v.tensor_copy(wcols[0:1, 0:32], psmall[0:1, 96:128]))
                on("vector", f)

                def f(t, k=k):
                    wait(t, sBE)
                    sBC.inc(t.matmul(psmall[:, 64:96], ones32[:], wcols[0:1, 0:32],
                                     start=True, stop=True))
                on("tensor", f)

                def f(v, k=k):
                    wait(v, sBC)
                    sBE.inc(v.tensor_copy(eslr[:, k * 32:(k + 1) * 32],
                                          psmall[0:64, 64:96]))
                on("vector", f)

            if d == "f":
                dbg_dump("t_dtsp", dtsp4[:], sDT, sBE)
                dbg_dump("t_a4", a4[:], sDT)
                dbg_dump("t_ets", dts[:, 0:64], sBE)

        def emit_phB(d, di):
            ld_t = sLD.n
            tgroups = [(0, 4), (4, 4), (8, 4), (12, 4), (16, 1)]
            gsz = [512, 512, 512, 512, 128]

            def f(g):
                wait(g, sHB, sY)
                g.memset(Hm[:], 0.0)
                g.memset(Hbf[:], 0.0)
                sHB.inc(g.memset(xbcw[:].rearrange("p (c w) -> p c w", c=17)[:, :, 0:3], 0.0))
            on("gpsimd", f)

            def front1(k):
                u = k % 2
                cs = k * Q

                def f(sync, k=k, u=u):
                    wait(sync, sEV)
                    sSZ.inc(sync.dma_start(out=szb[u][:], in_=szh[k * Q:(k + 1) * Q, :]), 16)
                on("sync", f)

                def f(t, cs=cs):
                    t.wait_ge(sLD.h, ld_t)
                    wait(t, sFE)
                    off = 0
                    for g, n in enumerate(gsz):
                        gi = sWG.n
                        if gi >= 2:
                            t.wait_ge(sWE.h, gi - 1)
                        for kt in range(NKT):
                            li = t.matmul(pwin[:, (g % 2) * 512:(g % 2) * 512 + n],
                                          xnfm[:, kt * L + cs:kt * L + cs + Q],
                                          winsb[:, kt * CCH + off:kt * CCH + off + n],
                                          start=(kt == 0), stop=(kt == NKT - 1))
                        sWG.inc(li)
                        off += n
                on("tensor", f)

                def f(v):
                    base = sWG.n - 5
                    wait(v, sFT)
                    off = 0
                    for g, n in enumerate(gsz):
                        v.wait_ge(sWG.h, base + g + 1)
                        sWE.inc(v.tensor_copy(xbc[:, off:off + n],
                                              pwin[:, (g % 2) * 512:(g % 2) * 512 + n]))
                        off += n
                on("vector", f)

                fe_b = sFE.n
                ft_b = sFT.n
                for g2, (r0, rn) in enumerate(tgroups):
                    def f(t, r0=r0, rn=rn, g2=g2, fe_b=fe_b):
                        if g2 == 0:
                            wait(t, sWE)
                        if g2 >= 2:
                            t.wait_ge(sFE.h, fe_b + g2 - 1)
                        for i in range(rn):
                            ct = r0 + i
                            b = (g2 % 2) * 512
                            li = t.matmul(pwin[:, b + i * 128:b + (i + 1) * 128],
                                          xbc[:, ct * 128:(ct + 1) * 128], ibf[:],
                                          start=True, stop=True)
                        sFT.inc(li)
                    on("tensor", f)

                    def f(v, r0=r0, rn=rn, g2=g2, ft_b=ft_b):
                        v.wait_ge(sFT.h, ft_b + g2 + 1)
                        o = xbcw[:].rearrange("p (c w) -> p c w", c=17)[:, r0:r0 + rn, 3:131]
                        b = (g2 % 2) * 512
                        i_ = pwin[:, b:b + rn * 128].rearrange("p (c w) -> p c w", c=rn)
                        sFE.inc(v.tensor_copy(o, i_))
                    on("vector", f)

                def f(v):
                    wait(v, sFT, sHB)
                    w3 = xbcw[:].rearrange("p (c w) -> p c w", c=17)
                    for ct in range(17):
                        v.tensor_scalar(cpc[:, ct * 128:(ct + 1) * 128],
                                        w3[:, ct, 0:128],
                                        wfm[:, ct:ct + 1], cbfm[:, ct:ct + 1],
                                        op0=ALU.mult, op1=ALU.add)
                    for j in (1, 2, 3):
                        vfence(v)
                        for ct in range(17):
                            acc = cpc[:, ct * 128:(ct + 1) * 128]
                            v.scalar_tensor_tensor(acc, w3[:, ct, j:j + 128],
                                                   wfm[:, j * 17 + ct:j * 17 + ct + 1],
                                                   acc, op0=ALU.mult, op1=ALU.add)
                    sCD.inc(v.engine_nop())
                on("vector", f)

                def f(v):
                    wait(v, sCD)
                    w3 = xbcw[:].rearrange("p (c w) -> p c w", c=17)
                    v.tensor_copy(w3[:, :, 0:3], w3[:, :, 128:131])
                on("vector", f)

                def f(sc):
                    wait(sc, sCD)
                    sCS.inc(sc.activation(cpc[:], cpc[:], AF.Silu))
                on("scalar", f)

            def front2(k):
                u = k % 2
                fe_b = sFE.n
                ft_b = sFT.n
                for g2, (r0, rn) in enumerate(tgroups):
                    def f(t, r0=r0, rn=rn, g2=g2, fe_b=fe_b):
                        if g2 == 0:
                            wait(t, sCS)
                            t.wait_ge(sFE.h, fe_b)
                        if g2 >= 2:
                            t.wait_ge(sFE.h, fe_b + g2 - 1)
                        for i in range(rn):
                            ct = r0 + i
                            b = (g2 % 2) * 512
                            li = t.matmul(pwin[:, b + i * 128:b + (i + 1) * 128],
                                          cpc[:, ct * 128:(ct + 1) * 128], ibf[:],
                                          start=True, stop=True)
                        sFT.inc(li)
                    on("tensor", f)

                    def f(v, r0=r0, rn=rn, g2=g2, ft_b=ft_b, u=u):
                        v.wait_ge(sFT.h, ft_b + g2 + 1)
                        if debug:
                            wait(v, sOUT)
                        b = (g2 % 2) * 512
                        sFE.inc(v.tensor_copy(xsbc2[u][:, r0 * 128:(r0 + rn) * 128],
                                              pwin[:, b:b + rn * 128]))
                    on("vector", f)

                if debug and d == "f" and k == 0:
                    dbg_dump("t_xsbc", xsbc2[u][:], sFE)

                def f(t, u=u):
                    wait(t, sFE, sGE, sQE)
                    t.matmul(psmall[0:64, 0:128], xsbc2[u][:, DIN:DIN + 64], ibf[:],
                             start=True, stop=True)
                    sBC.inc(t.matmul(psmall[0:64, 128:256],
                                     xsbc2[u][:, DIN + 64:DIN + 128],
                                     ibf[:], start=True, stop=True))
                on("tensor", f)

                def f(v):
                    wait(v, sBC, sY)
                    v.tensor_copy(sbf[:], psmall[0:64, 0:128])
                    sBE.inc(v.tensor_copy(scf[:], psmall[0:64, 128:256]))
                on("vector", f)

                def f(t):
                    wait(t, sBE)
                    sGW.inc(t.matmul(psmall[:, 256:384], sbf[:], scf[:],
                                     start=True, stop=True))
                on("tensor", f)

                def f(v):
                    wait(v, sGW)
                    sGE.inc(v.tensor_copy(sg[:], psmall[:, 256:384]))
                on("vector", f)

            def mid_half(k, half, u):
                h0 = half * 16

                for piece in range(4):
                    hh0 = h0 + piece * 4

                    def f(t, k=k, hh0=hh0):
                        wait(t, sQE, sAE)
                        for i in range(4):
                            h = hh0 + i
                            acolb = dts[:, k * 96 + 64 + h:k * 96 + 64 + h + 1] \
                                .broadcast_to((128, 128))
                            li = t.matmul(pQT[:, i * 128:(i + 1) * 128], acolb, if32[:],
                                          start=True, stop=True)
                        sAR.inc(li)
                    on("tensor", f)

                    def f(v, piece=piece):
                        wait(v, sAR)
                        sAE.inc(v.tensor_copy(arep[:, (piece % 2) * 512:(piece % 2 + 1) * 512],
                                              pQT[:]))
                    on("vector", f)

                    def f(v, hh0=hh0, piece=piece):
                        wait(v, sAE)
                        if debug:
                            wait(v, sOUT)
                        if piece == 0:
                            wait(v, sY, sST)
                        for i in range(4):
                            h = hh0 + i
                            hh = h % 16
                            v.tensor_tensor_scan(
                                Ebuf[:, hh * 128:(hh + 1) * 128],
                                arep[:, (piece % 2) * 512 + i * 128:
                                     (piece % 2) * 512 + (i + 1) * 128],
                                if32[:], 0.0, op0=ALU.mult, op1=ALU.add)
                        sSC.inc(v.engine_nop())
                    on("vector", f)

                def f(v, h0=h0, u=u):
                    wait(v, sSC, sST)
                    if debug:
                        wait(v, sOUT)
                    e127 = Ebuf[:].rearrange("p (h t) -> p h t", h=16)[:, :, 127]
                    v.tensor_copy(wcols[:, h0:h0 + 16], e127)
                    vfence(v)
                    for i in range(16):
                        h = h0 + i
                        li = v.tensor_scalar_mul(bwb[:, i * 64:(i + 1) * 64],
                                                 xsbc2[u][:, DIN:DIN + 64],
                                                 wcols[:, h:h + 1])
                    sBW.inc(li)
                on("vector", f)

                def f(v):
                    wait(v, sGE)
                    e3 = Ebuf[:].rearrange("p (h t) -> p h t", h=16)
                    gb = sg[:].unsqueeze(1).broadcast_to((128, 16, 128))
                    sMT.inc(v.tensor_tensor(e3, e3, gb, op=ALU.mult))
                on("vector", f)

                if d == "f" and k == 0 and half == 0:
                    dbg_dump("t_mt", Ebuf[:], sMT)

                def f(v, k=k, h0=h0, u=u):
                    wait(v, sFE, sST)
                    if debug:
                        wait(v, sOUT)
                    for i in range(16):
                        h = h0 + i
                        li = v.tensor_scalar_mul(xhat[:, i * 64:(i + 1) * 64],
                                                 xsbc2[u][:, h * 64:(h + 1) * 64],
                                                 dts[:, k * 96 + h:k * 96 + h + 1])
                    sXH.inc(li)
                on("vector", f)

                if d == "f" and k == 0 and half == 1:
                    dbg_dump("t_bw", bwb[:], sBW)
                if d == "f" and k == 0 and half == 0:
                    dbg_dump("t_xh", xhat[:], sXH)

            def back_half(k, half, u):
                h0 = half * 16

                def f(t, h0=h0):
                    wait(t, sMT, sXH, sHB, sEV, sHU, sOD)
                    for i in range(16):
                        t.matmul(pYA[:, i * 64:(i + 1) * 64],
                                 Ebuf[:, i * 128:(i + 1) * 128],
                                 xhat[:, i * 64:(i + 1) * 64], start=True, stop=True)
                    for i in range(16):
                        h = h0 + i
                        li = t.matmul(pYB[:, i * 64:(i + 1) * 64], scf[:],
                                      Hbf[:, h * 64:(h + 1) * 64],
                                      start=True, stop=True)
                    sY.inc(li)
                on("tensor", f)

                def f(v, h0=h0, u=u):
                    wait(v, sFE, sSQ)
                    sEV.inc(v.tensor_tensor(dxsb[:], xsbc2[u][:, h0 * 64:h0 * 64 + 1024],
                                            drsb[:, h0 * 64:h0 * 64 + 1024], op=ALU.mult))
                on("vector", f)

                def f(v, k=k, h0=h0, u=u):
                    wait(v, sY, sSZ, sSG, sEV)
                    if debug:
                        wait(v, sOUT)
                    v.tensor_tensor(tbuf[:], pYA[:], qb[:], op=ALU.add)
                    es = dts[:, k * 96 + 32 + h0:k * 96 + 32 + h0 + 16] \
                        .unsqueeze(-1).broadcast_to((128, 16, 64))
                    pyb3 = pYB[:].rearrange("p (h q) -> p h q", h=16)
                    vfence(v)
                    v.tensor_tensor(qb[:].rearrange("p (h q) -> p h q", h=16),
                                    pyb3, es, op=ALU.mult)
                    vfence(v)
                    v.tensor_tensor(qb[:], qb[:], tbuf[:], op=ALU.add)
                    vfence(v)
                    v.tensor_tensor(qb[:], qb[:], szb[u][:, h0 * 64:h0 * 64 + 1024],
                                    op=ALU.mult)
                    vfence(v)
                    sEV.inc(v.tensor_tensor(qhb[:, h0 * 64:h0 * 64 + 1024], qb[:],
                                            nwsb[:, h0 * 64:h0 * 64 + 1024], op=ALU.mult))
                on("vector", f)

                if d == "f" and k == 0 and half == 1:
                    dbg_dump("t_q", qb[:], sEV)

                def f(v, half=half):
                    wait(v, sEV)
                    if debug:
                        wait(v, sOUT)
                    sSQ.inc(v.scalar_tensor_tensor(qb[:], qb[:], 0.0, qb[:],
                                                   op0=ALU.add, op1=ALU.mult,
                                                   accum_out=ssqt[:, half:half + 1]))
                on("vector", f)

                def f(t, half=half):
                    wait(t, sBW, sEV)
                    tgt = pYA if half == 0 else pYB
                    for i in range(16):
                        li = t.matmul(tgt[0:64, i * 64:(i + 1) * 64],
                                      bwb[:, i * 64:(i + 1) * 64],
                                      xhat[:, i * 64:(i + 1) * 64],
                                      start=True, stop=True)
                    sST.inc(li)
                on("tensor", f)

                def f(v, k=k, h0=h0, half=half):
                    wait(v, sST)
                    if debug:
                        wait(v, sOUT)
                    tgt = pYA if half == 0 else pYB
                    el = eslr[:, k * 32 + h0:k * 32 + h0 + 16].unsqueeze(-1) \
                        .broadcast_to((64, 16, 64))
                    hview = Hm[0:64, h0 * 64:h0 * 64 + 1024] \
                        .rearrange("p (h q) -> p h q", h=16)
                    v.tensor_tensor(hview, hview, el, op=ALU.mult)
                    vfence(v)
                    sHU.inc(v.tensor_tensor(Hm[0:64, h0 * 64:h0 * 64 + 1024],
                                            Hm[0:64, h0 * 64:h0 * 64 + 1024],
                                            tgt[0:64, :], op=ALU.add))
                on("vector", f)

            def tail(k, u):
                def f(v, di=di, k=k):
                    wait(v, sSQ)
                    sEV.inc(v.tensor_tensor(ssqall[:, di * NCH + k:di * NCH + k + 1],
                                            ssqt[:, 0:1], ssqt[:, 1:2], op=ALU.add))
                on("vector", f)

                def f(v):
                    wait(v, sHU)
                    sHB.inc(v.tensor_copy(Hbf[:], Hm[:]))
                on("vector", f)

                if d == "f" and k == 0:
                    dbg_dump("t_H", Hm[:], sHU)
                    dbg_dump("t_qh", qhb[:], sEV)

                qt_b = sQT.n
                qe_b = sQE.n
                for r in range(4):
                    pb = pQT if r % 2 == 0 else psmall
                    def f(t, r=r, pb=pb, qe_b=qe_b):
                        if r == 0:
                            wait(t, sEV, sQE, sAE, sGE)
                        if r >= 2:
                            t.wait_ge(sQE.h, qe_b + r - 1)
                        for i in range(4):
                            j = r * 4 + i
                            li = t.matmul(pb[:, i * 128:(i + 1) * 128],
                                          qhb[:, j * 128:(j + 1) * 128], ibf[:],
                                          start=True, stop=True)
                        sQT.inc(li)
                    on("tensor", f)

                    def f(v, r=r, pb=pb, qt_b=qt_b):
                        v.wait_ge(sQT.h, qt_b + r + 1)
                        sQE.inc(v.tensor_copy(qfmb[:, (r % 2) * 512:(r % 2 + 1) * 512],
                                              pb[:]))
                    on("vector", f)

                    def f(t, r=r, qe_b=qe_b):
                        t.wait_ge(sQE.h, qe_b + r + 1)
                        if r == 0:
                            wait(t, sOD, sHU)
                        for i in range(4):
                            j = r * 4 + i
                            for nn in range(2):
                                li = t.matmul(pYB[:, nn * 512:(nn + 1) * 512],
                                              qfmb[:, (r % 2) * 512 + i * 128:(r % 2) * 512 + (i + 1) * 128],
                                              woutsb[:, j * DM + nn * 512:j * DM + (nn + 1) * 512],
                                              start=(j == 0), stop=(j == 15))
                        sWO.inc(li)
                    on("tensor", f)

                def f(v):
                    wait(v, sWO)
                    if debug:
                        wait(v, sOUT)
                    sOD.inc(v.tensor_copy(odb[:], pYB[:]))
                on("vector", f)

                def f(sync, k=k):
                    wait(sync, sOD)
                    sSG.inc(sync.dma_start(out=odh[d][k * Q:(k + 1) * Q, :], in_=odb[:]), 16)
                on("sync", f)

                if d == "f" and k == 0:
                    dbg_dump("t_od", odb[:], sOD)

            front1(0)
            front2(0)
            for k in range(NCH):
                u = k % 2
                mid_half(k, 0, u)
                if k + 1 < NCH:
                    front1(k + 1)
                back_half(k, 0, u)
                mid_half(k, 1, u)
                if k + 1 < NCH:
                    front2(k + 1)
                back_half(k, 1, u)
                tail(k, u)

        dirs = []
        if stages >= 2:
            dirs.append(("f", 0))
        if stages >= 4:
            dirs.append(("b", 1))
        for d, di in dirs:
            if d == "b":
                load_dir_small(d)
                load_win_z(d)
            load_xnfm(d)
            barrier()
            emit_phA(d, di)
            barrier()
            if stages >= 3 or d == "b":
                load_win_xbc(d)
                load_wout(d)
                barrier()
                emit_phB(d, di)
            barrier()

        if stages < 5:
            def f(sync):
                wait(sync, sOUT, sOU[0], sOU[1], sSG, sXNH, sSZ, sLD, sTR, sX)
            on("sync", f)
            return nc
        outfm = winsb
        wosb = woutsb

        def f(g):
            wait(g, sSG)
            for j in range(16):
                sLD.inc(g.dma_start(out=wosb[:, j * DM:(j + 1) * DM],
                                    in_=wo_in[j * 128:(j + 1) * 128, :]), 16)
            for j in range(NKT):
                sLD.inc(g.dma_start(out=wtsb[:, j * DM:(j + 1) * DM],
                                    in_=wt_in[j * 128:(j + 1) * 128, :]), 16)
            for kt in range(NKT):
                sLD.inc(g.dma_start(out=nwsb[:, kt * 32:(kt + 1) * 32],
                                    in_=vr_in[kt * 128:(kt + 1) * 128, :]), 16)
            sLD.inc(g.dma_start(out=drsb[0:32, 0:1024], in_=ct_in), 16)
            sLD.inc(g.dma_start(out=szb[0][:, 0:DM], in_=bo_in), 16)
            sLD.inc(g.dma_start(out=bwb[0:1, 0:1024], in_=ffb_in), 16)
        on("gpsimd", f)

        def f(sync):
            wait(sync, sSG)
            for ft in range(NKT):
                sTR.inc(sync.dma_start_transpose(
                    outfm[:, ft * 2048:(ft + 1) * 2048],
                    odh["f"][:, ft * 128:(ft + 1) * 128]), 16)
        on("sync", f)

        def f(sc):
            wait(sc, sEV)
            sc.activation(rall[:], ssqall[:], AF.Ln, bias=EPS_RMS, scale=1.0 / DIN)
            afence(sc)
            sO3.inc(sc.activation(rall[:], rall[:], AF.Exp, scale=-0.5))
        on("scalar", f)

        barrier()

        ph3_ld = sLD.n
        ph3_tr = sTR.n

        for k in range(NCH):
            u = k % 2
            kb = NCH - 1 - k

            def f(sync, k=k, u=u, kb=kb):
                wait(sync, sOU[u])
                if sQT.n >= 2:
                    sync.wait_ge(sQT.h, sQT.n - 2)
                sX.inc(sync.dma_start(out=sx[u], in_=x_in[k * Q:(k + 1) * Q, :]), 16)
                sX.inc(sync.dma_start(out=odb[:], in_=odh["b"][kb * Q:(kb + 1) * Q, :]), 16)
            on("sync", f)

            def f(v, kb=kb):
                wait(v, sX, sO3)
                sO3.inc(v.tensor_scalar_mul(odb[:], odb[:],
                                            rall[:, NCH + kb:NCH + kb + 1]))
            on("vector", f)

            for r in range(2):
                def f(t, r=r):
                    wait(t, sO3, sQE)
                    t.wait_ge(sLD.h, ph3_ld)
                    t.wait_ge(sTR.h, ph3_tr)
                    for i in range(4):
                        j = r * 4 + i
                        li = t.matmul(pQT[:, i * 128:(i + 1) * 128],
                                      odb[:, j * 128:(j + 1) * 128], ibf[:, 127::-1],
                                      start=True, stop=True)
                    sQT.inc(li)
                on("tensor", f)

                def f(v, r=r):
                    wait(v, sQT)
                    sQE.inc(v.tensor_copy(qhb[:, r * 512:(r + 1) * 512], pQT[:]))
                on("vector", f)

            def f(t, k=k):
                wait(t, sQE, sO3)
                for mt in range(8):
                    for nn in range(2):
                        t.matmul(pYA[:, nn * 512:(nn + 1) * 512],
                                 outfm[:, mt * 2048 + k * Q:mt * 2048 + (k + 1) * Q],
                                 wosb[:, mt * DM + nn * 512:mt * DM + (nn + 1) * 512],
                                 start=(mt == 0), stop=(mt == 7))
                for mt in range(8):
                    for nn in range(2):
                        li = t.matmul(pYB[:, nn * 512:(nn + 1) * 512],
                                      qhb[:, mt * 128:(mt + 1) * 128],
                                      wosb[:, (8 + mt) * DM + nn * 512:(8 + mt) * DM + (nn + 1) * 512],
                                      start=(mt == 0), stop=(mt == 7))
                sWO.inc(li)
            on("tensor", f)

            def f(v, k=k, u=u, kb=kb):
                wait(v, sWO)
                if debug:
                    wait(v, sOUT)
                v.scalar_tensor_tensor(sx[u], pYA[:], rall[:, k:k + 1], sx[u],
                                       op0=ALU.mult, op1=ALU.add)
                vfence(v)
                v.tensor_tensor(sx[u], pYB[:], sx[u], op=ALU.add)
                vfence(v)
                sO3.inc(v.tensor_tensor(sx[u], szb[0][:, 0:DM], sx[u], op=ALU.add))
            on("vector", f)

            if k == 0:
                dbg_dump("t_x2", sx[u], sO3)

            def f(v, u=u):
                wait(v, sO3)
                if debug:
                    wait(v, sOUT)
                v.scalar_tensor_tensor(qb[:], sx[u], 0.0, sx[u],
                                       op0=ALU.add, op1=ALU.bypass,
                                       accum_out=stat[:, 0:1])
                sSQ.inc(v.scalar_tensor_tensor(xn0, sx[u], 0.0, sx[u],
                                               op0=ALU.add, op1=ALU.mult,
                                               accum_out=stat[:, 1:2]))
            on("vector", f)

            def f(v):
                wait(v, sSQ)
                v.tensor_scalar_mul(stat[:, 2:3], stat[:, 0:1], 1.0 / DM)
                v.tensor_scalar_mul(stat[:, 3:4], stat[:, 1:2], 1.0 / DM)
                v.tensor_scalar(stat[:, 4:5], stat[:, 0:1], stat[:, 0:1],
                                1.0 / (DM * DM), op0=ALU.mult, op1=ALU.mult)
                vfence(v)
                sO3.inc(v.tensor_tensor(stat[:, 5:6], stat[:, 3:4], stat[:, 4:5],
                                        op=ALU.subtract))
            on("vector", f)

            def f(sc):
                wait(sc, sO3)
                sc.activation(stat[:, 6:7], stat[:, 5:6], AF.Ln, bias=EPS_LN)
                afence(sc)
                sSQ.inc(sc.activation(stat[:, 7:8], stat[:, 6:7], AF.Exp, scale=-0.5))
            on("scalar", f)

            def f(v, u=u):
                wait(v, sSQ, sQT)
                sO3.inc(v.tensor_scalar(xn0, sx[u], stat[:, 2:3], stat[:, 7:8],
                                        op0=ALU.subtract, op1=ALU.mult))
            on("vector", f)

            for r in range(2):
                def f(t, r=r):
                    wait(t, sO3, sQE)
                    for i in range(4):
                        j = r * 4 + i
                        li = t.matmul(pQT[:, i * 128:(i + 1) * 128],
                                      qhb[:, DM + j * 128:DM + (j + 1) * 128], ibf[:],
                                      start=True, stop=True)
                    sQT.inc(li)
                on("tensor", f)

                def f(v, r=r):
                    wait(v, sQT)
                    sQE.inc(v.tensor_copy(qb[:, r * 512:(r + 1) * 512], pQT[:]))
                on("vector", f)

            def f(t):
                wait(t, sQE, sBE)
                for kt in range(NKT):
                    li = t.matmul(psmall[0:32, 0:128],
                                  nwsb[:, kt * 32:(kt + 1) * 32],
                                  qb[:, kt * 128:(kt + 1) * 128],
                                  start=(kt == 0), stop=(kt == NKT - 1))
                sBC.inc(li)
            on("tensor", f)

            def f(v):
                wait(v, sBC)
                sBE.inc(v.tensor_copy(scf[0:32, 0:128], psmall[0:32, 0:128]))
            on("vector", f)

            def f(t):
                wait(t, sBE, sO3)
                for kt in range(NKT):
                    for nn in range(2):
                        t.matmul(pYA[:, nn * 512:(nn + 1) * 512],
                                 qb[:, kt * 128:(kt + 1) * 128],
                                 wtsb[:, kt * DM + nn * 512:kt * DM + (nn + 1) * 512],
                                 start=(kt == 0), stop=False)
                for nn in range(2):
                    t.matmul(pYA[:, nn * 512:(nn + 1) * 512], scf[0:32, 0:128],
                             drsb[0:32, nn * 512:(nn + 1) * 512],
                             start=False, stop=False)
                for nn in range(2):
                    li = t.matmul(pYA[:, nn * 512:(nn + 1) * 512], onesbf[:],
                                  bwb[0:1, nn * 512:(nn + 1) * 512],
                                  start=False, stop=True)
                sWO.inc(li)
            on("tensor", f)

            def f(v, u=u):
                wait(v, sWO)
                sO3.inc(v.tensor_tensor(sx[u], pYA[:], sx[u], op=ALU.add))
            on("vector", f)

            def f(sync, k=k, u=u):
                wait(sync, sO3)
                sOU[u].inc(sync.dma_start(out=y_out[k * Q:(k + 1) * Q, :], in_=sx[u]), 16)
            on("sync", f)

        def f(sync):
            wait(sync, sOUT, sOU[0], sOU[1])
        on("sync", f)

    return nc


def prep_host(inputs):
    g = lambda k: np.ascontiguousarray(np.asarray(inputs[k], dtype=np.float32))
    m = {}
    for d in "fb":
        m[f"win_{d}"] = g(f"L0_attn_{d}Win")
        m[f"wout_{d}"] = g(f"L0_attn_{d}Wout")
        cw = g(f"L0_attn_{d}convw")[:, 0, :]
        m[f"convw_{d}"] = np.ascontiguousarray(
            cw.reshape(4, 17, 128).transpose(2, 0, 1).reshape(128, 68))
        m[f"convb_{d}"] = np.ascontiguousarray(
            g(f"L0_attn_{d}convb").reshape(17, 128).T)
        m[f"dtb4_{d}"] = np.ascontiguousarray(np.tile(g(f"L0_attn_{d}dtb"), 4)[:, None])
        m[f"alog4_{d}"] = np.ascontiguousarray(np.tile(g(f"L0_attn_{d}Alog"), 4)[:, None])
        m[f"drep_{d}"] = np.ascontiguousarray(np.broadcast_to(
            np.repeat(g(f"L0_attn_{d}D"), P)[None, :], (128, DIN)))
        m[f"nwrep_{d}"] = np.ascontiguousarray(np.broadcast_to(
            g(f"L0_attn_{d}normw")[None, :], (128, DIN)))
    m["wo"] = g("L0_attn_Wo")
    m["bo"] = np.ascontiguousarray(np.broadcast_to(g("L0_attn_bo")[None, :], (128, DM)))
    m["wt"] = np.ascontiguousarray(g("L0_ffn_W").T)
    m["ffb"] = g("L0_ffn_b")[None, :]
    m["vr"] = g("L0_ffn_V_r")
    m["ct"] = np.ascontiguousarray(g("L0_ffn_C").T)
    m["if32"] = np.eye(128, dtype=np.float32)
    m["ibf"] = np.eye(128, dtype=ml_dtypes.bfloat16)
    return m


def kernel(**inputs):
    nc = build(debug=False)
    shared = prep_host(inputs)
    x = np.ascontiguousarray(np.asarray(inputs["L0_attn_x"], dtype=np.float32))
    in_maps = []
    for b in range(8):
        im = dict(shared)
        im["x"] = np.ascontiguousarray(x[b])
        im["xrev"] = np.ascontiguousarray(x[b][::-1])
        in_maps.append(im)
    res = run_bass_kernel_spmd(nc, in_maps, core_ids=list(range(8)))
    out = np.stack([r["y"] for r in res.results], axis=0)
    return out.astype(np.float32)
